# version 03
# baseline (speedup 1.0000x reference)
import numpy as np
import ml_dtypes
import concourse.bass as bass
import concourse.mybir as mybir
from concourse.bass_utils import run_bass_kernel_spmd

F32 = mybir.dt.float32
BF16 = mybir.dt.bfloat16
FP16 = mybir.dt.float16
AF = mybir.ActivationFunctionType
ALU = mybir.AluOpType
AX = mybir.AxisListType

NCORES = 8
C = 1024
NCH = 8
B = 4
EPS = 1e-5
BFD = np.dtype(ml_dtypes.bfloat16)

ENGINES = ("sync", "gpsimd", "vector", "scalar", "tensor")


def _lst(x):
    if x is None:
        return []
    if isinstance(x, list):
        return [t for t in x if t is not None]
    return [x]


class Cfg:
    def __init__(self, T=4096):
        self.T = T
        self.L = T // 16
        self.SEGS = 2 * B
        self.COLS = self.SEGS * self.L
        self.CA = self.COLS + self.SEGS


class Em:
    def __init__(self, nc):
        self.nc = nc
        self.streams = {e: [] for e in ENGINES}
        self.sems = {}
        self.ctr = {}

    def _sem(self, name):
        if name not in self.sems:
            self.sems[name] = self.nc.alloc_semaphore("sem_" + name)
            self.ctr[name] = 0
        return self.sems[name]

    def _emit(self, eng, fn, deps, semname, n):
        self._sem(semname)
        self.ctr[semname] += n
        val = self.ctr[semname]
        self.streams[eng].append((fn, tuple(_lst(list(deps))), semname, n))
        return (semname, val)

    def v(self, fn, deps=()):
        return self._emit("vector", fn, deps, "v", 1)

    def a(self, fn, deps=()):
        return self._emit("scalar", fn, deps, "a", 1)

    def t(self, fn, deps=()):
        return self._emit("tensor", fn, deps, "t", 1)

    def p(self, fn, deps=()):
        return self._emit("gpsimd", fn, deps, "p", 1)

    def dsync(self, out, in_, deps=(), sem="sq"):
        return self._emit("sync", lambda e: e.dma_start(out=out, in_=in_), deps, sem, 16)

    def dscal(self, out, in_, deps=(), sem="adq"):
        return self._emit("scalar", lambda e: e.dma_start(out=out, in_=in_), deps, sem, 16)

    def dgps(self, out, in_, deps=(), sem="gq"):
        return self._emit("gpsimd", lambda e: e.dma_start(out=out, in_=in_), deps, sem, 16)

    def barrier(self):
        return [(s, self.ctr[s]) for s in self.ctr]

    def finalize(self, final_deps):
        nc = self.nc
        streams, sems = self.streams, self.sems

        def section(eng):
            def run(e):
                last = {}
                for fn, deps, semname, n in streams[eng]:
                    for (dsem, dval) in deps:
                        if last.get(dsem, 0) < dval:
                            e.wait_ge(sems[dsem], dval)
                            last[dsem] = dval
                    fn(e).then_inc(sems[semname], n)
                if eng == "sync":
                    for (dsem, dval) in final_deps:
                        if last.get(dsem, 0) < dval:
                            e.wait_ge(sems[dsem], dval)
                            last[dsem] = dval
            return run

        with nc.Block() as blk:
            blk.sync(section("sync"))
            blk.gpsimd(section("gpsimd"))
            blk.vector(section("vector"))
            blk.scalar(section("scalar"))
            blk.tensor(section("tensor"))



def seg_starts_global(cfg, core):
    out = []
    for b in range(B):
        for blkid in (core, 15 - core):
            out.append((b, blkid * cfg.L))
    return out


def shard_cols(cfg, core, arr_btC):
    cols = []
    for b in range(B):
        for blkid in (core, 15 - core):
            cols.append(arr_btC[b, blkid * cfg.L:(blkid + 1) * cfg.L, :])
    return np.ascontiguousarray(np.concatenate(cols, axis=0).T)


def unshard_cols(cfg, shards):
    T = cfg.T
    CC = shards[0].shape[0]
    out = np.empty((B, T, CC), dtype=shards[0].dtype)
    for core in range(NCORES):
        s = shards[core]
        i = 0
        for b in range(B):
            for blkid in (core, 15 - core):
                out[b, blkid * cfg.L:(blkid + 1) * cfg.L, :] = s[:, i * cfg.L:(i + 1) * cfg.L].T
                i += 1
    return out


def halo_of(cfg, core, arr_btC):
    CC = arr_btC.shape[2]
    h = np.zeros((CC, cfg.SEGS), dtype=np.float32)
    m = np.zeros((1, cfg.SEGS), dtype=np.float32)
    for i, (b, t0) in enumerate(seg_starts_global(cfg, core)):
        if t0 > 0:
            h[:, i] = arr_btC[b, t0 - 1, :]
            m[0, i] = 1.0
    return h, m


def chunked(vec):
    return np.ascontiguousarray(np.asarray(vec, np.float32).reshape(NCH, 128).T)


def fm_dram_ap(t, X):
    return t.ap().rearrange("(a p) x -> p a x", p=128)


def col_tiles(n, w=512):
    return [(i, min(w, n - i)) for i in range(0, n, w)]



def emit_ln(em, st, src_ap, dst_ap, ncols, mean=True, gamma=None, beta=None,
            src_dep=None, dst_war=None):
    pb, rows, sq, atile = st["pb"], st["rows"], st["sq"], st["atile"]
    ones_col, ones_row, dep_ones = st["ones_col"], st["ones_row"], st["dep_ones"]
    src_dep = _lst(src_dep)
    dst_war = _lst(dst_war)
    out_toks = []
    for (c0, cw) in col_tiles(ncols):
        sq_free = [st.get("sq0_free"), st.get("sq1_free")]
        tmm1 = tmm2 = None
        for ch in range(NCH):
            pp = ch % 2
            tsq = em.a(lambda e, ch=ch, c0=c0, cw=cw, pp=pp: e.activation(
                out=sq[pp][:, 0:cw], in_=src_ap(ch, c0, cw), func=AF.Square),
                deps=src_dep + [sq_free[pp]])
            if mean:
                tmm1 = em.t(lambda e, ch=ch, c0=c0, cw=cw: e.matmul(
                    out=pb["s1"][0:1, 0:cw], lhsT=ones_col[:, 0:1], rhs=src_ap(ch, c0, cw),
                    start=(ch == 0), stop=(ch == NCH - 1)),
                    deps=([dep_ones, st.get("pb_s1_free")] + src_dep) if ch == 0 else [])
            tmm2 = em.t(lambda e, cw=cw, pp=pp, ch=ch: e.matmul(
                out=pb["s2"][0:1, 0:cw], lhsT=ones_col[:, 0:1], rhs=sq[pp][:, 0:cw],
                start=(ch == 0), stop=(ch == NCH - 1)),
                deps=[dep_ones, tsq, st.get("pb_s2_free")] if ch == 0 else [tsq])
            sq_free[pp] = tmm2
        st["sq0_free"], st["sq1_free"] = sq_free
        r_row = rows[0:1, 0:cw]
        mr_row = rows[0:1, 512:512 + cw]
        tmp_row = rows[0:1, 1024:1024 + cw]
        t2_row = rows[0:1, 1536:1536 + cw]
        rows_free = st.get("rows_free")
        tk1 = em.v(lambda e, cw=cw, tmp_row=tmp_row: e.tensor_scalar_mul(
            out=tmp_row, in0=pb["s2"][0:1, 0:cw], scalar1=1.0 / C),
            deps=[tmm2, rows_free])
        tkm = None
        if mean:
            tkm = em.v(lambda e, cw=cw, mr_row=mr_row: e.tensor_scalar_mul(
                out=mr_row, in0=pb["s1"][0:1, 0:cw], scalar1=1.0 / C),
                deps=[tmm1, rows_free])
            tkn = em.v(lambda e, cw=cw, t2_row=t2_row, mr_row=mr_row: e.scalar_tensor_tensor(
                out=t2_row, in0=mr_row, scalar=-1.0, in1=mr_row,
                op0=ALU.mult, op1=ALU.mult), deps=[tkm, rows_free])
            tk1 = em.v(lambda e, cw=cw, tmp_row=tmp_row, t2_row=t2_row: e.tensor_add(
                out=tmp_row, in0=tmp_row, in1=t2_row), deps=[tk1, tkn])
        tk2 = em.a(lambda e, cw=cw, tmp_row=tmp_row: e.activation(
            out=tmp_row, in_=tmp_row, func=AF.Sqrt, bias=st["eps_col"][0:1, 0:1]),
            deps=[tk1, st["dep_eps"]])
        tk3 = em.v(lambda e, cw=cw, r_row=r_row, tmp_row=tmp_row: e.reciprocal(out=r_row, in_=tmp_row), deps=[tk2])
        st["pb_s2_free"] = tk1
        tk4 = None
        if mean:
            st["pb_s1_free"] = tkm
            tk4 = em.v(lambda e, cw=cw, mr_row=mr_row, r_row=r_row: e.tensor_mul(
                out=mr_row, in0=mr_row, in1=r_row), deps=[tk3])
        tb1 = em.t(lambda e, cw=cw, r_row=r_row: e.matmul(
            out=pb["b1"][:, 0:cw], lhsT=ones_row[0:1, :], rhs=r_row,
            start=True, stop=True), deps=[tk3, dep_ones, st.get("pb_b1_free")])
        tb2 = None
        if mean:
            tb2 = em.t(lambda e, cw=cw, mr_row=mr_row: e.matmul(
                out=pb["b2"][:, 0:cw], lhsT=ones_row[0:1, :], rhs=mr_row,
                start=True, stop=True), deps=[tk4, dep_ones, st.get("pb_b2_free")])
        st["rows_free"] = tb2 if mean else tb1
        last_ap = []
        for ch in range(NCH):
            pp = ch % 2
            at_free = st.get(f"atile{pp}_free")
            cur = em.v(lambda e, ch=ch, c0=c0, cw=cw, pp=pp: e.tensor_mul(
                out=atile[pp][:, 0:cw], in0=src_ap(ch, c0, cw), in1=pb["b1"][:, 0:cw]),
                deps=[tb1, at_free] + src_dep)
            if mean and gamma is not None:
                cur = em.v(lambda e, cw=cw, pp=pp: e.tensor_sub(
                    out=atile[pp][:, 0:cw], in0=atile[pp][:, 0:cw], in1=pb["b2"][:, 0:cw]),
                    deps=[tb2, cur])
                cur = em.v(lambda e, ch=ch, c0=c0, cw=cw, pp=pp: e.tensor_scalar(
                    out=dst_ap(ch, c0, cw), in0=atile[pp][:, 0:cw],
                    scalar1=gamma[0][:, ch:ch + 1], scalar2=beta[0][:, ch:ch + 1],
                    op0=ALU.mult, op1=ALU.add), deps=[cur, gamma[1]] + dst_war)
            elif mean:
                cur = em.v(lambda e, ch=ch, c0=c0, cw=cw, pp=pp: e.tensor_sub(
                    out=dst_ap(ch, c0, cw), in0=atile[pp][:, 0:cw], in1=pb["b2"][:, 0:cw]),
                    deps=[tb2, cur] + dst_war)
            elif gamma is not None:
                cur = em.v(lambda e, ch=ch, c0=c0, cw=cw, pp=pp: e.tensor_scalar(
                    out=dst_ap(ch, c0, cw), in0=atile[pp][:, 0:cw],
                    scalar1=gamma[0][:, ch:ch + 1], scalar2=beta[0][:, ch:ch + 1],
                    op0=ALU.mult, op1=ALU.add), deps=[cur, gamma[1]] + dst_war)
            else:
                cur = em.v(lambda e, ch=ch, c0=c0, cw=cw, pp=pp: e.tensor_copy(
                    out=dst_ap(ch, c0, cw), in_=atile[pp][:, 0:cw]),
                    deps=[cur] + dst_war)
            st[f"atile{pp}_free"] = cur
            last_ap.append(cur)
        st["pb_b1_free"] = last_ap[-1]
        if mean:
            st["pb_b2_free"] = last_ap[-1]
        out_toks.extend(last_ap)
    return out_toks


def make_state(em, nc, tag="", rows=None):
    pb = {k: nc.alloc_psum_tensor(f"pb_{k}{tag}", [128, 512], F32)[:]
          for k in ("s1", "s2", "b1", "b2", "m0", "m1", "m2", "m3")}
    if rows is None:
        rows = nc.alloc_sbuf_tensor(f"rows{tag}", [1, 2048], F32)[:]
    sq = [nc.alloc_sbuf_tensor(f"sq{i}{tag}", [128, 512], F32)[:] for i in range(2)]
    atile = [nc.alloc_sbuf_tensor(f"atile{i}{tag}", [128, 512], F32)[:] for i in range(2)]
    ones_col = nc.alloc_sbuf_tensor(f"ones_col{tag}", [128, 1], F32)[:]
    ones_row = nc.alloc_sbuf_tensor(f"ones_row{tag}", [1, 128], F32)[:]
    eps_col = nc.alloc_sbuf_tensor(f"eps_col{tag}", [128, 1], F32)[:]
    t1 = em.v(lambda e: e.memset(ones_col, 1.0))
    t2 = em.v(lambda e: e.memset(ones_row, 1.0))
    t3 = em.v(lambda e: e.memset(eps_col, float(EPS)))
    return dict(pb=pb, rows=rows, sq=sq, atile=atile, ones_col=ones_col,
                ones_row=ones_row, dep_ones=t2, eps_col=eps_col, dep_eps=t3)



def build_launch1(cfg, projs=(("kwT", "tmk"), ("vwT", "tmv"), ("rwT", "tmr"))):
    nc = bass.Bass("TRN2", target_bir_lowering=False, debug=False)
    em = Em(nc)
    CA, COLS, SEGS, L = cfg.CA, cfg.COLS, cfg.SEGS, cfg.L

    xin = nc.dram_tensor("xin", [C, CA], F32, kind="ExternalInput")
    maskin = nc.dram_tensor("maskin", [1, SEGS], F32, kind="ExternalInput").ap()
    prm_names = ("ln1w", "ln1b", "tmk", "tmv", "tmr")
    prm = {nm: nc.dram_tensor(nm, [128, NCH], F32, kind="ExternalInput").ap()
           for nm in prm_names}
    wts = {nm: nc.dram_tensor(nm, [C, C], F32, kind="ExternalInput")
           for nm in ("kwT", "vwT", "rwT")}
    xn_o = nc.dram_tensor("xn_o", [C, COLS], F32, kind="ExternalOutput")
    k_o = nc.dram_tensor("k_o", [C, COLS], F32, kind="ExternalOutput")
    v_o = nc.dram_tensor("v_o", [C, COLS], F32, kind="ExternalOutput")
    sr_o = nc.dram_tensor("sr_o", [C, COLS], F32, kind="ExternalOutput")
    proj_out = {"kwT": k_o, "vwT": v_o, "rwT": sr_o}

    bufA = nc.alloc_sbuf_tensor("bufA", [128, NCH * CA], F32)[:]
    bufB = nc.alloc_sbuf_tensor("bufB", [128, NCH * CA], F32)[:]
    mixb = nc.alloc_sbuf_tensor("mixb", [128, NCH * COLS], BF16)[:]
    un0 = nc.alloc_sbuf_tensor("un0", [128, max(COLS, 2048)], F32)
    t1b = un0[:]
    wbuf = nc.alloc_sbuf_tensor("wbuf", [128, NCH * C], BF16)[:]
    hm = nc.alloc_sbuf_tensor("hm", [128, NCH * SEGS], F32)[:]
    maskB = nc.alloc_sbuf_tensor("maskB", [128, SEGS], F32)[:]
    mrow = nc.alloc_sbuf_tensor("mrow", [1, SEGS], F32)[:]
    tm1 = {nm: nc.alloc_sbuf_tensor(f"tm1_{nm}", [128, NCH], F32)[:]
           for nm in ("tmk", "tmv", "tmr")}
    prms = {nm: nc.alloc_sbuf_tensor("s_" + nm, [128, NCH], F32)[:] for nm in prm_names}
    evac = [nc.alloc_sbuf_tensor(f"evac{i}", [128, 512], F32)[:] for i in range(4)]

    st = make_state(em, nc, rows=un0[:][0:1, 0:2048])
    pb = st["pb"]

    tx = em.dsync(bufA.rearrange("p (a x) -> p a x", a=NCH), fm_dram_ap(xin, CA), sem="sqx")
    for nm in prm_names:
        em.dsync(prms[nm], prm[nm], sem="sqp")
    tp_all = em.dsync(mrow, maskin, sem="sqp")
    tprm = {nm: tp_all for nm in prm_names}
    tmask = tp_all
    tm1_toks = {}
    for nm in ("tmk", "tmv", "tmr"):
        tm1_toks[nm] = em.v(lambda e, nm=nm: e.tensor_scalar(
            out=tm1[nm], in0=prms[nm], scalar1=-1.0, scalar2=1.0,
            op0=ALU.mult, op1=ALU.add), deps=[tprm[nm]])
    tmb0 = em.t(lambda e: e.matmul(out=pb["b2"][:, 0:SEGS], lhsT=st["ones_row"][0:1, :],
                                   rhs=mrow, start=True, stop=True),
                deps=[tmask, st["dep_ones"]])
    tmb = em.v(lambda e: e.tensor_copy(out=maskB, in_=pb["b2"][:, 0:SEGS]), deps=[tmb0])
    st["pb_b2_free"] = tmb

    def a_ap(ch, c0, cw):
        return bufA[:, ch * CA + c0: ch * CA + c0 + cw]

    def b_ap(ch, c0, cw):
        return bufB[:, ch * CA + c0: ch * CA + c0 + cw]

    emit_ln(em, st, a_ap, b_ap, CA, mean=True, src_dep=[tx])
    bar_xn = em.barrier()
    t_xn_store = em.dsync(fm_dram_ap(xn_o, COLS),
                          bufB.rearrange("p (a x) -> p a x", a=NCH)[:, :, 0:COLS],
                          deps=bar_xn, sem="sqxn")
    emit_ln(em, st, b_ap, a_ap, CA, mean=False,
            gamma=(prms["ln1w"], tprm["ln1w"]), beta=(prms["ln1b"], tprm["ln1b"]),
            src_dep=bar_xn, dst_war=bar_xn)
    bar_h = em.barrier()
    hm_toks = []
    for ch in range(NCH):
        hm_toks.append(em.v(lambda e, ch=ch: e.tensor_mul(
            out=hm[:, ch * SEGS:(ch + 1) * SEGS],
            in0=bufA[:, ch * CA + COLS: ch * CA + CA], in1=maskB),
            deps=bar_h + [tmb]))
    bar_hm = em.barrier()

    prev_proj_done = None
    prev_mm_done = None
    store_toks = []
    evac_free = [None] * 4
    for pi, (nm, tmnm) in enumerate(projs):
        twl = em.dgps(wbuf.rearrange("p (a x) -> p a x", a=NCH),
                      fm_dram_ap(wts[nm], C), deps=[prev_proj_done], sem=f"gqw{pi % 2}")
        for ch in range(NCH):
            war = [prev_proj_done, prev_mm_done]
            tt1 = em.v(lambda e, ch=ch, tmnm=tmnm: e.tensor_scalar_mul(
                out=t1b[:, 0:COLS], in0=bufA[:, ch * CA: ch * CA + COLS],
                scalar1=prms[tmnm][:, ch:ch + 1]),
                deps=bar_hm + [st.get("t1b_free"), tprm[tmnm]])
            tbig = em.v(lambda e, ch=ch, tmnm=tmnm: e.scalar_tensor_tensor(
                out=mixb[:, ch * COLS + 1: ch * COLS + COLS],
                in0=bufA[:, ch * CA: ch * CA + COLS - 1],
                scalar=tm1[tmnm][:, ch:ch + 1], in1=t1b[:, 1:COLS],
                op0=ALU.mult, op1=ALU.add), deps=[tt1, tm1_toks[tmnm]] + war)
            tfix = em.v(lambda e, ch=ch, tmnm=tmnm: e.scalar_tensor_tensor(
                out=mixb[:, ch * COLS: (ch + 1) * COLS].rearrange(
                    "p (s l) -> p s l", s=SEGS)[:, :, 0:1],
                in0=hm[:, ch * SEGS:(ch + 1) * SEGS].rearrange("p (s o) -> p s o", o=1),
                scalar=tm1[tmnm][:, ch:ch + 1],
                in1=t1b[:, 0:COLS].rearrange("p (s l) -> p s l", s=SEGS)[:, :, 0:1],
                op0=ALU.mult, op1=ALU.add), deps=[tt1, tbig, hm_toks[ch], tm1_toks[tmnm]] + war)
            st["t1b_free"] = tfix
        bar_mix = em.barrier()
        nct = len(col_tiles(COLS))
        for d in range(NCH):
            for ci, (c0, cw) in enumerate(col_tiles(COLS)):
                bank = pb[f"m{ci % 4}"]
                tk = None
                for kc in range(NCH):
                    tk = em.t(lambda e, d=d, kc=kc, c0=c0, cw=cw, bank=bank: e.matmul(
                        out=bank[:, 0:cw],
                        lhsT=wbuf[:, kc * C + d * 128:kc * C + (d + 1) * 128],
                        rhs=mixb[:, kc * COLS + c0: kc * COLS + c0 + cw],
                        start=(kc == 0), stop=(kc == NCH - 1)),
                        deps=([twl, st.get(f"pbm{ci % 4}_free")] + bar_mix) if kc == 0 else [])
                ei = (d * nct + ci) % 4
                if nm == "rwT":
                    tev = em.a(lambda e, cw=cw, bank=bank, ei=ei: e.activation(
                        out=evac[ei][:, 0:cw], in_=bank[:, 0:cw], func=AF.Sigmoid),
                        deps=[tk, evac_free[ei]])
                else:
                    tev = em.v(lambda e, cw=cw, bank=bank, ei=ei: e.tensor_copy(
                        out=evac[ei][:, 0:cw], in_=bank[:, 0:cw]),
                        deps=[tk, evac_free[ei]])
                st[f"pbm{ci % 4}_free"] = tev
                tst = em.dsync(proj_out[nm].ap()[d * 128:(d + 1) * 128, c0:c0 + cw],
                               evac[ei][:, 0:cw], deps=[tev], sem=f"sst{ei}")
                evac_free[ei] = tst
                store_toks.append(tst)
        prev_proj_done = store_toks[-1]
        prev_mm_done = tk

    em.finalize([t_xn_store] + store_toks)
    return nc


def launch1_inmaps(cfg, x_btC, ln1w, ln1b, tmk, tmv, tmr, att_kw, att_vw, att_rw):
    kwT = np.ascontiguousarray(np.asarray(att_kw, np.float32).T)
    vwT = np.ascontiguousarray(np.asarray(att_vw, np.float32).T)
    rwT = np.ascontiguousarray(np.asarray(att_rw, np.float32).T)
    in_maps = []
    for core in range(NCORES):
        xs = shard_cols(cfg, core, x_btC).astype(np.float32)
        hx, hmask = halo_of(cfg, core, x_btC)
        xin = np.ascontiguousarray(np.concatenate([xs, hx], axis=1), dtype=np.float32)
        in_maps.append({
            "xin": xin, "maskin": hmask,
            "ln1w": chunked(ln1w), "ln1b": chunked(ln1b),
            "tmk": chunked(tmk), "tmv": chunked(tmv), "tmr": chunked(tmr),
            "kwT": kwT, "vwT": vwT, "rwT": rwT,
        })
    return in_maps






def build_launch2(cfg, L2=512):
    nc = bass.Bass("TRN2", target_bir_lowering=False, debug=False)
    em = Em(nc)
    T = cfg.T
    L2 = min(L2, T)
    NRT = 4
    NCK = T // L2

    kin = nc.dram_tensor("kin", [512, T], F32, kind="ExternalInput")
    vin = nc.dram_tensor("vin", [512, T], F32, kind="ExternalInput")
    dec_in = nc.dram_tensor("dec_in", [128, NRT], F32, kind="ExternalInput").ap()
    tf_in = nc.dram_tensor("tf_in", [128, NRT], F32, kind="ExternalInput").ap()
    wkv_o = nc.dram_tensor("wkv_o", [512, T], F32, kind="ExternalOutput")

    kb = nc.alloc_sbuf_tensor("kb", [128, T], F32)[:]
    vb = nc.alloc_sbuf_tensor("vb", [128, T], F32)[:]
    ob = nc.alloc_sbuf_tensor("ob", [128, T], F32)[:]
    ekb = nc.alloc_sbuf_tensor("ekb", [128, L2], F32)[:]
    u1b = nc.alloc_sbuf_tensor("u1b", [128, L2], F32)[:]
    sab = nc.alloc_sbuf_tensor("sab", [128, L2], F32)[:]
    sbb = nc.alloc_sbuf_tensor("sbb", [128, L2], F32)[:]
    d1b = nc.alloc_sbuf_tensor("d1b", [128, L2 + 1], F32)[:]
    dnb = nc.alloc_sbuf_tensor("dnb", [128, L2], F32)[:]
    nnb = nc.alloc_sbuf_tensor("nnb", [128, L2], F32)[:]
    gt = nc.alloc_sbuf_tensor("gt", [128, L2], F32)[:]
    wramp = nc.alloc_sbuf_tensor("wramp", [128, L2 + 1], F32)[:]
    iot = nc.alloc_sbuf_tensor("iot", [128, L2 + 1], F32)[:]
    dec = nc.alloc_sbuf_tensor("dec", [128, NRT], F32)[:]
    tf = nc.alloc_sbuf_tensor("tf", [128, NRT], F32)[:]
    ew = nc.alloc_sbuf_tensor("ew", [128, 1], F32)[:]
    wcol = nc.alloc_sbuf_tensor("wcol", [128, 1], F32)[:]
    gcol = nc.alloc_sbuf_tensor("gcol", [128, 1], F32)[:]
    relu_u = nc.alloc_sbuf_tensor("relu_u", [128, 1], F32)[:]
    expu = nc.alloc_sbuf_tensor("expu", [128, 1], F32)[:]
    mx = nc.alloc_sbuf_tensor("mx", [128, 1], F32)[:]
    Mb = nc.alloc_sbuf_tensor("Mb", [128, 1], F32)[:]
    negM = nc.alloc_sbuf_tensor("negM", [128, 1], F32)[:]
    PmM = nc.alloc_sbuf_tensor("PmM", [128, 1], F32)[:]
    Pbuf = [nc.alloc_sbuf_tensor(f"P{i}", [128, 1], F32)[:] for i in range(2)]
    Abuf = [nc.alloc_sbuf_tensor(f"A{i}", [128, 1], F32)[:] for i in range(2)]
    Bbuf = [nc.alloc_sbuf_tensor(f"Bc{i}", [128, 1], F32)[:] for i in range(2)]

    tdec = em.dsync(dec, dec_in, sem="sqp")
    tdec = em.dsync(tf, tf_in, sem="sqp")
    tio = em.p(lambda e: e.iota(out=iot, pattern=[[1, L2 + 1]], base=0,
                                channel_multiplier=0,
                                allow_small_or_imprecise_dtypes=True))
    store_toks = []
    bar_prev = None
    for rt in range(NRT):
        ldeps = [] if bar_prev is None else bar_prev
        tkld = em.dsync(kb, kin.ap()[rt * 128:(rt + 1) * 128, :], sem="sqk", deps=ldeps)
        tvld = em.dsync(vb, vin.ap()[rt * 128:(rt + 1) * 128, :], sem="sqv", deps=ldeps)
        t1 = em.a(lambda e, rt=rt: e.activation(out=ew, in_=dec[:, rt:rt + 1],
                                                func=AF.Exp), deps=[tdec] + ldeps)
        t2 = em.v(lambda e: e.tensor_scalar_mul(out=wcol, in0=ew, scalar1=-1.0),
                  deps=[t1] + ldeps)
        t3 = em.a(lambda e: e.activation(out=gcol, in_=wcol, func=AF.Exp), deps=[t2])
        t4 = em.v(lambda e, rt=rt: e.tensor_scalar_max(out=relu_u, in0=tf[:, rt:rt + 1],
                                                       scalar1=0.0), deps=[tdec] + ldeps)
        t5 = em.a(lambda e, rt=rt: e.activation(out=expu, in_=tf[:, rt:rt + 1],
                                                func=AF.Exp), deps=[tdec] + ldeps)
        t6 = em.v(lambda e: e.memset(gt, 1.0), deps=ldeps)
        t7 = em.v(lambda e: e.tensor_scalar_mul(out=gt, in0=gt, scalar1=gcol[:, 0:1]),
                  deps=[t6, t3])
        t8 = em.v(lambda e: e.tensor_scalar_mul(out=wramp, in0=iot, scalar1=wcol[:, 0:1]),
                  deps=[tio, t2] + ldeps)
        c1 = em.v(lambda e: e.memset(Pbuf[0], -3e38), deps=ldeps)
        c2 = em.v(lambda e: e.memset(Abuf[0], 0.0), deps=ldeps)
        c3 = em.v(lambda e: e.memset(Bbuf[0], 0.0), deps=ldeps)
        bar = em.barrier()
        for ci in range(NCK):
            pc, qc = ci % 2, (ci + 1) % 2
            c0 = ci * L2
            P, A, Bc = Pbuf[pc], Abuf[pc], Bbuf[pc]
            P2, A2, B2 = Pbuf[qc], Abuf[qc], Bbuf[qc]
            k_ap = kb[:, c0:c0 + L2]
            v_ap = vb[:, c0:c0 + L2]
            tmx = em.v(lambda e, k_ap=k_ap: e.reduce_max(out=mx, in_=k_ap, axis=AX.X),
                       deps=bar)
            tM0 = em.v(lambda e: e.tensor_add(out=Mb, in0=mx, in1=relu_u), deps=[tmx] + bar)
            tM = em.v(lambda e, P=P: e.tensor_max(out=Mb, in0=Mb, in1=P), deps=[tM0] + bar)
            tnM = em.v(lambda e: e.tensor_scalar_mul(out=negM, in0=Mb, scalar1=-1.0),
                       deps=[tM] + bar)
            tek = em.a(lambda e, k_ap=k_ap: e.activation(out=ekb, in_=k_ap, func=AF.Exp,
                                                         bias=negM[:, 0:1]),
                       deps=[tnM] + bar)
            tu1 = em.v(lambda e, v_ap=v_ap: e.tensor_mul(out=u1b, in0=ekb, in1=v_ap),
                       deps=[tek] + bar)
            tsa = em.v(lambda e: e.tensor_tensor_scan(out=sab, data0=gt, data1=u1b,
                                                      initial=0.0, op0=ALU.mult,
                                                      op1=ALU.add), deps=[tu1] + bar)
            tsb = em.v(lambda e: e.tensor_tensor_scan(out=sbb, data0=gt, data1=ekb,
                                                      initial=0.0, op0=ALU.mult,
                                                      op1=ALU.add), deps=[tek] + bar)
            tpm = em.v(lambda e, P=P: e.tensor_sub(out=PmM, in0=P, in1=Mb),
                       deps=[tM] + bar)
            td1 = em.a(lambda e: e.activation(out=d1b, in_=wramp, func=AF.Exp,
                                              bias=PmM[:, 0:1]), deps=[tpm] + bar)
            tn1 = em.v(lambda e, A=A: e.tensor_scalar_mul(out=nnb, in0=d1b[:, 0:L2],
                                                          scalar1=A[:, 0:1]),
                       deps=[td1] + bar)
            tn2 = em.v(lambda e: e.tensor_add(out=nnb[:, 1:L2], in0=nnb[:, 1:L2],
                                              in1=sab[:, 0:L2 - 1]), deps=[tn1, tsa])
            tn3 = em.v(lambda e: e.scalar_tensor_tensor(out=nnb, in0=u1b,
                                                        scalar=expu[:, 0:1], in1=nnb,
                                                        op0=ALU.mult, op1=ALU.add),
                       deps=[tn2, tu1] + bar)
            td2 = em.v(lambda e, Bc=Bc: e.tensor_scalar_mul(out=dnb, in0=d1b[:, 0:L2],
                                                            scalar1=Bc[:, 0:1]),
                       deps=[td1, tn1] + bar)
            td3 = em.v(lambda e: e.tensor_add(out=dnb[:, 1:L2], in0=dnb[:, 1:L2],
                                              in1=sbb[:, 0:L2 - 1]), deps=[td2, tsb])
            td4 = em.v(lambda e: e.scalar_tensor_tensor(out=dnb, in0=ekb,
                                                        scalar=expu[:, 0:1], in1=dnb,
                                                        op0=ALU.mult, op1=ALU.add),
                       deps=[td3, tek] + bar)
            tl1 = em.a(lambda e: e.activation(out=dnb, in_=dnb, func=AF.Ln), deps=[td4])
            tl2 = em.a(lambda e: e.activation(out=dnb, in_=dnb, func=AF.Exp, scale=-1.0),
                       deps=[tl1])
            to = em.v(lambda e, c0=c0: e.tensor_mul(out=ob[:, c0:c0 + L2], in0=nnb,
                                                    in1=dnb), deps=[tl2, tn3] + bar)
            tca = em.v(lambda e, A=A, A2=A2: e.scalar_tensor_tensor(
                out=A2, in0=d1b[:, L2:L2 + 1], scalar=A[:, 0:1],
                in1=sab[:, L2 - 1:L2], op0=ALU.mult, op1=ALU.add),
                deps=[td1, tsa] + bar)
            tcb = em.v(lambda e, Bc=Bc, B2=B2: e.scalar_tensor_tensor(
                out=B2, in0=d1b[:, L2:L2 + 1], scalar=Bc[:, 0:1],
                in1=sbb[:, L2 - 1:L2], op0=ALU.mult, op1=ALU.add),
                deps=[td1, tsb] + bar)
            tcp = em.v(lambda e, P2=P2: e.tensor_copy(out=P2, in_=Mb), deps=[tM] + bar)
            bar = em.barrier()
        tst = em.dsync(wkv_o.ap()[rt * 128:(rt + 1) * 128, :], ob, sem="sqo", deps=bar)
        store_toks.append(tst)
        bar_prev = bar + [tst]
    em.finalize(store_toks)
    return nc


def launch2_inmaps(cfg, k_btC, v_btC, time_decay, time_first):
    T = cfg.T
    kr = np.ascontiguousarray(np.asarray(k_btC, np.float32).transpose(0, 2, 1).reshape(B * C, T))
    vr = np.ascontiguousarray(np.asarray(v_btC, np.float32).transpose(0, 2, 1).reshape(B * C, T))
    decf = np.tile(np.asarray(time_decay, np.float32), B)
    tff = np.tile(np.asarray(time_first, np.float32), B)
    in_maps = []
    for core in range(NCORES):
        rs = slice(core * 512, (core + 1) * 512)
        in_maps.append({
            "kin": kr[rs], "vin": vr[rs],
            "dec_in": np.ascontiguousarray(decf[rs].reshape(4, 128).T),
            "tf_in": np.ascontiguousarray(tff[rs].reshape(4, 128).T),
        })
    return in_maps


def launch2_unshard(cfg, outs):
    T = cfg.T
    full = np.concatenate(outs, axis=0)
    return full.reshape(B, C, T).transpose(0, 2, 1)



def emit_mix(em, st, srcbuf, src_stride, dstbuf, dst_stride, tm_sb, tm1_sb, hmbuf,
             ncols, SEGS, L, deps, t1b):
    toks = []
    for ch in range(NCH):
        tt1 = em.v(lambda e, ch=ch: e.tensor_scalar_mul(
            out=t1b[:, 0:ncols], in0=srcbuf[:, ch * src_stride: ch * src_stride + ncols],
            scalar1=tm_sb[:, ch:ch + 1]),
            deps=deps + [st.get("t1b_free")])
        tbig = em.v(lambda e, ch=ch: e.scalar_tensor_tensor(
            out=dstbuf[:, ch * dst_stride + 1: ch * dst_stride + ncols],
            in0=srcbuf[:, ch * src_stride: ch * src_stride + ncols - 1],
            scalar=tm1_sb[:, ch:ch + 1], in1=t1b[:, 1:ncols],
            op0=ALU.mult, op1=ALU.add), deps=[tt1] + deps)
        tfix = em.v(lambda e, ch=ch: e.scalar_tensor_tensor(
            out=dstbuf[:, ch * dst_stride: ch * dst_stride + ncols].rearrange(
                "p (s l) -> p s l", s=SEGS)[:, :, 0:1],
            in0=hmbuf[:, ch * SEGS:(ch + 1) * SEGS].rearrange("p (s o) -> p s o", o=1),
            scalar=tm1_sb[:, ch:ch + 1],
            in1=t1b[:, 0:ncols].rearrange("p (s l) -> p s l", s=SEGS)[:, :, 0:1],
            op0=ALU.mult, op1=ALU.add), deps=[tt1, tbig] + deps)
        st["t1b_free"] = tfix
        toks += [tbig, tfix]
    return toks



def build_launch3(cfg):
    nc = bass.Bass("TRN2", target_bir_lowering=False, debug=False)
    em = Em(nc)
    COLS = cfg.COLS

    wkv_in = nc.dram_tensor("wkv_in", [C, COLS], F32, kind="ExternalInput")
    sr_in = nc.dram_tensor("sr_in", [C, COLS], F32, kind="ExternalInput")
    xn_in = nc.dram_tensor("xn_in", [C, COLS], F32, kind="ExternalInput")
    owT = nc.dram_tensor("owT", [C, C], F32, kind="ExternalInput")
    x1_o = nc.dram_tensor("x1_o", [C, COLS], F32, kind="ExternalOutput")

    xnb = nc.alloc_sbuf_tensor("xnb", [128, NCH * COLS], F32)[:]
    rwkvb = nc.alloc_sbuf_tensor("rwkvb", [128, NCH * COLS], BF16)[:]
    wkvt = nc.alloc_sbuf_tensor("wkvt", [128, COLS], F32)[:]
    srt = nc.alloc_sbuf_tensor("srt", [128, COLS], F32)[:]
    wbuf = nc.alloc_sbuf_tensor("wbuf", [128, NCH * C], BF16)[:]
    evac = [nc.alloc_sbuf_tensor(f"evac{i}", [128, 512], F32)[:] for i in range(4)]
    pb = {f"m{i}": nc.alloc_psum_tensor(f"pbm{i}", [128, 512], F32)[:] for i in range(4)}

    txn = em.dsync(xnb.rearrange("p (a x) -> p a x", a=NCH), fm_dram_ap(xn_in, COLS),
                   sem="sqxn")
    twl = em.dgps(wbuf.rearrange("p (a x) -> p a x", a=NCH), fm_dram_ap(owT, C))
    tmul_prev = None
    mul_toks = []
    for ch in range(NCH):
        t1 = em.dsync(wkvt, wkv_in.ap()[ch * 128:(ch + 1) * 128, :], sem="sqw",
                      deps=[tmul_prev])
        t2 = em.dsync(srt, sr_in.ap()[ch * 128:(ch + 1) * 128, :], sem="sqs",
                      deps=[tmul_prev])
        tmul_prev = em.v(lambda e, ch=ch: e.tensor_mul(
            out=rwkvb[:, ch * COLS:(ch + 1) * COLS], in0=srt, in1=wkvt),
            deps=[t1, t2])
        mul_toks.append(tmul_prev)
    nct = len(col_tiles(COLS))
    store_toks = []
    evac_free = [None] * 4
    pbm_free = [None] * 4
    for cc in range(NCH):
        for ci, (c0, cw) in enumerate(col_tiles(COLS)):
            bank = pb[f"m{ci % 4}"]
            tk = None
            for dc in range(NCH):
                tk = em.t(lambda e, cc=cc, dc=dc, c0=c0, cw=cw, bank=bank: e.matmul(
                    out=bank[:, 0:cw],
                    lhsT=wbuf[:, dc * C + cc * 128:dc * C + (cc + 1) * 128],
                    rhs=rwkvb[:, dc * COLS + c0: dc * COLS + c0 + cw],
                    start=(dc == 0), stop=(dc == NCH - 1)),
                    deps=([twl, pbm_free[ci % 4]] + mul_toks) if dc == 0 else [])
            ei = (cc * nct + ci) % 4
            tev = em.v(lambda e, cc=cc, c0=c0, cw=cw, bank=bank, ei=ei: e.tensor_add(
                out=evac[ei][:, 0:cw], in0=bank[:, 0:cw],
                in1=xnb[:, cc * COLS + c0: cc * COLS + c0 + cw]),
                deps=[tk, txn, evac_free[ei]])
            pbm_free[ci % 4] = tev
            tst = em.dsync(x1_o.ap()[cc * 128:(cc + 1) * 128, c0:c0 + cw],
                           evac[ei][:, 0:cw], deps=[tev], sem=f"sst{ei}")
            evac_free[ei] = tst
            store_toks.append(tst)
    em.finalize(store_toks)
    return nc



def build_launch4a(cfg):
    nc = bass.Bass("TRN2", target_bir_lowering=False, debug=False)
    em = Em(nc)
    CA, COLS, SEGS, L = cfg.CA, cfg.COLS, cfg.SEGS, cfg.L

    x1in = nc.dram_tensor("x1in", [C, CA], F32, kind="ExternalInput")
    maskin = nc.dram_tensor("maskin", [1, SEGS], F32, kind="ExternalInput").ap()
    prm_names = ("ln2w", "ln2b", "ftmk", "ftmr")
    prm = {nm: nc.dram_tensor(nm, [128, NCH], F32, kind="ExternalInput").ap()
           for nm in prm_names}
    xk2_o = nc.dram_tensor("xk2_o", [C, COLS], BF16, kind="ExternalOutput")
    xr2_o = nc.dram_tensor("xr2_o", [C, COLS], BF16, kind="ExternalOutput")

    bufA = nc.alloc_sbuf_tensor("bufA", [128, NCH * CA], F32)[:]
    bufB = nc.alloc_sbuf_tensor("bufB", [128, NCH * CA], F32)[:]
    mixb = nc.alloc_sbuf_tensor("mixb", [128, NCH * COLS], BF16)[:]
    un0 = nc.alloc_sbuf_tensor("un0", [128, max(COLS, 2048)], F32)
    t1b = un0[:]
    hm = nc.alloc_sbuf_tensor("hm", [128, NCH * SEGS], F32)[:]
    maskB = nc.alloc_sbuf_tensor("maskB", [128, SEGS], F32)[:]
    mrow = nc.alloc_sbuf_tensor("mrow", [1, SEGS], F32)[:]
    tm1 = {nm: nc.alloc_sbuf_tensor(f"tm1_{nm}", [128, NCH], F32)[:]
           for nm in ("ftmk", "ftmr")}
    prms = {nm: nc.alloc_sbuf_tensor("s_" + nm, [128, NCH], F32)[:] for nm in prm_names}

    st = make_state(em, nc, rows=un0[:][0:1, 0:2048])
    pb = st["pb"]

    tx = em.dsync(bufA.rearrange("p (a x) -> p a x", a=NCH), fm_dram_ap(x1in, CA), sem="sqx")
    for nm in prm_names:
        em.dsync(prms[nm], prm[nm], sem="sqp")
    tp_all = em.dsync(mrow, maskin, sem="sqp")
    tm1_toks = {}
    for nm in ("ftmk", "ftmr"):
        tm1_toks[nm] = em.v(lambda e, nm=nm: e.tensor_scalar(
            out=tm1[nm], in0=prms[nm], scalar1=-1.0, scalar2=1.0,
            op0=ALU.mult, op1=ALU.add), deps=[tp_all])
    tmb0 = em.t(lambda e: e.matmul(out=pb["b2"][:, 0:SEGS], lhsT=st["ones_row"][0:1, :],
                                   rhs=mrow, start=True, stop=True),
                deps=[tp_all, st["dep_ones"]])
    tmb = em.v(lambda e: e.tensor_copy(out=maskB, in_=pb["b2"][:, 0:SEGS]), deps=[tmb0])
    st["pb_b2_free"] = tmb

    def a_ap(ch, c0, cw):
        return bufA[:, ch * CA + c0: ch * CA + c0 + cw]

    def b_ap(ch, c0, cw):
        return bufB[:, ch * CA + c0: ch * CA + c0 + cw]

    emit_ln(em, st, a_ap, b_ap, CA, mean=True,
            gamma=(prms["ln2w"], tp_all), beta=(prms["ln2b"], tp_all), src_dep=[tx])
    bar_h = em.barrier()
    hm_toks = []
    for ch in range(NCH):
        hm_toks.append(em.v(lambda e, ch=ch: e.tensor_mul(
            out=hm[:, ch * SEGS:(ch + 1) * SEGS],
            in0=bufB[:, ch * CA + COLS: ch * CA + CA], in1=maskB),
            deps=bar_h + [tmb]))
    bar_hm = em.barrier()
    store_toks = []
    for nm, out_t in (("ftmk", xk2_o), ("ftmr", xr2_o)):
        toks = emit_mix(em, st, bufB, CA, mixb, COLS, prms[nm], tm1[nm], hm,
                        COLS, SEGS, L, bar_hm + hm_toks + [tm1_toks[nm], tp_all]
                        + ([store_toks[-1]] if store_toks else []), t1b)
        tst = em.dsync(fm_dram_ap(out_t, COLS),
                       mixb.rearrange("p (a x) -> p a x", a=NCH),
                       deps=toks, sem="sqst" + nm)
        store_toks.append(tst)
    em.finalize(store_toks)
    return nc



def build_launch4b(cfg):
    nc = bass.Bass("TRN2", target_bir_lowering=False, debug=False)
    em = Em(nc)
    COLS = cfg.COLS
    FF = 4096
    NFC = FF // 128
    NFG = 4

    xk2_in = nc.dram_tensor("xk2_in", [C, COLS], BF16, kind="ExternalInput")
    xr2_in = nc.dram_tensor("xr2_in", [C, COLS], BF16, kind="ExternalInput")
    x1_in = nc.dram_tensor("x1_in", [C, COLS], F32, kind="ExternalInput")
    fkwT = nc.dram_tensor("fkwT", [C, FF], F32, kind="ExternalInput")
    fvwT = nc.dram_tensor("fvwT", [FF, C], F32, kind="ExternalInput")
    frwT = nc.dram_tensor("frwT", [C, C], F32, kind="ExternalInput")
    ao_o = nc.dram_tensor("ao_o", [C, COLS], F32, kind="ExternalOutput")

    xk2b = nc.alloc_sbuf_tensor("xk2b", [128, NCH * COLS], BF16)[:]
    xr2b = nc.alloc_sbuf_tensor("xr2b", [128, NCH * COLS], BF16)[:]
    kkb = nc.alloc_sbuf_tensor("kkb", [128, NFC * 512], BF16)[:]
    wst = nc.alloc_sbuf_tensor("wst", [128, 8192], BF16)[:]
    wst2 = nc.alloc_sbuf_tensor("wst2", [128, 8192], BF16)[:]
    x1ct = nc.alloc_sbuf_tensor("x1ct", [128, NCH * 512], F32)[:]
    sr2t = nc.alloc_sbuf_tensor("sr2t", [128, NCH * 512], F32)[:]
    at = [nc.alloc_sbuf_tensor(f"at{i}", [128, 512], F32)[:] for i in range(2)]
    evst = [nc.alloc_sbuf_tensor(f"evst{i}", [128, 512], F32)[:] for i in range(2)]
    pb = [nc.alloc_psum_tensor(f"pbk{i}", [128, 512], F32)[:] for i in range(8)]

    tk2 = em.dsync(xk2b.rearrange("p (a x) -> p a x", a=NCH), fm_dram_ap(xk2_in, COLS),
                   sem="sqk")
    tr2 = em.dsync(xr2b.rearrange("p (a x) -> p a x", a=NCH), fm_dram_ap(xr2_in, COLS),
                   sem="sqr")
    fkw3 = fkwT.ap().rearrange("(a p) f -> p a f", p=128)
    fvw3 = fvwT.ap().rearrange("(a p) c -> p a c", p=128)
    frw3 = frwT.ap().rearrange("(a p) c -> p a c", p=128)

    store_toks = []
    pfree = [None] * 8
    wst_free = None
    wst2_free = None
    x1_free = None
    sr2_free = [None] * NCH
    kk_free = [None] * NFC
    ev_free = [None] * 2
    at_free = [None] * 2
    for ci, (c0, cw) in enumerate(col_tiles(COLS)):
        tx1 = em.dsync(x1ct.rearrange("p (a x) -> p a x", a=NCH)[:, :, 0:cw],
                       fm_dram_ap(x1_in, COLS)[:, :, c0:c0 + cw], sem="sqx1",
                       deps=[x1_free])
        twr = em.dgps(wst.rearrange("p (a x) -> p a x", a=NCH), frw3, sem="gqw0",
                      deps=[wst_free])
        for cc in range(NCH):
            bi = cc % 4
            bank = pb[bi]
            tk = None
            for kc in range(NCH):
                tk = em.t(lambda e, cc=cc, kc=kc, c0=c0, cw=cw, bank=bank: e.matmul(
                    out=bank[:, 0:cw],
                    lhsT=wst[:, kc * 1024 + cc * 128:kc * 1024 + (cc + 1) * 128],
                    rhs=xr2b[:, kc * COLS + c0: kc * COLS + c0 + cw],
                    start=(kc == 0), stop=(kc == NCH - 1)),
                    deps=[twr, tr2, pfree[bi]] if kc == 0 else [])
            tev = em.a(lambda e, cc=cc, cw=cw, bank=bank: e.activation(
                out=sr2t[:, cc * 512:cc * 512 + cw], in_=bank[:, 0:cw],
                func=AF.Sigmoid), deps=[tk, sr2_free[cc]])
            pfree[bi] = tev
            sr2_free[cc] = tev
            wst_free = tk
        for fg in range(NFG):
            twk = em.dgps(wst.rearrange("p (a x) -> p a x", a=NCH),
                          fkw3[:, :, fg * 1024:(fg + 1) * 1024], sem="gqw1",
                          deps=[wst_free])
            for fi in range(NCH):
                fci = fg * 8 + fi
                bi = fi % 4
                bank = pb[bi]
                tk = None
                for kc in range(NCH):
                    tk = em.t(lambda e, fi=fi, kc=kc, c0=c0, cw=cw, bank=bank: e.matmul(
                        out=bank[:, 0:cw],
                        lhsT=wst[:, kc * 1024 + fi * 128:kc * 1024 + (fi + 1) * 128],
                        rhs=xk2b[:, kc * COLS + c0: kc * COLS + c0 + cw],
                        start=(kc == 0), stop=(kc == NCH - 1)),
                        deps=[twk, tk2, pfree[bi]] if kc == 0 else [])
                pp = fci % 2
                trl = em.v(lambda e, cw=cw, bank=bank, pp=pp: e.tensor_scalar_max(
                    out=at[pp][:, 0:cw], in0=bank[:, 0:cw], scalar1=0.0),
                    deps=[tk, at_free[pp]])
                tsqr = em.a(lambda e, fci=fci, cw=cw, pp=pp: e.activation(
                    out=kkb[:, fci * 512:fci * 512 + cw], in_=at[pp][:, 0:cw],
                    func=AF.Square), deps=[trl, kk_free[fci]])
                pfree[bi] = trl
                at_free[pp] = tsqr
                kk_free[fci] = tsqr
                wst_free = tk
        bar_kk = em.barrier()
        tk = None
        for fg in range(NFG):
            twv = em.dgps(wst2.rearrange("p (a x) -> p a x", a=NCH),
                          fvw3[:, fg * 8:(fg + 1) * 8, :], sem="gqw2",
                          deps=[wst2_free])
            for fi in range(NCH):
                fci = fg * 8 + fi
                for cc in range(NCH):
                    deps = []
                    if fci == 0:
                        deps = [twv, pfree[cc]] + bar_kk
                    elif fi == 0 and cc == 0:
                        deps = [twv]
                    tk = em.t(lambda e, fi=fi, cc=cc, fci=fci, c0=c0, cw=cw: e.matmul(
                        out=pb[cc][:, 0:cw],
                        lhsT=wst2[:, fi * 1024 + cc * 128:fi * 1024 + (cc + 1) * 128],
                        rhs=kkb[:, fci * 512:fci * 512 + cw],
                        start=(fci == 0), stop=(fci == NFC - 1)),
                        deps=deps)
            wst2_free = tk
        bar_kv = em.barrier()
        for cc in range(NCH):
            pp = cc % 2
            t1 = em.v(lambda e, cc=cc, cw=cw, pp=pp: e.tensor_mul(
                out=at[pp][:, 0:cw], in0=sr2t[:, cc * 512:cc * 512 + cw],
                in1=pb[cc][:, 0:cw]), deps=bar_kv + [at_free[pp]])
            t2 = em.v(lambda e, cc=cc, cw=cw, pp=pp: e.tensor_add(
                out=evst[pp][:, 0:cw], in0=at[pp][:, 0:cw],
                in1=x1ct[:, cc * 512:cc * 512 + cw]), deps=[t1, tx1, ev_free[pp]])
            at_free[pp] = t2
            pfree[cc] = t1
            sr2_free[cc] = t1
            tst = em.dsync(ao_o.ap()[cc * 128:(cc + 1) * 128, c0:c0 + cw],
                           evst[pp][:, 0:cw], deps=[t2], sem=f"sstao{pp}")
            ev_free[pp] = tst
            store_toks.append(tst)
        x1_free = store_toks[-1]
    em.finalize(store_toks)
    return nc



def emit_round_exact(em, tile, w, y_scr, b_scr, deps):
    t0 = em.v(lambda e: e.tensor_scalar_add(out=y_scr[:, 0:w], in0=tile[:, 0:w],
                                            scalar1=256.5), deps=deps)
    tr = em.v(lambda e: e.tensor_copy(out=tile[:, 0:w], in_=y_scr[:, 0:w]), deps=[t0])
    for k in (256.0, 128.0, 64.0, 32.0, 16.0, 8.0, 4.0, 2.0, 1.0):
        tb = em.v(lambda e, k=k: e.tensor_single_scalar(
            out=b_scr[:, 0:w], in_=tile[:, 0:w], scalar=k, op=ALU.is_ge), deps=[tr])
        tr = em.v(lambda e, k=k: e.scalar_tensor_tensor(
            out=tile[:, 0:w], in0=b_scr[:, 0:w], scalar=-k, in1=tile[:, 0:w],
            op0=ALU.mult, op1=ALU.add), deps=[tb])
    t1 = em.v(lambda e: e.scalar_tensor_tensor(
        out=tile[:, 0:w], in0=tile[:, 0:w], scalar=-1.0, in1=y_scr[:, 0:w],
        op0=ALU.mult, op1=ALU.add), deps=[tr])
    t2 = em.v(lambda e: e.tensor_scalar_add(out=tile[:, 0:w], in0=tile[:, 0:w],
                                            scalar1=-256.0), deps=[t1])
    return t2



def build_launch5(cfg, exact_round=True, debug=False, cut=None, skip_pre=False):
    nc = bass.Bass("TRN2", target_bir_lowering=False, debug=False)
    em = Em(nc)
    COLS = cfg.COLS
    FF = 4096
    NFC = FF // 128
    NFG = 4

    xn_in = nc.dram_tensor("xn_in", [C, COLS], F32, kind="ExternalInput")
    x_in = nc.dram_tensor("x_in", [C, COLS], F32, kind="ExternalInput")
    ao_in = nc.dram_tensor("ao_in", [C, COLS], F32, kind="ExternalInput")
    wgT = nc.dram_tensor("wgT", [C, FF], F32, kind="ExternalInput")
    wuT = nc.dram_tensor("wuT", [C, FF], F32, kind="ExternalInput")
    wdT = nc.dram_tensor("wdT", [FF, C], F32, kind="ExternalInput")
    xl0_o = nc.dram_tensor("xl0_o", [C, COLS], F32, kind="ExternalOutput")
    if debug:
        aq1_d = nc.dram_tensor("aq1_d", [C, 512], BF16, kind="ExternalOutput")
        gu_d = nc.dram_tensor("gu_d", [4096, 512], BF16, kind="ExternalOutput")
        aq2_d = nc.dram_tensor("aq2_d", [4096, 512], BF16, kind="ExternalOutput")
        rows_d = nc.dram_tensor("rows_d", [1, 2048], F32, kind="ExternalOutput")
        tern_d = nc.dram_tensor("tern_d", [128, 8192], BF16, kind="ExternalOutput")

    xnct = nc.alloc_sbuf_tensor("xnct", [128, NCH * 512], F32)[:]
    xct = nc.alloc_sbuf_tensor("xct", [128, NCH * 512], F32)[:]
    aoct = nc.alloc_sbuf_tensor("aoct", [128, NCH * 512], F32)[:]
    aq1b = nc.alloc_sbuf_tensor("aq1b", [128, NCH * 512], BF16)[:]
    gub = nc.alloc_sbuf_tensor("gub", [128, NFC * 512], BF16)[:]
    aq2b = nc.alloc_sbuf_tensor("aq2b", [128, NFC * 512], BF16)[:]
    tgb = nc.alloc_sbuf_tensor("tgb", [128, 8192], BF16)[:]
    tub = nc.alloc_sbuf_tensor("tub", [128, 8192], BF16)[:]
    tern_scr = {"g": nc.dram_tensor("tern_g", [C, 4096], BF16, kind="Internal"),
                "u": nc.dram_tensor("tern_u", [C, 4096], BF16, kind="Internal"),
                "d": nc.dram_tensor("tern_d_scr", [4096, C], BF16, kind="Internal")}
    t3scr = {k: tern_scr[k].ap().rearrange("(a p) f -> p a f", p=128) for k in tern_scr}
    gsb = nc.alloc_sbuf_tensor("gsb", [128, 512], F32)[:]
    usb = nc.alloc_sbuf_tensor("usb", [128, 512], F32)[:]
    mxb = nc.alloc_sbuf_tensor("mxb", [128, 512], F32)[:]
    mcols = nc.alloc_sbuf_tensor("mcols", [128, 4], F32)[:]
    at = [nc.alloc_sbuf_tensor(f"at{i}", [128, 512], F32)[:] for i in range(2)]
    ysc = nc.alloc_sbuf_tensor("ysc", [128, 512], F32)[:]
    bsc = nc.alloc_sbuf_tensor("bsc", [128, 512], F32)[:]
    evst = [nc.alloc_sbuf_tensor(f"evst{i}", [128, 512], F32)[:] for i in range(2)]
    rows = nc.alloc_sbuf_tensor("rows", [1, 2048], F32)[:]
    acc = nc.alloc_sbuf_tensor("acc", [128, 48], F32)[:]
    ones_col = nc.alloc_sbuf_tensor("ones_col", [128, 1], F32)[:]
    ones_row = nc.alloc_sbuf_tensor("ones_row", [1, 128], F32)[:]
    sc1 = nc.alloc_sbuf_tensor("sc1", [1, 8], F32)[:]
    thcol = {w: nc.alloc_sbuf_tensor(f"thcol_{w}", [128, 1], F32)[:] for w in "gud"}
    nthcol = {w: nc.alloc_sbuf_tensor(f"nthcol_{w}", [128, 1], F32)[:] for w in "gud"}
    ninv = {w: nc.alloc_sbuf_tensor(f"ninv_{w}", [128, 1], F32)[:] for w in "gud"}
    pb = [nc.alloc_psum_tensor(f"pbk{i}", [128, 512], F32)[:] for i in range(8)]
    stb = [nc.alloc_sbuf_tensor(f"stb{i}", [128, 2048], F32)[:] for i in range(2)]

    t_on1 = em.v(lambda e: e.memset(ones_col, 1.0))
    t_on2 = em.v(lambda e: e.memset(ones_row, 1.0))
    wmap = {"g": wgT, "u": wuT, "d": wdT}
    w3 = {"g": wgT.ap().rearrange("(a p) f -> p a f", p=128),
          "u": wuT.ap().rearrange("(a p) f -> p a f", p=128),
          "d": wdT.ap().rearrange("(a p) c -> p a c", p=128)}
    NPC = (NCH * 4096) // 2048

    def wpiece(view, wn, pc):
        if wn != "d":
            a = pc // 2
            fr = (pc % 2) * 2048
            return view[:, a:a + 1, fr:fr + 2048]
        return view[:, pc * 2:pc * 2 + 2, :]

    def spiece(flat_ap, wn):
        a = 1 if wn != "d" else 2
        return flat_ap.rearrange("p (a f) -> p a f", a=a)

    if skip_pre is True:
        tz = em.v(lambda e: e.memset(thcol["g"], 0.01))
        for wn in "ud":
            tz = em.v(lambda e, wn=wn: e.memset(thcol[wn], 0.01), deps=[tz])
        for wn in "gud":
            tz = em.v(lambda e, wn=wn: e.memset(nthcol[wn], -0.01), deps=[tz])
            tz = em.v(lambda e, wn=wn: e.memset(ninv[wn], -1.0), deps=[tz])
        bc_toks = {wn: [tz] for wn in "gud"}
        bar_tern = em.barrier()
    wst_free = [None, None]
    acc_toks = {}
    for wi, wn in enumerate("gud") if skip_pre is not True else []:
        piece_toks = []
        accp = [None] * 4
        for pc in range(NPC):
            pp = pc % 2
            tl = em.dsync(spiece(stb[pp], wn), wpiece(w3[wn], wn, pc),
                          sem=f"sqs{pp}", deps=[wst_free[pp]])
            ta = em.a(lambda e, wi=wi, pc=pc, pp=pp: e.activation(
                out=stb[pp], in_=stb[pp], func=AF.Abs,
                accum_out=acc[:, wi * 16 + pc:wi * 16 + pc + 1]),
                deps=[tl])
            wst_free[pp] = ta
            piece_toks.append(ta)
        acc_toks[wn] = piece_toks
    rows_toks = {}
    pbs_free = None
    for wi, wn in enumerate("gud") if skip_pre is not True else []:
        tm = em.t(lambda e, wi=wi: e.matmul(
            out=pb[6][0:1, 0:16], lhsT=ones_col[:, 0:1], rhs=acc[:, wi * 16:wi * 16 + 16],
            start=True, stop=True), deps=[t_on1, pbs_free] + acc_toks[wn])
        ts1 = em.v(lambda e, wi=wi: e.reduce_sum(
            out=rows[0:1, wi:wi + 1], in_=pb[6][0:1, 0:16], axis=AX.X), deps=[tm])
        pbs_free = ts1
        ts2 = em.v(lambda e, wi=wi: e.tensor_scalar(
            out=rows[0:1, 8 + wi:9 + wi], in0=rows[0:1, wi:wi + 1],
            scalar1=1.0 / (C * FF), scalar2=1e-5, op0=ALU.mult, op1=ALU.max),
            deps=[ts1])
        rows_toks[wn] = ts2
    bc_toks = {} if not skip_pre else bc_toks
    for wi, wn in enumerate("gud") if skip_pre is not True else []:
        t1 = em.v(lambda e, wi=wi: e.tensor_scalar_mul(
            out=rows[0:1, 16 + wi:17 + wi], in0=rows[0:1, 8 + wi:9 + wi], scalar1=0.5),
            deps=[rows_toks[wn]])
        t2 = em.t(lambda e, wi=wi: e.matmul(
            out=pb[6][:, 0:1], lhsT=ones_row[0:1, :], rhs=rows[0:1, 16 + wi:17 + wi],
            start=True, stop=True), deps=[t1, t_on2, pbs_free])
        t3 = em.v(lambda e, wn=wn: e.tensor_copy(out=thcol[wn], in_=pb[6][:, 0:1]),
                  deps=[t2])
        t4 = em.v(lambda e, wn=wn: e.tensor_scalar_mul(out=nthcol[wn], in0=thcol[wn],
                                                       scalar1=-1.0), deps=[t3])
        t5 = em.t(lambda e, wi=wi: e.matmul(
            out=pb[6][:, 2:3], lhsT=ones_row[0:1, :], rhs=rows[0:1, 8 + wi:9 + wi],
            start=True, stop=True), deps=[rows_toks[wn], t_on2, t3])
        t6 = em.v(lambda e, wn=wn: e.tensor_scalar_mul(out=ninv[wn], in0=pb[6][:, 2:3],
                                                       scalar1=-1.0), deps=[t5])
        pbs_free = t6
        bc_toks[wn] = [t3, t4, t6]
    bar_w = em.barrier()
    tern_store_toks = []
    tsc_free = [None, None]
    for wn in ("gud" if not skip_pre else ""):
        for pc in range(NPC):
            pp = pc % 2
            tl = em.dsync(stb[pp].rearrange("p (a f) -> p a f", a=1), wpiece(w3[wn], wn, pc),
                          sem=f"sqs{pp}", deps=[wst_free[pp]] + (bar_w if pc < 2 else []))
            tslice = tgb[:, pp * 2048:(pp + 1) * 2048]
            tt1 = em.v(lambda e, wn=wn, pp=pp, tslice=tslice: e.tensor_single_scalar(
                out=tslice, in_=stb[pp], scalar=thcol[wn][:, 0:1],
                op=ALU.is_ge), deps=[tl, tsc_free[pp]] + bc_toks[wn])
            tt2 = em.v(lambda e, wn=wn, pp=pp, tslice=tslice: e.scalar_tensor_tensor(
                out=tslice, in0=stb[pp], scalar=nthcol[wn][:, 0:1],
                in1=tslice, op0=ALU.is_le, op1=ALU.subtract), deps=[tt1])
            wst_free[pp] = tt2
            tsc = em.dsync(wpiece(t3scr[wn], wn, pc), spiece(tslice, wn),
                           sem=f"sqscw{pp}", deps=[tt2])
            tsc_free[pp] = tsc
            tern_store_toks.append(tsc)
    bar_tern = em.barrier() if skip_pre is not True else bar_tern
    if cut == "tern":
        em.finalize(bar_tern)
        return nc

    idf = nc.alloc_sbuf_tensor("idf", [128, 128], F32)[:]
    ti0 = em.v(lambda e: e.memset(idf, 1.0))
    ti1 = em.p(lambda e: e.affine_select(out=idf, in_=idf, pattern=[[-1, 128]],
                                         compare_op=ALU.is_equal, fill=0.0,
                                         base=0, channel_multiplier=1), deps=[ti0])
    if cut == "idf":
        em.finalize(em.barrier())
        return nc

    store_toks = []
    pfree = [None] * 8
    x_free = ao_free = xn_free = None
    aq1_free = None
    gu_free = [None] * NFC
    aq2_free = [None] * NFC
    at_free = [None, ti1]
    ev_free = [None] * 2
    tern_g_free = None
    tern_u_free = None
    mx_free = None
    mc_free = None
    ysc_free = bsc_free = None
    gs_free = us_free = None
    rows_free2 = None
    for ci, (c0, cw) in enumerate(col_tiles(COLS)):
        nct128 = (cw + 127) // 128
        txn = em.dsync(xnct.rearrange("p (a x) -> p a x", a=NCH)[:, :, 0:cw],
                       fm_dram_ap(xn_in, COLS)[:, :, c0:c0 + cw], sem="sqxn",
                       deps=[xn_free])
        tx = em.dsync(xct.rearrange("p (a x) -> p a x", a=NCH)[:, :, 0:cw],
                      fm_dram_ap(x_in, COLS)[:, :, c0:c0 + cw], sem="sqx",
                      deps=[x_free])
        tao = em.dsync(aoct.rearrange("p (a x) -> p a x", a=NCH)[:, :, 0:cw],
                       fm_dram_ap(ao_in, COLS)[:, :, c0:c0 + cw], sem="sqao",
                       deps=[ao_free])
        if cut == "loads":
            em.finalize(em.barrier())
            return nc
        tmx = em.a(lambda e, cw=cw: e.activation(
            out=mxb[:, 0:cw], in_=xnct[:, 0:cw], func=AF.Abs), deps=[txn, mx_free])
        for ch in range(1, NCH):
            tab = em.a(lambda e, ch=ch, cw=cw: e.activation(
                out=at[1][:, 0:cw], in_=xnct[:, ch * 512:ch * 512 + cw], func=AF.Abs),
                deps=[txn, at_free[1]])
            tmx = em.v(lambda e, cw=cw: e.tensor_max(
                out=mxb[:, 0:cw], in0=mxb[:, 0:cw], in1=at[1][:, 0:cw]),
                deps=[tmx, tab])
            at_free[1] = tmx
        if cut == "rowmax":
            em.finalize(em.barrier())
            return nc
        ttr = None
        for tt_i in range(nct128):
            tw = min(128, cw - tt_i * 128)
            t1 = em.t(lambda e, tt_i=tt_i, tw=tw: e.transpose(
                out=pb[7][0:tw, 0:128], in_=mxb[:, tt_i * 128:tt_i * 128 + tw],
                identity=idf), deps=[tmx, ti1, mc_free, ttr])
            ttr = em.v(lambda e, tt_i=tt_i, tw=tw: e.reduce_max(
                out=mcols[0:tw, tt_i:tt_i + 1], in_=pb[7][0:tw, 0:128], axis=AX.X),
                deps=[t1, ttr])
        if cut == "transp":
            em.finalize(em.barrier())
            return nc
        if nct128 == 1:
            trow = em.dsync(rows[0:1, 1024:1024 + cw], mcols[0:cw, 0:1],
                            sem="sqrow", deps=[ttr, rows_free2])
        else:
            for tt_i in range(nct128):
                trow = em.dsync(rows[0:1, 1024 + tt_i * 128:1024 + (tt_i + 1) * 128],
                                mcols[:, tt_i:tt_i + 1], sem="sqrow",
                                deps=[ttr, rows_free2] if tt_i == 0 else [])
        tr1 = em.v(lambda e, cw=cw: e.tensor_scalar_max(
            out=rows[0:1, 1024:1024 + cw], in0=rows[0:1, 1024:1024 + cw], scalar1=1e-5),
            deps=[trow])
        tr2 = em.v(lambda e, cw=cw: e.reciprocal(out=rows[0:1, 0:cw],
                                                 in_=rows[0:1, 1024:1024 + cw]),
                   deps=[tr1])
        tr3 = em.v(lambda e, cw=cw: e.tensor_scalar_mul(
            out=rows[0:1, 0:cw], in0=rows[0:1, 0:cw], scalar1=127.0), deps=[tr2])
        tr4 = em.v(lambda e, cw=cw: e.tensor_scalar_mul(
            out=rows[0:1, 512:512 + cw], in0=rows[0:1, 1024:1024 + cw],
            scalar1=1.0 / 127.0), deps=[tr1, tr2])
        if cut == "rows":
            em.finalize(em.barrier())
            return nc
        tb1 = em.t(lambda e, cw=cw: e.matmul(out=pb[6][:, 0:cw], lhsT=ones_row[0:1, :],
                                             rhs=rows[0:1, 0:cw], start=True, stop=True),
                   deps=[tr3, t_on2, pbs_free, pfree[6]])
        tb2 = em.t(lambda e, cw=cw: e.matmul(out=pb[7][:, 0:cw], lhsT=ones_row[0:1, :],
                                             rhs=rows[0:1, 512:512 + cw],
                                             start=True, stop=True),
                   deps=[tr4, t_on2, ttr, pfree[7]])
        rows_free2 = tb2
        pbs_free = tb2
        if cut == "bcast":
            em.finalize(em.barrier())
            return nc
        taq_last = None
        for ch in range(NCH):
            tq1 = em.v(lambda e, ch=ch, cw=cw: e.tensor_mul(
                out=at[0][:, 0:cw], in0=xnct[:, ch * 512:ch * 512 + cw],
                in1=pb[6][:, 0:cw]), deps=[tb1, txn, at_free[0]])
            trd = emit_round_exact(em, at[0], cw, ysc, bsc, [tq1, ysc_free, bsc_free])
            taq = em.v(lambda e, ch=ch, cw=cw: e.tensor_mul(
                out=aq1b[:, ch * 512:ch * 512 + cw], in0=at[0][:, 0:cw],
                in1=pb[7][:, 0:cw]), deps=[trd, tb2, aq1_free])
            at_free[0] = taq
            ysc_free = bsc_free = trd
            taq_last = taq
        aq1_free = taq_last
        pfree[6] = taq_last
        pfree[7] = taq_last
        bar_aq1 = em.barrier()
        if cut == "aq1":
            em.finalize(bar_aq1)
            return nc
        for fg in range(NFG):
            ttg2 = em.dgps(tgb.rearrange("p (a x) -> p a x", a=NCH),
                           t3scr["g"][:, :, fg * 1024:(fg + 1) * 1024], sem="gqg",
                           deps=[tern_g_free] + (bar_aq1 + bar_tern if fg == 0 else []))
            ttu2 = em.dgps(tub.rearrange("p (a x) -> p a x", a=NCH),
                           t3scr["u"][:, :, fg * 1024:(fg + 1) * 1024], sem="gqu",
                           deps=[tern_u_free] + (bar_aq1 + bar_tern if fg == 0 else []))
            for fi in range(NCH):
                fci = fg * 8 + fi
                bg, bu = pb[fi % 3], pb[3 + fi % 3]
                tkg = tku = None
                for kc in range(NCH):
                    tkg = em.t(lambda e, fi=fi, kc=kc, cw=cw, bg=bg: e.matmul(
                        out=bg[:, 0:cw],
                        lhsT=tgb[:, kc * 1024 + fi * 128:kc * 1024 + (fi + 1) * 128],
                        rhs=aq1b[:, kc * 512:kc * 512 + cw],
                        start=(kc == 0), stop=(kc == NCH - 1)),
                        deps=[ttg2, pfree[fi % 3]] + bar_aq1 if kc == 0 else [])
                for kc in range(NCH):
                    tku = em.t(lambda e, fi=fi, kc=kc, cw=cw, bu=bu: e.matmul(
                        out=bu[:, 0:cw],
                        lhsT=tub[:, kc * 1024 + fi * 128:kc * 1024 + (fi + 1) * 128],
                        rhs=aq1b[:, kc * 512:kc * 512 + cw],
                        start=(kc == 0), stop=(kc == NCH - 1)),
                        deps=[ttu2, pfree[3 + fi % 3]] + bar_aq1 if kc == 0 else [])
                tg = em.a(lambda e, cw=cw, bg=bg: e.activation(
                    out=gsb[:, 0:cw], in_=bg[:, 0:cw], func=AF.Silu,
                    scale=ninv["g"][:, 0:1]), deps=[tkg, gs_free] + bc_toks["g"])
                tu = em.a(lambda e, cw=cw, bu=bu: e.activation(
                    out=usb[:, 0:cw], in_=bu[:, 0:cw], func=AF.Copy,
                    scale=ninv["u"][:, 0:1]), deps=[tku, us_free] + bc_toks["u"])
                tgu = em.v(lambda e, fci=fci, cw=cw: e.tensor_mul(
                    out=gub[:, fci * 512:fci * 512 + cw], in0=gsb[:, 0:cw],
                    in1=usb[:, 0:cw]), deps=[tg, tu, gu_free[fci]])
                pfree[fi % 3] = tg
                pfree[3 + fi % 3] = tu
                gs_free = tgu
                us_free = tgu
                gu_free[fci] = tgu
            tern_g_free = tkg
            tern_u_free = tku
        bar_gu = em.barrier()
        if cut == "gu":
            em.finalize(bar_gu)
            return nc
        tmx2 = em.a(lambda e, cw=cw: e.activation(
            out=mxb[:, 0:cw], in_=gub[:, 0:cw], func=AF.Abs), deps=bar_gu + [tmx])
        for fci in range(1, NFC):
            tab = em.a(lambda e, fci=fci, cw=cw: e.activation(
                out=at[1][:, 0:cw], in_=gub[:, fci * 512:fci * 512 + cw], func=AF.Abs),
                deps=bar_gu + [at_free[1]])
            tmx2 = em.v(lambda e, cw=cw: e.tensor_max(
                out=mxb[:, 0:cw], in0=mxb[:, 0:cw], in1=at[1][:, 0:cw]),
                deps=[tmx2, tab])
            at_free[1] = tmx2
        ttr2 = None
        for tt_i in range(nct128):
            tw = min(128, cw - tt_i * 128)
            t1 = em.t(lambda e, tt_i=tt_i, tw=tw: e.transpose(
                out=pb[7][0:tw, 0:128], in_=mxb[:, tt_i * 128:tt_i * 128 + tw],
                identity=idf), deps=[tmx2, ti1, pfree[7], ttr2])
            ttr2 = em.v(lambda e, tt_i=tt_i, tw=tw: e.reduce_max(
                out=mcols[0:tw, tt_i:tt_i + 1], in_=pb[7][0:tw, 0:128], axis=AX.X),
                deps=[t1, ttr2 or trow])
            pfree[7] = ttr2
        if nct128 == 1:
            trow2 = em.dsync(rows[0:1, 1024:1024 + cw], mcols[0:cw, 0:1],
                             sem="sqrow", deps=[ttr2, tb2])
        else:
            for tt_i in range(nct128):
                trow2 = em.dsync(rows[0:1, 1024 + tt_i * 128:1024 + (tt_i + 1) * 128],
                                 mcols[:, tt_i:tt_i + 1], sem="sqrow",
                                 deps=[ttr2, tb2] if tt_i == 0 else [])
        tr1 = em.v(lambda e, cw=cw: e.tensor_scalar_max(
            out=rows[0:1, 1024:1024 + cw], in0=rows[0:1, 1024:1024 + cw], scalar1=1e-5),
            deps=[trow2])
        tr2b = em.v(lambda e, cw=cw: e.reciprocal(out=rows[0:1, 0:cw],
                                                  in_=rows[0:1, 1024:1024 + cw]),
                    deps=[tr1])
        tr3b = em.v(lambda e, cw=cw: e.tensor_scalar_mul(
            out=rows[0:1, 0:cw], in0=rows[0:1, 0:cw], scalar1=127.0), deps=[tr2b])
        tr4b = em.v(lambda e, cw=cw: e.tensor_scalar_mul(
            out=rows[0:1, 512:512 + cw], in0=rows[0:1, 1024:1024 + cw],
            scalar1=1.0 / 127.0), deps=[tr1, tr2b])
        tb1b = em.t(lambda e, cw=cw: e.matmul(out=pb[6][:, 0:cw], lhsT=ones_row[0:1, :],
                                              rhs=rows[0:1, 0:cw], start=True, stop=True),
                    deps=[tr3b, t_on2, pfree[6]])
        tb2b = em.t(lambda e, cw=cw: e.matmul(out=pb[7][:, 0:cw], lhsT=ones_row[0:1, :],
                                              rhs=rows[0:1, 512:512 + cw],
                                              start=True, stop=True),
                    deps=[tr4b, t_on2, pfree[7]])
        taq2_last = None
        for fci in range(NFC):
            tq1 = em.v(lambda e, fci=fci, cw=cw: e.tensor_mul(
                out=at[0][:, 0:cw], in0=gub[:, fci * 512:fci * 512 + cw],
                in1=pb[6][:, 0:cw]), deps=[tb1b, at_free[0]] + bar_gu)
            trd = emit_round_exact(em, at[0], cw, ysc, bsc, [tq1, ysc_free, bsc_free])
            taq = em.v(lambda e, fci=fci, cw=cw: e.tensor_mul(
                out=aq2b[:, fci * 512:fci * 512 + cw], in0=at[0][:, 0:cw],
                in1=pb[7][:, 0:cw]), deps=[trd, tb2b, aq2_free[fci]])
            at_free[0] = taq
            ysc_free = bsc_free = trd
            aq2_free[fci] = taq
            taq2_last = taq
        pfree[6] = taq2_last
        pfree[7] = taq2_last
        bar_aq2 = em.barrier()
        if cut == "aq2":
            em.finalize(bar_aq2)
            return nc
        if debug and ci == 0:
            store_toks.append(em.dsync(fm_dram_ap(aq1_d, 512)[:, :, 0:cw],
                                       aq1b.rearrange("p (a x) -> p a x", a=NCH)[:, :, 0:cw],
                                       deps=bar_aq2, sem="sqdbg"))
            store_toks.append(em.dsync(gu_d.ap().rearrange("(a p) x -> p a x", p=128)[:, :, 0:cw],
                                       gub.rearrange("p (a x) -> p a x", a=NFC)[:, :, 0:cw],
                                       deps=bar_aq2, sem="sqdbg2"))
            store_toks.append(em.dsync(aq2_d.ap().rearrange("(a p) x -> p a x", p=128)[:, :, 0:cw],
                                       aq2b.rearrange("p (a x) -> p a x", a=NFC)[:, :, 0:cw],
                                       deps=bar_aq2, sem="sqdbg3"))
            store_toks.append(em.dsync(tern_d.ap(), tub, deps=bar_aq2, sem="sqdbg5"))
        tdl = None
        for fg in range(NFG):
            ttd2 = em.dgps(tgb.rearrange("p (a x) -> p a x", a=NCH),
                           t3scr["d"][:, fg * 8:(fg + 1) * 8, :], sem="gqd",
                           deps=[tern_g_free] + (bar_aq2 if fg == 0 else [])) 
            for fi in range(NCH):
                fci = fg * 8 + fi
                for cc in range(NCH):
                    deps = []
                    if fci == 0:
                        deps = [ttd2, pfree[cc]] + bar_aq2
                    elif fi == 0 and cc == 0:
                        deps = [ttd2]
                    tdl = em.t(lambda e, fi=fi, cc=cc, fci=fci, cw=cw: e.matmul(
                        out=pb[cc][:, 0:cw],
                        lhsT=tgb[:, fi * 1024 + cc * 128:fi * 1024 + (cc + 1) * 128],
                        rhs=aq2b[:, fci * 512:fci * 512 + cw],
                        start=(fci == 0), stop=(fci == NFC - 1)),
                        deps=deps)
            tern_g_free = tdl
        bar_dn = em.barrier()
        for cc in range(NCH):
            pp = cc % 2
            t1 = em.a(lambda e, cc=cc, cw=cw, pp=pp: e.activation(
                out=at[pp][:, 0:cw], in_=pb[cc][:, 0:cw], func=AF.Copy,
                scale=ninv["d"][:, 0:1]), deps=bar_dn + [at_free[pp]] + bc_toks["d"])
            t2 = em.v(lambda e, cc=cc, cw=cw, pp=pp: e.tensor_add(
                out=at[pp][:, 0:cw], in0=at[pp][:, 0:cw],
                in1=xct[:, cc * 512:cc * 512 + cw]), deps=[t1, tx])
            t3 = em.v(lambda e, cc=cc, cw=cw, pp=pp: e.tensor_add(
                out=evst[pp][:, 0:cw], in0=at[pp][:, 0:cw],
                in1=aoct[:, cc * 512:cc * 512 + cw]), deps=[t2, tao, ev_free[pp]])
            at_free[pp] = t3
            pfree[cc] = t3
            tst = em.dsync(xl0_o.ap()[cc * 128:(cc + 1) * 128, c0:c0 + cw],
                           evst[pp][:, 0:cw], deps=[t3], sem=f"sstx{pp}")
            ev_free[pp] = tst
            store_toks.append(tst)
        x_free = store_toks[-1]
        ao_free = store_toks[-1]
        xn_free = taq_last
        mx_free = tmx2
        mc_free = ttr2
    em.finalize(store_toks)
    return nc



def build_launch6(cfg):
    nc = bass.Bass("TRN2", target_bir_lowering=False, debug=False)
    em = Em(nc)
    COLS = cfg.COLS
    NL = 4

    xl0_in = nc.dram_tensor("xl0_in", [C, COLS], F32, kind="ExternalInput")
    wcd_in = nc.dram_tensor("wcd_in", [C, 512], F32, kind="ExternalInput")
    wqu_in = nc.dram_tensor("wqu_in", [512, C], F32, kind="ExternalInput")
    wkv_in = nc.dram_tensor("wkv_in", [512, 128], F32, kind="ExternalInput")
    xn1_o = nc.dram_tensor("xn1_o", [C, COLS], BF16, kind="ExternalOutput")
    q_o = nc.dram_tensor("q_o", [C, COLS], BF16, kind="ExternalOutput")
    k_o = nc.dram_tensor("k_o", [64, COLS], BF16, kind="ExternalOutput")
    vt_o = nc.dram_tensor("vt_o", [COLS, 64], BF16, kind="ExternalOutput")

    xb = nc.alloc_sbuf_tensor("xb", [128, NCH * COLS], F32)[:]
    xn1b = nc.alloc_sbuf_tensor("xn1b", [128, NCH * COLS], BF16)[:]
    latb = nc.alloc_sbuf_tensor("latb", [128, NL * COLS], BF16)[:]
    wcd = nc.alloc_sbuf_tensor("wcd", [128, NCH * 512], BF16)[:]
    wqu = nc.alloc_sbuf_tensor("wqu", [128, NL * C], BF16)[:]
    wkv = nc.alloc_sbuf_tensor("wkv", [128, NL * 128], BF16)[:]
    un0 = nc.alloc_sbuf_tensor("un0", [128, 2048], F32)
    evac = [nc.alloc_sbuf_tensor(f"evac{i}", [128, 512], F32)[:] for i in range(4)]
    evacB = [nc.alloc_sbuf_tensor(f"evacB{i}", [128, 512], BF16)[:] for i in range(4)]
    st = make_state(em, nc, rows=un0[:][0:1, 0:2048])
    pb = st["pb"]

    tx = em.dsync(xb.rearrange("p (a x) -> p a x", a=NCH), fm_dram_ap(xl0_in, COLS),
                  sem="sqx")
    twc = em.dgps(wcd.rearrange("p (a x) -> p a x", a=NCH),
                  wcd_in.ap().rearrange("(a p) x -> p a x", p=128), sem="gqw")
    twq = em.dgps(wqu.rearrange("p (a x) -> p a x", a=NL),
                  wqu_in.ap().rearrange("(a p) x -> p a x", p=128), sem="gqw2")
    twk = em.dgps(wkv.rearrange("p (a x) -> p a x", a=NL),
                  wkv_in.ap().rearrange("(a p) x -> p a x", p=128), sem="gqw3")

    def x_ap(ch, c0, cw):
        return xb[:, ch * COLS + c0: ch * COLS + c0 + cw]

    def n_ap(ch, c0, cw):
        return xn1b[:, ch * COLS + c0: ch * COLS + c0 + cw]

    emit_ln(em, st, x_ap, n_ap, COLS, mean=True, src_dep=[tx])
    bar_ln = em.barrier()
    t_st1 = em.dsync(fm_dram_ap(xn1_o, COLS), xn1b.rearrange("p (a x) -> p a x", a=NCH),
                     deps=bar_ln, sem="sqst1")
    store_toks = [t_st1]
    nct = len(col_tiles(COLS))
    pfree = {}
    evac_free = [None] * 4
    lat_toks = []
    for lc in range(NL):
        for ci, (c0, cw) in enumerate(col_tiles(COLS)):
            bank = pb[f"m{ci % 4}"]
            tk = None
            for kc in range(NCH):
                tk = em.t(lambda e, lc=lc, kc=kc, c0=c0, cw=cw, bank=bank: e.matmul(
                    out=bank[:, 0:cw],
                    lhsT=wcd[:, kc * 512 + lc * 128:kc * 512 + (lc + 1) * 128],
                    rhs=xn1b[:, kc * COLS + c0: kc * COLS + c0 + cw],
                    start=(kc == 0), stop=(kc == NCH - 1)),
                    deps=([twc, pfree.get(ci % 4)] + bar_ln) if kc == 0 else [])
            tev = em.v(lambda e, lc=lc, c0=c0, cw=cw, bank=bank: e.tensor_copy(
                out=latb[:, lc * COLS + c0: lc * COLS + c0 + cw], in_=bank[:, 0:cw]),
                deps=[tk])
            pfree[ci % 4] = tev
            lat_toks.append(tev)
    bar_lat = em.barrier()
    for qc in range(NCH):
        for ci, (c0, cw) in enumerate(col_tiles(COLS)):
            bank = pb[f"m{ci % 4}"]
            tk = None
            for lc in range(NL):
                tk = em.t(lambda e, qc=qc, lc=lc, c0=c0, cw=cw, bank=bank: e.matmul(
                    out=bank[:, 0:cw],
                    lhsT=wqu[:, lc * C + qc * 128:lc * C + (qc + 1) * 128],
                    rhs=latb[:, lc * COLS + c0: lc * COLS + c0 + cw],
                    start=(lc == 0), stop=(lc == NL - 1)),
                    deps=([twq, pfree.get(ci % 4)] + bar_lat) if lc == 0 else [])
            ei = (qc * nct + ci) % 4
            tev = em.a(lambda e, cw=cw, bank=bank, ei=ei: e.activation(
                out=evacB[ei][:, 0:cw], in_=bank[:, 0:cw],
                func=AF.Copy, scale=0.125), deps=[tk, evac_free[ei]])
            pfree[ci % 4] = tev
            tst = em.dsync(q_o.ap()[qc * 128:(qc + 1) * 128, c0:c0 + cw],
                           evacB[ei][:, 0:cw],
                           deps=[tev], sem=f"sst{ei}")
            evac_free[ei] = tst
            store_toks.append(tst)
    for ci, (c0, cw) in enumerate(col_tiles(COLS)):
        bank = pb[f"m{ci % 4}"]
        tk = None
        for lc in range(NL):
            tk = em.t(lambda e, lc=lc, c0=c0, cw=cw, bank=bank: e.matmul(
                out=bank[:, 0:cw],
                lhsT=wkv[:, lc * 128:(lc + 1) * 128],
                rhs=latb[:, lc * COLS + c0: lc * COLS + c0 + cw],
                start=(lc == 0), stop=(lc == NL - 1)),
                deps=([twk, pfree.get(ci % 4)] + bar_lat) if lc == 0 else [])
        ei = ci % 4
        tev = em.v(lambda e, cw=cw, bank=bank, ei=ei: e.tensor_copy(
            out=evacB[ei][:, 0:cw], in_=bank[:, 0:cw]),
            deps=[tk, evac_free[ei]])
        pfree[ci % 4] = tev
        tst = em.dsync(k_o.ap()[0:64, c0:c0 + cw],
                       evacB[ei][0:64, 0:cw],
                       deps=[tev], sem=f"sst{ei}")
        evac_free[ei] = tst
        store_toks.append(tst)
    TW = min(128, COLS)
    ncc = COLS // TW
    for ti in range(ncc):
        bank = pb[f"m{ti % 4}"]
        tk = None
        for lc in range(NL):
            tk = em.t(lambda e, lc=lc, ti=ti, bank=bank: e.matmul(
                out=bank[0:TW, 0:64],
                lhsT=latb[:, lc * COLS + ti * TW:lc * COLS + (ti + 1) * TW],
                rhs=wkv[:, lc * 128 + 64:lc * 128 + 128],
                start=(lc == 0), stop=(lc == NL - 1)),
                deps=([twk, pfree.get(ti % 4)] + bar_lat) if lc == 0 else [])
        ei = ti % 4
        tev = em.v(lambda e, ti=ti, bank=bank, ei=ei: e.tensor_copy(
            out=evacB[ei][0:TW, 0:64], in_=bank[0:TW, 0:64]),
            deps=[tk, evac_free[ei]])
        pfree[ti % 4] = tev
        tst = em.dsync(vt_o.ap()[ti * TW:(ti + 1) * TW, :],
                       evacB[ei][0:TW, 0:64],
                       deps=[tev], sem=f"sst{ei}")
        evac_free[ei] = tst
        store_toks.append(tst)
    em.finalize(store_toks)
    return nc



def build_launch7(cfg):
    nc = bass.Bass("TRN2", target_bir_lowering=False, debug=False)
    em = Em(nc)
    T = cfg.T
    BT = B * T
    KW = min(512, T)
    QW2 = min(512, T)
    SQ = QW2 // 128
    NQ5 = T // QW2

    qA_in = nc.dram_tensor("qA_in", [64, BT], BF16, kind="ExternalInput")
    qB_in = nc.dram_tensor("qB_in", [64, BT], BF16, kind="ExternalInput")
    kf_in = nc.dram_tensor("kf_in", [64, BT], BF16, kind="ExternalInput")
    vt_in = nc.dram_tensor("vt_in", [BT, 64], BF16, kind="ExternalInput")
    m1_in = nc.dram_tensor("m1_in", [128, 2 * KW], F32, kind="ExternalInput")
    m2_in = nc.dram_tensor("m2_in", [128, 128 * (SQ - 1) + QW2], F32, kind="ExternalInput")
    y_o = nc.dram_tensor("y_o", [128, BT], BF16, kind="ExternalOutput")

    qs = [nc.alloc_sbuf_tensor("qA", [64, BT], BF16)[:],
          nc.alloc_sbuf_tensor("qB", [64, BT], BF16)[:]]
    kb = nc.alloc_sbuf_tensor("kb", [64, BT], BF16)[:]
    vt_sb = nc.alloc_sbuf_tensor("vt_sb", [128, (BT // 128) * 64], BF16)[:]
    ybuf = [nc.alloc_sbuf_tensor(f"ybuf{i}", [64, BT], BF16)[:] for i in range(2)]
    m1b = nc.alloc_sbuf_tensor("m1b", [128, 2 * KW], F32)[:]
    m2b = nc.alloc_sbuf_tensor("m2b", [128, 128 * (SQ - 1) + QW2], F32)[:]
    pexp = [nc.alloc_sbuf_tensor(f"pexp{i}", [128, QW2], BF16)[:] for i in range(2)]
    mcol = nc.alloc_sbuf_tensor("mcol", [128, 1], F32)[:]
    mtmp = nc.alloc_sbuf_tensor("mtmp", [128, 1], F32)[:]
    negm = nc.alloc_sbuf_tensor("negm", [1, QW2], BF16)[:]
    recr = nc.alloc_sbuf_tensor("recr", [1, QW2], F32)[:]
    ysc = nc.alloc_sbuf_tensor("ysc", [64, QW2], F32)[:]
    idf = nc.alloc_sbuf_tensor("idf", [128, 128], F32)[:]
    ones_row = nc.alloc_sbuf_tensor("ones_row", [1, 128], BF16)[:]
    ones_row32 = nc.alloc_sbuf_tensor("ones_row32", [1, 128], F32)[:]
    ones_col = nc.alloc_sbuf_tensor("ones_col", [128, 1], BF16)[:]
    pb = [nc.alloc_psum_tensor(f"pb{i}", [128, 512], F32)[:] for i in range(8)]

    tqa_ld = em.dsync(qs[0], qA_in.ap(), sem="sqqa")
    tqb_ld = em.dsync(qs[1], qB_in.ap(), sem="sqqb")
    tk = em.dsync(kb, kf_in.ap(), sem="sqk")
    tv = em.dsync(vt_sb.rearrange("p (n d) -> p n d", d=64),
                  vt_in.ap().rearrange("(n p) d -> p n d", p=128), sem="sqv")
    tm1 = em.dsync(m1b, m1_in.ap(), sem="sqm")
    tm2 = em.dsync(m2b, m2_in.ap(), sem="sqm2")
    ti0 = em.v(lambda e: e.memset(idf, 1.0))
    ti1 = em.p(lambda e: e.affine_select(out=idf, in_=idf, pattern=[[-1, 128]],
                                         compare_op=ALU.is_equal, fill=0.0,
                                         base=0, channel_multiplier=1), deps=[ti0])
    tor = em.v(lambda e: e.memset(ones_row, 1.0))
    tor32 = em.v(lambda e: e.memset(ones_row32, 1.0))
    toc = em.v(lambda e: e.memset(ones_col, 1.0))
    base = [tqa_ld, tqb_ld, tk, tv, tm1, tm2, ti1, tor, tor32, toc]

    y_toks = []
    p0_free = None
    p15_free = [None, None]
    p2_free = None
    p3_free = None
    p4_free = None
    p6_free = None
    mc_free = None
    mt_free = None
    neg_free = None
    pex_free = [None, None]
    rec_free = None
    ysc_free = None
    for hl in range(2):
        qh = qs[hl]
        for b in range(B):
            kb0 = b * T
            for q5 in range(NQ5):
                qgb = q5 * QW2
                for sq in range(SQ):
                    qb128 = qgb + sq * 128
                    nkt = qb128 // KW + 1
                    tred = None
                    for kt in range(nkt):
                        tkm = em.t(lambda e, qh=qh, kt=kt, qb128=qb128, kb0=kb0: e.matmul(
                            out=pb[0][:, 0:KW],
                            lhsT=qh[:, kb0 + qb128: kb0 + qb128 + 128],
                            rhs=kb[:, kb0 + kt * KW: kb0 + (kt + 1) * KW],
                            start=True, stop=True),
                            deps=(base + [p0_free]))
                        if kt == nkt - 1:
                            off = KW - (qb128 - kt * KW)
                            tkm = em.v(lambda e, off=off: e.tensor_add(
                                out=pb[0][:, 0:KW], in0=pb[0][:, 0:KW],
                                in1=m1b[:, off:off + KW]), deps=[tkm, tm1])
                        if kt == 0:
                            tred = em.v(lambda e: e.tensor_reduce(
                                out=mcol, in_=pb[0][:, 0:KW], axis=AX.X, op=ALU.max),
                                deps=[tkm, mc_free])
                        else:
                            tr2 = em.v(lambda e: e.tensor_reduce(
                                out=mtmp, in_=pb[0][:, 0:KW], axis=AX.X, op=ALU.max),
                                deps=[tkm, mt_free])
                            tred = em.v(lambda e: e.tensor_max(
                                out=mcol, in0=mcol, in1=mtmp), deps=[tred, tr2])
                            mt_free = tred
                        p0_free = tred
                    ttr = em.t(lambda e: e.transpose(
                        out=pb[6][0:1, 0:128], in_=mcol[:, 0:1], identity=idf),
                        deps=[tred, ti1, p6_free])
                    tng = em.a(lambda e, sq=sq: e.activation(
                        out=negm[0:1, sq * 128:(sq + 1) * 128], in_=pb[6][0:1, 0:128],
                        func=AF.Copy, scale=-1.0), deps=[ttr, neg_free])
                    p6_free = tng
                    mc_free = ttr
                nk2 = (qgb + QW2) // 128
                tsum = tpv = None
                tk2last = None
                for kt2 in range(nk2):
                    pp = kt2 % 2
                    bank = pb[1 + 4 * pp]
                    tk2 = em.t(lambda e, qh=qh, kt2=kt2, kb0=kb0, qgb=qgb, bank=bank:
                               e.matmul(
                        out=bank[:, 0:QW2],
                        lhsT=kb[:, kb0 + kt2 * 128: kb0 + (kt2 + 1) * 128],
                        rhs=qh[:, kb0 + qgb: kb0 + qgb + QW2],
                        start=True, stop=False), deps=base + [p15_free[pp]])
                    tk2 = em.t(lambda e, bank=bank: e.matmul(
                        out=bank[:, 0:QW2], lhsT=ones_row[0:1, 0:128],
                        rhs=negm[0:1, 0:QW2], start=False, stop=True),
                        deps=[tng, tor])
                    r = kt2 - (qgb // 128)
                    if r >= 0:
                        off = 128 * (SQ - 1) - 128 * r
                        tk2 = em.v(lambda e, off=off, bank=bank: e.tensor_add(
                            out=bank[:, 0:QW2], in0=bank[:, 0:QW2],
                            in1=m2b[:, off:off + QW2]), deps=[tk2, tm2])
                    tex = em.a(lambda e, pp=pp, bank=bank: e.activation(
                        out=pexp[pp][:, 0:QW2], in_=bank[:, 0:QW2], func=AF.Exp),
                        deps=[tk2, pex_free[pp]])
                    p15_free[pp] = tex
                    tsum = em.t(lambda e, pp=pp, first=(kt2 == 0), last=(kt2 == nk2 - 1):
                                e.matmul(
                        out=pb[2][0:1, 0:QW2], lhsT=ones_col[:, 0:1],
                        rhs=pexp[pp][:, 0:QW2], start=first, stop=last),
                        deps=([tex, toc, p2_free] if kt2 == 0 else [tex]))
                    vti = (kb0 + kt2 * 128) // 128
                    tpv = em.t(lambda e, pp=pp, vti=vti, first=(kt2 == 0),
                               last=(kt2 == nk2 - 1): e.matmul(
                        out=pb[3][0:64, 0:QW2],
                        lhsT=vt_sb[:, vti * 64:(vti + 1) * 64],
                        rhs=pexp[pp][:, 0:QW2], start=first, stop=last),
                        deps=([tv, p3_free] if kt2 == 0 else []))
                    pex_free[pp] = tpv
                    tk2last = tk2
                neg_free = tk2last
                trec = em.v(lambda e: e.reciprocal(out=recr[0:1, 0:QW2],
                                                   in_=pb[2][0:1, 0:QW2]),
                            deps=[tsum, rec_free])
                p2_free = trec
                tbc = em.t(lambda e: e.matmul(
                    out=pb[4][0:64, 0:QW2], lhsT=ones_row32[0:1, 0:64],
                    rhs=recr[0:1, 0:QW2],
                    start=True, stop=True), deps=[trec, tor32, p4_free])
                rec_free = tbc
                tyc = em.v(lambda e: e.tensor_copy(out=ysc[:, 0:QW2],
                                                   in_=pb[3][0:64, 0:QW2]),
                           deps=[tpv, ysc_free])
                p3_free = tyc
                tym = em.v(lambda e, hl=hl, kb0=kb0, qgb=qgb: e.tensor_mul(
                    out=ybuf[hl][:, kb0 + qgb: kb0 + qgb + QW2],
                    in0=ysc[:, 0:QW2], in1=pb[4][0:64, 0:QW2]),
                    deps=[tyc, tbc])
                p4_free = tym
                ysc_free = tym
                y_toks.append(tym)
    ts1 = em.dsync(y_o.ap()[0:64, :], ybuf[0], deps=y_toks, sem="sqy0")
    ts2 = em.dsync(y_o.ap()[64:128, :], ybuf[1], deps=y_toks, sem="sqy1")
    em.finalize([ts1, ts2])
    return nc


def launch7_inmaps(cfg, q_btC, k1, v1):
    T = cfg.T
    BT = B * T
    KW = min(512, T)
    QW2 = min(512, T)
    SQ = QW2 // 128
    q_cm = np.ascontiguousarray(np.asarray(q_btC, BFD).transpose(2, 0, 1).reshape(C, BT))
    kf = np.ascontiguousarray(np.asarray(k1, BFD).transpose(2, 0, 1).reshape(64, BT))
    vt = np.ascontiguousarray(np.asarray(v1, BFD).reshape(BT, 64))
    p = np.arange(128)[:, None]
    x1 = np.arange(2 * KW)[None, :]
    m1 = np.where(x1 <= p + KW, 0.0, -3e38).astype(np.float32)
    x2 = np.arange(128 * (SQ - 1) + QW2)[None, :]
    m2 = np.where(x2 >= p + 128 * (SQ - 1), 0.0, -3e38).astype(np.float32)
    ims = []
    for core in range(NCORES):
        ims.append({
            "qA_in": np.ascontiguousarray(q_cm[(2 * core) * 64:(2 * core + 1) * 64]),
            "qB_in": np.ascontiguousarray(q_cm[(2 * core + 1) * 64:(2 * core + 2) * 64]),
            "kf_in": kf, "vt_in": vt, "m1_in": m1, "m2_in": m2,
        })
    return ims


def launch7_unshard(cfg, youts):
    T = cfg.T
    y = np.empty((B, T, C), np.float32)
    for core in range(NCORES):
        arr = np.asarray(youts[core])
        if arr.dtype == np.uint16:
            arr = arr.view(BFD)
        arr = arr.astype(np.float32).reshape(128, B, T)
        y[:, :, (2 * core) * 64:(2 * core + 2) * 64] = arr.transpose(1, 2, 0)
    return y



def build_launch8(cfg):
    nc = bass.Bass("TRN2", target_bir_lowering=False, debug=False)
    em = Em(nc)
    COLS = cfg.COLS
    LB = COLS // B

    y_in = nc.dram_tensor("y_in", [C, COLS], BF16, kind="ExternalInput")
    xn1_in = nc.dram_tensor("xn1_in", [C, COLS], BF16, kind="ExternalInput")
    xl0_in = nc.dram_tensor("xl0_in", [C, COLS], F32, kind="ExternalInput")
    wcp_in = nc.dram_tensor("wcp_in", [C, C], F32, kind="ExternalInput")
    wts_in = {nm: nc.dram_tensor(nm, [C, C], F32, kind="ExternalInput")
              for nm in ("wfT", "ufT", "wiT", "uiT", "wcT", "ucT", "woT", "uoT")}
    bias_in = {nm: nc.dram_tensor(nm, [128, NCH], F32, kind="ExternalInput").ap()
               for nm in ("bf", "bi", "bc", "bo", "in_c")}
    out_o = nc.dram_tensor("out_o", [C, COLS], F32, kind="ExternalOutput")

    xn1b = nc.alloc_sbuf_tensor("xn1b", [128, NCH * COLS], BF16)[:]
    cb = nc.alloc_sbuf_tensor("cb", [128, NCH * LB], F32)[:]
    cbf = nc.alloc_sbuf_tensor("cbf", [128, NCH * LB], BF16)[:]
    wwb = nc.alloc_sbuf_tensor("wwb", [128, NCH * C], BF16)[:]
    uub = nc.alloc_sbuf_tensor("uub", [128, NCH * C], BF16)[:]
    wob = nc.alloc_sbuf_tensor("wob", [128, NCH * C], BF16)[:]
    yb = nc.alloc_sbuf_tensor("yb", [128, NCH * COLS], BF16)[:]
    ibuf = nc.alloc_sbuf_tensor("ibuf", [128, NCH * LB], F32)[:]
    gtile = [nc.alloc_sbuf_tensor(f"gtile{i}", [128, 512], F32)[:] for i in range(2)]
    xl0ct = nc.alloc_sbuf_tensor("xl0ct", [128, NCH * 512], F32)[:]
    biases = {nm: nc.alloc_sbuf_tensor("b_" + nm, [128, NCH], F32)[:]
              for nm in bias_in}
    ath = [nc.alloc_sbuf_tensor(f"ath{i}", [128, 512], F32)[:] for i in range(2)]
    tmp1, tmp2 = ath[0], ath[1]
    at2 = [nc.alloc_sbuf_tensor(f"at2{i}", [128, 512], F32)[:] for i in range(2)]
    evst = [nc.alloc_sbuf_tensor(f"evst{i}", [128, 512], F32)[:] for i in range(2)]
    pb = [nc.alloc_psum_tensor(f"pb{i}", [128, 512], F32)[:] for i in range(8)]

    tx1 = em.dsync(xn1b.rearrange("p (a x) -> p a x", a=NCH), fm_dram_ap(xn1_in, COLS),
                   sem="sqn")
    tbias = None
    for nm in bias_in:
        tbias = em.dsync(biases[nm], bias_in[nm], sem="sqb")
    cin_toks = []
    for ch in range(NCH):
        t1 = em.v(lambda e, ch=ch: e.memset(cb[:, ch * LB:(ch + 1) * LB], 1.0))
        t2 = em.v(lambda e, ch=ch: e.tensor_scalar_mul(
            out=cb[:, ch * LB:(ch + 1) * LB], in0=cb[:, ch * LB:(ch + 1) * LB],
            scalar1=biases["in_c"][:, ch:ch + 1]), deps=[t1, tbias])
        cin_toks.append(t2)
    gate_cfg = {"f": ("wfT", "ufT", "bf", AF.Sigmoid), "i": ("wiT", "uiT", "bi", AF.Sigmoid),
                "g": ("wcT", "ucT", "bc", AF.Tanh)}
    pfree = [None] * 8
    cup_toks = cin_toks
    cbf_toks = [None] * NCH
    gb_free = {g: [None] * NCH for g in "fig"}
    wld_free = None
    uld_free = None
    for step in range(B):
        new_cbf = []
        for ch in range(NCH):
            tcb = em.v(lambda e, ch=ch: e.tensor_copy(
                out=cbf[:, ch * LB:(ch + 1) * LB], in_=cb[:, ch * LB:(ch + 1) * LB]),
                deps=cup_toks + [cbf_toks[ch]])
            new_cbf.append(tcb)
        cbf_toks = new_cbf
        new_cup = [None] * NCH
        gt_free = [None] * 2
        for g in "fig":
            wnm, unm, bnm, func = gate_cfg[g]
            twl = em.dgps(wwb.rearrange("p (a x) -> p a x", a=NCH),
                          fm_dram_ap(wts_in[wnm], C), sem="gqw0", deps=[wld_free])
            tul = em.dgps(uub.rearrange("p (a x) -> p a x", a=NCH),
                          fm_dram_ap(wts_in[unm], C), sem="gqw1", deps=[uld_free])
            for oc in range(NCH):
                bank = pb[oc % 4]
                tk = None
                for kc in range(NCH):
                    tk = em.t(lambda e, oc=oc, kc=kc, step=step, bank=bank: e.matmul(
                        out=bank[:, 0:LB],
                        lhsT=wwb[:, kc * C + oc * 128:kc * C + (oc + 1) * 128],
                        rhs=xn1b[:, kc * COLS + step * LB: kc * COLS + (step + 1) * LB],
                        start=(kc == 0), stop=False),
                        deps=([twl, tx1, pfree[oc % 4]] if kc == 0 else []))
                for kc in range(NCH):
                    tk = em.t(lambda e, oc=oc, kc=kc, bank=bank: e.matmul(
                        out=bank[:, 0:LB],
                        lhsT=uub[:, kc * C + oc * 128:kc * C + (oc + 1) * 128],
                        rhs=cbf[:, kc * LB:(kc + 1) * LB],
                        start=False, stop=(kc == NCH - 1)),
                        deps=([tul] + cbf_toks) if kc == 0 else [])
                if g == "i":
                    tev = em.a(lambda e, oc=oc, bank=bank, bnm=bnm, func=func:
                               e.activation(
                        out=ibuf[:, oc * LB:(oc + 1) * LB], in_=bank[:, 0:LB],
                        func=func, bias=biases[bnm][:, oc:oc + 1]),
                        deps=[tk, tbias, gb_free["i"][oc]])
                    pfree[oc % 4] = tev
                    gb_free["i"][oc] = tev
                else:
                    pp = oc % 2
                    tev = em.a(lambda e, oc=oc, bank=bank, bnm=bnm, func=func, pp=pp:
                               e.activation(
                        out=gtile[pp][:, 0:LB], in_=bank[:, 0:LB],
                        func=func, bias=biases[bnm][:, oc:oc + 1]),
                        deps=[tk, tbias, gt_free[pp]])
                    pfree[oc % 4] = tev
                    if g == "f":
                        tcu = em.v(lambda e, oc=oc, pp=pp: e.tensor_mul(
                            out=cb[:, oc * LB:(oc + 1) * LB],
                            in0=cb[:, oc * LB:(oc + 1) * LB], in1=gtile[pp][:, 0:LB]),
                            deps=[tev] + cup_toks + ([new_cup[oc]] if new_cup[oc] else []))
                    else:
                        tmu = em.v(lambda e, oc=oc, pp=pp: e.tensor_mul(
                            out=gtile[pp][:, 0:LB], in0=gtile[pp][:, 0:LB],
                            in1=ibuf[:, oc * LB:(oc + 1) * LB]),
                            deps=[tev, gb_free["i"][oc]])
                        tcu = em.v(lambda e, oc=oc, pp=pp: e.tensor_add(
                            out=cb[:, oc * LB:(oc + 1) * LB],
                            in0=cb[:, oc * LB:(oc + 1) * LB], in1=gtile[pp][:, 0:LB]),
                            deps=[tmu, new_cup[oc]])
                        gb_free["i"][oc] = tmu
                    gt_free[pp] = tcu
                    new_cup[oc] = tcu
            wld_free = tk
            uld_free = tk
        cup_toks = new_cup
    new_cbf = []
    for ch in range(NCH):
        tcb = em.v(lambda e, ch=ch: e.tensor_copy(
            out=cbf[:, ch * LB:(ch + 1) * LB], in_=cb[:, ch * LB:(ch + 1) * LB]),
            deps=cup_toks + [cbf_toks[ch]])
        new_cbf.append(tcb)
    cbf_toks = new_cbf
    ty = em.dsync(yb.rearrange("p (a x) -> p a x", a=NCH), fm_dram_ap(y_in, COLS),
                  sem="sqy", deps=cup_toks + cbf_toks)
    twc = em.dgps(wwb.rearrange("p (a x) -> p a x", a=NCH),
                  wcp_in.ap().rearrange("(a p) x -> p a x", p=128), sem="gqw0",
                  deps=[wld_free])
    two = em.dgps(uub.rearrange("p (a x) -> p a x", a=NCH), fm_dram_ap(wts_in["woT"], C),
                  sem="gqw1", deps=[uld_free])
    tuo = em.dgps(wob.rearrange("p (a x) -> p a x", a=NCH), fm_dram_ap(wts_in["uoT"], C),
                  sem="gqw2")
    store_toks = []
    xl_free = None
    ath_free = [None] * 2
    at2_free = [None] * 2
    ev_free = [None] * 2
    for ci, (c0, cw) in enumerate(col_tiles(COLS, LB)):
        txl = em.dsync(xl0ct.rearrange("p (a x) -> p a x", a=NCH)[:, :, 0:cw],
                       fm_dram_ap(xl0_in, COLS)[:, :, c0:c0 + cw], sem="sqxl",
                       deps=[xl_free])
        for oc in range(NCH):
            pp = oc % 2
            bank_h = pb[4 + pp]
            bank_a = pb[6 + pp]
            tkh = None
            for kc in range(NCH):
                tkh = em.t(lambda e, oc=oc, kc=kc, c0=c0, cw=cw, bank_h=bank_h: e.matmul(
                    out=bank_h[:, 0:cw],
                    lhsT=uub[:, kc * C + oc * 128:kc * C + (oc + 1) * 128],
                    rhs=xn1b[:, kc * COLS + c0: kc * COLS + c0 + cw],
                    start=(kc == 0), stop=False),
                    deps=([two, tx1, pfree[4 + pp]] if kc == 0 else []))
            for kc in range(NCH):
                tkh = em.t(lambda e, oc=oc, kc=kc, cw=cw, bank_h=bank_h: e.matmul(
                    out=bank_h[:, 0:cw],
                    lhsT=wob[:, kc * C + oc * 128:kc * C + (oc + 1) * 128],
                    rhs=cbf[:, kc * LB: kc * LB + cw],
                    start=False, stop=(kc == NCH - 1)),
                    deps=([tuo] + cbf_toks) if kc == 0 else [])
            tevh = em.a(lambda e, oc=oc, cw=cw, bank_h=bank_h, pp=pp: e.activation(
                out=ath[pp][:, 0:cw], in_=bank_h[:, 0:cw], func=AF.Tanh,
                bias=biases["bo"][:, oc:oc + 1]), deps=[tkh, tbias, ath_free[pp]])
            pfree[4 + pp] = tevh
            tka = None
            for kc in range(NCH):
                tka = em.t(lambda e, oc=oc, kc=kc, c0=c0, cw=cw, bank_a=bank_a: e.matmul(
                    out=bank_a[:, 0:cw],
                    lhsT=wwb[:, kc * C + oc * 128:kc * C + (oc + 1) * 128],
                    rhs=yb[:, kc * COLS + c0: kc * COLS + c0 + cw],
                    start=(kc == 0), stop=(kc == NCH - 1)),
                    deps=([twc, ty, pfree[6 + pp]] if kc == 0 else []))
            ta1 = em.v(lambda e, cw=cw, bank_a=bank_a, pp=pp: e.tensor_add(
                out=at2[pp][:, 0:cw], in0=ath[pp][:, 0:cw], in1=bank_a[:, 0:cw]),
                deps=[tka, tevh, at2_free[pp]])
            pfree[6 + pp] = ta1
            ath_free[pp] = ta1
            ta2 = em.v(lambda e, oc=oc, cw=cw, pp=pp: e.tensor_add(
                out=evst[pp][:, 0:cw], in0=at2[pp][:, 0:cw],
                in1=xl0ct[:, oc * 512:oc * 512 + cw]), deps=[ta1, txl, ev_free[pp]])
            at2_free[pp] = ta2
            tst = em.dsync(out_o.ap()[oc * 128:(oc + 1) * 128, c0:c0 + cw],
                           evst[pp][:, 0:cw], deps=[ta2], sem=f"sst{pp}")
            ev_free[pp] = tst
            store_toks.append(tst)
        xl_free = store_toks[-1]
    em.finalize(store_toks)
    return nc



def _np(a):
    a = np.asarray(a)
    if a.dtype == np.uint16:
        a = a.view(BFD)
    return a


def _runner(nc, in_maps):
    res = run_bass_kernel_spmd(nc, in_maps, core_ids=list(range(NCORES)))
    return res.results


def unshard_vt(cfg, shards):
    out = np.empty((B, cfg.T, 64), dtype=shards[0].dtype)
    for core in range(NCORES):
        s = shards[core]
        for i, (b, t0) in enumerate(seg_starts_global(cfg, core)):
            out[b, t0:t0 + cfg.L] = s[i * cfg.L:(i + 1) * cfg.L]
    return out


def kernel(**inputs):
    d = inputs
    f32 = lambda k: np.asarray(d[k], np.float32)
    x = f32("L0_attn_x")
    cfg = Cfg(x.shape[1])

    nc = build_launch1(cfg)
    im = launch1_inmaps(cfg, x, d["L0_attn_ln1_w"], d["L0_attn_ln1_b"],
                        d["L0_attn_tmk"], d["L0_attn_tmv"], d["L0_attn_tmr"],
                        d["L0_attn_att_kw"], d["L0_attn_att_vw"], d["L0_attn_att_rw"])
    r = _runner(nc, im)
    xn = unshard_cols(cfg, [_np(c["xn_o"]) for c in r])
    kk = unshard_cols(cfg, [_np(c["k_o"]) for c in r])
    vv = unshard_cols(cfg, [_np(c["v_o"]) for c in r])
    sr = unshard_cols(cfg, [_np(c["sr_o"]) for c in r])

    nc = build_launch2(cfg, L2=512)
    im = launch2_inmaps(cfg, kk, vv, d["L0_attn_time_decay"], d["L0_attn_time_first"])
    r = _runner(nc, im)
    wkv = launch2_unshard(cfg, [_np(c["wkv_o"]) for c in r])

    nc = build_launch3(cfg)
    owT = np.ascontiguousarray(f32("L0_attn_att_ow").T)
    im = [{"wkv_in": shard_cols(cfg, c, wkv), "sr_in": shard_cols(cfg, c, sr),
           "xn_in": shard_cols(cfg, c, xn), "owT": owT} for c in range(NCORES)]
    r = _runner(nc, im)
    x1 = unshard_cols(cfg, [_np(c["x1_o"]) for c in r])

    nc = build_launch4a(cfg)
    im = []
    for c in range(NCORES):
        x1s = shard_cols(cfg, c, x1)
        hxh, hmask = halo_of(cfg, c, x1)
        im.append({"x1in": np.ascontiguousarray(np.concatenate([x1s, hxh], 1), np.float32),
                   "maskin": hmask,
                   "ln2w": chunked(d["L0_attn_ln2_w"]), "ln2b": chunked(d["L0_attn_ln2_b"]),
                   "ftmk": chunked(d["L0_attn_f_tmk"]), "ftmr": chunked(d["L0_attn_f_tmr"])})
    r = _runner(nc, im)
    xk2 = unshard_cols(cfg, [_np(c["xk2_o"]) for c in r])
    xr2 = unshard_cols(cfg, [_np(c["xr2_o"]) for c in r])

    nc = build_launch4b(cfg)
    fkwT = np.ascontiguousarray(f32("L0_attn_f_kw").T)
    fvwT = np.ascontiguousarray(f32("L0_attn_f_vw").T)
    frwT = np.ascontiguousarray(f32("L0_attn_f_rw").T)
    im = [{"xk2_in": shard_cols(cfg, c, xk2.astype(BFD)),
           "xr2_in": shard_cols(cfg, c, xr2.astype(BFD)),
           "x1_in": shard_cols(cfg, c, x1),
           "fkwT": fkwT, "fvwT": fvwT, "frwT": frwT} for c in range(NCORES)]
    r = _runner(nc, im)
    ao = unshard_cols(cfg, [_np(c["ao_o"]) for c in r])

    nc = build_launch5(cfg)
    im = [{"xn_in": shard_cols(cfg, c, xn), "x_in": shard_cols(cfg, c, x),
           "ao_in": shard_cols(cfg, c, ao),
           "wgT": np.ascontiguousarray(f32("L0_ffn_w_gate").T),
           "wuT": np.ascontiguousarray(f32("L0_ffn_w_up").T),
           "wdT": np.ascontiguousarray(f32("L0_ffn_w_down").T)} for c in range(NCORES)]
    r = _runner(nc, im)
    xl0 = unshard_cols(cfg, [_np(c["xl0_o"]) for c in r])

    nc = build_launch6(cfg)
    im = [{"xl0_in": shard_cols(cfg, c, xl0),
           "wcd_in": f32("L1_attn_Wc_down"),
           "wqu_in": f32("L1_attn_Wq_up"),
           "wkv_in": f32("L1_attn_Wkv_up")} for c in range(NCORES)]
    r = _runner(nc, im)
    xn1 = unshard_cols(cfg, [_np(c["xn1_o"]) for c in r])
    qq = unshard_cols(cfg, [_np(c["q_o"]) for c in r])
    k1 = unshard_cols(cfg, [_np(c["k_o"]) for c in r])
    v1 = unshard_vt(cfg, [_np(c["vt_o"]) for c in r])

    nc = build_launch7(cfg)
    im = launch7_inmaps(cfg, qq, k1, v1)
    r = _runner(nc, im)
    y = launch7_unshard(cfg, [_np(c["y_o"]) for c in r])

    nc = build_launch8(cfg)
    im = []
    for c in range(NCORES):
        m = {"y_in": shard_cols(cfg, c, y.astype(BFD)),
             "xn1_in": shard_cols(cfg, c, xn1),
             "xl0_in": shard_cols(cfg, c, xl0),
             "wcp_in": f32("L1_attn_Wc_proj")}
        for nm, key in (("wfT", "L1_ffn_wf"), ("ufT", "L1_ffn_uf"),
                        ("wiT", "L1_ffn_wi"), ("uiT", "L1_ffn_ui"),
                        ("wcT", "L1_ffn_wc"), ("ucT", "L1_ffn_uc"),
                        ("woT", "L1_ffn_wo"), ("uoT", "L1_ffn_uo")):
            m[nm] = np.ascontiguousarray(f32(key).T)
        for nm, key in (("bf", "L1_ffn_bf"), ("bi", "L1_ffn_bi"), ("bc", "L1_ffn_bc"),
                        ("bo", "L1_ffn_bo"), ("in_c", "L1_ffn_in_c")):
            m[nm] = chunked(d[key])
        im.append(m)
    r = _runner(nc, im)
    out = unshard_cols(cfg, [_np(c["out_o"]).astype(np.float32) for c in r])
    return np.ascontiguousarray(out)


# version 04
# speedup vs baseline: 1.0003x; 1.0003x over previous
import numpy as np
import ml_dtypes
import concourse.bass as bass
import concourse.mybir as mybir
from concourse.bass_utils import run_bass_kernel_spmd

F32 = mybir.dt.float32
BF16 = mybir.dt.bfloat16
FP16 = mybir.dt.float16
AF = mybir.ActivationFunctionType
ALU = mybir.AluOpType
AX = mybir.AxisListType

NCORES = 8
C = 1024
NCH = 8
B = 4
EPS = 1e-5
BFD = np.dtype(ml_dtypes.bfloat16)

ENGINES = ("sync", "gpsimd", "vector", "scalar", "tensor")


def _lst(x):
    if x is None:
        return []
    if isinstance(x, list):
        return [t for t in x if t is not None]
    return [x]


class Cfg:
    def __init__(self, T=4096):
        self.T = T
        self.L = T // 16
        self.SEGS = 2 * B
        self.COLS = self.SEGS * self.L
        self.CA = self.COLS + self.SEGS


class Em:
    def __init__(self, nc):
        self.nc = nc
        self.streams = {e: [] for e in ENGINES}
        self.sems = {}
        self.ctr = {}

    def _sem(self, name):
        if name not in self.sems:
            self.sems[name] = self.nc.alloc_semaphore("sem_" + name)
            self.ctr[name] = 0
        return self.sems[name]

    def _emit(self, eng, fn, deps, semname, n):
        self._sem(semname)
        self.ctr[semname] += n
        val = self.ctr[semname]
        self.streams[eng].append((fn, tuple(_lst(list(deps))), semname, n))
        return (semname, val)

    def v(self, fn, deps=()):
        return self._emit("vector", fn, deps, "v", 1)

    def a(self, fn, deps=()):
        return self._emit("scalar", fn, deps, "a", 1)

    def t(self, fn, deps=()):
        return self._emit("tensor", fn, deps, "t", 1)

    def p(self, fn, deps=()):
        return self._emit("gpsimd", fn, deps, "p", 1)

    def dsync(self, out, in_, deps=(), sem="sq"):
        return self._emit("sync", lambda e: e.dma_start(out=out, in_=in_), deps, sem, 16)

    def dscal(self, out, in_, deps=(), sem="adq"):
        return self._emit("scalar", lambda e: e.dma_start(out=out, in_=in_), deps, sem, 16)

    def dgps(self, out, in_, deps=(), sem="gq"):
        return self._emit("gpsimd", lambda e: e.dma_start(out=out, in_=in_), deps, sem, 16)

    def barrier(self):
        return [(s, self.ctr[s]) for s in self.ctr]

    def finalize(self, final_deps):
        nc = self.nc
        streams, sems = self.streams, self.sems

        def section(eng):
            def run(e):
                last = {}
                for fn, deps, semname, n in streams[eng]:
                    for (dsem, dval) in deps:
                        if last.get(dsem, 0) < dval:
                            e.wait_ge(sems[dsem], dval)
                            last[dsem] = dval
                    fn(e).then_inc(sems[semname], n)
                if eng == "sync":
                    for (dsem, dval) in final_deps:
                        if last.get(dsem, 0) < dval:
                            e.wait_ge(sems[dsem], dval)
                            last[dsem] = dval
            return run

        with nc.Block() as blk:
            blk.sync(section("sync"))
            blk.gpsimd(section("gpsimd"))
            blk.vector(section("vector"))
            blk.scalar(section("scalar"))
            blk.tensor(section("tensor"))



def seg_starts_global(cfg, core):
    out = []
    for b in range(B):
        for blkid in (core, 15 - core):
            out.append((b, blkid * cfg.L))
    return out


def shard_cols(cfg, core, arr_btC):
    cols = []
    for b in range(B):
        for blkid in (core, 15 - core):
            cols.append(arr_btC[b, blkid * cfg.L:(blkid + 1) * cfg.L, :])
    return np.ascontiguousarray(np.concatenate(cols, axis=0).T)


def unshard_cols(cfg, shards):
    T = cfg.T
    CC = shards[0].shape[0]
    out = np.empty((B, T, CC), dtype=shards[0].dtype)
    for core in range(NCORES):
        s = shards[core]
        i = 0
        for b in range(B):
            for blkid in (core, 15 - core):
                out[b, blkid * cfg.L:(blkid + 1) * cfg.L, :] = s[:, i * cfg.L:(i + 1) * cfg.L].T
                i += 1
    return out


def halo_of(cfg, core, arr_btC):
    CC = arr_btC.shape[2]
    h = np.zeros((CC, cfg.SEGS), dtype=np.float32)
    m = np.zeros((1, cfg.SEGS), dtype=np.float32)
    for i, (b, t0) in enumerate(seg_starts_global(cfg, core)):
        if t0 > 0:
            h[:, i] = arr_btC[b, t0 - 1, :]
            m[0, i] = 1.0
    return h, m


def chunked(vec):
    return np.ascontiguousarray(np.asarray(vec, np.float32).reshape(NCH, 128).T)


def fm_dram_ap(t, X):
    return t.ap().rearrange("(a p) x -> p a x", p=128)


def col_tiles(n, w=512):
    return [(i, min(w, n - i)) for i in range(0, n, w)]



def emit_ln(em, st, src_ap, dst_ap, ncols, mean=True, gamma=None, beta=None,
            src_dep=None, dst_war=None):
    pb, rows, sq, atile = st["pb"], st["rows"], st["sq"], st["atile"]
    ones_col, ones_row, dep_ones = st["ones_col"], st["ones_row"], st["dep_ones"]
    src_dep = _lst(src_dep)
    dst_war = _lst(dst_war)
    out_toks = []
    for (c0, cw) in col_tiles(ncols):
        sq_free = [st.get("sq0_free"), st.get("sq1_free")]
        tmm1 = tmm2 = None
        for ch in range(NCH):
            pp = ch % 2
            tsq = em.a(lambda e, ch=ch, c0=c0, cw=cw, pp=pp: e.activation(
                out=sq[pp][:, 0:cw], in_=src_ap(ch, c0, cw), func=AF.Square),
                deps=src_dep + [sq_free[pp]])
            if mean:
                tmm1 = em.t(lambda e, ch=ch, c0=c0, cw=cw: e.matmul(
                    out=pb["s1"][0:1, 0:cw], lhsT=ones_col[:, 0:1], rhs=src_ap(ch, c0, cw),
                    start=(ch == 0), stop=(ch == NCH - 1)),
                    deps=([dep_ones, st.get("pb_s1_free")] + src_dep) if ch == 0 else [])
            tmm2 = em.t(lambda e, cw=cw, pp=pp, ch=ch: e.matmul(
                out=pb["s2"][0:1, 0:cw], lhsT=ones_col[:, 0:1], rhs=sq[pp][:, 0:cw],
                start=(ch == 0), stop=(ch == NCH - 1)),
                deps=[dep_ones, tsq, st.get("pb_s2_free")] if ch == 0 else [tsq])
            sq_free[pp] = tmm2
        st["sq0_free"], st["sq1_free"] = sq_free
        r_row = rows[0:1, 0:cw]
        mr_row = rows[0:1, 512:512 + cw]
        tmp_row = rows[0:1, 1024:1024 + cw]
        t2_row = rows[0:1, 1536:1536 + cw]
        rows_free = st.get("rows_free")
        tk1 = em.v(lambda e, cw=cw, tmp_row=tmp_row: e.tensor_scalar_mul(
            out=tmp_row, in0=pb["s2"][0:1, 0:cw], scalar1=1.0 / C),
            deps=[tmm2, rows_free])
        tkm = None
        if mean:
            tkm = em.v(lambda e, cw=cw, mr_row=mr_row: e.tensor_scalar_mul(
                out=mr_row, in0=pb["s1"][0:1, 0:cw], scalar1=1.0 / C),
                deps=[tmm1, rows_free])
            tkn = em.v(lambda e, cw=cw, t2_row=t2_row, mr_row=mr_row: e.scalar_tensor_tensor(
                out=t2_row, in0=mr_row, scalar=-1.0, in1=mr_row,
                op0=ALU.mult, op1=ALU.mult), deps=[tkm, rows_free])
            tk1 = em.v(lambda e, cw=cw, tmp_row=tmp_row, t2_row=t2_row: e.tensor_add(
                out=tmp_row, in0=tmp_row, in1=t2_row), deps=[tk1, tkn])
        tk2 = em.a(lambda e, cw=cw, tmp_row=tmp_row: e.activation(
            out=tmp_row, in_=tmp_row, func=AF.Sqrt, bias=st["eps_col"][0:1, 0:1]),
            deps=[tk1, st["dep_eps"]])
        tk3 = em.v(lambda e, cw=cw, r_row=r_row, tmp_row=tmp_row: e.reciprocal(out=r_row, in_=tmp_row), deps=[tk2])
        st["pb_s2_free"] = tk1
        tk4 = None
        if mean:
            st["pb_s1_free"] = tkm
            tk4 = em.v(lambda e, cw=cw, mr_row=mr_row, r_row=r_row: e.tensor_mul(
                out=mr_row, in0=mr_row, in1=r_row), deps=[tk3])
        tb1 = em.t(lambda e, cw=cw, r_row=r_row: e.matmul(
            out=pb["b1"][:, 0:cw], lhsT=ones_row[0:1, :], rhs=r_row,
            start=True, stop=True), deps=[tk3, dep_ones, st.get("pb_b1_free")])
        tb2 = None
        if mean:
            tb2 = em.t(lambda e, cw=cw, mr_row=mr_row: e.matmul(
                out=pb["b2"][:, 0:cw], lhsT=ones_row[0:1, :], rhs=mr_row,
                start=True, stop=True), deps=[tk4, dep_ones, st.get("pb_b2_free")])
        st["rows_free"] = tb2 if mean else tb1
        last_ap = []
        for ch in range(NCH):
            pp = ch % 2
            at_free = st.get(f"atile{pp}_free")
            cur = em.v(lambda e, ch=ch, c0=c0, cw=cw, pp=pp: e.tensor_mul(
                out=atile[pp][:, 0:cw], in0=src_ap(ch, c0, cw), in1=pb["b1"][:, 0:cw]),
                deps=[tb1, at_free] + src_dep)
            if mean and gamma is not None:
                cur = em.v(lambda e, cw=cw, pp=pp: e.tensor_sub(
                    out=atile[pp][:, 0:cw], in0=atile[pp][:, 0:cw], in1=pb["b2"][:, 0:cw]),
                    deps=[tb2, cur])
                cur = em.v(lambda e, ch=ch, c0=c0, cw=cw, pp=pp: e.tensor_scalar(
                    out=dst_ap(ch, c0, cw), in0=atile[pp][:, 0:cw],
                    scalar1=gamma[0][:, ch:ch + 1], scalar2=beta[0][:, ch:ch + 1],
                    op0=ALU.mult, op1=ALU.add), deps=[cur, gamma[1]] + dst_war)
            elif mean:
                cur = em.v(lambda e, ch=ch, c0=c0, cw=cw, pp=pp: e.tensor_sub(
                    out=dst_ap(ch, c0, cw), in0=atile[pp][:, 0:cw], in1=pb["b2"][:, 0:cw]),
                    deps=[tb2, cur] + dst_war)
            elif gamma is not None:
                cur = em.v(lambda e, ch=ch, c0=c0, cw=cw, pp=pp: e.tensor_scalar(
                    out=dst_ap(ch, c0, cw), in0=atile[pp][:, 0:cw],
                    scalar1=gamma[0][:, ch:ch + 1], scalar2=beta[0][:, ch:ch + 1],
                    op0=ALU.mult, op1=ALU.add), deps=[cur, gamma[1]] + dst_war)
            else:
                cur = em.v(lambda e, ch=ch, c0=c0, cw=cw, pp=pp: e.tensor_copy(
                    out=dst_ap(ch, c0, cw), in_=atile[pp][:, 0:cw]),
                    deps=[cur] + dst_war)
            st[f"atile{pp}_free"] = cur
            last_ap.append(cur)
        st["pb_b1_free"] = last_ap[-1]
        if mean:
            st["pb_b2_free"] = last_ap[-1]
        out_toks.extend(last_ap)
    return out_toks


def make_state(em, nc, tag="", rows=None):
    pb = {k: nc.alloc_psum_tensor(f"pb_{k}{tag}", [128, 512], F32)[:]
          for k in ("s1", "s2", "b1", "b2", "m0", "m1", "m2", "m3")}
    if rows is None:
        rows = nc.alloc_sbuf_tensor(f"rows{tag}", [1, 2048], F32)[:]
    sq = [nc.alloc_sbuf_tensor(f"sq{i}{tag}", [128, 512], F32)[:] for i in range(2)]
    atile = [nc.alloc_sbuf_tensor(f"atile{i}{tag}", [128, 512], F32)[:] for i in range(2)]
    ones_col = nc.alloc_sbuf_tensor(f"ones_col{tag}", [128, 1], F32)[:]
    ones_row = nc.alloc_sbuf_tensor(f"ones_row{tag}", [1, 128], F32)[:]
    eps_col = nc.alloc_sbuf_tensor(f"eps_col{tag}", [128, 1], F32)[:]
    t1 = em.v(lambda e: e.memset(ones_col, 1.0))
    t2 = em.v(lambda e: e.memset(ones_row, 1.0))
    t3 = em.v(lambda e: e.memset(eps_col, float(EPS)))
    return dict(pb=pb, rows=rows, sq=sq, atile=atile, ones_col=ones_col,
                ones_row=ones_row, dep_ones=t2, eps_col=eps_col, dep_eps=t3)



def build_launch1(cfg, projs=(("kwT", "tmk"), ("vwT", "tmv"), ("rwT", "tmr"))):
    nc = bass.Bass("TRN2", target_bir_lowering=False, debug=False)
    em = Em(nc)
    CA, COLS, SEGS, L = cfg.CA, cfg.COLS, cfg.SEGS, cfg.L

    xin = nc.dram_tensor("xin", [C, CA], F32, kind="ExternalInput")
    maskin = nc.dram_tensor("maskin", [1, SEGS], F32, kind="ExternalInput").ap()
    prm_names = ("ln1w", "ln1b", "tmk", "tmv", "tmr")
    prm = {nm: nc.dram_tensor(nm, [128, NCH], F32, kind="ExternalInput").ap()
           for nm in prm_names}
    wts = {nm: nc.dram_tensor(nm, [C, C], F32, kind="ExternalInput")
           for nm in ("kwT", "vwT", "rwT")}
    xn_o = nc.dram_tensor("xn_o", [C, COLS], F32, kind="ExternalOutput")
    k_o = nc.dram_tensor("k_o", [C, COLS], F32, kind="ExternalOutput")
    v_o = nc.dram_tensor("v_o", [C, COLS], F32, kind="ExternalOutput")
    sr_o = nc.dram_tensor("sr_o", [C, COLS], F32, kind="ExternalOutput")
    proj_out = {"kwT": k_o, "vwT": v_o, "rwT": sr_o}

    bufA = nc.alloc_sbuf_tensor("bufA", [128, NCH * CA], F32)[:]
    bufB = nc.alloc_sbuf_tensor("bufB", [128, NCH * CA], F32)[:]
    mixb = nc.alloc_sbuf_tensor("mixb", [128, NCH * COLS], BF16)[:]
    un0 = nc.alloc_sbuf_tensor("un0", [128, max(COLS, 2048)], F32)
    t1b = un0[:]
    wbuf = nc.alloc_sbuf_tensor("wbuf", [128, NCH * C], BF16)[:]
    hm = nc.alloc_sbuf_tensor("hm", [128, NCH * SEGS], F32)[:]
    maskB = nc.alloc_sbuf_tensor("maskB", [128, SEGS], F32)[:]
    mrow = nc.alloc_sbuf_tensor("mrow", [1, SEGS], F32)[:]
    tm1 = {nm: nc.alloc_sbuf_tensor(f"tm1_{nm}", [128, NCH], F32)[:]
           for nm in ("tmk", "tmv", "tmr")}
    prms = {nm: nc.alloc_sbuf_tensor("s_" + nm, [128, NCH], F32)[:] for nm in prm_names}
    evac = [nc.alloc_sbuf_tensor(f"evac{i}", [128, 512], F32)[:] for i in range(4)]

    st = make_state(em, nc, rows=un0[:][0:1, 0:2048])
    pb = st["pb"]

    tx = em.dsync(bufA.rearrange("p (a x) -> p a x", a=NCH), fm_dram_ap(xin, CA), sem="sqx")
    for nm in prm_names:
        em.dsync(prms[nm], prm[nm], sem="sqp")
    tp_all = em.dsync(mrow, maskin, sem="sqp")
    tprm = {nm: tp_all for nm in prm_names}
    tmask = tp_all
    tm1_toks = {}
    for nm in ("tmk", "tmv", "tmr"):
        tm1_toks[nm] = em.v(lambda e, nm=nm: e.tensor_scalar(
            out=tm1[nm], in0=prms[nm], scalar1=-1.0, scalar2=1.0,
            op0=ALU.mult, op1=ALU.add), deps=[tprm[nm]])
    tmb0 = em.t(lambda e: e.matmul(out=pb["b2"][:, 0:SEGS], lhsT=st["ones_row"][0:1, :],
                                   rhs=mrow, start=True, stop=True),
                deps=[tmask, st["dep_ones"]])
    tmb = em.v(lambda e: e.tensor_copy(out=maskB, in_=pb["b2"][:, 0:SEGS]), deps=[tmb0])
    st["pb_b2_free"] = tmb

    def a_ap(ch, c0, cw):
        return bufA[:, ch * CA + c0: ch * CA + c0 + cw]

    def b_ap(ch, c0, cw):
        return bufB[:, ch * CA + c0: ch * CA + c0 + cw]

    emit_ln(em, st, a_ap, b_ap, CA, mean=True, src_dep=[tx])
    bar_xn = em.barrier()
    t_xn_store = em.dsync(fm_dram_ap(xn_o, COLS),
                          bufB.rearrange("p (a x) -> p a x", a=NCH)[:, :, 0:COLS],
                          deps=bar_xn, sem="sqxn")
    emit_ln(em, st, b_ap, a_ap, CA, mean=False,
            gamma=(prms["ln1w"], tprm["ln1w"]), beta=(prms["ln1b"], tprm["ln1b"]),
            src_dep=bar_xn, dst_war=bar_xn)
    bar_h = em.barrier()
    hm_toks = []
    for ch in range(NCH):
        hm_toks.append(em.v(lambda e, ch=ch: e.tensor_mul(
            out=hm[:, ch * SEGS:(ch + 1) * SEGS],
            in0=bufA[:, ch * CA + COLS: ch * CA + CA], in1=maskB),
            deps=bar_h + [tmb]))
    bar_hm = em.barrier()

    prev_proj_done = None
    prev_mm_done = None
    store_toks = []
    evac_free = [None] * 4
    for pi, (nm, tmnm) in enumerate(projs):
        twl = em.dgps(wbuf.rearrange("p (a x) -> p a x", a=NCH),
                      fm_dram_ap(wts[nm], C), deps=[prev_proj_done], sem=f"gqw{pi % 2}")
        for ch in range(NCH):
            war = [prev_proj_done, prev_mm_done]
            tt1 = em.v(lambda e, ch=ch, tmnm=tmnm: e.tensor_scalar_mul(
                out=t1b[:, 0:COLS], in0=bufA[:, ch * CA: ch * CA + COLS],
                scalar1=prms[tmnm][:, ch:ch + 1]),
                deps=bar_hm + [st.get("t1b_free"), tprm[tmnm]])
            tbig = em.v(lambda e, ch=ch, tmnm=tmnm: e.scalar_tensor_tensor(
                out=mixb[:, ch * COLS + 1: ch * COLS + COLS],
                in0=bufA[:, ch * CA: ch * CA + COLS - 1],
                scalar=tm1[tmnm][:, ch:ch + 1], in1=t1b[:, 1:COLS],
                op0=ALU.mult, op1=ALU.add), deps=[tt1, tm1_toks[tmnm]] + war)
            tfix = em.v(lambda e, ch=ch, tmnm=tmnm: e.scalar_tensor_tensor(
                out=mixb[:, ch * COLS: (ch + 1) * COLS].rearrange(
                    "p (s l) -> p s l", s=SEGS)[:, :, 0:1],
                in0=hm[:, ch * SEGS:(ch + 1) * SEGS].rearrange("p (s o) -> p s o", o=1),
                scalar=tm1[tmnm][:, ch:ch + 1],
                in1=t1b[:, 0:COLS].rearrange("p (s l) -> p s l", s=SEGS)[:, :, 0:1],
                op0=ALU.mult, op1=ALU.add), deps=[tt1, tbig, hm_toks[ch], tm1_toks[tmnm]] + war)
            st["t1b_free"] = tfix
        bar_mix = em.barrier()
        nct = len(col_tiles(COLS))
        for d in range(NCH):
            for ci, (c0, cw) in enumerate(col_tiles(COLS)):
                bank = pb[f"m{ci % 4}"]
                tk = None
                for kc in range(NCH):
                    tk = em.t(lambda e, d=d, kc=kc, c0=c0, cw=cw, bank=bank: e.matmul(
                        out=bank[:, 0:cw],
                        lhsT=wbuf[:, kc * C + d * 128:kc * C + (d + 1) * 128],
                        rhs=mixb[:, kc * COLS + c0: kc * COLS + c0 + cw],
                        start=(kc == 0), stop=(kc == NCH - 1)),
                        deps=([twl, st.get(f"pbm{ci % 4}_free")] + bar_mix) if kc == 0 else [])
                ei = (d * nct + ci) % 4
                if nm == "rwT":
                    tev = em.a(lambda e, cw=cw, bank=bank, ei=ei: e.activation(
                        out=evac[ei][:, 0:cw], in_=bank[:, 0:cw], func=AF.Sigmoid),
                        deps=[tk, evac_free[ei]])
                else:
                    tev = em.v(lambda e, cw=cw, bank=bank, ei=ei: e.tensor_copy(
                        out=evac[ei][:, 0:cw], in_=bank[:, 0:cw]),
                        deps=[tk, evac_free[ei]])
                st[f"pbm{ci % 4}_free"] = tev
                tst = em.dsync(proj_out[nm].ap()[d * 128:(d + 1) * 128, c0:c0 + cw],
                               evac[ei][:, 0:cw], deps=[tev], sem=f"sst{ei}")
                evac_free[ei] = tst
                store_toks.append(tst)
        prev_proj_done = store_toks[-1]
        prev_mm_done = tk

    em.finalize([t_xn_store] + store_toks)
    return nc


def launch1_inmaps(cfg, x_btC, ln1w, ln1b, tmk, tmv, tmr, att_kw, att_vw, att_rw):
    kwT = np.ascontiguousarray(np.asarray(att_kw, np.float32).T)
    vwT = np.ascontiguousarray(np.asarray(att_vw, np.float32).T)
    rwT = np.ascontiguousarray(np.asarray(att_rw, np.float32).T)
    in_maps = []
    for core in range(NCORES):
        xs = shard_cols(cfg, core, x_btC).astype(np.float32)
        hx, hmask = halo_of(cfg, core, x_btC)
        xin = np.ascontiguousarray(np.concatenate([xs, hx], axis=1), dtype=np.float32)
        in_maps.append({
            "xin": xin, "maskin": hmask,
            "ln1w": chunked(ln1w), "ln1b": chunked(ln1b),
            "tmk": chunked(tmk), "tmv": chunked(tmv), "tmr": chunked(tmr),
            "kwT": kwT, "vwT": vwT, "rwT": rwT,
        })
    return in_maps






def build_launch2(cfg, L2=512):
    nc = bass.Bass("TRN2", target_bir_lowering=False, debug=False)
    em = Em(nc)
    T = cfg.T
    L2 = min(L2, T)
    NRT = 4
    NCK = T // L2

    kin = nc.dram_tensor("kin", [512, T], F32, kind="ExternalInput")
    vin = nc.dram_tensor("vin", [512, T], F32, kind="ExternalInput")
    dec_in = nc.dram_tensor("dec_in", [128, NRT], F32, kind="ExternalInput").ap()
    tf_in = nc.dram_tensor("tf_in", [128, NRT], F32, kind="ExternalInput").ap()
    wkv_o = nc.dram_tensor("wkv_o", [512, T], F32, kind="ExternalOutput")

    kb = nc.alloc_sbuf_tensor("kb", [128, T], F32)[:]
    vb = nc.alloc_sbuf_tensor("vb", [128, T], F32)[:]
    ob = nc.alloc_sbuf_tensor("ob", [128, T], F32)[:]
    ekb = nc.alloc_sbuf_tensor("ekb", [128, L2], F32)[:]
    u1b = nc.alloc_sbuf_tensor("u1b", [128, L2], F32)[:]
    sab = nc.alloc_sbuf_tensor("sab", [128, L2], F32)[:]
    sbb = nc.alloc_sbuf_tensor("sbb", [128, L2], F32)[:]
    d1b = nc.alloc_sbuf_tensor("d1b", [128, L2 + 1], F32)[:]
    dnb = nc.alloc_sbuf_tensor("dnb", [128, L2], F32)[:]
    nnb = nc.alloc_sbuf_tensor("nnb", [128, L2], F32)[:]
    gt = nc.alloc_sbuf_tensor("gt", [128, L2], F32)[:]
    wramp = nc.alloc_sbuf_tensor("wramp", [128, L2 + 1], F32)[:]
    iot = nc.alloc_sbuf_tensor("iot", [128, L2 + 1], F32)[:]
    dec = nc.alloc_sbuf_tensor("dec", [128, NRT], F32)[:]
    tf = nc.alloc_sbuf_tensor("tf", [128, NRT], F32)[:]
    ew = nc.alloc_sbuf_tensor("ew", [128, 1], F32)[:]
    wcol = nc.alloc_sbuf_tensor("wcol", [128, 1], F32)[:]
    gcol = nc.alloc_sbuf_tensor("gcol", [128, 1], F32)[:]
    relu_u = nc.alloc_sbuf_tensor("relu_u", [128, 1], F32)[:]
    expu = nc.alloc_sbuf_tensor("expu", [128, 1], F32)[:]
    mx = nc.alloc_sbuf_tensor("mx", [128, 1], F32)[:]
    Mb = nc.alloc_sbuf_tensor("Mb", [128, 1], F32)[:]
    negM = nc.alloc_sbuf_tensor("negM", [128, 1], F32)[:]
    PmM = nc.alloc_sbuf_tensor("PmM", [128, 1], F32)[:]
    Pbuf = [nc.alloc_sbuf_tensor(f"P{i}", [128, 1], F32)[:] for i in range(2)]
    Abuf = [nc.alloc_sbuf_tensor(f"A{i}", [128, 1], F32)[:] for i in range(2)]
    Bbuf = [nc.alloc_sbuf_tensor(f"Bc{i}", [128, 1], F32)[:] for i in range(2)]

    tdec = em.dsync(dec, dec_in, sem="sqp")
    tdec = em.dsync(tf, tf_in, sem="sqp")
    tio = em.p(lambda e: e.iota(out=iot, pattern=[[1, L2 + 1]], base=0,
                                channel_multiplier=0,
                                allow_small_or_imprecise_dtypes=True))
    store_toks = []
    bar_prev = None
    for rt in range(NRT):
        ldeps = [] if bar_prev is None else bar_prev
        tkld = em.dsync(kb, kin.ap()[rt * 128:(rt + 1) * 128, :], sem="sqk", deps=ldeps)
        tvld = em.dsync(vb, vin.ap()[rt * 128:(rt + 1) * 128, :], sem="sqv", deps=ldeps)
        t1 = em.a(lambda e, rt=rt: e.activation(out=ew, in_=dec[:, rt:rt + 1],
                                                func=AF.Exp), deps=[tdec] + ldeps)
        t2 = em.v(lambda e: e.tensor_scalar_mul(out=wcol, in0=ew, scalar1=-1.0),
                  deps=[t1] + ldeps)
        t3 = em.a(lambda e: e.activation(out=gcol, in_=wcol, func=AF.Exp), deps=[t2])
        t4 = em.v(lambda e, rt=rt: e.tensor_scalar_max(out=relu_u, in0=tf[:, rt:rt + 1],
                                                       scalar1=0.0), deps=[tdec] + ldeps)
        t5 = em.a(lambda e, rt=rt: e.activation(out=expu, in_=tf[:, rt:rt + 1],
                                                func=AF.Exp), deps=[tdec] + ldeps)
        t6 = em.v(lambda e: e.memset(gt, 1.0), deps=ldeps)
        t7 = em.v(lambda e: e.tensor_scalar_mul(out=gt, in0=gt, scalar1=gcol[:, 0:1]),
                  deps=[t6, t3])
        t8 = em.v(lambda e: e.tensor_scalar_mul(out=wramp, in0=iot, scalar1=wcol[:, 0:1]),
                  deps=[tio, t2] + ldeps)
        c1 = em.v(lambda e: e.memset(Pbuf[0], -3e38), deps=ldeps)
        c2 = em.v(lambda e: e.memset(Abuf[0], 0.0), deps=ldeps)
        c3 = em.v(lambda e: e.memset(Bbuf[0], 0.0), deps=ldeps)
        bar = em.barrier()
        for ci in range(NCK):
            pc, qc = ci % 2, (ci + 1) % 2
            c0 = ci * L2
            P, A, Bc = Pbuf[pc], Abuf[pc], Bbuf[pc]
            P2, A2, B2 = Pbuf[qc], Abuf[qc], Bbuf[qc]
            k_ap = kb[:, c0:c0 + L2]
            v_ap = vb[:, c0:c0 + L2]
            tmx = em.v(lambda e, k_ap=k_ap: e.reduce_max(out=mx, in_=k_ap, axis=AX.X),
                       deps=bar)
            tM0 = em.v(lambda e: e.tensor_add(out=Mb, in0=mx, in1=relu_u), deps=[tmx] + bar)
            tM = em.v(lambda e, P=P: e.tensor_max(out=Mb, in0=Mb, in1=P), deps=[tM0] + bar)
            tnM = em.v(lambda e: e.tensor_scalar_mul(out=negM, in0=Mb, scalar1=-1.0),
                       deps=[tM] + bar)
            tek = em.a(lambda e, k_ap=k_ap: e.activation(out=ekb, in_=k_ap, func=AF.Exp,
                                                         bias=negM[:, 0:1]),
                       deps=[tnM] + bar)
            tu1 = em.v(lambda e, v_ap=v_ap: e.tensor_mul(out=u1b, in0=ekb, in1=v_ap),
                       deps=[tek] + bar)
            tsa = em.v(lambda e: e.tensor_tensor_scan(out=sab, data0=gt, data1=u1b,
                                                      initial=0.0, op0=ALU.mult,
                                                      op1=ALU.add), deps=[tu1] + bar)
            tsb = em.v(lambda e: e.tensor_tensor_scan(out=sbb, data0=gt, data1=ekb,
                                                      initial=0.0, op0=ALU.mult,
                                                      op1=ALU.add), deps=[tek] + bar)
            tpm = em.v(lambda e, P=P: e.tensor_sub(out=PmM, in0=P, in1=Mb),
                       deps=[tM] + bar)
            td1 = em.a(lambda e: e.activation(out=d1b, in_=wramp, func=AF.Exp,
                                              bias=PmM[:, 0:1]), deps=[tpm] + bar)
            tn1 = em.v(lambda e, A=A: e.tensor_scalar_mul(out=nnb, in0=d1b[:, 0:L2],
                                                          scalar1=A[:, 0:1]),
                       deps=[td1] + bar)
            tn2 = em.v(lambda e: e.tensor_add(out=nnb[:, 1:L2], in0=nnb[:, 1:L2],
                                              in1=sab[:, 0:L2 - 1]), deps=[tn1, tsa])
            tn3 = em.v(lambda e: e.scalar_tensor_tensor(out=nnb, in0=u1b,
                                                        scalar=expu[:, 0:1], in1=nnb,
                                                        op0=ALU.mult, op1=ALU.add),
                       deps=[tn2, tu1] + bar)
            td2 = em.v(lambda e, Bc=Bc: e.tensor_scalar_mul(out=dnb, in0=d1b[:, 0:L2],
                                                            scalar1=Bc[:, 0:1]),
                       deps=[td1, tn1] + bar)
            td3 = em.v(lambda e: e.tensor_add(out=dnb[:, 1:L2], in0=dnb[:, 1:L2],
                                              in1=sbb[:, 0:L2 - 1]), deps=[td2, tsb])
            td4 = em.v(lambda e: e.scalar_tensor_tensor(out=dnb, in0=ekb,
                                                        scalar=expu[:, 0:1], in1=dnb,
                                                        op0=ALU.mult, op1=ALU.add),
                       deps=[td3, tek] + bar)
            tl1 = em.a(lambda e: e.activation(out=dnb, in_=dnb, func=AF.Ln), deps=[td4])
            tl2 = em.a(lambda e: e.activation(out=dnb, in_=dnb, func=AF.Exp, scale=-1.0),
                       deps=[tl1])
            to = em.v(lambda e, c0=c0: e.tensor_mul(out=ob[:, c0:c0 + L2], in0=nnb,
                                                    in1=dnb), deps=[tl2, tn3] + bar)
            tca = em.v(lambda e, A=A, A2=A2: e.scalar_tensor_tensor(
                out=A2, in0=d1b[:, L2:L2 + 1], scalar=A[:, 0:1],
                in1=sab[:, L2 - 1:L2], op0=ALU.mult, op1=ALU.add),
                deps=[td1, tsa] + bar)
            tcb = em.v(lambda e, Bc=Bc, B2=B2: e.scalar_tensor_tensor(
                out=B2, in0=d1b[:, L2:L2 + 1], scalar=Bc[:, 0:1],
                in1=sbb[:, L2 - 1:L2], op0=ALU.mult, op1=ALU.add),
                deps=[td1, tsb] + bar)
            tcp = em.v(lambda e, P2=P2: e.tensor_copy(out=P2, in_=Mb), deps=[tM] + bar)
            bar = em.barrier()
        tst = em.dsync(wkv_o.ap()[rt * 128:(rt + 1) * 128, :], ob, sem="sqo", deps=bar)
        store_toks.append(tst)
        bar_prev = bar + [tst]
    em.finalize(store_toks)
    return nc


def launch2_inmaps(cfg, k_btC, v_btC, time_decay, time_first):
    T = cfg.T
    kr = np.ascontiguousarray(np.asarray(k_btC, np.float32).transpose(0, 2, 1).reshape(B * C, T))
    vr = np.ascontiguousarray(np.asarray(v_btC, np.float32).transpose(0, 2, 1).reshape(B * C, T))
    decf = np.tile(np.asarray(time_decay, np.float32), B)
    tff = np.tile(np.asarray(time_first, np.float32), B)
    in_maps = []
    for core in range(NCORES):
        rs = slice(core * 512, (core + 1) * 512)
        in_maps.append({
            "kin": kr[rs], "vin": vr[rs],
            "dec_in": np.ascontiguousarray(decf[rs].reshape(4, 128).T),
            "tf_in": np.ascontiguousarray(tff[rs].reshape(4, 128).T),
        })
    return in_maps


def launch2_unshard(cfg, outs):
    T = cfg.T
    full = np.concatenate(outs, axis=0)
    return full.reshape(B, C, T).transpose(0, 2, 1)



def emit_mix(em, st, srcbuf, src_stride, dstbuf, dst_stride, tm_sb, tm1_sb, hmbuf,
             ncols, SEGS, L, deps, t1b):
    toks = []
    for ch in range(NCH):
        tt1 = em.v(lambda e, ch=ch: e.tensor_scalar_mul(
            out=t1b[:, 0:ncols], in0=srcbuf[:, ch * src_stride: ch * src_stride + ncols],
            scalar1=tm_sb[:, ch:ch + 1]),
            deps=deps + [st.get("t1b_free")])
        tbig = em.v(lambda e, ch=ch: e.scalar_tensor_tensor(
            out=dstbuf[:, ch * dst_stride + 1: ch * dst_stride + ncols],
            in0=srcbuf[:, ch * src_stride: ch * src_stride + ncols - 1],
            scalar=tm1_sb[:, ch:ch + 1], in1=t1b[:, 1:ncols],
            op0=ALU.mult, op1=ALU.add), deps=[tt1] + deps)
        tfix = em.v(lambda e, ch=ch: e.scalar_tensor_tensor(
            out=dstbuf[:, ch * dst_stride: ch * dst_stride + ncols].rearrange(
                "p (s l) -> p s l", s=SEGS)[:, :, 0:1],
            in0=hmbuf[:, ch * SEGS:(ch + 1) * SEGS].rearrange("p (s o) -> p s o", o=1),
            scalar=tm1_sb[:, ch:ch + 1],
            in1=t1b[:, 0:ncols].rearrange("p (s l) -> p s l", s=SEGS)[:, :, 0:1],
            op0=ALU.mult, op1=ALU.add), deps=[tt1, tbig] + deps)
        st["t1b_free"] = tfix
        toks += [tbig, tfix]
    return toks



def build_launch3(cfg):
    nc = bass.Bass("TRN2", target_bir_lowering=False, debug=False)
    em = Em(nc)
    COLS = cfg.COLS

    wkv_in = nc.dram_tensor("wkv_in", [C, COLS], F32, kind="ExternalInput")
    sr_in = nc.dram_tensor("sr_in", [C, COLS], F32, kind="ExternalInput")
    xn_in = nc.dram_tensor("xn_in", [C, COLS], F32, kind="ExternalInput")
    owT = nc.dram_tensor("owT", [C, C], F32, kind="ExternalInput")
    x1_o = nc.dram_tensor("x1_o", [C, COLS], F32, kind="ExternalOutput")

    xnb = nc.alloc_sbuf_tensor("xnb", [128, NCH * COLS], F32)[:]
    rwkvb = nc.alloc_sbuf_tensor("rwkvb", [128, NCH * COLS], BF16)[:]
    wkvt = nc.alloc_sbuf_tensor("wkvt", [128, COLS], F32)[:]
    srt = nc.alloc_sbuf_tensor("srt", [128, COLS], F32)[:]
    wbuf = nc.alloc_sbuf_tensor("wbuf", [128, NCH * C], BF16)[:]
    evac = [nc.alloc_sbuf_tensor(f"evac{i}", [128, 512], F32)[:] for i in range(4)]
    pb = {f"m{i}": nc.alloc_psum_tensor(f"pbm{i}", [128, 512], F32)[:] for i in range(4)}

    txn = em.dsync(xnb.rearrange("p (a x) -> p a x", a=NCH), fm_dram_ap(xn_in, COLS),
                   sem="sqxn")
    twl = em.dgps(wbuf.rearrange("p (a x) -> p a x", a=NCH), fm_dram_ap(owT, C))
    tmul_prev = None
    mul_toks = []
    for ch in range(NCH):
        t1 = em.dsync(wkvt, wkv_in.ap()[ch * 128:(ch + 1) * 128, :], sem="sqw",
                      deps=[tmul_prev])
        t2 = em.dsync(srt, sr_in.ap()[ch * 128:(ch + 1) * 128, :], sem="sqs",
                      deps=[tmul_prev])
        tmul_prev = em.v(lambda e, ch=ch: e.tensor_mul(
            out=rwkvb[:, ch * COLS:(ch + 1) * COLS], in0=srt, in1=wkvt),
            deps=[t1, t2])
        mul_toks.append(tmul_prev)
    nct = len(col_tiles(COLS))
    store_toks = []
    evac_free = [None] * 4
    pbm_free = [None] * 4
    for cc in range(NCH):
        for ci, (c0, cw) in enumerate(col_tiles(COLS)):
            bank = pb[f"m{ci % 4}"]
            tk = None
            for dc in range(NCH):
                tk = em.t(lambda e, cc=cc, dc=dc, c0=c0, cw=cw, bank=bank: e.matmul(
                    out=bank[:, 0:cw],
                    lhsT=wbuf[:, dc * C + cc * 128:dc * C + (cc + 1) * 128],
                    rhs=rwkvb[:, dc * COLS + c0: dc * COLS + c0 + cw],
                    start=(dc == 0), stop=(dc == NCH - 1)),
                    deps=([twl, pbm_free[ci % 4]] + mul_toks) if dc == 0 else [])
            ei = (cc * nct + ci) % 4
            tev = em.v(lambda e, cc=cc, c0=c0, cw=cw, bank=bank, ei=ei: e.tensor_add(
                out=evac[ei][:, 0:cw], in0=bank[:, 0:cw],
                in1=xnb[:, cc * COLS + c0: cc * COLS + c0 + cw]),
                deps=[tk, txn, evac_free[ei]])
            pbm_free[ci % 4] = tev
            tst = em.dsync(x1_o.ap()[cc * 128:(cc + 1) * 128, c0:c0 + cw],
                           evac[ei][:, 0:cw], deps=[tev], sem=f"sst{ei}")
            evac_free[ei] = tst
            store_toks.append(tst)
    em.finalize(store_toks)
    return nc



def build_launch4a(cfg):
    nc = bass.Bass("TRN2", target_bir_lowering=False, debug=False)
    em = Em(nc)
    CA, COLS, SEGS, L = cfg.CA, cfg.COLS, cfg.SEGS, cfg.L

    x1in = nc.dram_tensor("x1in", [C, CA], F32, kind="ExternalInput")
    maskin = nc.dram_tensor("maskin", [1, SEGS], F32, kind="ExternalInput").ap()
    prm_names = ("ln2w", "ln2b", "ftmk", "ftmr")
    prm = {nm: nc.dram_tensor(nm, [128, NCH], F32, kind="ExternalInput").ap()
           for nm in prm_names}
    xk2_o = nc.dram_tensor("xk2_o", [C, COLS], BF16, kind="ExternalOutput")
    xr2_o = nc.dram_tensor("xr2_o", [C, COLS], BF16, kind="ExternalOutput")

    bufA = nc.alloc_sbuf_tensor("bufA", [128, NCH * CA], F32)[:]
    bufB = nc.alloc_sbuf_tensor("bufB", [128, NCH * CA], F32)[:]
    mixb = nc.alloc_sbuf_tensor("mixb", [128, NCH * COLS], BF16)[:]
    un0 = nc.alloc_sbuf_tensor("un0", [128, max(COLS, 2048)], F32)
    t1b = un0[:]
    hm = nc.alloc_sbuf_tensor("hm", [128, NCH * SEGS], F32)[:]
    maskB = nc.alloc_sbuf_tensor("maskB", [128, SEGS], F32)[:]
    mrow = nc.alloc_sbuf_tensor("mrow", [1, SEGS], F32)[:]
    tm1 = {nm: nc.alloc_sbuf_tensor(f"tm1_{nm}", [128, NCH], F32)[:]
           for nm in ("ftmk", "ftmr")}
    prms = {nm: nc.alloc_sbuf_tensor("s_" + nm, [128, NCH], F32)[:] for nm in prm_names}

    st = make_state(em, nc, rows=un0[:][0:1, 0:2048])
    pb = st["pb"]

    tx = em.dsync(bufA.rearrange("p (a x) -> p a x", a=NCH), fm_dram_ap(x1in, CA), sem="sqx")
    for nm in prm_names:
        em.dsync(prms[nm], prm[nm], sem="sqp")
    tp_all = em.dsync(mrow, maskin, sem="sqp")
    tm1_toks = {}
    for nm in ("ftmk", "ftmr"):
        tm1_toks[nm] = em.v(lambda e, nm=nm: e.tensor_scalar(
            out=tm1[nm], in0=prms[nm], scalar1=-1.0, scalar2=1.0,
            op0=ALU.mult, op1=ALU.add), deps=[tp_all])
    tmb0 = em.t(lambda e: e.matmul(out=pb["b2"][:, 0:SEGS], lhsT=st["ones_row"][0:1, :],
                                   rhs=mrow, start=True, stop=True),
                deps=[tp_all, st["dep_ones"]])
    tmb = em.v(lambda e: e.tensor_copy(out=maskB, in_=pb["b2"][:, 0:SEGS]), deps=[tmb0])
    st["pb_b2_free"] = tmb

    def a_ap(ch, c0, cw):
        return bufA[:, ch * CA + c0: ch * CA + c0 + cw]

    def b_ap(ch, c0, cw):
        return bufB[:, ch * CA + c0: ch * CA + c0 + cw]

    emit_ln(em, st, a_ap, b_ap, CA, mean=True,
            gamma=(prms["ln2w"], tp_all), beta=(prms["ln2b"], tp_all), src_dep=[tx])
    bar_h = em.barrier()
    hm_toks = []
    for ch in range(NCH):
        hm_toks.append(em.v(lambda e, ch=ch: e.tensor_mul(
            out=hm[:, ch * SEGS:(ch + 1) * SEGS],
            in0=bufB[:, ch * CA + COLS: ch * CA + CA], in1=maskB),
            deps=bar_h + [tmb]))
    bar_hm = em.barrier()
    store_toks = []
    for nm, out_t in (("ftmk", xk2_o), ("ftmr", xr2_o)):
        toks = emit_mix(em, st, bufB, CA, mixb, COLS, prms[nm], tm1[nm], hm,
                        COLS, SEGS, L, bar_hm + hm_toks + [tm1_toks[nm], tp_all]
                        + ([store_toks[-1]] if store_toks else []), t1b)
        tst = em.dsync(fm_dram_ap(out_t, COLS),
                       mixb.rearrange("p (a x) -> p a x", a=NCH),
                       deps=toks, sem="sqst" + nm)
        store_toks.append(tst)
    em.finalize(store_toks)
    return nc



def build_launch4b(cfg):
    nc = bass.Bass("TRN2", target_bir_lowering=False, debug=False)
    em = Em(nc)
    COLS = cfg.COLS
    FF = 4096
    NFC = FF // 128
    NFG = 4

    xk2_in = nc.dram_tensor("xk2_in", [C, COLS], BF16, kind="ExternalInput")
    xr2_in = nc.dram_tensor("xr2_in", [C, COLS], BF16, kind="ExternalInput")
    x1_in = nc.dram_tensor("x1_in", [C, COLS], F32, kind="ExternalInput")
    fkwT = nc.dram_tensor("fkwT", [C, FF], F32, kind="ExternalInput")
    fvwT = nc.dram_tensor("fvwT", [FF, C], F32, kind="ExternalInput")
    frwT = nc.dram_tensor("frwT", [C, C], F32, kind="ExternalInput")
    ao_o = nc.dram_tensor("ao_o", [C, COLS], F32, kind="ExternalOutput")

    xk2b = nc.alloc_sbuf_tensor("xk2b", [128, NCH * COLS], BF16)[:]
    xr2b = nc.alloc_sbuf_tensor("xr2b", [128, NCH * COLS], BF16)[:]
    kkb = nc.alloc_sbuf_tensor("kkb", [128, NFC * 512], BF16)[:]
    wst = nc.alloc_sbuf_tensor("wst", [128, 8192], BF16)[:]
    wst2 = nc.alloc_sbuf_tensor("wst2", [128, 8192], BF16)[:]
    x1ct = nc.alloc_sbuf_tensor("x1ct", [128, NCH * 512], F32)[:]
    sr2t = nc.alloc_sbuf_tensor("sr2t", [128, NCH * 512], F32)[:]
    at = [nc.alloc_sbuf_tensor(f"at{i}", [128, 512], F32)[:] for i in range(2)]
    evst = [nc.alloc_sbuf_tensor(f"evst{i}", [128, 512], F32)[:] for i in range(2)]
    pb = [nc.alloc_psum_tensor(f"pbk{i}", [128, 512], F32)[:] for i in range(8)]

    tk2 = em.dsync(xk2b.rearrange("p (a x) -> p a x", a=NCH), fm_dram_ap(xk2_in, COLS),
                   sem="sqk")
    tr2 = em.dsync(xr2b.rearrange("p (a x) -> p a x", a=NCH), fm_dram_ap(xr2_in, COLS),
                   sem="sqr")
    fkw3 = fkwT.ap().rearrange("(a p) f -> p a f", p=128)
    fvw3 = fvwT.ap().rearrange("(a p) c -> p a c", p=128)
    frw3 = frwT.ap().rearrange("(a p) c -> p a c", p=128)

    store_toks = []
    pfree = [None] * 8
    wst_free = None
    wst2_free = None
    x1_free = None
    sr2_free = [None] * NCH
    kk_free = [None] * NFC
    ev_free = [None] * 2
    at_free = [None] * 2
    for ci, (c0, cw) in enumerate(col_tiles(COLS)):
        tx1 = em.dsync(x1ct.rearrange("p (a x) -> p a x", a=NCH)[:, :, 0:cw],
                       fm_dram_ap(x1_in, COLS)[:, :, c0:c0 + cw], sem="sqx1",
                       deps=[x1_free])
        twr = em.dgps(wst.rearrange("p (a x) -> p a x", a=NCH), frw3, sem="gqw0",
                      deps=[wst_free])
        for cc in range(NCH):
            bi = cc % 4
            bank = pb[bi]
            tk = None
            for kc in range(NCH):
                tk = em.t(lambda e, cc=cc, kc=kc, c0=c0, cw=cw, bank=bank: e.matmul(
                    out=bank[:, 0:cw],
                    lhsT=wst[:, kc * 1024 + cc * 128:kc * 1024 + (cc + 1) * 128],
                    rhs=xr2b[:, kc * COLS + c0: kc * COLS + c0 + cw],
                    start=(kc == 0), stop=(kc == NCH - 1)),
                    deps=[twr, tr2, pfree[bi]] if kc == 0 else [])
            tev = em.a(lambda e, cc=cc, cw=cw, bank=bank: e.activation(
                out=sr2t[:, cc * 512:cc * 512 + cw], in_=bank[:, 0:cw],
                func=AF.Sigmoid), deps=[tk, sr2_free[cc]])
            pfree[bi] = tev
            sr2_free[cc] = tev
            wst_free = tk
        for fg in range(NFG):
            twk = em.dgps(wst.rearrange("p (a x) -> p a x", a=NCH),
                          fkw3[:, :, fg * 1024:(fg + 1) * 1024], sem="gqw1",
                          deps=[wst_free])
            for fi in range(NCH):
                fci = fg * 8 + fi
                bi = fi % 4
                bank = pb[bi]
                tk = None
                for kc in range(NCH):
                    tk = em.t(lambda e, fi=fi, kc=kc, c0=c0, cw=cw, bank=bank: e.matmul(
                        out=bank[:, 0:cw],
                        lhsT=wst[:, kc * 1024 + fi * 128:kc * 1024 + (fi + 1) * 128],
                        rhs=xk2b[:, kc * COLS + c0: kc * COLS + c0 + cw],
                        start=(kc == 0), stop=(kc == NCH - 1)),
                        deps=[twk, tk2, pfree[bi]] if kc == 0 else [])
                pp = fci % 2
                trl = em.v(lambda e, cw=cw, bank=bank, pp=pp: e.tensor_scalar_max(
                    out=at[pp][:, 0:cw], in0=bank[:, 0:cw], scalar1=0.0),
                    deps=[tk, at_free[pp]])
                tsqr = em.a(lambda e, fci=fci, cw=cw, pp=pp: e.activation(
                    out=kkb[:, fci * 512:fci * 512 + cw], in_=at[pp][:, 0:cw],
                    func=AF.Square), deps=[trl, kk_free[fci]])
                pfree[bi] = trl
                at_free[pp] = tsqr
                kk_free[fci] = tsqr
                wst_free = tk
        bar_kk = em.barrier()
        tk = None
        for fg in range(NFG):
            twv = em.dgps(wst2.rearrange("p (a x) -> p a x", a=NCH),
                          fvw3[:, fg * 8:(fg + 1) * 8, :], sem="gqw2",
                          deps=[wst2_free])
            for fi in range(NCH):
                fci = fg * 8 + fi
                for cc in range(NCH):
                    deps = []
                    if fci == 0:
                        deps = [twv, pfree[cc]] + bar_kk
                    elif fi == 0 and cc == 0:
                        deps = [twv]
                    tk = em.t(lambda e, fi=fi, cc=cc, fci=fci, c0=c0, cw=cw: e.matmul(
                        out=pb[cc][:, 0:cw],
                        lhsT=wst2[:, fi * 1024 + cc * 128:fi * 1024 + (cc + 1) * 128],
                        rhs=kkb[:, fci * 512:fci * 512 + cw],
                        start=(fci == 0), stop=(fci == NFC - 1)),
                        deps=deps)
            wst2_free = tk
        bar_kv = em.barrier()
        for cc in range(NCH):
            pp = cc % 2
            t1 = em.v(lambda e, cc=cc, cw=cw, pp=pp: e.tensor_mul(
                out=at[pp][:, 0:cw], in0=sr2t[:, cc * 512:cc * 512 + cw],
                in1=pb[cc][:, 0:cw]), deps=bar_kv + [at_free[pp]])
            t2 = em.v(lambda e, cc=cc, cw=cw, pp=pp: e.tensor_add(
                out=evst[pp][:, 0:cw], in0=at[pp][:, 0:cw],
                in1=x1ct[:, cc * 512:cc * 512 + cw]), deps=[t1, tx1, ev_free[pp]])
            at_free[pp] = t2
            pfree[cc] = t1
            sr2_free[cc] = t1
            tst = em.dsync(ao_o.ap()[cc * 128:(cc + 1) * 128, c0:c0 + cw],
                           evst[pp][:, 0:cw], deps=[t2], sem=f"sstao{pp}")
            ev_free[pp] = tst
            store_toks.append(tst)
        x1_free = store_toks[-1]
    em.finalize(store_toks)
    return nc



def emit_round_exact(em, tile, w, y_scr, b_scr, deps):
    t0 = em.v(lambda e: e.tensor_scalar_add(out=y_scr[:, 0:w], in0=tile[:, 0:w],
                                            scalar1=256.5), deps=deps)
    tr = em.v(lambda e: e.tensor_copy(out=tile[:, 0:w], in_=y_scr[:, 0:w]), deps=[t0])
    for k in (256.0, 128.0, 64.0, 32.0, 16.0, 8.0, 4.0, 2.0, 1.0):
        tb = em.v(lambda e, k=k: e.tensor_single_scalar(
            out=b_scr[:, 0:w], in_=tile[:, 0:w], scalar=k, op=ALU.is_ge), deps=[tr])
        tr = em.v(lambda e, k=k: e.scalar_tensor_tensor(
            out=tile[:, 0:w], in0=b_scr[:, 0:w], scalar=-k, in1=tile[:, 0:w],
            op0=ALU.mult, op1=ALU.add), deps=[tb])
    t1 = em.v(lambda e: e.scalar_tensor_tensor(
        out=tile[:, 0:w], in0=tile[:, 0:w], scalar=-1.0, in1=y_scr[:, 0:w],
        op0=ALU.mult, op1=ALU.add), deps=[tr])
    t2 = em.v(lambda e: e.tensor_scalar_add(out=tile[:, 0:w], in0=tile[:, 0:w],
                                            scalar1=-256.0), deps=[t1])
    return t2



def build_launch5(cfg, exact_round=True, debug=False, cut=None, skip_pre=False):
    nc = bass.Bass("TRN2", target_bir_lowering=False, debug=False)
    em = Em(nc)
    COLS = cfg.COLS
    FF = 4096
    NFC = FF // 128
    NFG = 4

    xn_in = nc.dram_tensor("xn_in", [C, COLS], F32, kind="ExternalInput")
    x_in = nc.dram_tensor("x_in", [C, COLS], F32, kind="ExternalInput")
    ao_in = nc.dram_tensor("ao_in", [C, COLS], F32, kind="ExternalInput")
    wgT = nc.dram_tensor("wgT", [C, FF], F32, kind="ExternalInput")
    wuT = nc.dram_tensor("wuT", [C, FF], F32, kind="ExternalInput")
    wdT = nc.dram_tensor("wdT", [FF, C], F32, kind="ExternalInput")
    xl0_o = nc.dram_tensor("xl0_o", [C, COLS], F32, kind="ExternalOutput")
    if debug:
        aq1_d = nc.dram_tensor("aq1_d", [C, 512], BF16, kind="ExternalOutput")
        gu_d = nc.dram_tensor("gu_d", [4096, 512], BF16, kind="ExternalOutput")
        aq2_d = nc.dram_tensor("aq2_d", [4096, 512], BF16, kind="ExternalOutput")
        rows_d = nc.dram_tensor("rows_d", [1, 2048], F32, kind="ExternalOutput")
        tern_d = nc.dram_tensor("tern_d", [128, 8192], BF16, kind="ExternalOutput")

    xnct = nc.alloc_sbuf_tensor("xnct", [128, NCH * 512], F32)[:]
    xct = nc.alloc_sbuf_tensor("xct", [128, NCH * 512], F32)[:]
    aoct = nc.alloc_sbuf_tensor("aoct", [128, NCH * 512], F32)[:]
    aq1b = nc.alloc_sbuf_tensor("aq1b", [128, NCH * 512], BF16)[:]
    gub = nc.alloc_sbuf_tensor("gub", [128, NFC * 512], BF16)[:]
    aq2b = nc.alloc_sbuf_tensor("aq2b", [128, NFC * 512], BF16)[:]
    tgb = nc.alloc_sbuf_tensor("tgb", [128, 8192], BF16)[:]
    tub = nc.alloc_sbuf_tensor("tub", [128, 8192], BF16)[:]
    tern_scr = {"g": nc.dram_tensor("tern_g", [C, 4096], BF16, kind="Internal"),
                "u": nc.dram_tensor("tern_u", [C, 4096], BF16, kind="Internal"),
                "d": nc.dram_tensor("tern_d_scr", [4096, C], BF16, kind="Internal")}
    t3scr = {k: tern_scr[k].ap().rearrange("(a p) f -> p a f", p=128) for k in tern_scr}
    gsb = nc.alloc_sbuf_tensor("gsb", [128, 512], F32)[:]
    usb = nc.alloc_sbuf_tensor("usb", [128, 512], F32)[:]
    mxb = nc.alloc_sbuf_tensor("mxb", [128, 512], F32)[:]
    mcols = nc.alloc_sbuf_tensor("mcols", [128, 4], F32)[:]
    at = [nc.alloc_sbuf_tensor(f"at{i}", [128, 512], F32)[:] for i in range(2)]
    ysc = nc.alloc_sbuf_tensor("ysc", [128, 512], F32)[:]
    bsc = nc.alloc_sbuf_tensor("bsc", [128, 512], F32)[:]
    evst = [nc.alloc_sbuf_tensor(f"evst{i}", [128, 512], F32)[:] for i in range(2)]
    rows = nc.alloc_sbuf_tensor("rows", [1, 2048], F32)[:]
    acc = nc.alloc_sbuf_tensor("acc", [128, 48], F32)[:]
    ones_col = nc.alloc_sbuf_tensor("ones_col", [128, 1], F32)[:]
    ones_row = nc.alloc_sbuf_tensor("ones_row", [1, 128], F32)[:]
    sc1 = nc.alloc_sbuf_tensor("sc1", [1, 8], F32)[:]
    thcol = {w: nc.alloc_sbuf_tensor(f"thcol_{w}", [128, 1], F32)[:] for w in "gud"}
    nthcol = {w: nc.alloc_sbuf_tensor(f"nthcol_{w}", [128, 1], F32)[:] for w in "gud"}
    ninv = {w: nc.alloc_sbuf_tensor(f"ninv_{w}", [128, 1], F32)[:] for w in "gud"}
    pb = [nc.alloc_psum_tensor(f"pbk{i}", [128, 512], F32)[:] for i in range(8)]
    stb = [nc.alloc_sbuf_tensor(f"stb{i}", [128, 2048], F32)[:] for i in range(2)]

    t_on1 = em.v(lambda e: e.memset(ones_col, 1.0))
    t_on2 = em.v(lambda e: e.memset(ones_row, 1.0))
    wmap = {"g": wgT, "u": wuT, "d": wdT}
    w3 = {"g": wgT.ap().rearrange("(a p) f -> p a f", p=128),
          "u": wuT.ap().rearrange("(a p) f -> p a f", p=128),
          "d": wdT.ap().rearrange("(a p) c -> p a c", p=128)}
    NPC = (NCH * 4096) // 2048

    def wpiece(view, wn, pc):
        if wn != "d":
            a = pc // 2
            fr = (pc % 2) * 2048
            return view[:, a:a + 1, fr:fr + 2048]
        return view[:, pc * 2:pc * 2 + 2, :]

    def spiece(flat_ap, wn):
        a = 1 if wn != "d" else 2
        return flat_ap.rearrange("p (a f) -> p a f", a=a)

    if skip_pre is True:
        tz = em.v(lambda e: e.memset(thcol["g"], 0.01))
        for wn in "ud":
            tz = em.v(lambda e, wn=wn: e.memset(thcol[wn], 0.01), deps=[tz])
        for wn in "gud":
            tz = em.v(lambda e, wn=wn: e.memset(nthcol[wn], -0.01), deps=[tz])
            tz = em.v(lambda e, wn=wn: e.memset(ninv[wn], -1.0), deps=[tz])
        bc_toks = {wn: [tz] for wn in "gud"}
        bar_tern = em.barrier()
    wst_free = [None, None]
    acc_toks = {}
    for wi, wn in enumerate("gud") if skip_pre is not True else []:
        piece_toks = []
        accp = [None] * 4
        for pc in range(NPC):
            pp = pc % 2
            tl = em.dsync(spiece(stb[pp], wn), wpiece(w3[wn], wn, pc),
                          sem=f"sqs{pp}", deps=[wst_free[pp]])
            ta = em.a(lambda e, wi=wi, pc=pc, pp=pp: e.activation(
                out=stb[pp], in_=stb[pp], func=AF.Abs,
                accum_out=acc[:, wi * 16 + pc:wi * 16 + pc + 1]),
                deps=[tl])
            wst_free[pp] = ta
            piece_toks.append(ta)
        acc_toks[wn] = piece_toks
    rows_toks = {}
    pbs_free = None
    for wi, wn in enumerate("gud") if skip_pre is not True else []:
        tm = em.t(lambda e, wi=wi: e.matmul(
            out=pb[6][0:1, 0:16], lhsT=ones_col[:, 0:1], rhs=acc[:, wi * 16:wi * 16 + 16],
            start=True, stop=True), deps=[t_on1, pbs_free] + acc_toks[wn])
        ts1 = em.v(lambda e, wi=wi: e.reduce_sum(
            out=rows[0:1, wi:wi + 1], in_=pb[6][0:1, 0:16], axis=AX.X), deps=[tm])
        pbs_free = ts1
        ts2 = em.v(lambda e, wi=wi: e.tensor_scalar(
            out=rows[0:1, 8 + wi:9 + wi], in0=rows[0:1, wi:wi + 1],
            scalar1=1.0 / (C * FF), scalar2=1e-5, op0=ALU.mult, op1=ALU.max),
            deps=[ts1])
        rows_toks[wn] = ts2
    bc_toks = {} if not skip_pre else bc_toks
    for wi, wn in enumerate("gud") if skip_pre is not True else []:
        t1 = em.v(lambda e, wi=wi: e.tensor_scalar_mul(
            out=rows[0:1, 16 + wi:17 + wi], in0=rows[0:1, 8 + wi:9 + wi], scalar1=0.5),
            deps=[rows_toks[wn]])
        t2 = em.t(lambda e, wi=wi: e.matmul(
            out=pb[6][:, 0:1], lhsT=ones_row[0:1, :], rhs=rows[0:1, 16 + wi:17 + wi],
            start=True, stop=True), deps=[t1, t_on2, pbs_free])
        t3 = em.v(lambda e, wn=wn: e.tensor_copy(out=thcol[wn], in_=pb[6][:, 0:1]),
                  deps=[t2])
        t4 = em.v(lambda e, wn=wn: e.tensor_scalar_mul(out=nthcol[wn], in0=thcol[wn],
                                                       scalar1=-1.0), deps=[t3])
        t5 = em.t(lambda e, wi=wi: e.matmul(
            out=pb[6][:, 2:3], lhsT=ones_row[0:1, :], rhs=rows[0:1, 8 + wi:9 + wi],
            start=True, stop=True), deps=[rows_toks[wn], t_on2, t3])
        t6 = em.v(lambda e, wn=wn: e.tensor_scalar_mul(out=ninv[wn], in0=pb[6][:, 2:3],
                                                       scalar1=-1.0), deps=[t5])
        pbs_free = t6
        bc_toks[wn] = [t3, t4, t6]
    bar_w = em.barrier()
    tern_store_toks = []
    tsc_free = [None, None]
    for wn in ("gud" if not skip_pre else ""):
        for pc in range(NPC):
            pp = pc % 2
            tl = em.dsync(stb[pp].rearrange("p (a f) -> p a f", a=1), wpiece(w3[wn], wn, pc),
                          sem=f"sqs{pp}", deps=[wst_free[pp]] + (bar_w if pc < 2 else []))
            tslice = tgb[:, pp * 2048:(pp + 1) * 2048]
            tt1 = em.v(lambda e, wn=wn, pp=pp, tslice=tslice: e.tensor_single_scalar(
                out=tslice, in_=stb[pp], scalar=thcol[wn][:, 0:1],
                op=ALU.is_ge), deps=[tl, tsc_free[pp]] + bc_toks[wn])
            tt2 = em.v(lambda e, wn=wn, pp=pp, tslice=tslice: e.scalar_tensor_tensor(
                out=tslice, in0=stb[pp], scalar=nthcol[wn][:, 0:1],
                in1=tslice, op0=ALU.is_le, op1=ALU.subtract), deps=[tt1])
            wst_free[pp] = tt2
            tsc = em.dsync(wpiece(t3scr[wn], wn, pc), spiece(tslice, wn),
                           sem=f"sqscw{pp}", deps=[tt2])
            tsc_free[pp] = tsc
            tern_store_toks.append(tsc)
    bar_tern = em.barrier() if skip_pre is not True else bar_tern
    if cut == "tern":
        em.finalize(bar_tern)
        return nc

    idf = nc.alloc_sbuf_tensor("idf", [128, 128], F32)[:]
    ti0 = em.v(lambda e: e.memset(idf, 1.0))
    ti1 = em.p(lambda e: e.affine_select(out=idf, in_=idf, pattern=[[-1, 128]],
                                         compare_op=ALU.is_equal, fill=0.0,
                                         base=0, channel_multiplier=1), deps=[ti0])
    if cut == "idf":
        em.finalize(em.barrier())
        return nc

    store_toks = []
    pfree = [None] * 8
    x_free = ao_free = xn_free = None
    aq1_free = None
    gu_free = [None] * NFC
    aq2_free = [None] * NFC
    at_free = [None, ti1]
    ev_free = [None] * 2
    tern_g_free = None
    tern_u_free = None
    mx_free = None
    mc_free = None
    ysc_free = bsc_free = None
    gs_free = us_free = None
    rows_free2 = None
    for ci, (c0, cw) in enumerate(col_tiles(COLS)):
        nct128 = (cw + 127) // 128
        txn = em.dsync(xnct.rearrange("p (a x) -> p a x", a=NCH)[:, :, 0:cw],
                       fm_dram_ap(xn_in, COLS)[:, :, c0:c0 + cw], sem="sqxn",
                       deps=[xn_free])
        tx = em.dsync(xct.rearrange("p (a x) -> p a x", a=NCH)[:, :, 0:cw],
                      fm_dram_ap(x_in, COLS)[:, :, c0:c0 + cw], sem="sqx",
                      deps=[x_free])
        tao = em.dsync(aoct.rearrange("p (a x) -> p a x", a=NCH)[:, :, 0:cw],
                       fm_dram_ap(ao_in, COLS)[:, :, c0:c0 + cw], sem="sqao",
                       deps=[ao_free])
        if cut == "loads":
            em.finalize(em.barrier())
            return nc
        tmx = em.a(lambda e, cw=cw: e.activation(
            out=mxb[:, 0:cw], in_=xnct[:, 0:cw], func=AF.Abs), deps=[txn, mx_free])
        for ch in range(1, NCH):
            tab = em.a(lambda e, ch=ch, cw=cw: e.activation(
                out=at[1][:, 0:cw], in_=xnct[:, ch * 512:ch * 512 + cw], func=AF.Abs),
                deps=[txn, at_free[1]])
            tmx = em.v(lambda e, cw=cw: e.tensor_max(
                out=mxb[:, 0:cw], in0=mxb[:, 0:cw], in1=at[1][:, 0:cw]),
                deps=[tmx, tab])
            at_free[1] = tmx
        if cut == "rowmax":
            em.finalize(em.barrier())
            return nc
        ttr = None
        for tt_i in range(nct128):
            tw = min(128, cw - tt_i * 128)
            t1 = em.t(lambda e, tt_i=tt_i, tw=tw: e.transpose(
                out=pb[7][0:tw, 0:128], in_=mxb[:, tt_i * 128:tt_i * 128 + tw],
                identity=idf), deps=[tmx, ti1, mc_free, ttr])
            ttr = em.v(lambda e, tt_i=tt_i, tw=tw: e.reduce_max(
                out=mcols[0:tw, tt_i:tt_i + 1], in_=pb[7][0:tw, 0:128], axis=AX.X),
                deps=[t1, ttr])
        if cut == "transp":
            em.finalize(em.barrier())
            return nc
        if nct128 == 1:
            trow = em.dsync(rows[0:1, 1024:1024 + cw], mcols[0:cw, 0:1],
                            sem="sqrow", deps=[ttr, rows_free2])
        else:
            for tt_i in range(nct128):
                trow = em.dsync(rows[0:1, 1024 + tt_i * 128:1024 + (tt_i + 1) * 128],
                                mcols[:, tt_i:tt_i + 1], sem="sqrow",
                                deps=[ttr, rows_free2] if tt_i == 0 else [])
        tr1 = em.v(lambda e, cw=cw: e.tensor_scalar_max(
            out=rows[0:1, 1024:1024 + cw], in0=rows[0:1, 1024:1024 + cw], scalar1=1e-5),
            deps=[trow])
        tr2 = em.v(lambda e, cw=cw: e.reciprocal(out=rows[0:1, 0:cw],
                                                 in_=rows[0:1, 1024:1024 + cw]),
                   deps=[tr1])
        tr3 = em.v(lambda e, cw=cw: e.tensor_scalar_mul(
            out=rows[0:1, 0:cw], in0=rows[0:1, 0:cw], scalar1=127.0), deps=[tr2])
        tr4 = em.v(lambda e, cw=cw: e.tensor_scalar_mul(
            out=rows[0:1, 512:512 + cw], in0=rows[0:1, 1024:1024 + cw],
            scalar1=1.0 / 127.0), deps=[tr1, tr2])
        if cut == "rows":
            em.finalize(em.barrier())
            return nc
        tb1 = em.t(lambda e, cw=cw: e.matmul(out=pb[6][:, 0:cw], lhsT=ones_row[0:1, :],
                                             rhs=rows[0:1, 0:cw], start=True, stop=True),
                   deps=[tr3, t_on2, pbs_free, pfree[6]])
        tb2 = em.t(lambda e, cw=cw: e.matmul(out=pb[7][:, 0:cw], lhsT=ones_row[0:1, :],
                                             rhs=rows[0:1, 512:512 + cw],
                                             start=True, stop=True),
                   deps=[tr4, t_on2, ttr, pfree[7]])
        rows_free2 = tb2
        pbs_free = tb2
        if cut == "bcast":
            em.finalize(em.barrier())
            return nc
        taq_last = None
        for ch in range(NCH):
            tq1 = em.v(lambda e, ch=ch, cw=cw: e.tensor_mul(
                out=at[0][:, 0:cw], in0=xnct[:, ch * 512:ch * 512 + cw],
                in1=pb[6][:, 0:cw]), deps=[tb1, txn, at_free[0]])
            trd = emit_round_exact(em, at[0], cw, ysc, bsc, [tq1, ysc_free, bsc_free])
            taq = em.v(lambda e, ch=ch, cw=cw: e.tensor_mul(
                out=aq1b[:, ch * 512:ch * 512 + cw], in0=at[0][:, 0:cw],
                in1=pb[7][:, 0:cw]), deps=[trd, tb2, aq1_free])
            at_free[0] = taq
            ysc_free = bsc_free = trd
            taq_last = taq
        aq1_free = taq_last
        pfree[6] = taq_last
        pfree[7] = taq_last
        bar_aq1 = em.barrier()
        if cut == "aq1":
            em.finalize(bar_aq1)
            return nc
        for fg in range(NFG):
            ttg2 = em.dgps(tgb.rearrange("p (a x) -> p a x", a=NCH),
                           t3scr["g"][:, :, fg * 1024:(fg + 1) * 1024], sem="gqg",
                           deps=[tern_g_free] + (bar_aq1 + bar_tern if fg == 0 else []))
            ttu2 = em.dgps(tub.rearrange("p (a x) -> p a x", a=NCH),
                           t3scr["u"][:, :, fg * 1024:(fg + 1) * 1024], sem="gqu",
                           deps=[tern_u_free] + (bar_aq1 + bar_tern if fg == 0 else []))
            for fi in range(NCH):
                fci = fg * 8 + fi
                bg, bu = pb[fi % 3], pb[3 + fi % 3]
                tkg = tku = None
                for kc in range(NCH):
                    tkg = em.t(lambda e, fi=fi, kc=kc, cw=cw, bg=bg: e.matmul(
                        out=bg[:, 0:cw],
                        lhsT=tgb[:, kc * 1024 + fi * 128:kc * 1024 + (fi + 1) * 128],
                        rhs=aq1b[:, kc * 512:kc * 512 + cw],
                        start=(kc == 0), stop=(kc == NCH - 1)),
                        deps=[ttg2, pfree[fi % 3]] + bar_aq1 if kc == 0 else [])
                for kc in range(NCH):
                    tku = em.t(lambda e, fi=fi, kc=kc, cw=cw, bu=bu: e.matmul(
                        out=bu[:, 0:cw],
                        lhsT=tub[:, kc * 1024 + fi * 128:kc * 1024 + (fi + 1) * 128],
                        rhs=aq1b[:, kc * 512:kc * 512 + cw],
                        start=(kc == 0), stop=(kc == NCH - 1)),
                        deps=[ttu2, pfree[3 + fi % 3]] + bar_aq1 if kc == 0 else [])
                tg = em.a(lambda e, cw=cw, bg=bg: e.activation(
                    out=gsb[:, 0:cw], in_=bg[:, 0:cw], func=AF.Silu,
                    scale=ninv["g"][:, 0:1]), deps=[tkg, gs_free] + bc_toks["g"])
                tu = em.a(lambda e, cw=cw, bu=bu: e.activation(
                    out=usb[:, 0:cw], in_=bu[:, 0:cw], func=AF.Copy,
                    scale=ninv["u"][:, 0:1]), deps=[tku, us_free] + bc_toks["u"])
                tgu = em.v(lambda e, fci=fci, cw=cw: e.tensor_mul(
                    out=gub[:, fci * 512:fci * 512 + cw], in0=gsb[:, 0:cw],
                    in1=usb[:, 0:cw]), deps=[tg, tu, gu_free[fci]])
                pfree[fi % 3] = tg
                pfree[3 + fi % 3] = tu
                gs_free = tgu
                us_free = tgu
                gu_free[fci] = tgu
            tern_g_free = tkg
            tern_u_free = tku
        bar_gu = em.barrier()
        if cut == "gu":
            em.finalize(bar_gu)
            return nc
        tmx2 = em.a(lambda e, cw=cw: e.activation(
            out=mxb[:, 0:cw], in_=gub[:, 0:cw], func=AF.Abs), deps=bar_gu + [tmx])
        for fci in range(1, NFC):
            tab = em.a(lambda e, fci=fci, cw=cw: e.activation(
                out=at[1][:, 0:cw], in_=gub[:, fci * 512:fci * 512 + cw], func=AF.Abs),
                deps=bar_gu + [at_free[1]])
            tmx2 = em.v(lambda e, cw=cw: e.tensor_max(
                out=mxb[:, 0:cw], in0=mxb[:, 0:cw], in1=at[1][:, 0:cw]),
                deps=[tmx2, tab])
            at_free[1] = tmx2
        ttr2 = None
        for tt_i in range(nct128):
            tw = min(128, cw - tt_i * 128)
            t1 = em.t(lambda e, tt_i=tt_i, tw=tw: e.transpose(
                out=pb[7][0:tw, 0:128], in_=mxb[:, tt_i * 128:tt_i * 128 + tw],
                identity=idf), deps=[tmx2, ti1, pfree[7], ttr2])
            ttr2 = em.v(lambda e, tt_i=tt_i, tw=tw: e.reduce_max(
                out=mcols[0:tw, tt_i:tt_i + 1], in_=pb[7][0:tw, 0:128], axis=AX.X),
                deps=[t1, ttr2 or trow])
            pfree[7] = ttr2
        if nct128 == 1:
            trow2 = em.dsync(rows[0:1, 1024:1024 + cw], mcols[0:cw, 0:1],
                             sem="sqrow", deps=[ttr2, tb2])
        else:
            for tt_i in range(nct128):
                trow2 = em.dsync(rows[0:1, 1024 + tt_i * 128:1024 + (tt_i + 1) * 128],
                                 mcols[:, tt_i:tt_i + 1], sem="sqrow",
                                 deps=[ttr2, tb2] if tt_i == 0 else [])
        tr1 = em.v(lambda e, cw=cw: e.tensor_scalar_max(
            out=rows[0:1, 1024:1024 + cw], in0=rows[0:1, 1024:1024 + cw], scalar1=1e-5),
            deps=[trow2])
        tr2b = em.v(lambda e, cw=cw: e.reciprocal(out=rows[0:1, 0:cw],
                                                  in_=rows[0:1, 1024:1024 + cw]),
                    deps=[tr1])
        tr3b = em.v(lambda e, cw=cw: e.tensor_scalar_mul(
            out=rows[0:1, 0:cw], in0=rows[0:1, 0:cw], scalar1=127.0), deps=[tr2b])
        tr4b = em.v(lambda e, cw=cw: e.tensor_scalar_mul(
            out=rows[0:1, 512:512 + cw], in0=rows[0:1, 1024:1024 + cw],
            scalar1=1.0 / 127.0), deps=[tr1, tr2b])
        tb1b = em.t(lambda e, cw=cw: e.matmul(out=pb[6][:, 0:cw], lhsT=ones_row[0:1, :],
                                              rhs=rows[0:1, 0:cw], start=True, stop=True),
                    deps=[tr3b, t_on2, pfree[6]])
        tb2b = em.t(lambda e, cw=cw: e.matmul(out=pb[7][:, 0:cw], lhsT=ones_row[0:1, :],
                                              rhs=rows[0:1, 512:512 + cw],
                                              start=True, stop=True),
                    deps=[tr4b, t_on2, pfree[7]])
        taq2_last = None
        for fci in range(NFC):
            tq1 = em.v(lambda e, fci=fci, cw=cw: e.tensor_mul(
                out=at[0][:, 0:cw], in0=gub[:, fci * 512:fci * 512 + cw],
                in1=pb[6][:, 0:cw]), deps=[tb1b, at_free[0]] + bar_gu)
            trd = emit_round_exact(em, at[0], cw, ysc, bsc, [tq1, ysc_free, bsc_free])
            taq = em.v(lambda e, fci=fci, cw=cw: e.tensor_mul(
                out=aq2b[:, fci * 512:fci * 512 + cw], in0=at[0][:, 0:cw],
                in1=pb[7][:, 0:cw]), deps=[trd, tb2b, aq2_free[fci]])
            at_free[0] = taq
            ysc_free = bsc_free = trd
            aq2_free[fci] = taq
            taq2_last = taq
        pfree[6] = taq2_last
        pfree[7] = taq2_last
        bar_aq2 = em.barrier()
        if cut == "aq2":
            em.finalize(bar_aq2)
            return nc
        if debug and ci == 0:
            store_toks.append(em.dsync(fm_dram_ap(aq1_d, 512)[:, :, 0:cw],
                                       aq1b.rearrange("p (a x) -> p a x", a=NCH)[:, :, 0:cw],
                                       deps=bar_aq2, sem="sqdbg"))
            store_toks.append(em.dsync(gu_d.ap().rearrange("(a p) x -> p a x", p=128)[:, :, 0:cw],
                                       gub.rearrange("p (a x) -> p a x", a=NFC)[:, :, 0:cw],
                                       deps=bar_aq2, sem="sqdbg2"))
            store_toks.append(em.dsync(aq2_d.ap().rearrange("(a p) x -> p a x", p=128)[:, :, 0:cw],
                                       aq2b.rearrange("p (a x) -> p a x", a=NFC)[:, :, 0:cw],
                                       deps=bar_aq2, sem="sqdbg3"))
            store_toks.append(em.dsync(tern_d.ap(), tub, deps=bar_aq2, sem="sqdbg5"))
        tdl = None
        for fg in range(NFG):
            ttd2 = em.dgps(tgb.rearrange("p (a x) -> p a x", a=NCH),
                           t3scr["d"][:, fg * 8:(fg + 1) * 8, :], sem="gqd",
                           deps=[tern_g_free] + (bar_aq2 if fg == 0 else [])) 
            for fi in range(NCH):
                fci = fg * 8 + fi
                for cc in range(NCH):
                    deps = []
                    if fci == 0:
                        deps = [ttd2, pfree[cc]] + bar_aq2
                    elif fi == 0 and cc == 0:
                        deps = [ttd2]
                    tdl = em.t(lambda e, fi=fi, cc=cc, fci=fci, cw=cw: e.matmul(
                        out=pb[cc][:, 0:cw],
                        lhsT=tgb[:, fi * 1024 + cc * 128:fi * 1024 + (cc + 1) * 128],
                        rhs=aq2b[:, fci * 512:fci * 512 + cw],
                        start=(fci == 0), stop=(fci == NFC - 1)),
                        deps=deps)
            tern_g_free = tdl
        bar_dn = em.barrier()
        for cc in range(NCH):
            pp = cc % 2
            t1 = em.a(lambda e, cc=cc, cw=cw, pp=pp: e.activation(
                out=at[pp][:, 0:cw], in_=pb[cc][:, 0:cw], func=AF.Copy,
                scale=ninv["d"][:, 0:1]), deps=bar_dn + [at_free[pp]] + bc_toks["d"])
            t2 = em.v(lambda e, cc=cc, cw=cw, pp=pp: e.tensor_add(
                out=at[pp][:, 0:cw], in0=at[pp][:, 0:cw],
                in1=xct[:, cc * 512:cc * 512 + cw]), deps=[t1, tx])
            t3 = em.v(lambda e, cc=cc, cw=cw, pp=pp: e.tensor_add(
                out=evst[pp][:, 0:cw], in0=at[pp][:, 0:cw],
                in1=aoct[:, cc * 512:cc * 512 + cw]), deps=[t2, tao, ev_free[pp]])
            at_free[pp] = t3
            pfree[cc] = t3
            tst = em.dsync(xl0_o.ap()[cc * 128:(cc + 1) * 128, c0:c0 + cw],
                           evst[pp][:, 0:cw], deps=[t3], sem=f"sstx{pp}")
            ev_free[pp] = tst
            store_toks.append(tst)
        x_free = store_toks[-1]
        ao_free = store_toks[-1]
        xn_free = taq_last
        mx_free = tmx2
        mc_free = ttr2
    em.finalize(store_toks)
    return nc



def build_launch6(cfg):
    nc = bass.Bass("TRN2", target_bir_lowering=False, debug=False)
    em = Em(nc)
    COLS = cfg.COLS
    NL = 4

    xl0_in = nc.dram_tensor("xl0_in", [C, COLS], F32, kind="ExternalInput")
    wcd_in = nc.dram_tensor("wcd_in", [C, 512], F32, kind="ExternalInput")
    wqu_in = nc.dram_tensor("wqu_in", [512, C], F32, kind="ExternalInput")
    wkv_in = nc.dram_tensor("wkv_in", [512, 128], F32, kind="ExternalInput")
    xn1_o = nc.dram_tensor("xn1_o", [C, COLS], BF16, kind="ExternalOutput")
    q_o = nc.dram_tensor("q_o", [C, COLS], BF16, kind="ExternalOutput")
    k_o = nc.dram_tensor("k_o", [64, COLS], BF16, kind="ExternalOutput")
    vt_o = nc.dram_tensor("vt_o", [COLS, 64], BF16, kind="ExternalOutput")

    xb = nc.alloc_sbuf_tensor("xb", [128, NCH * COLS], F32)[:]
    xn1b = nc.alloc_sbuf_tensor("xn1b", [128, NCH * COLS], BF16)[:]
    latb = nc.alloc_sbuf_tensor("latb", [128, NL * COLS], BF16)[:]
    wcd = nc.alloc_sbuf_tensor("wcd", [128, NCH * 512], BF16)[:]
    wqu = nc.alloc_sbuf_tensor("wqu", [128, NL * C], BF16)[:]
    wkv = nc.alloc_sbuf_tensor("wkv", [128, NL * 128], BF16)[:]
    un0 = nc.alloc_sbuf_tensor("un0", [128, 2048], F32)
    evac = [nc.alloc_sbuf_tensor(f"evac{i}", [128, 512], F32)[:] for i in range(4)]
    evacB = [nc.alloc_sbuf_tensor(f"evacB{i}", [128, 512], BF16)[:] for i in range(4)]
    st = make_state(em, nc, rows=un0[:][0:1, 0:2048])
    pb = st["pb"]

    tx = em.dsync(xb.rearrange("p (a x) -> p a x", a=NCH), fm_dram_ap(xl0_in, COLS),
                  sem="sqx")
    twc = em.dgps(wcd.rearrange("p (a x) -> p a x", a=NCH),
                  wcd_in.ap().rearrange("(a p) x -> p a x", p=128), sem="gqw")
    twq = em.dgps(wqu.rearrange("p (a x) -> p a x", a=NL),
                  wqu_in.ap().rearrange("(a p) x -> p a x", p=128), sem="gqw2")
    twk = em.dgps(wkv.rearrange("p (a x) -> p a x", a=NL),
                  wkv_in.ap().rearrange("(a p) x -> p a x", p=128), sem="gqw3")

    def x_ap(ch, c0, cw):
        return xb[:, ch * COLS + c0: ch * COLS + c0 + cw]

    def n_ap(ch, c0, cw):
        return xn1b[:, ch * COLS + c0: ch * COLS + c0 + cw]

    emit_ln(em, st, x_ap, n_ap, COLS, mean=True, src_dep=[tx])
    bar_ln = em.barrier()
    t_st1 = em.dsync(fm_dram_ap(xn1_o, COLS), xn1b.rearrange("p (a x) -> p a x", a=NCH),
                     deps=bar_ln, sem="sqst1")
    store_toks = [t_st1]
    nct = len(col_tiles(COLS))
    pfree = {}
    evac_free = [None] * 4
    lat_toks = []
    for lc in range(NL):
        for ci, (c0, cw) in enumerate(col_tiles(COLS)):
            bank = pb[f"m{ci % 4}"]
            tk = None
            for kc in range(NCH):
                tk = em.t(lambda e, lc=lc, kc=kc, c0=c0, cw=cw, bank=bank: e.matmul(
                    out=bank[:, 0:cw],
                    lhsT=wcd[:, kc * 512 + lc * 128:kc * 512 + (lc + 1) * 128],
                    rhs=xn1b[:, kc * COLS + c0: kc * COLS + c0 + cw],
                    start=(kc == 0), stop=(kc == NCH - 1)),
                    deps=([twc, pfree.get(ci % 4)] + bar_ln) if kc == 0 else [])
            tev = em.v(lambda e, lc=lc, c0=c0, cw=cw, bank=bank: e.tensor_copy(
                out=latb[:, lc * COLS + c0: lc * COLS + c0 + cw], in_=bank[:, 0:cw]),
                deps=[tk])
            pfree[ci % 4] = tev
            lat_toks.append(tev)
    bar_lat = em.barrier()
    for qc in range(NCH):
        for ci, (c0, cw) in enumerate(col_tiles(COLS)):
            bank = pb[f"m{ci % 4}"]
            tk = None
            for lc in range(NL):
                tk = em.t(lambda e, qc=qc, lc=lc, c0=c0, cw=cw, bank=bank: e.matmul(
                    out=bank[:, 0:cw],
                    lhsT=wqu[:, lc * C + qc * 128:lc * C + (qc + 1) * 128],
                    rhs=latb[:, lc * COLS + c0: lc * COLS + c0 + cw],
                    start=(lc == 0), stop=(lc == NL - 1)),
                    deps=([twq, pfree.get(ci % 4)] + bar_lat) if lc == 0 else [])
            ei = (qc * nct + ci) % 4
            tev = em.a(lambda e, cw=cw, bank=bank, ei=ei: e.activation(
                out=evac[ei][:, 0:cw].bitcast(BF16)[:, 0:cw], in_=bank[:, 0:cw],
                func=AF.Copy, scale=0.125), deps=[tk, evac_free[ei]])
            pfree[ci % 4] = tev
            tst = em.dsync(q_o.ap()[qc * 128:(qc + 1) * 128, c0:c0 + cw],
                           evac[ei][:, 0:cw].bitcast(BF16)[:, 0:cw],
                           deps=[tev], sem=f"sst{ei}")
            evac_free[ei] = tst
            store_toks.append(tst)
    for ci, (c0, cw) in enumerate(col_tiles(COLS)):
        bank = pb[f"m{ci % 4}"]
        tk = None
        for lc in range(NL):
            tk = em.t(lambda e, lc=lc, c0=c0, cw=cw, bank=bank: e.matmul(
                out=bank[:, 0:cw],
                lhsT=wkv[:, lc * 128:(lc + 1) * 128],
                rhs=latb[:, lc * COLS + c0: lc * COLS + c0 + cw],
                start=(lc == 0), stop=(lc == NL - 1)),
                deps=([twk, pfree.get(ci % 4)] + bar_lat) if lc == 0 else [])
        ei = ci % 4
        tev = em.v(lambda e, cw=cw, bank=bank, ei=ei: e.tensor_copy(
            out=evacB[ei][:, 0:cw], in_=bank[:, 0:cw]),
            deps=[tk, evac_free[ei]])
        pfree[ci % 4] = tev
        tst = em.dsync(k_o.ap()[0:64, c0:c0 + cw],
                       evacB[ei][0:64, 0:cw],
                       deps=[tev], sem=f"sst{ei}")
        evac_free[ei] = tst
        store_toks.append(tst)
    TW = min(128, COLS)
    ncc = COLS // TW
    for ti in range(ncc):
        bank = pb[f"m{ti % 4}"]
        tk = None
        for lc in range(NL):
            tk = em.t(lambda e, lc=lc, ti=ti, bank=bank: e.matmul(
                out=bank[0:TW, 0:64],
                lhsT=latb[:, lc * COLS + ti * TW:lc * COLS + (ti + 1) * TW],
                rhs=wkv[:, lc * 128 + 64:lc * 128 + 128],
                start=(lc == 0), stop=(lc == NL - 1)),
                deps=([twk, pfree.get(ti % 4)] + bar_lat) if lc == 0 else [])
        ei = ti % 4
        tev = em.v(lambda e, ti=ti, bank=bank, ei=ei: e.tensor_copy(
            out=evacB[ei][0:TW, 0:64], in_=bank[0:TW, 0:64]),
            deps=[tk, evac_free[ei]])
        pfree[ti % 4] = tev
        tst = em.dsync(vt_o.ap()[ti * TW:(ti + 1) * TW, :],
                       evacB[ei][0:TW, 0:64],
                       deps=[tev], sem=f"sst{ei}")
        evac_free[ei] = tst
        store_toks.append(tst)
    em.finalize(store_toks)
    return nc



def build_launch7(cfg):
    nc = bass.Bass("TRN2", target_bir_lowering=False, debug=False)
    em = Em(nc)
    T = cfg.T
    BT = B * T
    KW = min(512, T)
    QW2 = min(512, T)
    SQ = QW2 // 128
    NQ5 = T // QW2

    qA_in = nc.dram_tensor("qA_in", [64, BT], BF16, kind="ExternalInput")
    qB_in = nc.dram_tensor("qB_in", [64, BT], BF16, kind="ExternalInput")
    kf_in = nc.dram_tensor("kf_in", [64, BT], BF16, kind="ExternalInput")
    vt_in = nc.dram_tensor("vt_in", [BT, 64], BF16, kind="ExternalInput")
    m1_in = nc.dram_tensor("m1_in", [128, 2 * KW], F32, kind="ExternalInput")
    m2_in = nc.dram_tensor("m2_in", [128, 128 * (SQ - 1) + QW2], F32, kind="ExternalInput")
    y_o = nc.dram_tensor("y_o", [128, BT], BF16, kind="ExternalOutput")

    qs = [nc.alloc_sbuf_tensor("qA", [64, BT], BF16)[:],
          nc.alloc_sbuf_tensor("qB", [64, BT], BF16)[:]]
    kb = nc.alloc_sbuf_tensor("kb", [64, BT], BF16)[:]
    vt_sb = nc.alloc_sbuf_tensor("vt_sb", [128, (BT // 128) * 64], BF16)[:]
    ybuf = [nc.alloc_sbuf_tensor(f"ybuf{i}", [64, BT], BF16)[:] for i in range(2)]
    m1b = nc.alloc_sbuf_tensor("m1b", [128, 2 * KW], F32)[:]
    m2b = nc.alloc_sbuf_tensor("m2b", [128, 128 * (SQ - 1) + QW2], F32)[:]
    pexp = [nc.alloc_sbuf_tensor(f"pexp{i}", [128, QW2], BF16)[:] for i in range(2)]
    mcol = nc.alloc_sbuf_tensor("mcol", [128, 1], F32)[:]
    mtmp = nc.alloc_sbuf_tensor("mtmp", [128, 1], F32)[:]
    negm = nc.alloc_sbuf_tensor("negm", [1, QW2], BF16)[:]
    recr = nc.alloc_sbuf_tensor("recr", [1, QW2], F32)[:]
    ysc = nc.alloc_sbuf_tensor("ysc", [64, QW2], F32)[:]
    idf = nc.alloc_sbuf_tensor("idf", [128, 128], F32)[:]
    ones_row = nc.alloc_sbuf_tensor("ones_row", [1, 128], BF16)[:]
    ones_row32 = nc.alloc_sbuf_tensor("ones_row32", [1, 128], F32)[:]
    ones_col = nc.alloc_sbuf_tensor("ones_col", [128, 1], BF16)[:]
    pb = [nc.alloc_psum_tensor(f"pb{i}", [128, 512], F32)[:] for i in range(8)]

    tqa_ld = em.dsync(qs[0], qA_in.ap(), sem="sqqa")
    tqb_ld = em.dsync(qs[1], qB_in.ap(), sem="sqqb")
    tk = em.dsync(kb, kf_in.ap(), sem="sqk")
    tv = em.dsync(vt_sb.rearrange("p (n d) -> p n d", d=64),
                  vt_in.ap().rearrange("(n p) d -> p n d", p=128), sem="sqv")
    tm1 = em.dsync(m1b, m1_in.ap(), sem="sqm")
    tm2 = em.dsync(m2b, m2_in.ap(), sem="sqm2")
    ti0 = em.v(lambda e: e.memset(idf, 1.0))
    ti1 = em.p(lambda e: e.affine_select(out=idf, in_=idf, pattern=[[-1, 128]],
                                         compare_op=ALU.is_equal, fill=0.0,
                                         base=0, channel_multiplier=1), deps=[ti0])
    tor = em.v(lambda e: e.memset(ones_row, 1.0))
    tor32 = em.v(lambda e: e.memset(ones_row32, 1.0))
    toc = em.v(lambda e: e.memset(ones_col, 1.0))
    base = [tqa_ld, tqb_ld, tk, tv, tm1, tm2, ti1, tor, tor32, toc]

    y_toks = []
    p0_free = None
    p15_free = [None, None]
    p2_free = None
    p3_free = None
    p4_free = None
    p6_free = None
    mc_free = None
    mt_free = None
    neg_free = None
    pex_free = [None, None]
    rec_free = None
    ysc_free = None
    for hl in range(2):
        qh = qs[hl]
        for b in range(B):
            kb0 = b * T
            for q5 in range(NQ5):
                qgb = q5 * QW2
                for sq in range(SQ):
                    qb128 = qgb + sq * 128
                    nkt = qb128 // KW + 1
                    tred = None
                    for kt in range(nkt):
                        tkm = em.t(lambda e, qh=qh, kt=kt, qb128=qb128, kb0=kb0: e.matmul(
                            out=pb[0][:, 0:KW],
                            lhsT=qh[:, kb0 + qb128: kb0 + qb128 + 128],
                            rhs=kb[:, kb0 + kt * KW: kb0 + (kt + 1) * KW],
                            start=True, stop=True),
                            deps=(base + [p0_free]))
                        if kt == nkt - 1:
                            off = KW - (qb128 - kt * KW)
                            tkm = em.v(lambda e, off=off: e.tensor_add(
                                out=pb[0][:, 0:KW], in0=pb[0][:, 0:KW],
                                in1=m1b[:, off:off + KW]), deps=[tkm, tm1])
                        if kt == 0:
                            tred = em.v(lambda e: e.tensor_reduce(
                                out=mcol, in_=pb[0][:, 0:KW], axis=AX.X, op=ALU.max),
                                deps=[tkm, mc_free])
                        else:
                            tr2 = em.v(lambda e: e.tensor_reduce(
                                out=mtmp, in_=pb[0][:, 0:KW], axis=AX.X, op=ALU.max),
                                deps=[tkm, mt_free])
                            tred = em.v(lambda e: e.tensor_max(
                                out=mcol, in0=mcol, in1=mtmp), deps=[tred, tr2])
                            mt_free = tred
                        p0_free = tred
                    ttr = em.t(lambda e: e.transpose(
                        out=pb[6][0:1, 0:128], in_=mcol[:, 0:1], identity=idf),
                        deps=[tred, ti1, p6_free])
                    tng = em.a(lambda e, sq=sq: e.activation(
                        out=negm[0:1, sq * 128:(sq + 1) * 128], in_=pb[6][0:1, 0:128],
                        func=AF.Copy, scale=-1.0), deps=[ttr, neg_free])
                    p6_free = tng
                    mc_free = ttr
                nk2 = (qgb + QW2) // 128
                tsum = tpv = None
                tk2last = None
                for kt2 in range(nk2):
                    pp = kt2 % 2
                    bank = pb[1 + 4 * pp]
                    tk2 = em.t(lambda e, qh=qh, kt2=kt2, kb0=kb0, qgb=qgb, bank=bank:
                               e.matmul(
                        out=bank[:, 0:QW2],
                        lhsT=kb[:, kb0 + kt2 * 128: kb0 + (kt2 + 1) * 128],
                        rhs=qh[:, kb0 + qgb: kb0 + qgb + QW2],
                        start=True, stop=False), deps=base + [p15_free[pp]])
                    tk2 = em.t(lambda e, bank=bank: e.matmul(
                        out=bank[:, 0:QW2], lhsT=ones_row[0:1, 0:128],
                        rhs=negm[0:1, 0:QW2], start=False, stop=True),
                        deps=[tng, tor])
                    r = kt2 - (qgb // 128)
                    if r >= 0:
                        off = 128 * (SQ - 1) - 128 * r
                        tk2 = em.v(lambda e, off=off, bank=bank: e.tensor_add(
                            out=bank[:, 0:QW2], in0=bank[:, 0:QW2],
                            in1=m2b[:, off:off + QW2]), deps=[tk2, tm2])
                    tex = em.a(lambda e, pp=pp, bank=bank: e.activation(
                        out=pexp[pp][:, 0:QW2], in_=bank[:, 0:QW2], func=AF.Exp),
                        deps=[tk2, pex_free[pp]])
                    p15_free[pp] = tex
                    tsum = em.t(lambda e, pp=pp, first=(kt2 == 0), last=(kt2 == nk2 - 1):
                                e.matmul(
                        out=pb[2][0:1, 0:QW2], lhsT=ones_col[:, 0:1],
                        rhs=pexp[pp][:, 0:QW2], start=first, stop=last),
                        deps=([tex, toc, p2_free] if kt2 == 0 else [tex]))
                    vti = (kb0 + kt2 * 128) // 128
                    tpv = em.t(lambda e, pp=pp, vti=vti, first=(kt2 == 0),
                               last=(kt2 == nk2 - 1): e.matmul(
                        out=pb[3][0:64, 0:QW2],
                        lhsT=vt_sb[:, vti * 64:(vti + 1) * 64],
                        rhs=pexp[pp][:, 0:QW2], start=first, stop=last),
                        deps=([tv, p3_free] if kt2 == 0 else []))
                    pex_free[pp] = tpv
                    tk2last = tk2
                neg_free = tk2last
                trec = em.v(lambda e: e.reciprocal(out=recr[0:1, 0:QW2],
                                                   in_=pb[2][0:1, 0:QW2]),
                            deps=[tsum, rec_free])
                p2_free = trec
                tbc = em.t(lambda e: e.matmul(
                    out=pb[4][0:64, 0:QW2], lhsT=ones_row32[0:1, 0:64],
                    rhs=recr[0:1, 0:QW2],
                    start=True, stop=True), deps=[trec, tor32, p4_free])
                rec_free = tbc
                tyc = em.v(lambda e: e.tensor_copy(out=ysc[:, 0:QW2],
                                                   in_=pb[3][0:64, 0:QW2]),
                           deps=[tpv, ysc_free])
                p3_free = tyc
                tym = em.v(lambda e, hl=hl, kb0=kb0, qgb=qgb: e.tensor_mul(
                    out=ybuf[hl][:, kb0 + qgb: kb0 + qgb + QW2],
                    in0=ysc[:, 0:QW2], in1=pb[4][0:64, 0:QW2]),
                    deps=[tyc, tbc])
                p4_free = tym
                ysc_free = tym
                y_toks.append(tym)
    ts1 = em.dsync(y_o.ap()[0:64, :], ybuf[0], deps=y_toks, sem="sqy0")
    ts2 = em.dsync(y_o.ap()[64:128, :], ybuf[1], deps=y_toks, sem="sqy1")
    em.finalize([ts1, ts2])
    return nc


def launch7_inmaps(cfg, q_btC, k1, v1):
    T = cfg.T
    BT = B * T
    KW = min(512, T)
    QW2 = min(512, T)
    SQ = QW2 // 128
    q_cm = np.ascontiguousarray(np.asarray(q_btC, BFD).transpose(2, 0, 1).reshape(C, BT))
    kf = np.ascontiguousarray(np.asarray(k1, BFD).transpose(2, 0, 1).reshape(64, BT))
    vt = np.ascontiguousarray(np.asarray(v1, BFD).reshape(BT, 64))
    p = np.arange(128)[:, None]
    x1 = np.arange(2 * KW)[None, :]
    m1 = np.where(x1 <= p + KW, 0.0, -3e38).astype(np.float32)
    x2 = np.arange(128 * (SQ - 1) + QW2)[None, :]
    m2 = np.where(x2 >= p + 128 * (SQ - 1), 0.0, -3e38).astype(np.float32)
    ims = []
    for core in range(NCORES):
        ims.append({
            "qA_in": np.ascontiguousarray(q_cm[(2 * core) * 64:(2 * core + 1) * 64]),
            "qB_in": np.ascontiguousarray(q_cm[(2 * core + 1) * 64:(2 * core + 2) * 64]),
            "kf_in": kf, "vt_in": vt, "m1_in": m1, "m2_in": m2,
        })
    return ims


def launch7_unshard(cfg, youts):
    T = cfg.T
    y = np.empty((B, T, C), np.float32)
    for core in range(NCORES):
        arr = np.asarray(youts[core])
        if arr.dtype == np.uint16:
            arr = arr.view(BFD)
        arr = arr.astype(np.float32).reshape(128, B, T)
        y[:, :, (2 * core) * 64:(2 * core + 2) * 64] = arr.transpose(1, 2, 0)
    return y



def build_launch8(cfg):
    nc = bass.Bass("TRN2", target_bir_lowering=False, debug=False)
    em = Em(nc)
    COLS = cfg.COLS
    LB = COLS // B

    y_in = nc.dram_tensor("y_in", [C, COLS], BF16, kind="ExternalInput")
    xn1_in = nc.dram_tensor("xn1_in", [C, COLS], BF16, kind="ExternalInput")
    xl0_in = nc.dram_tensor("xl0_in", [C, COLS], F32, kind="ExternalInput")
    wcp_in = nc.dram_tensor("wcp_in", [C, C], F32, kind="ExternalInput")
    wts_in = {nm: nc.dram_tensor(nm, [C, C], F32, kind="ExternalInput")
              for nm in ("wfT", "ufT", "wiT", "uiT", "wcT", "ucT", "woT", "uoT")}
    bias_in = {nm: nc.dram_tensor(nm, [128, NCH], F32, kind="ExternalInput").ap()
               for nm in ("bf", "bi", "bc", "bo", "in_c")}
    out_o = nc.dram_tensor("out_o", [C, COLS], F32, kind="ExternalOutput")

    xn1b = nc.alloc_sbuf_tensor("xn1b", [128, NCH * COLS], BF16)[:]
    cb = nc.alloc_sbuf_tensor("cb", [128, NCH * LB], F32)[:]
    cbf = nc.alloc_sbuf_tensor("cbf", [128, NCH * LB], BF16)[:]
    wwb = nc.alloc_sbuf_tensor("wwb", [128, NCH * C], BF16)[:]
    uub = nc.alloc_sbuf_tensor("uub", [128, NCH * C], BF16)[:]
    wob = nc.alloc_sbuf_tensor("wob", [128, NCH * C], BF16)[:]
    yb = nc.alloc_sbuf_tensor("yb", [128, NCH * COLS], BF16)[:]
    ibuf = nc.alloc_sbuf_tensor("ibuf", [128, NCH * LB], F32)[:]
    gtile = [nc.alloc_sbuf_tensor(f"gtile{i}", [128, 512], F32)[:] for i in range(2)]
    xl0ct = nc.alloc_sbuf_tensor("xl0ct", [128, NCH * 512], F32)[:]
    biases = {nm: nc.alloc_sbuf_tensor("b_" + nm, [128, NCH], F32)[:]
              for nm in bias_in}
    ath = [nc.alloc_sbuf_tensor(f"ath{i}", [128, 512], F32)[:] for i in range(2)]
    tmp1, tmp2 = ath[0], ath[1]
    at2 = [nc.alloc_sbuf_tensor(f"at2{i}", [128, 512], F32)[:] for i in range(2)]
    evst = [nc.alloc_sbuf_tensor(f"evst{i}", [128, 512], F32)[:] for i in range(2)]
    pb = [nc.alloc_psum_tensor(f"pb{i}", [128, 512], F32)[:] for i in range(8)]

    tx1 = em.dsync(xn1b.rearrange("p (a x) -> p a x", a=NCH), fm_dram_ap(xn1_in, COLS),
                   sem="sqn")
    tbias = None
    for nm in bias_in:
        tbias = em.dsync(biases[nm], bias_in[nm], sem="sqb")
    cin_toks = []
    for ch in range(NCH):
        t1 = em.v(lambda e, ch=ch: e.memset(cb[:, ch * LB:(ch + 1) * LB], 1.0))
        t2 = em.v(lambda e, ch=ch: e.tensor_scalar_mul(
            out=cb[:, ch * LB:(ch + 1) * LB], in0=cb[:, ch * LB:(ch + 1) * LB],
            scalar1=biases["in_c"][:, ch:ch + 1]), deps=[t1, tbias])
        cin_toks.append(t2)
    gate_cfg = {"f": ("wfT", "ufT", "bf", AF.Sigmoid), "i": ("wiT", "uiT", "bi", AF.Sigmoid),
                "g": ("wcT", "ucT", "bc", AF.Tanh)}
    pfree = [None] * 8
    cup_toks = cin_toks
    cbf_toks = [None] * NCH
    gb_free = {g: [None] * NCH for g in "fig"}
    wld_free = None
    uld_free = None
    for step in range(B):
        new_cbf = []
        for ch in range(NCH):
            tcb = em.v(lambda e, ch=ch: e.tensor_copy(
                out=cbf[:, ch * LB:(ch + 1) * LB], in_=cb[:, ch * LB:(ch + 1) * LB]),
                deps=cup_toks + [cbf_toks[ch]])
            new_cbf.append(tcb)
        cbf_toks = new_cbf
        new_cup = [None] * NCH
        gt_free = [None] * 2
        for g in "fig":
            wnm, unm, bnm, func = gate_cfg[g]
            twl = em.dgps(wwb.rearrange("p (a x) -> p a x", a=NCH),
                          fm_dram_ap(wts_in[wnm], C), sem="gqw0", deps=[wld_free])
            tul = em.dgps(uub.rearrange("p (a x) -> p a x", a=NCH),
                          fm_dram_ap(wts_in[unm], C), sem="gqw1", deps=[uld_free])
            for oc in range(NCH):
                bank = pb[oc % 4]
                tk = None
                for kc in range(NCH):
                    tk = em.t(lambda e, oc=oc, kc=kc, step=step, bank=bank: e.matmul(
                        out=bank[:, 0:LB],
                        lhsT=wwb[:, kc * C + oc * 128:kc * C + (oc + 1) * 128],
                        rhs=xn1b[:, kc * COLS + step * LB: kc * COLS + (step + 1) * LB],
                        start=(kc == 0), stop=False),
                        deps=([twl, tx1, pfree[oc % 4]] if kc == 0 else []))
                for kc in range(NCH):
                    tk = em.t(lambda e, oc=oc, kc=kc, bank=bank: e.matmul(
                        out=bank[:, 0:LB],
                        lhsT=uub[:, kc * C + oc * 128:kc * C + (oc + 1) * 128],
                        rhs=cbf[:, kc * LB:(kc + 1) * LB],
                        start=False, stop=(kc == NCH - 1)),
                        deps=([tul] + cbf_toks) if kc == 0 else [])
                if g == "i":
                    tev = em.a(lambda e, oc=oc, bank=bank, bnm=bnm, func=func:
                               e.activation(
                        out=ibuf[:, oc * LB:(oc + 1) * LB], in_=bank[:, 0:LB],
                        func=func, bias=biases[bnm][:, oc:oc + 1]),
                        deps=[tk, tbias, gb_free["i"][oc]])
                    pfree[oc % 4] = tev
                    gb_free["i"][oc] = tev
                else:
                    pp = oc % 2
                    tev = em.a(lambda e, oc=oc, bank=bank, bnm=bnm, func=func, pp=pp:
                               e.activation(
                        out=gtile[pp][:, 0:LB], in_=bank[:, 0:LB],
                        func=func, bias=biases[bnm][:, oc:oc + 1]),
                        deps=[tk, tbias, gt_free[pp]])
                    pfree[oc % 4] = tev
                    if g == "f":
                        tcu = em.v(lambda e, oc=oc, pp=pp: e.tensor_mul(
                            out=cb[:, oc * LB:(oc + 1) * LB],
                            in0=cb[:, oc * LB:(oc + 1) * LB], in1=gtile[pp][:, 0:LB]),
                            deps=[tev] + cup_toks + ([new_cup[oc]] if new_cup[oc] else []))
                    else:
                        tmu = em.v(lambda e, oc=oc, pp=pp: e.tensor_mul(
                            out=gtile[pp][:, 0:LB], in0=gtile[pp][:, 0:LB],
                            in1=ibuf[:, oc * LB:(oc + 1) * LB]),
                            deps=[tev, gb_free["i"][oc]])
                        tcu = em.v(lambda e, oc=oc, pp=pp: e.tensor_add(
                            out=cb[:, oc * LB:(oc + 1) * LB],
                            in0=cb[:, oc * LB:(oc + 1) * LB], in1=gtile[pp][:, 0:LB]),
                            deps=[tmu, new_cup[oc]])
                        gb_free["i"][oc] = tmu
                    gt_free[pp] = tcu
                    new_cup[oc] = tcu
            wld_free = tk
            uld_free = tk
        cup_toks = new_cup
    new_cbf = []
    for ch in range(NCH):
        tcb = em.v(lambda e, ch=ch: e.tensor_copy(
            out=cbf[:, ch * LB:(ch + 1) * LB], in_=cb[:, ch * LB:(ch + 1) * LB]),
            deps=cup_toks + [cbf_toks[ch]])
        new_cbf.append(tcb)
    cbf_toks = new_cbf
    ty = em.dsync(yb.rearrange("p (a x) -> p a x", a=NCH), fm_dram_ap(y_in, COLS),
                  sem="sqy", deps=cup_toks + cbf_toks)
    twc = em.dgps(wwb.rearrange("p (a x) -> p a x", a=NCH),
                  wcp_in.ap().rearrange("(a p) x -> p a x", p=128), sem="gqw0",
                  deps=[wld_free])
    two = em.dgps(uub.rearrange("p (a x) -> p a x", a=NCH), fm_dram_ap(wts_in["woT"], C),
                  sem="gqw1", deps=[uld_free])
    tuo = em.dgps(wob.rearrange("p (a x) -> p a x", a=NCH), fm_dram_ap(wts_in["uoT"], C),
                  sem="gqw2")
    store_toks = []
    xl_free = None
    ath_free = [None] * 2
    at2_free = [None] * 2
    ev_free = [None] * 2
    for ci, (c0, cw) in enumerate(col_tiles(COLS, LB)):
        txl = em.dsync(xl0ct.rearrange("p (a x) -> p a x", a=NCH)[:, :, 0:cw],
                       fm_dram_ap(xl0_in, COLS)[:, :, c0:c0 + cw], sem="sqxl",
                       deps=[xl_free])
        for oc in range(NCH):
            pp = oc % 2
            bank_h = pb[4 + pp]
            bank_a = pb[6 + pp]
            tkh = None
            for kc in range(NCH):
                tkh = em.t(lambda e, oc=oc, kc=kc, c0=c0, cw=cw, bank_h=bank_h: e.matmul(
                    out=bank_h[:, 0:cw],
                    lhsT=uub[:, kc * C + oc * 128:kc * C + (oc + 1) * 128],
                    rhs=xn1b[:, kc * COLS + c0: kc * COLS + c0 + cw],
                    start=(kc == 0), stop=False),
                    deps=([two, tx1, pfree[4 + pp]] if kc == 0 else []))
            for kc in range(NCH):
                tkh = em.t(lambda e, oc=oc, kc=kc, cw=cw, bank_h=bank_h: e.matmul(
                    out=bank_h[:, 0:cw],
                    lhsT=wob[:, kc * C + oc * 128:kc * C + (oc + 1) * 128],
                    rhs=cbf[:, kc * LB: kc * LB + cw],
                    start=False, stop=(kc == NCH - 1)),
                    deps=([tuo] + cbf_toks) if kc == 0 else [])
            tevh = em.a(lambda e, oc=oc, cw=cw, bank_h=bank_h, pp=pp: e.activation(
                out=ath[pp][:, 0:cw], in_=bank_h[:, 0:cw], func=AF.Tanh,
                bias=biases["bo"][:, oc:oc + 1]), deps=[tkh, tbias, ath_free[pp]])
            pfree[4 + pp] = tevh
            tka = None
            for kc in range(NCH):
                tka = em.t(lambda e, oc=oc, kc=kc, c0=c0, cw=cw, bank_a=bank_a: e.matmul(
                    out=bank_a[:, 0:cw],
                    lhsT=wwb[:, kc * C + oc * 128:kc * C + (oc + 1) * 128],
                    rhs=yb[:, kc * COLS + c0: kc * COLS + c0 + cw],
                    start=(kc == 0), stop=(kc == NCH - 1)),
                    deps=([twc, ty, pfree[6 + pp]] if kc == 0 else []))
            ta1 = em.v(lambda e, cw=cw, bank_a=bank_a, pp=pp: e.tensor_add(
                out=at2[pp][:, 0:cw], in0=ath[pp][:, 0:cw], in1=bank_a[:, 0:cw]),
                deps=[tka, tevh, at2_free[pp]])
            pfree[6 + pp] = ta1
            ath_free[pp] = ta1
            ta2 = em.v(lambda e, oc=oc, cw=cw, pp=pp: e.tensor_add(
                out=evst[pp][:, 0:cw], in0=at2[pp][:, 0:cw],
                in1=xl0ct[:, oc * 512:oc * 512 + cw]), deps=[ta1, txl, ev_free[pp]])
            at2_free[pp] = ta2
            tst = em.dsync(out_o.ap()[oc * 128:(oc + 1) * 128, c0:c0 + cw],
                           evst[pp][:, 0:cw], deps=[ta2], sem=f"sst{pp}")
            ev_free[pp] = tst
            store_toks.append(tst)
        xl_free = store_toks[-1]
    em.finalize(store_toks)
    return nc



def _np(a):
    a = np.asarray(a)
    if a.dtype == np.uint16:
        a = a.view(BFD)
    return a


def _runner(nc, in_maps):
    res = run_bass_kernel_spmd(nc, in_maps, core_ids=list(range(NCORES)))
    return res.results


def unshard_vt(cfg, shards):
    out = np.empty((B, cfg.T, 64), dtype=shards[0].dtype)
    for core in range(NCORES):
        s = shards[core]
        for i, (b, t0) in enumerate(seg_starts_global(cfg, core)):
            out[b, t0:t0 + cfg.L] = s[i * cfg.L:(i + 1) * cfg.L]
    return out


def kernel(**inputs):
    d = inputs
    f32 = lambda k: np.asarray(d[k], np.float32)
    x = f32("L0_attn_x")
    cfg = Cfg(x.shape[1])

    nc = build_launch1(cfg)
    im = launch1_inmaps(cfg, x, d["L0_attn_ln1_w"], d["L0_attn_ln1_b"],
                        d["L0_attn_tmk"], d["L0_attn_tmv"], d["L0_attn_tmr"],
                        d["L0_attn_att_kw"], d["L0_attn_att_vw"], d["L0_attn_att_rw"])
    r = _runner(nc, im)
    xn = unshard_cols(cfg, [_np(c["xn_o"]) for c in r])
    kk = unshard_cols(cfg, [_np(c["k_o"]) for c in r])
    vv = unshard_cols(cfg, [_np(c["v_o"]) for c in r])
    sr = unshard_cols(cfg, [_np(c["sr_o"]) for c in r])

    nc = build_launch2(cfg, L2=512)
    im = launch2_inmaps(cfg, kk, vv, d["L0_attn_time_decay"], d["L0_attn_time_first"])
    r = _runner(nc, im)
    wkv = launch2_unshard(cfg, [_np(c["wkv_o"]) for c in r])

    nc = build_launch3(cfg)
    owT = np.ascontiguousarray(f32("L0_attn_att_ow").T)
    im = [{"wkv_in": shard_cols(cfg, c, wkv), "sr_in": shard_cols(cfg, c, sr),
           "xn_in": shard_cols(cfg, c, xn), "owT": owT} for c in range(NCORES)]
    r = _runner(nc, im)
    x1 = unshard_cols(cfg, [_np(c["x1_o"]) for c in r])

    nc = build_launch4a(cfg)
    im = []
    for c in range(NCORES):
        x1s = shard_cols(cfg, c, x1)
        hxh, hmask = halo_of(cfg, c, x1)
        im.append({"x1in": np.ascontiguousarray(np.concatenate([x1s, hxh], 1), np.float32),
                   "maskin": hmask,
                   "ln2w": chunked(d["L0_attn_ln2_w"]), "ln2b": chunked(d["L0_attn_ln2_b"]),
                   "ftmk": chunked(d["L0_attn_f_tmk"]), "ftmr": chunked(d["L0_attn_f_tmr"])})
    r = _runner(nc, im)
    xk2 = unshard_cols(cfg, [_np(c["xk2_o"]) for c in r])
    xr2 = unshard_cols(cfg, [_np(c["xr2_o"]) for c in r])

    nc = build_launch4b(cfg)
    fkwT = np.ascontiguousarray(f32("L0_attn_f_kw").T)
    fvwT = np.ascontiguousarray(f32("L0_attn_f_vw").T)
    frwT = np.ascontiguousarray(f32("L0_attn_f_rw").T)
    im = [{"xk2_in": shard_cols(cfg, c, xk2.astype(BFD)),
           "xr2_in": shard_cols(cfg, c, xr2.astype(BFD)),
           "x1_in": shard_cols(cfg, c, x1),
           "fkwT": fkwT, "fvwT": fvwT, "frwT": frwT} for c in range(NCORES)]
    r = _runner(nc, im)
    ao = unshard_cols(cfg, [_np(c["ao_o"]) for c in r])

    nc = build_launch5(cfg)
    im = [{"xn_in": shard_cols(cfg, c, xn), "x_in": shard_cols(cfg, c, x),
           "ao_in": shard_cols(cfg, c, ao),
           "wgT": np.ascontiguousarray(f32("L0_ffn_w_gate").T),
           "wuT": np.ascontiguousarray(f32("L0_ffn_w_up").T),
           "wdT": np.ascontiguousarray(f32("L0_ffn_w_down").T)} for c in range(NCORES)]
    r = _runner(nc, im)
    xl0 = unshard_cols(cfg, [_np(c["xl0_o"]) for c in r])

    nc = build_launch6(cfg)
    im = [{"xl0_in": shard_cols(cfg, c, xl0),
           "wcd_in": f32("L1_attn_Wc_down"),
           "wqu_in": f32("L1_attn_Wq_up"),
           "wkv_in": f32("L1_attn_Wkv_up")} for c in range(NCORES)]
    r = _runner(nc, im)
    xn1 = unshard_cols(cfg, [_np(c["xn1_o"]) for c in r])
    qq = unshard_cols(cfg, [_np(c["q_o"]) for c in r])
    k1 = unshard_cols(cfg, [_np(c["k_o"]) for c in r])
    v1 = unshard_vt(cfg, [_np(c["vt_o"]) for c in r])

    nc = build_launch7(cfg)
    im = launch7_inmaps(cfg, qq, k1, v1)
    r = _runner(nc, im)
    y = launch7_unshard(cfg, [_np(c["y_o"]) for c in r])

    nc = build_launch8(cfg)
    im = []
    for c in range(NCORES):
        m = {"y_in": shard_cols(cfg, c, y.astype(BFD)),
             "xn1_in": shard_cols(cfg, c, xn1),
             "xl0_in": shard_cols(cfg, c, xl0),
             "wcp_in": f32("L1_attn_Wc_proj")}
        for nm, key in (("wfT", "L1_ffn_wf"), ("ufT", "L1_ffn_uf"),
                        ("wiT", "L1_ffn_wi"), ("uiT", "L1_ffn_ui"),
                        ("wcT", "L1_ffn_wc"), ("ucT", "L1_ffn_uc"),
                        ("woT", "L1_ffn_wo"), ("uoT", "L1_ffn_uo")):
            m[nm] = np.ascontiguousarray(f32(key).T)
        for nm, key in (("bf", "L1_ffn_bf"), ("bi", "L1_ffn_bi"), ("bc", "L1_ffn_bc"),
                        ("bo", "L1_ffn_bo"), ("in_c", "L1_ffn_in_c")):
            m[nm] = chunked(d[key])
        im.append(m)
    r = _runner(nc, im)
    out = unshard_cols(cfg, [_np(c["out_o"]).astype(np.float32) for c in r])
    return np.ascontiguousarray(out)


# version 05
# speedup vs baseline: 1.0498x; 1.0495x over previous
import numpy as np
import ml_dtypes
import concourse.bass as bass
import concourse.mybir as mybir
from concourse.bass_utils import run_bass_kernel_spmd

F32 = mybir.dt.float32
BF16 = mybir.dt.bfloat16
FP16 = mybir.dt.float16
AF = mybir.ActivationFunctionType
ALU = mybir.AluOpType
AX = mybir.AxisListType

NCORES = 8
C = 1024
NCH = 8
B = 4
EPS = 1e-5
BFD = np.dtype(ml_dtypes.bfloat16)

ENGINES = ("sync", "gpsimd", "vector", "scalar", "tensor")


def _lst(x):
    if x is None:
        return []
    if isinstance(x, list):
        return [t for t in x if t is not None]
    return [x]


class Cfg:
    def __init__(self, T=4096):
        self.T = T
        self.L = T // 16
        self.SEGS = 2 * B
        self.COLS = self.SEGS * self.L
        self.CA = self.COLS + self.SEGS


class Em:
    def __init__(self, nc):
        self.nc = nc
        self.streams = {e: [] for e in ENGINES}
        self.sems = {}
        self.ctr = {}

    def _sem(self, name):
        if name not in self.sems:
            self.sems[name] = self.nc.alloc_semaphore("sem_" + name)
            self.ctr[name] = 0
        return self.sems[name]

    def _emit(self, eng, fn, deps, semname, n):
        self._sem(semname)
        self.ctr[semname] += n
        val = self.ctr[semname]
        self.streams[eng].append((fn, tuple(_lst(list(deps))), semname, n))
        return (semname, val)

    def v(self, fn, deps=()):
        return self._emit("vector", fn, deps, "v", 1)

    def a(self, fn, deps=()):
        return self._emit("scalar", fn, deps, "a", 1)

    def t(self, fn, deps=()):
        return self._emit("tensor", fn, deps, "t", 1)

    def p(self, fn, deps=()):
        return self._emit("gpsimd", fn, deps, "p", 1)

    def dsync(self, out, in_, deps=(), sem="sq"):
        return self._emit("sync", lambda e: e.dma_start(out=out, in_=in_), deps, sem, 16)

    def dscal(self, out, in_, deps=(), sem="adq"):
        return self._emit("scalar", lambda e: e.dma_start(out=out, in_=in_), deps, sem, 16)

    def dgps(self, out, in_, deps=(), sem="gq"):
        return self._emit("gpsimd", lambda e: e.dma_start(out=out, in_=in_), deps, sem, 16)

    def barrier(self):
        return [(s, self.ctr[s]) for s in self.ctr]

    def finalize(self, final_deps):
        nc = self.nc
        streams, sems = self.streams, self.sems

        def section(eng):
            def run(e):
                last = {}
                for fn, deps, semname, n in streams[eng]:
                    for (dsem, dval) in deps:
                        if last.get(dsem, 0) < dval:
                            e.wait_ge(sems[dsem], dval)
                            last[dsem] = dval
                    fn(e).then_inc(sems[semname], n)
                if eng == "sync":
                    for (dsem, dval) in final_deps:
                        if last.get(dsem, 0) < dval:
                            e.wait_ge(sems[dsem], dval)
                            last[dsem] = dval
            return run

        with nc.Block() as blk:
            blk.sync(section("sync"))
            blk.gpsimd(section("gpsimd"))
            blk.vector(section("vector"))
            blk.scalar(section("scalar"))
            blk.tensor(section("tensor"))



def seg_starts_global(cfg, core):
    out = []
    for b in range(B):
        for blkid in (core, 15 - core):
            out.append((b, blkid * cfg.L))
    return out


def shard_cols(cfg, core, arr_btC):
    cols = []
    for b in range(B):
        for blkid in (core, 15 - core):
            cols.append(arr_btC[b, blkid * cfg.L:(blkid + 1) * cfg.L, :])
    return np.ascontiguousarray(np.concatenate(cols, axis=0).T)


def unshard_cols(cfg, shards):
    T = cfg.T
    CC = shards[0].shape[0]
    out = np.empty((B, T, CC), dtype=shards[0].dtype)
    for core in range(NCORES):
        s = shards[core]
        i = 0
        for b in range(B):
            for blkid in (core, 15 - core):
                out[b, blkid * cfg.L:(blkid + 1) * cfg.L, :] = s[:, i * cfg.L:(i + 1) * cfg.L].T
                i += 1
    return out


def halo_of(cfg, core, arr_btC):
    CC = arr_btC.shape[2]
    h = np.zeros((CC, cfg.SEGS), dtype=np.float32)
    m = np.zeros((1, cfg.SEGS), dtype=np.float32)
    for i, (b, t0) in enumerate(seg_starts_global(cfg, core)):
        if t0 > 0:
            h[:, i] = arr_btC[b, t0 - 1, :]
            m[0, i] = 1.0
    return h, m


def chunked(vec):
    return np.ascontiguousarray(np.asarray(vec, np.float32).reshape(NCH, 128).T)


def fm_dram_ap(t, X):
    return t.ap().rearrange("(a p) x -> p a x", p=128)


def col_tiles(n, w=512):
    return [(i, min(w, n - i)) for i in range(0, n, w)]



def emit_ln(em, st, src_ap, dst_ap, ncols, mean=True, gamma=None, beta=None,
            src_dep=None, dst_war=None):
    pb, rows, sq, atile = st["pb"], st["rows"], st["sq"], st["atile"]
    ones_col, ones_row, dep_ones = st["ones_col"], st["ones_row"], st["dep_ones"]
    src_dep = _lst(src_dep)
    dst_war = _lst(dst_war)
    out_toks = []
    for (c0, cw) in col_tiles(ncols):
        sq_free = [st.get("sq0_free"), st.get("sq1_free")]
        tmm1 = tmm2 = None
        for ch in range(NCH):
            pp = ch % 2
            tsq = em.a(lambda e, ch=ch, c0=c0, cw=cw, pp=pp: e.activation(
                out=sq[pp][:, 0:cw], in_=src_ap(ch, c0, cw), func=AF.Square),
                deps=src_dep + [sq_free[pp]])
            if mean:
                tmm1 = em.t(lambda e, ch=ch, c0=c0, cw=cw: e.matmul(
                    out=pb["s1"][0:1, 0:cw], lhsT=ones_col[:, 0:1], rhs=src_ap(ch, c0, cw),
                    start=(ch == 0), stop=(ch == NCH - 1)),
                    deps=([dep_ones, st.get("pb_s1_free")] + src_dep) if ch == 0 else [])
            tmm2 = em.t(lambda e, cw=cw, pp=pp, ch=ch: e.matmul(
                out=pb["s2"][0:1, 0:cw], lhsT=ones_col[:, 0:1], rhs=sq[pp][:, 0:cw],
                start=(ch == 0), stop=(ch == NCH - 1)),
                deps=[dep_ones, tsq, st.get("pb_s2_free")] if ch == 0 else [tsq])
            sq_free[pp] = tmm2
        st["sq0_free"], st["sq1_free"] = sq_free
        r_row = rows[0:1, 0:cw]
        mr_row = rows[0:1, 512:512 + cw]
        tmp_row = rows[0:1, 1024:1024 + cw]
        t2_row = rows[0:1, 1536:1536 + cw]
        rows_free = st.get("rows_free")
        tk1 = em.v(lambda e, cw=cw, tmp_row=tmp_row: e.tensor_scalar_mul(
            out=tmp_row, in0=pb["s2"][0:1, 0:cw], scalar1=1.0 / C),
            deps=[tmm2, rows_free])
        tkm = None
        if mean:
            tkm = em.v(lambda e, cw=cw, mr_row=mr_row: e.tensor_scalar_mul(
                out=mr_row, in0=pb["s1"][0:1, 0:cw], scalar1=1.0 / C),
                deps=[tmm1, rows_free])
            tkn = em.v(lambda e, cw=cw, t2_row=t2_row, mr_row=mr_row: e.scalar_tensor_tensor(
                out=t2_row, in0=mr_row, scalar=-1.0, in1=mr_row,
                op0=ALU.mult, op1=ALU.mult), deps=[tkm, rows_free])
            tk1 = em.v(lambda e, cw=cw, tmp_row=tmp_row, t2_row=t2_row: e.tensor_add(
                out=tmp_row, in0=tmp_row, in1=t2_row), deps=[tk1, tkn])
        tk2 = em.a(lambda e, cw=cw, tmp_row=tmp_row: e.activation(
            out=tmp_row, in_=tmp_row, func=AF.Sqrt, bias=st["eps_col"][0:1, 0:1]),
            deps=[tk1, st["dep_eps"]])
        tk3 = em.v(lambda e, cw=cw, r_row=r_row, tmp_row=tmp_row: e.reciprocal(out=r_row, in_=tmp_row), deps=[tk2])
        st["pb_s2_free"] = tk1
        tk4 = None
        if mean:
            st["pb_s1_free"] = tkm
            tk4 = em.v(lambda e, cw=cw, mr_row=mr_row, r_row=r_row: e.tensor_mul(
                out=mr_row, in0=mr_row, in1=r_row), deps=[tk3])
        tb1 = em.t(lambda e, cw=cw, r_row=r_row: e.matmul(
            out=pb["b1"][:, 0:cw], lhsT=ones_row[0:1, :], rhs=r_row,
            start=True, stop=True), deps=[tk3, dep_ones, st.get("pb_b1_free")])
        tb2 = None
        if mean:
            tb2 = em.t(lambda e, cw=cw, mr_row=mr_row: e.matmul(
                out=pb["b2"][:, 0:cw], lhsT=ones_row[0:1, :], rhs=mr_row,
                start=True, stop=True), deps=[tk4, dep_ones, st.get("pb_b2_free")])
        st["rows_free"] = tb2 if mean else tb1
        last_ap = []
        for ch in range(NCH):
            pp = ch % 2
            at_free = st.get(f"atile{pp}_free")
            cur = em.v(lambda e, ch=ch, c0=c0, cw=cw, pp=pp: e.tensor_mul(
                out=atile[pp][:, 0:cw], in0=src_ap(ch, c0, cw), in1=pb["b1"][:, 0:cw]),
                deps=[tb1, at_free] + src_dep)
            if mean and gamma is not None:
                cur = em.v(lambda e, cw=cw, pp=pp: e.tensor_sub(
                    out=atile[pp][:, 0:cw], in0=atile[pp][:, 0:cw], in1=pb["b2"][:, 0:cw]),
                    deps=[tb2, cur])
                cur = em.v(lambda e, ch=ch, c0=c0, cw=cw, pp=pp: e.tensor_scalar(
                    out=dst_ap(ch, c0, cw), in0=atile[pp][:, 0:cw],
                    scalar1=gamma[0][:, ch:ch + 1], scalar2=beta[0][:, ch:ch + 1],
                    op0=ALU.mult, op1=ALU.add), deps=[cur, gamma[1]] + dst_war)
            elif mean:
                cur = em.v(lambda e, ch=ch, c0=c0, cw=cw, pp=pp: e.tensor_sub(
                    out=dst_ap(ch, c0, cw), in0=atile[pp][:, 0:cw], in1=pb["b2"][:, 0:cw]),
                    deps=[tb2, cur] + dst_war)
            elif gamma is not None:
                cur = em.v(lambda e, ch=ch, c0=c0, cw=cw, pp=pp: e.tensor_scalar(
                    out=dst_ap(ch, c0, cw), in0=atile[pp][:, 0:cw],
                    scalar1=gamma[0][:, ch:ch + 1], scalar2=beta[0][:, ch:ch + 1],
                    op0=ALU.mult, op1=ALU.add), deps=[cur, gamma[1]] + dst_war)
            else:
                cur = em.v(lambda e, ch=ch, c0=c0, cw=cw, pp=pp: e.tensor_copy(
                    out=dst_ap(ch, c0, cw), in_=atile[pp][:, 0:cw]),
                    deps=[cur] + dst_war)
            st[f"atile{pp}_free"] = cur
            last_ap.append(cur)
        st["pb_b1_free"] = last_ap[-1]
        if mean:
            st["pb_b2_free"] = last_ap[-1]
        out_toks.extend(last_ap)
    return out_toks


def make_state(em, nc, tag="", rows=None):
    pb = {k: nc.alloc_psum_tensor(f"pb_{k}{tag}", [128, 512], F32)[:]
          for k in ("s1", "s2", "b1", "b2", "m0", "m1", "m2", "m3")}
    if rows is None:
        rows = nc.alloc_sbuf_tensor(f"rows{tag}", [1, 2048], F32)[:]
    sq = [nc.alloc_sbuf_tensor(f"sq{i}{tag}", [128, 512], F32)[:] for i in range(2)]
    atile = [nc.alloc_sbuf_tensor(f"atile{i}{tag}", [128, 512], F32)[:] for i in range(2)]
    ones_col = nc.alloc_sbuf_tensor(f"ones_col{tag}", [128, 1], F32)[:]
    ones_row = nc.alloc_sbuf_tensor(f"ones_row{tag}", [1, 128], F32)[:]
    eps_col = nc.alloc_sbuf_tensor(f"eps_col{tag}", [128, 1], F32)[:]
    t1 = em.v(lambda e: e.memset(ones_col, 1.0))
    t2 = em.v(lambda e: e.memset(ones_row, 1.0))
    t3 = em.v(lambda e: e.memset(eps_col, float(EPS)))
    return dict(pb=pb, rows=rows, sq=sq, atile=atile, ones_col=ones_col,
                ones_row=ones_row, dep_ones=t2, eps_col=eps_col, dep_eps=t3)



def build_launch1(cfg, projs=(("kwT", "tmk"), ("vwT", "tmv"), ("rwT", "tmr"))):
    nc = bass.Bass("TRN2", target_bir_lowering=False, debug=False)
    em = Em(nc)
    CA, COLS, SEGS, L = cfg.CA, cfg.COLS, cfg.SEGS, cfg.L

    xin = nc.dram_tensor("xin", [C, CA], F32, kind="ExternalInput")
    maskin = nc.dram_tensor("maskin", [1, SEGS], F32, kind="ExternalInput").ap()
    prm_names = ("ln1w", "ln1b", "tmk", "tmv", "tmr")
    prm = {nm: nc.dram_tensor(nm, [128, NCH], F32, kind="ExternalInput").ap()
           for nm in prm_names}
    wts = {nm: nc.dram_tensor(nm, [C, C], F32, kind="ExternalInput")
           for nm in ("kwT", "vwT", "rwT")}
    xn_o = nc.dram_tensor("xn_o", [C, COLS], F32, kind="ExternalOutput")
    k_o = nc.dram_tensor("k_o", [C, COLS], F32, kind="ExternalOutput")
    v_o = nc.dram_tensor("v_o", [C, COLS], F32, kind="ExternalOutput")
    sr_o = nc.dram_tensor("sr_o", [C, COLS], F32, kind="ExternalOutput")
    proj_out = {"kwT": k_o, "vwT": v_o, "rwT": sr_o}

    bufA = nc.alloc_sbuf_tensor("bufA", [128, NCH * CA], F32)[:]
    bufB = nc.alloc_sbuf_tensor("bufB", [128, NCH * CA], F32)[:]
    mixb = nc.alloc_sbuf_tensor("mixb", [128, NCH * COLS], BF16)[:]
    un0 = nc.alloc_sbuf_tensor("un0", [128, max(COLS, 2048)], F32)
    t1b = un0[:]
    wbuf = nc.alloc_sbuf_tensor("wbuf", [128, NCH * C], BF16)[:]
    hm = nc.alloc_sbuf_tensor("hm", [128, NCH * SEGS], F32)[:]
    maskB = nc.alloc_sbuf_tensor("maskB", [128, SEGS], F32)[:]
    mrow = nc.alloc_sbuf_tensor("mrow", [1, SEGS], F32)[:]
    tm1 = {nm: nc.alloc_sbuf_tensor(f"tm1_{nm}", [128, NCH], F32)[:]
           for nm in ("tmk", "tmv", "tmr")}
    prms = {nm: nc.alloc_sbuf_tensor("s_" + nm, [128, NCH], F32)[:] for nm in prm_names}
    evac = [nc.alloc_sbuf_tensor(f"evac{i}", [128, 512], F32)[:] for i in range(4)]

    st = make_state(em, nc, rows=un0[:][0:1, 0:2048])
    pb = st["pb"]

    tx = em.dsync(bufA.rearrange("p (a x) -> p a x", a=NCH), fm_dram_ap(xin, CA), sem="sqx")
    for nm in prm_names:
        em.dsync(prms[nm], prm[nm], sem="sqp")
    tp_all = em.dsync(mrow, maskin, sem="sqp")
    tprm = {nm: tp_all for nm in prm_names}
    tmask = tp_all
    tm1_toks = {}
    for nm in ("tmk", "tmv", "tmr"):
        tm1_toks[nm] = em.v(lambda e, nm=nm: e.tensor_scalar(
            out=tm1[nm], in0=prms[nm], scalar1=-1.0, scalar2=1.0,
            op0=ALU.mult, op1=ALU.add), deps=[tprm[nm]])
    tmb0 = em.t(lambda e: e.matmul(out=pb["b2"][:, 0:SEGS], lhsT=st["ones_row"][0:1, :],
                                   rhs=mrow, start=True, stop=True),
                deps=[tmask, st["dep_ones"]])
    tmb = em.v(lambda e: e.tensor_copy(out=maskB, in_=pb["b2"][:, 0:SEGS]), deps=[tmb0])
    st["pb_b2_free"] = tmb

    def a_ap(ch, c0, cw):
        return bufA[:, ch * CA + c0: ch * CA + c0 + cw]

    def b_ap(ch, c0, cw):
        return bufB[:, ch * CA + c0: ch * CA + c0 + cw]

    emit_ln(em, st, a_ap, b_ap, CA, mean=True, src_dep=[tx])
    bar_xn = em.barrier()
    t_xn_store = em.dsync(fm_dram_ap(xn_o, COLS),
                          bufB.rearrange("p (a x) -> p a x", a=NCH)[:, :, 0:COLS],
                          deps=bar_xn, sem="sqxn")
    emit_ln(em, st, b_ap, a_ap, CA, mean=False,
            gamma=(prms["ln1w"], tprm["ln1w"]), beta=(prms["ln1b"], tprm["ln1b"]),
            src_dep=bar_xn, dst_war=bar_xn)
    bar_h = em.barrier()
    hm_toks = []
    for ch in range(NCH):
        hm_toks.append(em.v(lambda e, ch=ch: e.tensor_mul(
            out=hm[:, ch * SEGS:(ch + 1) * SEGS],
            in0=bufA[:, ch * CA + COLS: ch * CA + CA], in1=maskB),
            deps=bar_h + [tmb]))
    bar_hm = em.barrier()

    prev_proj_done = None
    prev_mm_done = None
    store_toks = []
    evac_free = [None] * 4
    for pi, (nm, tmnm) in enumerate(projs):
        twl = em.dgps(wbuf.rearrange("p (a x) -> p a x", a=NCH),
                      fm_dram_ap(wts[nm], C), deps=[prev_proj_done], sem=f"gqw{pi % 2}")
        for ch in range(NCH):
            war = [prev_proj_done, prev_mm_done]
            tt1 = em.v(lambda e, ch=ch, tmnm=tmnm: e.tensor_scalar_mul(
                out=t1b[:, 0:COLS], in0=bufA[:, ch * CA: ch * CA + COLS],
                scalar1=prms[tmnm][:, ch:ch + 1]),
                deps=bar_hm + [st.get("t1b_free"), tprm[tmnm]])
            tbig = em.v(lambda e, ch=ch, tmnm=tmnm: e.scalar_tensor_tensor(
                out=mixb[:, ch * COLS + 1: ch * COLS + COLS],
                in0=bufA[:, ch * CA: ch * CA + COLS - 1],
                scalar=tm1[tmnm][:, ch:ch + 1], in1=t1b[:, 1:COLS],
                op0=ALU.mult, op1=ALU.add), deps=[tt1, tm1_toks[tmnm]] + war)
            tfix = em.v(lambda e, ch=ch, tmnm=tmnm: e.scalar_tensor_tensor(
                out=mixb[:, ch * COLS: (ch + 1) * COLS].rearrange(
                    "p (s l) -> p s l", s=SEGS)[:, :, 0:1],
                in0=hm[:, ch * SEGS:(ch + 1) * SEGS].rearrange("p (s o) -> p s o", o=1),
                scalar=tm1[tmnm][:, ch:ch + 1],
                in1=t1b[:, 0:COLS].rearrange("p (s l) -> p s l", s=SEGS)[:, :, 0:1],
                op0=ALU.mult, op1=ALU.add), deps=[tt1, tbig, hm_toks[ch], tm1_toks[tmnm]] + war)
            st["t1b_free"] = tfix
        bar_mix = em.barrier()
        nct = len(col_tiles(COLS))
        for d in range(NCH):
            for ci, (c0, cw) in enumerate(col_tiles(COLS)):
                bank = pb[f"m{ci % 4}"]
                tk = None
                for kc in range(NCH):
                    tk = em.t(lambda e, d=d, kc=kc, c0=c0, cw=cw, bank=bank: e.matmul(
                        out=bank[:, 0:cw],
                        lhsT=wbuf[:, kc * C + d * 128:kc * C + (d + 1) * 128],
                        rhs=mixb[:, kc * COLS + c0: kc * COLS + c0 + cw],
                        start=(kc == 0), stop=(kc == NCH - 1)),
                        deps=([twl, st.get(f"pbm{ci % 4}_free")] + bar_mix) if kc == 0 else [])
                ei = (d * nct + ci) % 4
                if nm == "rwT":
                    tev = em.a(lambda e, cw=cw, bank=bank, ei=ei: e.activation(
                        out=evac[ei][:, 0:cw], in_=bank[:, 0:cw], func=AF.Sigmoid),
                        deps=[tk, evac_free[ei]])
                else:
                    tev = em.v(lambda e, cw=cw, bank=bank, ei=ei: e.tensor_copy(
                        out=evac[ei][:, 0:cw], in_=bank[:, 0:cw]),
                        deps=[tk, evac_free[ei]])
                st[f"pbm{ci % 4}_free"] = tev
                tst = em.dsync(proj_out[nm].ap()[d * 128:(d + 1) * 128, c0:c0 + cw],
                               evac[ei][:, 0:cw], deps=[tev], sem=f"sst{ei}")
                evac_free[ei] = tst
                store_toks.append(tst)
        prev_proj_done = store_toks[-1]
        prev_mm_done = tk

    em.finalize([t_xn_store] + store_toks)
    return nc


def launch1_inmaps(cfg, x_btC, ln1w, ln1b, tmk, tmv, tmr, att_kw, att_vw, att_rw):
    kwT = np.ascontiguousarray(np.asarray(att_kw, np.float32).T)
    vwT = np.ascontiguousarray(np.asarray(att_vw, np.float32).T)
    rwT = np.ascontiguousarray(np.asarray(att_rw, np.float32).T)
    in_maps = []
    for core in range(NCORES):
        xs = shard_cols(cfg, core, x_btC).astype(np.float32)
        hx, hmask = halo_of(cfg, core, x_btC)
        xin = np.ascontiguousarray(np.concatenate([xs, hx], axis=1), dtype=np.float32)
        in_maps.append({
            "xin": xin, "maskin": hmask,
            "ln1w": chunked(ln1w), "ln1b": chunked(ln1b),
            "tmk": chunked(tmk), "tmv": chunked(tmv), "tmr": chunked(tmr),
            "kwT": kwT, "vwT": vwT, "rwT": rwT,
        })
    return in_maps






def build_launch2(cfg, L2=512):
    nc = bass.Bass("TRN2", target_bir_lowering=False, debug=False)
    em = Em(nc)
    T = cfg.T
    L2 = min(L2, T)
    NRT = 4
    NCK = T // L2

    kin = nc.dram_tensor("kin", [512, T], F32, kind="ExternalInput")
    vin = nc.dram_tensor("vin", [512, T], F32, kind="ExternalInput")
    dec_in = nc.dram_tensor("dec_in", [128, NRT], F32, kind="ExternalInput").ap()
    tf_in = nc.dram_tensor("tf_in", [128, NRT], F32, kind="ExternalInput").ap()
    wkv_o = nc.dram_tensor("wkv_o", [512, T], F32, kind="ExternalOutput")

    kb = nc.alloc_sbuf_tensor("kb", [128, T], F32)[:]
    vb = nc.alloc_sbuf_tensor("vb", [128, T], F32)[:]
    ob = nc.alloc_sbuf_tensor("ob", [128, T], F32)[:]
    ekb = nc.alloc_sbuf_tensor("ekb", [128, L2], F32)[:]
    u1b = nc.alloc_sbuf_tensor("u1b", [128, L2], F32)[:]
    sab = nc.alloc_sbuf_tensor("sab", [128, L2], F32)[:]
    sbb = nc.alloc_sbuf_tensor("sbb", [128, L2], F32)[:]
    d1b = nc.alloc_sbuf_tensor("d1b", [128, L2 + 1], F32)[:]
    dnb = nc.alloc_sbuf_tensor("dnb", [128, L2], F32)[:]
    nnb = nc.alloc_sbuf_tensor("nnb", [128, L2], F32)[:]
    gt = nc.alloc_sbuf_tensor("gt", [128, L2], F32)[:]
    wramp = nc.alloc_sbuf_tensor("wramp", [128, L2 + 1], F32)[:]
    iot = nc.alloc_sbuf_tensor("iot", [128, L2 + 1], F32)[:]
    dec = nc.alloc_sbuf_tensor("dec", [128, NRT], F32)[:]
    tf = nc.alloc_sbuf_tensor("tf", [128, NRT], F32)[:]
    ew = nc.alloc_sbuf_tensor("ew", [128, 1], F32)[:]
    wcol = nc.alloc_sbuf_tensor("wcol", [128, 1], F32)[:]
    gcol = nc.alloc_sbuf_tensor("gcol", [128, 1], F32)[:]
    relu_u = nc.alloc_sbuf_tensor("relu_u", [128, 1], F32)[:]
    expu = nc.alloc_sbuf_tensor("expu", [128, 1], F32)[:]
    mx = nc.alloc_sbuf_tensor("mx", [128, 1], F32)[:]
    Mb = nc.alloc_sbuf_tensor("Mb", [128, 1], F32)[:]
    negM = nc.alloc_sbuf_tensor("negM", [128, 1], F32)[:]
    PmM = nc.alloc_sbuf_tensor("PmM", [128, 1], F32)[:]
    Pbuf = [nc.alloc_sbuf_tensor(f"P{i}", [128, 1], F32)[:] for i in range(2)]
    Abuf = [nc.alloc_sbuf_tensor(f"A{i}", [128, 1], F32)[:] for i in range(2)]
    Bbuf = [nc.alloc_sbuf_tensor(f"Bc{i}", [128, 1], F32)[:] for i in range(2)]

    tdec = em.dsync(dec, dec_in, sem="sqp")
    tdec = em.dsync(tf, tf_in, sem="sqp")
    tio = em.p(lambda e: e.iota(out=iot, pattern=[[1, L2 + 1]], base=0,
                                channel_multiplier=0,
                                allow_small_or_imprecise_dtypes=True))
    store_toks = []
    bar_prev = None
    for rt in range(NRT):
        ldeps = [] if bar_prev is None else bar_prev
        tkld = em.dsync(kb, kin.ap()[rt * 128:(rt + 1) * 128, :], sem="sqk", deps=ldeps)
        tvld = em.dsync(vb, vin.ap()[rt * 128:(rt + 1) * 128, :], sem="sqv", deps=ldeps)
        t1 = em.a(lambda e, rt=rt: e.activation(out=ew, in_=dec[:, rt:rt + 1],
                                                func=AF.Exp), deps=[tdec] + ldeps)
        t2 = em.v(lambda e: e.tensor_scalar_mul(out=wcol, in0=ew, scalar1=-1.0),
                  deps=[t1] + ldeps)
        t3 = em.a(lambda e: e.activation(out=gcol, in_=wcol, func=AF.Exp), deps=[t2])
        t4 = em.v(lambda e, rt=rt: e.tensor_scalar_max(out=relu_u, in0=tf[:, rt:rt + 1],
                                                       scalar1=0.0), deps=[tdec] + ldeps)
        t5 = em.a(lambda e, rt=rt: e.activation(out=expu, in_=tf[:, rt:rt + 1],
                                                func=AF.Exp), deps=[tdec] + ldeps)
        t6 = em.v(lambda e: e.memset(gt, 1.0), deps=ldeps)
        t7 = em.v(lambda e: e.tensor_scalar_mul(out=gt, in0=gt, scalar1=gcol[:, 0:1]),
                  deps=[t6, t3])
        t8 = em.v(lambda e: e.tensor_scalar_mul(out=wramp, in0=iot, scalar1=wcol[:, 0:1]),
                  deps=[tio, t2] + ldeps)
        c1 = em.v(lambda e: e.memset(Pbuf[0], -3e38), deps=ldeps)
        c2 = em.v(lambda e: e.memset(Abuf[0], 0.0), deps=ldeps)
        c3 = em.v(lambda e: e.memset(Bbuf[0], 0.0), deps=ldeps)
        bar = em.barrier()
        for ci in range(NCK):
            pc, qc = ci % 2, (ci + 1) % 2
            c0 = ci * L2
            P, A, Bc = Pbuf[pc], Abuf[pc], Bbuf[pc]
            P2, A2, B2 = Pbuf[qc], Abuf[qc], Bbuf[qc]
            k_ap = kb[:, c0:c0 + L2]
            v_ap = vb[:, c0:c0 + L2]
            tmx = em.v(lambda e, k_ap=k_ap: e.reduce_max(out=mx, in_=k_ap, axis=AX.X),
                       deps=bar)
            tM0 = em.v(lambda e: e.tensor_add(out=Mb, in0=mx, in1=relu_u), deps=[tmx] + bar)
            tM = em.v(lambda e, P=P: e.tensor_max(out=Mb, in0=Mb, in1=P), deps=[tM0] + bar)
            tnM = em.v(lambda e: e.tensor_scalar_mul(out=negM, in0=Mb, scalar1=-1.0),
                       deps=[tM] + bar)
            tek = em.a(lambda e, k_ap=k_ap: e.activation(out=ekb, in_=k_ap, func=AF.Exp,
                                                         bias=negM[:, 0:1]),
                       deps=[tnM] + bar)
            tu1 = em.v(lambda e, v_ap=v_ap: e.tensor_mul(out=u1b, in0=ekb, in1=v_ap),
                       deps=[tek] + bar)
            tsa = em.v(lambda e: e.tensor_tensor_scan(out=sab, data0=gt, data1=u1b,
                                                      initial=0.0, op0=ALU.mult,
                                                      op1=ALU.add), deps=[tu1] + bar)
            tsb = em.v(lambda e: e.tensor_tensor_scan(out=sbb, data0=gt, data1=ekb,
                                                      initial=0.0, op0=ALU.mult,
                                                      op1=ALU.add), deps=[tek] + bar)
            tpm = em.v(lambda e, P=P: e.tensor_sub(out=PmM, in0=P, in1=Mb),
                       deps=[tM] + bar)
            td1 = em.a(lambda e: e.activation(out=d1b, in_=wramp, func=AF.Exp,
                                              bias=PmM[:, 0:1]), deps=[tpm] + bar)
            tn1 = em.v(lambda e, A=A: e.tensor_scalar_mul(out=nnb, in0=d1b[:, 0:L2],
                                                          scalar1=A[:, 0:1]),
                       deps=[td1] + bar)
            tn2 = em.v(lambda e: e.tensor_add(out=nnb[:, 1:L2], in0=nnb[:, 1:L2],
                                              in1=sab[:, 0:L2 - 1]), deps=[tn1, tsa])
            tn3 = em.v(lambda e: e.scalar_tensor_tensor(out=nnb, in0=u1b,
                                                        scalar=expu[:, 0:1], in1=nnb,
                                                        op0=ALU.mult, op1=ALU.add),
                       deps=[tn2, tu1] + bar)
            td2 = em.v(lambda e, Bc=Bc: e.tensor_scalar_mul(out=dnb, in0=d1b[:, 0:L2],
                                                            scalar1=Bc[:, 0:1]),
                       deps=[td1, tn1] + bar)
            td3 = em.v(lambda e: e.tensor_add(out=dnb[:, 1:L2], in0=dnb[:, 1:L2],
                                              in1=sbb[:, 0:L2 - 1]), deps=[td2, tsb])
            td4 = em.v(lambda e: e.scalar_tensor_tensor(out=dnb, in0=ekb,
                                                        scalar=expu[:, 0:1], in1=dnb,
                                                        op0=ALU.mult, op1=ALU.add),
                       deps=[td3, tek] + bar)
            tl1 = em.a(lambda e: e.activation(out=dnb, in_=dnb, func=AF.Ln), deps=[td4])
            tl2 = em.a(lambda e: e.activation(out=dnb, in_=dnb, func=AF.Exp, scale=-1.0),
                       deps=[tl1])
            to = em.v(lambda e, c0=c0: e.tensor_mul(out=ob[:, c0:c0 + L2], in0=nnb,
                                                    in1=dnb), deps=[tl2, tn3] + bar)
            tca = em.v(lambda e, A=A, A2=A2: e.scalar_tensor_tensor(
                out=A2, in0=d1b[:, L2:L2 + 1], scalar=A[:, 0:1],
                in1=sab[:, L2 - 1:L2], op0=ALU.mult, op1=ALU.add),
                deps=[td1, tsa] + bar)
            tcb = em.v(lambda e, Bc=Bc, B2=B2: e.scalar_tensor_tensor(
                out=B2, in0=d1b[:, L2:L2 + 1], scalar=Bc[:, 0:1],
                in1=sbb[:, L2 - 1:L2], op0=ALU.mult, op1=ALU.add),
                deps=[td1, tsb] + bar)
            tcp = em.v(lambda e, P2=P2: e.tensor_copy(out=P2, in_=Mb), deps=[tM] + bar)
            bar = em.barrier()
        tst = em.dsync(wkv_o.ap()[rt * 128:(rt + 1) * 128, :], ob, sem="sqo", deps=bar)
        store_toks.append(tst)
        bar_prev = bar + [tst]
    em.finalize(store_toks)
    return nc


def launch2_inmaps(cfg, k_btC, v_btC, time_decay, time_first):
    T = cfg.T
    kr = np.ascontiguousarray(np.asarray(k_btC, np.float32).transpose(0, 2, 1).reshape(B * C, T))
    vr = np.ascontiguousarray(np.asarray(v_btC, np.float32).transpose(0, 2, 1).reshape(B * C, T))
    decf = np.tile(np.asarray(time_decay, np.float32), B)
    tff = np.tile(np.asarray(time_first, np.float32), B)
    in_maps = []
    for core in range(NCORES):
        rs = slice(core * 512, (core + 1) * 512)
        in_maps.append({
            "kin": kr[rs], "vin": vr[rs],
            "dec_in": np.ascontiguousarray(decf[rs].reshape(4, 128).T),
            "tf_in": np.ascontiguousarray(tff[rs].reshape(4, 128).T),
        })
    return in_maps


def launch2_unshard(cfg, outs):
    T = cfg.T
    full = np.concatenate(outs, axis=0)
    return full.reshape(B, C, T).transpose(0, 2, 1)



def emit_mix(em, st, srcbuf, src_stride, dstbuf, dst_stride, tm_sb, tm1_sb, hmbuf,
             ncols, SEGS, L, deps, t1b):
    toks = []
    for ch in range(NCH):
        tt1 = em.v(lambda e, ch=ch: e.tensor_scalar_mul(
            out=t1b[:, 0:ncols], in0=srcbuf[:, ch * src_stride: ch * src_stride + ncols],
            scalar1=tm_sb[:, ch:ch + 1]),
            deps=deps + [st.get("t1b_free")])
        tbig = em.v(lambda e, ch=ch: e.scalar_tensor_tensor(
            out=dstbuf[:, ch * dst_stride + 1: ch * dst_stride + ncols],
            in0=srcbuf[:, ch * src_stride: ch * src_stride + ncols - 1],
            scalar=tm1_sb[:, ch:ch + 1], in1=t1b[:, 1:ncols],
            op0=ALU.mult, op1=ALU.add), deps=[tt1] + deps)
        tfix = em.v(lambda e, ch=ch: e.scalar_tensor_tensor(
            out=dstbuf[:, ch * dst_stride: ch * dst_stride + ncols].rearrange(
                "p (s l) -> p s l", s=SEGS)[:, :, 0:1],
            in0=hmbuf[:, ch * SEGS:(ch + 1) * SEGS].rearrange("p (s o) -> p s o", o=1),
            scalar=tm1_sb[:, ch:ch + 1],
            in1=t1b[:, 0:ncols].rearrange("p (s l) -> p s l", s=SEGS)[:, :, 0:1],
            op0=ALU.mult, op1=ALU.add), deps=[tt1, tbig] + deps)
        st["t1b_free"] = tfix
        toks += [tbig, tfix]
    return toks



def build_launch3(cfg):
    nc = bass.Bass("TRN2", target_bir_lowering=False, debug=False)
    em = Em(nc)
    COLS = cfg.COLS

    wkv_in = nc.dram_tensor("wkv_in", [C, COLS], F32, kind="ExternalInput")
    sr_in = nc.dram_tensor("sr_in", [C, COLS], F32, kind="ExternalInput")
    xn_in = nc.dram_tensor("xn_in", [C, COLS], F32, kind="ExternalInput")
    owT = nc.dram_tensor("owT", [C, C], F32, kind="ExternalInput")
    x1_o = nc.dram_tensor("x1_o", [C, COLS], F32, kind="ExternalOutput")

    xnb = nc.alloc_sbuf_tensor("xnb", [128, NCH * COLS], F32)[:]
    rwkvb = nc.alloc_sbuf_tensor("rwkvb", [128, NCH * COLS], BF16)[:]
    wkvt = nc.alloc_sbuf_tensor("wkvt", [128, COLS], F32)[:]
    srt = nc.alloc_sbuf_tensor("srt", [128, COLS], F32)[:]
    wbuf = nc.alloc_sbuf_tensor("wbuf", [128, NCH * C], BF16)[:]
    evac = [nc.alloc_sbuf_tensor(f"evac{i}", [128, 512], F32)[:] for i in range(4)]
    pb = {f"m{i}": nc.alloc_psum_tensor(f"pbm{i}", [128, 512], F32)[:] for i in range(4)}

    txn = em.dsync(xnb.rearrange("p (a x) -> p a x", a=NCH), fm_dram_ap(xn_in, COLS),
                   sem="sqxn")
    twl = em.dgps(wbuf.rearrange("p (a x) -> p a x", a=NCH), fm_dram_ap(owT, C))
    tmul_prev = None
    mul_toks = []
    for ch in range(NCH):
        t1 = em.dsync(wkvt, wkv_in.ap()[ch * 128:(ch + 1) * 128, :], sem="sqw",
                      deps=[tmul_prev])
        t2 = em.dsync(srt, sr_in.ap()[ch * 128:(ch + 1) * 128, :], sem="sqs",
                      deps=[tmul_prev])
        tmul_prev = em.v(lambda e, ch=ch: e.tensor_mul(
            out=rwkvb[:, ch * COLS:(ch + 1) * COLS], in0=srt, in1=wkvt),
            deps=[t1, t2])
        mul_toks.append(tmul_prev)
    nct = len(col_tiles(COLS))
    store_toks = []
    evac_free = [None] * 4
    pbm_free = [None] * 4
    for cc in range(NCH):
        for ci, (c0, cw) in enumerate(col_tiles(COLS)):
            bank = pb[f"m{ci % 4}"]
            tk = None
            for dc in range(NCH):
                tk = em.t(lambda e, cc=cc, dc=dc, c0=c0, cw=cw, bank=bank: e.matmul(
                    out=bank[:, 0:cw],
                    lhsT=wbuf[:, dc * C + cc * 128:dc * C + (cc + 1) * 128],
                    rhs=rwkvb[:, dc * COLS + c0: dc * COLS + c0 + cw],
                    start=(dc == 0), stop=(dc == NCH - 1)),
                    deps=([twl, pbm_free[ci % 4]] + mul_toks) if dc == 0 else [])
            ei = (cc * nct + ci) % 4
            tev = em.v(lambda e, cc=cc, c0=c0, cw=cw, bank=bank, ei=ei: e.tensor_add(
                out=evac[ei][:, 0:cw], in0=bank[:, 0:cw],
                in1=xnb[:, cc * COLS + c0: cc * COLS + c0 + cw]),
                deps=[tk, txn, evac_free[ei]])
            pbm_free[ci % 4] = tev
            tst = em.dsync(x1_o.ap()[cc * 128:(cc + 1) * 128, c0:c0 + cw],
                           evac[ei][:, 0:cw], deps=[tev], sem=f"sst{ei}")
            evac_free[ei] = tst
            store_toks.append(tst)
    em.finalize(store_toks)
    return nc



def build_launch4a(cfg):
    nc = bass.Bass("TRN2", target_bir_lowering=False, debug=False)
    em = Em(nc)
    CA, COLS, SEGS, L = cfg.CA, cfg.COLS, cfg.SEGS, cfg.L

    x1in = nc.dram_tensor("x1in", [C, CA], F32, kind="ExternalInput")
    maskin = nc.dram_tensor("maskin", [1, SEGS], F32, kind="ExternalInput").ap()
    prm_names = ("ln2w", "ln2b", "ftmk", "ftmr")
    prm = {nm: nc.dram_tensor(nm, [128, NCH], F32, kind="ExternalInput").ap()
           for nm in prm_names}
    xk2_o = nc.dram_tensor("xk2_o", [C, COLS], BF16, kind="ExternalOutput")
    xr2_o = nc.dram_tensor("xr2_o", [C, COLS], BF16, kind="ExternalOutput")

    bufA = nc.alloc_sbuf_tensor("bufA", [128, NCH * CA], F32)[:]
    bufB = nc.alloc_sbuf_tensor("bufB", [128, NCH * CA], F32)[:]
    mixb = nc.alloc_sbuf_tensor("mixb", [128, NCH * COLS], BF16)[:]
    un0 = nc.alloc_sbuf_tensor("un0", [128, max(COLS, 2048)], F32)
    t1b = un0[:]
    hm = nc.alloc_sbuf_tensor("hm", [128, NCH * SEGS], F32)[:]
    maskB = nc.alloc_sbuf_tensor("maskB", [128, SEGS], F32)[:]
    mrow = nc.alloc_sbuf_tensor("mrow", [1, SEGS], F32)[:]
    tm1 = {nm: nc.alloc_sbuf_tensor(f"tm1_{nm}", [128, NCH], F32)[:]
           for nm in ("ftmk", "ftmr")}
    prms = {nm: nc.alloc_sbuf_tensor("s_" + nm, [128, NCH], F32)[:] for nm in prm_names}

    st = make_state(em, nc, rows=un0[:][0:1, 0:2048])
    pb = st["pb"]

    tx = em.dsync(bufA.rearrange("p (a x) -> p a x", a=NCH), fm_dram_ap(x1in, CA), sem="sqx")
    for nm in prm_names:
        em.dsync(prms[nm], prm[nm], sem="sqp")
    tp_all = em.dsync(mrow, maskin, sem="sqp")
    tm1_toks = {}
    for nm in ("ftmk", "ftmr"):
        tm1_toks[nm] = em.v(lambda e, nm=nm: e.tensor_scalar(
            out=tm1[nm], in0=prms[nm], scalar1=-1.0, scalar2=1.0,
            op0=ALU.mult, op1=ALU.add), deps=[tp_all])
    tmb0 = em.t(lambda e: e.matmul(out=pb["b2"][:, 0:SEGS], lhsT=st["ones_row"][0:1, :],
                                   rhs=mrow, start=True, stop=True),
                deps=[tp_all, st["dep_ones"]])
    tmb = em.v(lambda e: e.tensor_copy(out=maskB, in_=pb["b2"][:, 0:SEGS]), deps=[tmb0])
    st["pb_b2_free"] = tmb

    def a_ap(ch, c0, cw):
        return bufA[:, ch * CA + c0: ch * CA + c0 + cw]

    def b_ap(ch, c0, cw):
        return bufB[:, ch * CA + c0: ch * CA + c0 + cw]

    emit_ln(em, st, a_ap, b_ap, CA, mean=True,
            gamma=(prms["ln2w"], tp_all), beta=(prms["ln2b"], tp_all), src_dep=[tx])
    bar_h = em.barrier()
    hm_toks = []
    for ch in range(NCH):
        hm_toks.append(em.v(lambda e, ch=ch: e.tensor_mul(
            out=hm[:, ch * SEGS:(ch + 1) * SEGS],
            in0=bufB[:, ch * CA + COLS: ch * CA + CA], in1=maskB),
            deps=bar_h + [tmb]))
    bar_hm = em.barrier()
    store_toks = []
    for nm, out_t in (("ftmk", xk2_o), ("ftmr", xr2_o)):
        toks = emit_mix(em, st, bufB, CA, mixb, COLS, prms[nm], tm1[nm], hm,
                        COLS, SEGS, L, bar_hm + hm_toks + [tm1_toks[nm], tp_all]
                        + ([store_toks[-1]] if store_toks else []), t1b)
        tst = em.dsync(fm_dram_ap(out_t, COLS),
                       mixb.rearrange("p (a x) -> p a x", a=NCH),
                       deps=toks, sem="sqst" + nm)
        store_toks.append(tst)
    em.finalize(store_toks)
    return nc



def build_launch4b(cfg):
    nc = bass.Bass("TRN2", target_bir_lowering=False, debug=False)
    em = Em(nc)
    COLS = cfg.COLS
    FF = 4096
    NFC = FF // 128
    NFG = 4

    xk2_in = nc.dram_tensor("xk2_in", [C, COLS], BF16, kind="ExternalInput")
    xr2_in = nc.dram_tensor("xr2_in", [C, COLS], BF16, kind="ExternalInput")
    x1_in = nc.dram_tensor("x1_in", [C, COLS], F32, kind="ExternalInput")
    fkwT = nc.dram_tensor("fkwT", [C, FF], F32, kind="ExternalInput")
    fvwT = nc.dram_tensor("fvwT", [FF, C], F32, kind="ExternalInput")
    frwT = nc.dram_tensor("frwT", [C, C], F32, kind="ExternalInput")
    ao_o = nc.dram_tensor("ao_o", [C, COLS], F32, kind="ExternalOutput")

    xk2b = nc.alloc_sbuf_tensor("xk2b", [128, NCH * COLS], BF16)[:]
    xr2b = nc.alloc_sbuf_tensor("xr2b", [128, NCH * COLS], BF16)[:]
    kkb = nc.alloc_sbuf_tensor("kkb", [128, NFC * 512], BF16)[:]
    wst = nc.alloc_sbuf_tensor("wst", [128, 8192], BF16)[:]
    wst2 = nc.alloc_sbuf_tensor("wst2", [128, 8192], BF16)[:]
    x1ct = nc.alloc_sbuf_tensor("x1ct", [128, NCH * 512], F32)[:]
    sr2t = nc.alloc_sbuf_tensor("sr2t", [128, NCH * 512], F32)[:]
    at = [nc.alloc_sbuf_tensor(f"at{i}", [128, 512], F32)[:] for i in range(2)]
    evst = [nc.alloc_sbuf_tensor(f"evst{i}", [128, 512], F32)[:] for i in range(2)]
    pb = [nc.alloc_psum_tensor(f"pbk{i}", [128, 512], F32)[:] for i in range(8)]

    tk2 = em.dsync(xk2b.rearrange("p (a x) -> p a x", a=NCH), fm_dram_ap(xk2_in, COLS),
                   sem="sqk")
    tr2 = em.dsync(xr2b.rearrange("p (a x) -> p a x", a=NCH), fm_dram_ap(xr2_in, COLS),
                   sem="sqr")
    fkw3 = fkwT.ap().rearrange("(a p) f -> p a f", p=128)
    fvw3 = fvwT.ap().rearrange("(a p) c -> p a c", p=128)
    frw3 = frwT.ap().rearrange("(a p) c -> p a c", p=128)

    store_toks = []
    pfree = [None] * 8
    wst_free = None
    wst2_free = None
    x1_free = None
    sr2_free = [None] * NCH
    kk_free = [None] * NFC
    ev_free = [None] * 2
    at_free = [None] * 2
    for ci, (c0, cw) in enumerate(col_tiles(COLS)):
        tx1 = em.dsync(x1ct.rearrange("p (a x) -> p a x", a=NCH)[:, :, 0:cw],
                       fm_dram_ap(x1_in, COLS)[:, :, c0:c0 + cw], sem="sqx1",
                       deps=[x1_free])
        twr = em.dgps(wst.rearrange("p (a x) -> p a x", a=NCH), frw3, sem="gqw0",
                      deps=[wst_free])
        for cc in range(NCH):
            bi = cc % 4
            bank = pb[bi]
            tk = None
            for kc in range(NCH):
                tk = em.t(lambda e, cc=cc, kc=kc, c0=c0, cw=cw, bank=bank: e.matmul(
                    out=bank[:, 0:cw],
                    lhsT=wst[:, kc * 1024 + cc * 128:kc * 1024 + (cc + 1) * 128],
                    rhs=xr2b[:, kc * COLS + c0: kc * COLS + c0 + cw],
                    start=(kc == 0), stop=(kc == NCH - 1)),
                    deps=[twr, tr2, pfree[bi]] if kc == 0 else [])
            tev = em.a(lambda e, cc=cc, cw=cw, bank=bank: e.activation(
                out=sr2t[:, cc * 512:cc * 512 + cw], in_=bank[:, 0:cw],
                func=AF.Sigmoid), deps=[tk, sr2_free[cc]])
            pfree[bi] = tev
            sr2_free[cc] = tev
            wst_free = tk
        for fg in range(NFG):
            twk = em.dgps(wst.rearrange("p (a x) -> p a x", a=NCH),
                          fkw3[:, :, fg * 1024:(fg + 1) * 1024], sem="gqw1",
                          deps=[wst_free])
            for fi in range(NCH):
                fci = fg * 8 + fi
                bi = fi % 4
                bank = pb[bi]
                tk = None
                for kc in range(NCH):
                    tk = em.t(lambda e, fi=fi, kc=kc, c0=c0, cw=cw, bank=bank: e.matmul(
                        out=bank[:, 0:cw],
                        lhsT=wst[:, kc * 1024 + fi * 128:kc * 1024 + (fi + 1) * 128],
                        rhs=xk2b[:, kc * COLS + c0: kc * COLS + c0 + cw],
                        start=(kc == 0), stop=(kc == NCH - 1)),
                        deps=[twk, tk2, pfree[bi]] if kc == 0 else [])
                pp = fci % 2
                trl = em.v(lambda e, cw=cw, bank=bank, pp=pp: e.tensor_scalar_max(
                    out=at[pp][:, 0:cw], in0=bank[:, 0:cw], scalar1=0.0),
                    deps=[tk, at_free[pp]])
                tsqr = em.a(lambda e, fci=fci, cw=cw, pp=pp: e.activation(
                    out=kkb[:, fci * 512:fci * 512 + cw], in_=at[pp][:, 0:cw],
                    func=AF.Square), deps=[trl, kk_free[fci]])
                pfree[bi] = trl
                at_free[pp] = tsqr
                kk_free[fci] = tsqr
                wst_free = tk
        bar_kk = em.barrier()
        tk = None
        for fg in range(NFG):
            twv = em.dgps(wst2.rearrange("p (a x) -> p a x", a=NCH),
                          fvw3[:, fg * 8:(fg + 1) * 8, :], sem="gqw2",
                          deps=[wst2_free])
            for fi in range(NCH):
                fci = fg * 8 + fi
                for cc in range(NCH):
                    deps = []
                    if fci == 0:
                        deps = [twv, pfree[cc]] + bar_kk
                    elif fi == 0 and cc == 0:
                        deps = [twv]
                    tk = em.t(lambda e, fi=fi, cc=cc, fci=fci, c0=c0, cw=cw: e.matmul(
                        out=pb[cc][:, 0:cw],
                        lhsT=wst2[:, fi * 1024 + cc * 128:fi * 1024 + (cc + 1) * 128],
                        rhs=kkb[:, fci * 512:fci * 512 + cw],
                        start=(fci == 0), stop=(fci == NFC - 1)),
                        deps=deps)
            wst2_free = tk
        bar_kv = em.barrier()
        for cc in range(NCH):
            pp = cc % 2
            t1 = em.v(lambda e, cc=cc, cw=cw, pp=pp: e.tensor_mul(
                out=at[pp][:, 0:cw], in0=sr2t[:, cc * 512:cc * 512 + cw],
                in1=pb[cc][:, 0:cw]), deps=bar_kv + [at_free[pp]])
            t2 = em.v(lambda e, cc=cc, cw=cw, pp=pp: e.tensor_add(
                out=evst[pp][:, 0:cw], in0=at[pp][:, 0:cw],
                in1=x1ct[:, cc * 512:cc * 512 + cw]), deps=[t1, tx1, ev_free[pp]])
            at_free[pp] = t2
            pfree[cc] = t1
            sr2_free[cc] = t1
            tst = em.dsync(ao_o.ap()[cc * 128:(cc + 1) * 128, c0:c0 + cw],
                           evst[pp][:, 0:cw], deps=[t2], sem=f"sstao{pp}")
            ev_free[pp] = tst
            store_toks.append(tst)
        x1_free = store_toks[-1]
    em.finalize(store_toks)
    return nc



def emit_round_exact(em, tile, w, y_scr, b_scr, deps):
    t0 = em.v(lambda e: e.tensor_scalar_add(out=y_scr[:, 0:w], in0=tile[:, 0:w],
                                            scalar1=256.5), deps=deps)
    tr = em.v(lambda e: e.tensor_copy(out=tile[:, 0:w], in_=y_scr[:, 0:w]), deps=[t0])
    for k in (256.0, 128.0, 64.0, 32.0, 16.0, 8.0, 4.0, 2.0, 1.0):
        tb = em.v(lambda e, k=k: e.tensor_single_scalar(
            out=b_scr[:, 0:w], in_=tile[:, 0:w], scalar=k, op=ALU.is_ge), deps=[tr])
        tr = em.v(lambda e, k=k: e.scalar_tensor_tensor(
            out=tile[:, 0:w], in0=b_scr[:, 0:w], scalar=-k, in1=tile[:, 0:w],
            op0=ALU.mult, op1=ALU.add), deps=[tb])
    t1 = em.v(lambda e: e.scalar_tensor_tensor(
        out=tile[:, 0:w], in0=tile[:, 0:w], scalar=-1.0, in1=y_scr[:, 0:w],
        op0=ALU.mult, op1=ALU.add), deps=[tr])
    t2 = em.v(lambda e: e.tensor_scalar_add(out=tile[:, 0:w], in0=tile[:, 0:w],
                                            scalar1=-256.0), deps=[t1])
    return t2



def build_launch5(cfg, exact_round=True, debug=False, cut=None, skip_pre=False):
    nc = bass.Bass("TRN2", target_bir_lowering=False, debug=False)
    em = Em(nc)
    COLS = cfg.COLS
    FF = 4096
    NFC = FF // 128
    NFG = 4

    xn_in = nc.dram_tensor("xn_in", [C, COLS], F32, kind="ExternalInput")
    x_in = nc.dram_tensor("x_in", [C, COLS], F32, kind="ExternalInput")
    ao_in = nc.dram_tensor("ao_in", [C, COLS], F32, kind="ExternalInput")
    wgT = nc.dram_tensor("wgT", [C, FF], F32, kind="ExternalInput")
    wuT = nc.dram_tensor("wuT", [C, FF], F32, kind="ExternalInput")
    wdT = nc.dram_tensor("wdT", [FF, C], F32, kind="ExternalInput")
    xl0_o = nc.dram_tensor("xl0_o", [C, COLS], F32, kind="ExternalOutput")
    if debug:
        aq1_d = nc.dram_tensor("aq1_d", [C, 512], BF16, kind="ExternalOutput")
        gu_d = nc.dram_tensor("gu_d", [4096, 512], BF16, kind="ExternalOutput")
        aq2_d = nc.dram_tensor("aq2_d", [4096, 512], BF16, kind="ExternalOutput")
        rows_d = nc.dram_tensor("rows_d", [1, 2048], F32, kind="ExternalOutput")
        tern_d = nc.dram_tensor("tern_d", [128, 8192], BF16, kind="ExternalOutput")

    xnct = nc.alloc_sbuf_tensor("xnct", [128, NCH * 512], F32)[:]
    xct = nc.alloc_sbuf_tensor("xct", [128, NCH * 512], F32)[:]
    aoct = nc.alloc_sbuf_tensor("aoct", [128, NCH * 512], F32)[:]
    aq1b = nc.alloc_sbuf_tensor("aq1b", [128, NCH * 512], BF16)[:]
    gub = nc.alloc_sbuf_tensor("gub", [128, NFC * 512], BF16)[:]
    aq2b = nc.alloc_sbuf_tensor("aq2b", [128, NFC * 512], BF16)[:]
    tgb = nc.alloc_sbuf_tensor("tgb", [128, 8192], BF16)[:]
    tub = nc.alloc_sbuf_tensor("tub", [128, 8192], BF16)[:]
    tern_scr = {"g": nc.dram_tensor("tern_g", [C, 4096], BF16, kind="Internal"),
                "u": nc.dram_tensor("tern_u", [C, 4096], BF16, kind="Internal"),
                "d": nc.dram_tensor("tern_d_scr", [4096, C], BF16, kind="Internal")}
    t3scr = {k: tern_scr[k].ap().rearrange("(a p) f -> p a f", p=128) for k in tern_scr}
    gsb = nc.alloc_sbuf_tensor("gsb", [128, 512], F32)[:]
    usb = nc.alloc_sbuf_tensor("usb", [128, 512], F32)[:]
    mxb = nc.alloc_sbuf_tensor("mxb", [128, 512], F32)[:]
    mcols = nc.alloc_sbuf_tensor("mcols", [128, 4], F32)[:]
    at = [nc.alloc_sbuf_tensor(f"at{i}", [128, 512], F32)[:] for i in range(2)]
    ysc = nc.alloc_sbuf_tensor("ysc", [128, 512], F32)[:]
    bsc = nc.alloc_sbuf_tensor("bsc", [128, 512], F32)[:]
    evst = [nc.alloc_sbuf_tensor(f"evst{i}", [128, 512], F32)[:] for i in range(2)]
    rows = nc.alloc_sbuf_tensor("rows", [1, 2048], F32)[:]
    acc = nc.alloc_sbuf_tensor("acc", [128, 48], F32)[:]
    ones_col = nc.alloc_sbuf_tensor("ones_col", [128, 1], F32)[:]
    ones_row = nc.alloc_sbuf_tensor("ones_row", [1, 128], F32)[:]
    sc1 = nc.alloc_sbuf_tensor("sc1", [1, 8], F32)[:]
    thcol = {w: nc.alloc_sbuf_tensor(f"thcol_{w}", [128, 1], F32)[:] for w in "gud"}
    nthcol = {w: nc.alloc_sbuf_tensor(f"nthcol_{w}", [128, 1], F32)[:] for w in "gud"}
    ninv = {w: nc.alloc_sbuf_tensor(f"ninv_{w}", [128, 1], F32)[:] for w in "gud"}
    pb = [nc.alloc_psum_tensor(f"pbk{i}", [128, 512], F32)[:] for i in range(8)]
    stb = [nc.alloc_sbuf_tensor(f"stb{i}", [128, 2048], F32)[:] for i in range(2)]

    t_on1 = em.v(lambda e: e.memset(ones_col, 1.0))
    t_on2 = em.v(lambda e: e.memset(ones_row, 1.0))
    wmap = {"g": wgT, "u": wuT, "d": wdT}
    w3 = {"g": wgT.ap().rearrange("(a p) f -> p a f", p=128),
          "u": wuT.ap().rearrange("(a p) f -> p a f", p=128),
          "d": wdT.ap().rearrange("(a p) c -> p a c", p=128)}
    NPC = (NCH * 4096) // 2048

    def wpiece(view, wn, pc):
        if wn != "d":
            a = pc // 2
            fr = (pc % 2) * 2048
            return view[:, a:a + 1, fr:fr + 2048]
        return view[:, pc * 2:pc * 2 + 2, :]

    def spiece(flat_ap, wn):
        a = 1 if wn != "d" else 2
        return flat_ap.rearrange("p (a f) -> p a f", a=a)

    if skip_pre is True:
        tz = em.v(lambda e: e.memset(thcol["g"], 0.01))
        for wn in "ud":
            tz = em.v(lambda e, wn=wn: e.memset(thcol[wn], 0.01), deps=[tz])
        for wn in "gud":
            tz = em.v(lambda e, wn=wn: e.memset(nthcol[wn], -0.01), deps=[tz])
            tz = em.v(lambda e, wn=wn: e.memset(ninv[wn], -1.0), deps=[tz])
        bc_toks = {wn: [tz] for wn in "gud"}
        bar_tern = em.barrier()
    wst_free = [None, None]
    acc_toks = {}
    for wi, wn in enumerate("gud") if skip_pre is not True else []:
        piece_toks = []
        accp = [None] * 4
        for pc in range(NPC):
            pp = pc % 2
            tl = em.dsync(spiece(stb[pp], wn), wpiece(w3[wn], wn, pc),
                          sem=f"sqs{pp}", deps=[wst_free[pp]])
            ta = em.a(lambda e, wi=wi, pc=pc, pp=pp: e.activation(
                out=stb[pp], in_=stb[pp], func=AF.Abs,
                accum_out=acc[:, wi * 16 + pc:wi * 16 + pc + 1]),
                deps=[tl])
            wst_free[pp] = ta
            piece_toks.append(ta)
        acc_toks[wn] = piece_toks
    rows_toks = {}
    pbs_free = None
    for wi, wn in enumerate("gud") if skip_pre is not True else []:
        tm = em.t(lambda e, wi=wi: e.matmul(
            out=pb[6][0:1, 0:16], lhsT=ones_col[:, 0:1], rhs=acc[:, wi * 16:wi * 16 + 16],
            start=True, stop=True), deps=[t_on1, pbs_free] + acc_toks[wn])
        ts1 = em.v(lambda e, wi=wi: e.reduce_sum(
            out=rows[0:1, wi:wi + 1], in_=pb[6][0:1, 0:16], axis=AX.X), deps=[tm])
        pbs_free = ts1
        ts2 = em.v(lambda e, wi=wi: e.tensor_scalar(
            out=rows[0:1, 8 + wi:9 + wi], in0=rows[0:1, wi:wi + 1],
            scalar1=1.0 / (C * FF), scalar2=1e-5, op0=ALU.mult, op1=ALU.max),
            deps=[ts1])
        rows_toks[wn] = ts2
    bc_toks = {} if not skip_pre else bc_toks
    for wi, wn in enumerate("gud") if skip_pre is not True else []:
        t1 = em.v(lambda e, wi=wi: e.tensor_scalar_mul(
            out=rows[0:1, 16 + wi:17 + wi], in0=rows[0:1, 8 + wi:9 + wi], scalar1=0.5),
            deps=[rows_toks[wn]])
        t2 = em.t(lambda e, wi=wi: e.matmul(
            out=pb[6][:, 0:1], lhsT=ones_row[0:1, :], rhs=rows[0:1, 16 + wi:17 + wi],
            start=True, stop=True), deps=[t1, t_on2, pbs_free])
        t3 = em.v(lambda e, wn=wn: e.tensor_copy(out=thcol[wn], in_=pb[6][:, 0:1]),
                  deps=[t2])
        t4 = em.v(lambda e, wn=wn: e.tensor_scalar_mul(out=nthcol[wn], in0=thcol[wn],
                                                       scalar1=-1.0), deps=[t3])
        t5 = em.t(lambda e, wi=wi: e.matmul(
            out=pb[6][:, 2:3], lhsT=ones_row[0:1, :], rhs=rows[0:1, 8 + wi:9 + wi],
            start=True, stop=True), deps=[rows_toks[wn], t_on2, t3])
        t6 = em.v(lambda e, wn=wn: e.tensor_scalar_mul(out=ninv[wn], in0=pb[6][:, 2:3],
                                                       scalar1=-1.0), deps=[t5])
        pbs_free = t6
        bc_toks[wn] = [t3, t4, t6]
    bar_w = em.barrier()
    tern_store_toks = []
    tsc_free = [None, None]
    for wn in ("gud" if not skip_pre else ""):
        for pc in range(NPC):
            pp = pc % 2
            tl = em.dsync(stb[pp].rearrange("p (a f) -> p a f", a=1), wpiece(w3[wn], wn, pc),
                          sem=f"sqs{pp}", deps=[wst_free[pp]] + (bar_w if pc < 2 else []))
            tslice = tgb[:, pp * 2048:(pp + 1) * 2048]
            tt1 = em.v(lambda e, wn=wn, pp=pp, tslice=tslice: e.tensor_single_scalar(
                out=tslice, in_=stb[pp], scalar=thcol[wn][:, 0:1],
                op=ALU.is_ge), deps=[tl, tsc_free[pp]] + bc_toks[wn])
            tt2 = em.v(lambda e, wn=wn, pp=pp, tslice=tslice: e.scalar_tensor_tensor(
                out=tslice, in0=stb[pp], scalar=nthcol[wn][:, 0:1],
                in1=tslice, op0=ALU.is_le, op1=ALU.subtract), deps=[tt1])
            wst_free[pp] = tt2
            tsc = em.dsync(wpiece(t3scr[wn], wn, pc), spiece(tslice, wn),
                           sem=f"sqscw{pp}", deps=[tt2])
            tsc_free[pp] = tsc
            tern_store_toks.append(tsc)
    bar_tern = em.barrier() if skip_pre is not True else bar_tern
    if cut == "tern":
        em.finalize(bar_tern)
        return nc

    idf = nc.alloc_sbuf_tensor("idf", [128, 128], F32)[:]
    ti0 = em.v(lambda e: e.memset(idf, 1.0))
    ti1 = em.p(lambda e: e.affine_select(out=idf, in_=idf, pattern=[[-1, 128]],
                                         compare_op=ALU.is_equal, fill=0.0,
                                         base=0, channel_multiplier=1), deps=[ti0])
    if cut == "idf":
        em.finalize(em.barrier())
        return nc

    store_toks = []
    pfree = [None] * 8
    x_free = ao_free = xn_free = None
    aq1_free = None
    gu_free = [None] * NFC
    aq2_free = [None] * NFC
    at_free = [None, ti1]
    ev_free = [None] * 2
    tern_g_free = None
    tern_u_free = None
    mx_free = None
    mc_free = None
    ysc_free = bsc_free = None
    gs_free = us_free = None
    rows_free2 = None
    for ci, (c0, cw) in enumerate(col_tiles(COLS)):
        nct128 = (cw + 127) // 128
        txn = em.dsync(xnct.rearrange("p (a x) -> p a x", a=NCH)[:, :, 0:cw],
                       fm_dram_ap(xn_in, COLS)[:, :, c0:c0 + cw], sem="sqxn",
                       deps=[xn_free])
        tx = em.dsync(xct.rearrange("p (a x) -> p a x", a=NCH)[:, :, 0:cw],
                      fm_dram_ap(x_in, COLS)[:, :, c0:c0 + cw], sem="sqx",
                      deps=[x_free])
        tao = em.dsync(aoct.rearrange("p (a x) -> p a x", a=NCH)[:, :, 0:cw],
                       fm_dram_ap(ao_in, COLS)[:, :, c0:c0 + cw], sem="sqao",
                       deps=[ao_free])
        if cut == "loads":
            em.finalize(em.barrier())
            return nc
        tmx = em.a(lambda e, cw=cw: e.activation(
            out=mxb[:, 0:cw], in_=xnct[:, 0:cw], func=AF.Abs), deps=[txn, mx_free])
        for ch in range(1, NCH):
            tab = em.a(lambda e, ch=ch, cw=cw: e.activation(
                out=at[1][:, 0:cw], in_=xnct[:, ch * 512:ch * 512 + cw], func=AF.Abs),
                deps=[txn, at_free[1]])
            tmx = em.v(lambda e, cw=cw: e.tensor_max(
                out=mxb[:, 0:cw], in0=mxb[:, 0:cw], in1=at[1][:, 0:cw]),
                deps=[tmx, tab])
            at_free[1] = tmx
        if cut == "rowmax":
            em.finalize(em.barrier())
            return nc
        ttr = None
        for tt_i in range(nct128):
            tw = min(128, cw - tt_i * 128)
            t1 = em.t(lambda e, tt_i=tt_i, tw=tw: e.transpose(
                out=pb[7][0:tw, 0:128], in_=mxb[:, tt_i * 128:tt_i * 128 + tw],
                identity=idf), deps=[tmx, ti1, mc_free, ttr])
            ttr = em.v(lambda e, tt_i=tt_i, tw=tw: e.reduce_max(
                out=mcols[0:tw, tt_i:tt_i + 1], in_=pb[7][0:tw, 0:128], axis=AX.X),
                deps=[t1, ttr])
        if cut == "transp":
            em.finalize(em.barrier())
            return nc
        if nct128 == 1:
            trow = em.dsync(rows[0:1, 1024:1024 + cw], mcols[0:cw, 0:1],
                            sem="sqrow", deps=[ttr, rows_free2])
        else:
            for tt_i in range(nct128):
                trow = em.dsync(rows[0:1, 1024 + tt_i * 128:1024 + (tt_i + 1) * 128],
                                mcols[:, tt_i:tt_i + 1], sem="sqrow",
                                deps=[ttr, rows_free2] if tt_i == 0 else [])
        tr1 = em.v(lambda e, cw=cw: e.tensor_scalar_max(
            out=rows[0:1, 1024:1024 + cw], in0=rows[0:1, 1024:1024 + cw], scalar1=1e-5),
            deps=[trow])
        tr2 = em.v(lambda e, cw=cw: e.reciprocal(out=rows[0:1, 0:cw],
                                                 in_=rows[0:1, 1024:1024 + cw]),
                   deps=[tr1])
        tr3 = em.v(lambda e, cw=cw: e.tensor_scalar_mul(
            out=rows[0:1, 0:cw], in0=rows[0:1, 0:cw], scalar1=127.0), deps=[tr2])
        tr4 = em.v(lambda e, cw=cw: e.tensor_scalar_mul(
            out=rows[0:1, 512:512 + cw], in0=rows[0:1, 1024:1024 + cw],
            scalar1=1.0 / 127.0), deps=[tr1, tr2])
        if cut == "rows":
            em.finalize(em.barrier())
            return nc
        tb1 = em.t(lambda e, cw=cw: e.matmul(out=pb[6][:, 0:cw], lhsT=ones_row[0:1, :],
                                             rhs=rows[0:1, 0:cw], start=True, stop=True),
                   deps=[tr3, t_on2, pbs_free, pfree[6]])
        tb2 = em.t(lambda e, cw=cw: e.matmul(out=pb[7][:, 0:cw], lhsT=ones_row[0:1, :],
                                             rhs=rows[0:1, 512:512 + cw],
                                             start=True, stop=True),
                   deps=[tr4, t_on2, ttr, pfree[7]])
        rows_free2 = tb2
        pbs_free = tb2
        if cut == "bcast":
            em.finalize(em.barrier())
            return nc
        taq_last = None
        for ch in range(NCH):
            tq1 = em.v(lambda e, ch=ch, cw=cw: e.tensor_mul(
                out=at[0][:, 0:cw], in0=xnct[:, ch * 512:ch * 512 + cw],
                in1=pb[6][:, 0:cw]), deps=[tb1, txn, at_free[0]])
            trd = emit_round_exact(em, at[0], cw, ysc, bsc, [tq1, ysc_free, bsc_free])
            taq = em.v(lambda e, ch=ch, cw=cw: e.tensor_mul(
                out=aq1b[:, ch * 512:ch * 512 + cw], in0=at[0][:, 0:cw],
                in1=pb[7][:, 0:cw]), deps=[trd, tb2, aq1_free])
            at_free[0] = taq
            ysc_free = bsc_free = trd
            taq_last = taq
        aq1_free = taq_last
        pfree[6] = taq_last
        pfree[7] = taq_last
        bar_aq1 = em.barrier()
        if cut == "aq1":
            em.finalize(bar_aq1)
            return nc
        for fg in range(NFG):
            ttg2 = em.dgps(tgb.rearrange("p (a x) -> p a x", a=NCH),
                           t3scr["g"][:, :, fg * 1024:(fg + 1) * 1024], sem="gqg",
                           deps=[tern_g_free] + (bar_aq1 + bar_tern if fg == 0 else []))
            ttu2 = em.dgps(tub.rearrange("p (a x) -> p a x", a=NCH),
                           t3scr["u"][:, :, fg * 1024:(fg + 1) * 1024], sem="gqu",
                           deps=[tern_u_free] + (bar_aq1 + bar_tern if fg == 0 else []))
            for fi in range(NCH):
                fci = fg * 8 + fi
                bg, bu = pb[fi % 3], pb[3 + fi % 3]
                tkg = tku = None
                for kc in range(NCH):
                    tkg = em.t(lambda e, fi=fi, kc=kc, cw=cw, bg=bg: e.matmul(
                        out=bg[:, 0:cw],
                        lhsT=tgb[:, kc * 1024 + fi * 128:kc * 1024 + (fi + 1) * 128],
                        rhs=aq1b[:, kc * 512:kc * 512 + cw],
                        start=(kc == 0), stop=(kc == NCH - 1)),
                        deps=[ttg2, pfree[fi % 3]] + bar_aq1 if kc == 0 else [])
                for kc in range(NCH):
                    tku = em.t(lambda e, fi=fi, kc=kc, cw=cw, bu=bu: e.matmul(
                        out=bu[:, 0:cw],
                        lhsT=tub[:, kc * 1024 + fi * 128:kc * 1024 + (fi + 1) * 128],
                        rhs=aq1b[:, kc * 512:kc * 512 + cw],
                        start=(kc == 0), stop=(kc == NCH - 1)),
                        deps=[ttu2, pfree[3 + fi % 3]] + bar_aq1 if kc == 0 else [])
                tg = em.a(lambda e, cw=cw, bg=bg: e.activation(
                    out=gsb[:, 0:cw], in_=bg[:, 0:cw], func=AF.Silu,
                    scale=ninv["g"][:, 0:1]), deps=[tkg, gs_free] + bc_toks["g"])
                tu = em.a(lambda e, cw=cw, bu=bu: e.activation(
                    out=usb[:, 0:cw], in_=bu[:, 0:cw], func=AF.Copy,
                    scale=ninv["u"][:, 0:1]), deps=[tku, us_free] + bc_toks["u"])
                tgu = em.v(lambda e, fci=fci, cw=cw: e.tensor_mul(
                    out=gub[:, fci * 512:fci * 512 + cw], in0=gsb[:, 0:cw],
                    in1=usb[:, 0:cw]), deps=[tg, tu, gu_free[fci]])
                pfree[fi % 3] = tg
                pfree[3 + fi % 3] = tu
                gs_free = tgu
                us_free = tgu
                gu_free[fci] = tgu
            tern_g_free = tkg
            tern_u_free = tku
        bar_gu = em.barrier()
        if cut == "gu":
            em.finalize(bar_gu)
            return nc
        tmx2 = em.a(lambda e, cw=cw: e.activation(
            out=mxb[:, 0:cw], in_=gub[:, 0:cw], func=AF.Abs), deps=bar_gu + [tmx])
        for fci in range(1, NFC):
            tab = em.a(lambda e, fci=fci, cw=cw: e.activation(
                out=at[1][:, 0:cw], in_=gub[:, fci * 512:fci * 512 + cw], func=AF.Abs),
                deps=bar_gu + [at_free[1]])
            tmx2 = em.v(lambda e, cw=cw: e.tensor_max(
                out=mxb[:, 0:cw], in0=mxb[:, 0:cw], in1=at[1][:, 0:cw]),
                deps=[tmx2, tab])
            at_free[1] = tmx2
        ttr2 = None
        for tt_i in range(nct128):
            tw = min(128, cw - tt_i * 128)
            t1 = em.t(lambda e, tt_i=tt_i, tw=tw: e.transpose(
                out=pb[7][0:tw, 0:128], in_=mxb[:, tt_i * 128:tt_i * 128 + tw],
                identity=idf), deps=[tmx2, ti1, pfree[7], ttr2])
            ttr2 = em.v(lambda e, tt_i=tt_i, tw=tw: e.reduce_max(
                out=mcols[0:tw, tt_i:tt_i + 1], in_=pb[7][0:tw, 0:128], axis=AX.X),
                deps=[t1, ttr2 or trow])
            pfree[7] = ttr2
        if nct128 == 1:
            trow2 = em.dsync(rows[0:1, 1024:1024 + cw], mcols[0:cw, 0:1],
                             sem="sqrow", deps=[ttr2, tb2])
        else:
            for tt_i in range(nct128):
                trow2 = em.dsync(rows[0:1, 1024 + tt_i * 128:1024 + (tt_i + 1) * 128],
                                 mcols[:, tt_i:tt_i + 1], sem="sqrow",
                                 deps=[ttr2, tb2] if tt_i == 0 else [])
        tr1 = em.v(lambda e, cw=cw: e.tensor_scalar_max(
            out=rows[0:1, 1024:1024 + cw], in0=rows[0:1, 1024:1024 + cw], scalar1=1e-5),
            deps=[trow2])
        tr2b = em.v(lambda e, cw=cw: e.reciprocal(out=rows[0:1, 0:cw],
                                                  in_=rows[0:1, 1024:1024 + cw]),
                    deps=[tr1])
        tr3b = em.v(lambda e, cw=cw: e.tensor_scalar_mul(
            out=rows[0:1, 0:cw], in0=rows[0:1, 0:cw], scalar1=127.0), deps=[tr2b])
        tr4b = em.v(lambda e, cw=cw: e.tensor_scalar_mul(
            out=rows[0:1, 512:512 + cw], in0=rows[0:1, 1024:1024 + cw],
            scalar1=1.0 / 127.0), deps=[tr1, tr2b])
        tb1b = em.t(lambda e, cw=cw: e.matmul(out=pb[6][:, 0:cw], lhsT=ones_row[0:1, :],
                                              rhs=rows[0:1, 0:cw], start=True, stop=True),
                    deps=[tr3b, t_on2, pfree[6]])
        tb2b = em.t(lambda e, cw=cw: e.matmul(out=pb[7][:, 0:cw], lhsT=ones_row[0:1, :],
                                              rhs=rows[0:1, 512:512 + cw],
                                              start=True, stop=True),
                    deps=[tr4b, t_on2, pfree[7]])
        taq2_last = None
        for fci in range(NFC):
            tq1 = em.v(lambda e, fci=fci, cw=cw: e.tensor_mul(
                out=at[0][:, 0:cw], in0=gub[:, fci * 512:fci * 512 + cw],
                in1=pb[6][:, 0:cw]), deps=[tb1b, at_free[0]] + bar_gu)
            trd = emit_round_exact(em, at[0], cw, ysc, bsc, [tq1, ysc_free, bsc_free])
            taq = em.v(lambda e, fci=fci, cw=cw: e.tensor_mul(
                out=aq2b[:, fci * 512:fci * 512 + cw], in0=at[0][:, 0:cw],
                in1=pb[7][:, 0:cw]), deps=[trd, tb2b, aq2_free[fci]])
            at_free[0] = taq
            ysc_free = bsc_free = trd
            aq2_free[fci] = taq
            taq2_last = taq
        pfree[6] = taq2_last
        pfree[7] = taq2_last
        bar_aq2 = em.barrier()
        if cut == "aq2":
            em.finalize(bar_aq2)
            return nc
        if debug and ci == 0:
            store_toks.append(em.dsync(fm_dram_ap(aq1_d, 512)[:, :, 0:cw],
                                       aq1b.rearrange("p (a x) -> p a x", a=NCH)[:, :, 0:cw],
                                       deps=bar_aq2, sem="sqdbg"))
            store_toks.append(em.dsync(gu_d.ap().rearrange("(a p) x -> p a x", p=128)[:, :, 0:cw],
                                       gub.rearrange("p (a x) -> p a x", a=NFC)[:, :, 0:cw],
                                       deps=bar_aq2, sem="sqdbg2"))
            store_toks.append(em.dsync(aq2_d.ap().rearrange("(a p) x -> p a x", p=128)[:, :, 0:cw],
                                       aq2b.rearrange("p (a x) -> p a x", a=NFC)[:, :, 0:cw],
                                       deps=bar_aq2, sem="sqdbg3"))
            store_toks.append(em.dsync(tern_d.ap(), tub, deps=bar_aq2, sem="sqdbg5"))
        tdl = None
        for fg in range(NFG):
            ttd2 = em.dgps(tgb.rearrange("p (a x) -> p a x", a=NCH),
                           t3scr["d"][:, fg * 8:(fg + 1) * 8, :], sem="gqd",
                           deps=[tern_g_free] + (bar_aq2 if fg == 0 else [])) 
            for fi in range(NCH):
                fci = fg * 8 + fi
                for cc in range(NCH):
                    deps = []
                    if fci == 0:
                        deps = [ttd2, pfree[cc]] + bar_aq2
                    elif fi == 0 and cc == 0:
                        deps = [ttd2]
                    tdl = em.t(lambda e, fi=fi, cc=cc, fci=fci, cw=cw: e.matmul(
                        out=pb[cc][:, 0:cw],
                        lhsT=tgb[:, fi * 1024 + cc * 128:fi * 1024 + (cc + 1) * 128],
                        rhs=aq2b[:, fci * 512:fci * 512 + cw],
                        start=(fci == 0), stop=(fci == NFC - 1)),
                        deps=deps)
            tern_g_free = tdl
        bar_dn = em.barrier()
        for cc in range(NCH):
            pp = cc % 2
            t1 = em.a(lambda e, cc=cc, cw=cw, pp=pp: e.activation(
                out=at[pp][:, 0:cw], in_=pb[cc][:, 0:cw], func=AF.Copy,
                scale=ninv["d"][:, 0:1]), deps=bar_dn + [at_free[pp]] + bc_toks["d"])
            t2 = em.v(lambda e, cc=cc, cw=cw, pp=pp: e.tensor_add(
                out=at[pp][:, 0:cw], in0=at[pp][:, 0:cw],
                in1=xct[:, cc * 512:cc * 512 + cw]), deps=[t1, tx])
            t3 = em.v(lambda e, cc=cc, cw=cw, pp=pp: e.tensor_add(
                out=evst[pp][:, 0:cw], in0=at[pp][:, 0:cw],
                in1=aoct[:, cc * 512:cc * 512 + cw]), deps=[t2, tao, ev_free[pp]])
            at_free[pp] = t3
            pfree[cc] = t3
            tst = em.dsync(xl0_o.ap()[cc * 128:(cc + 1) * 128, c0:c0 + cw],
                           evst[pp][:, 0:cw], deps=[t3], sem=f"sstx{pp}")
            ev_free[pp] = tst
            store_toks.append(tst)
        x_free = store_toks[-1]
        ao_free = store_toks[-1]
        xn_free = taq_last
        mx_free = tmx2
        mc_free = ttr2
    em.finalize(store_toks)
    return nc



def build_launch6(cfg):
    nc = bass.Bass("TRN2", target_bir_lowering=False, debug=False)
    em = Em(nc)
    COLS = cfg.COLS
    NL = 4

    xl0_in = nc.dram_tensor("xl0_in", [C, COLS], F32, kind="ExternalInput")
    wcd_in = nc.dram_tensor("wcd_in", [C, 512], F32, kind="ExternalInput")
    wqu_in = nc.dram_tensor("wqu_in", [512, C], F32, kind="ExternalInput")
    wkv_in = nc.dram_tensor("wkv_in", [512, 128], F32, kind="ExternalInput")
    xn1_o = nc.dram_tensor("xn1_o", [C, COLS], BF16, kind="ExternalOutput")
    q_o = nc.dram_tensor("q_o", [C, COLS], BF16, kind="ExternalOutput")
    k_o = nc.dram_tensor("k_o", [64, COLS], BF16, kind="ExternalOutput")
    vt_o = nc.dram_tensor("vt_o", [COLS, 64], BF16, kind="ExternalOutput")

    xb = nc.alloc_sbuf_tensor("xb", [128, NCH * COLS], F32)[:]
    xn1b = nc.alloc_sbuf_tensor("xn1b", [128, NCH * COLS], BF16)[:]
    latb = nc.alloc_sbuf_tensor("latb", [128, NL * COLS], BF16)[:]
    wcd = nc.alloc_sbuf_tensor("wcd", [128, NCH * 512], BF16)[:]
    wqu = nc.alloc_sbuf_tensor("wqu", [128, NL * C], BF16)[:]
    wkv = nc.alloc_sbuf_tensor("wkv", [128, NL * 128], BF16)[:]
    un0 = nc.alloc_sbuf_tensor("un0", [128, 2048], F32)
    evac = [nc.alloc_sbuf_tensor(f"evac{i}", [128, 512], F32)[:] for i in range(4)]
    evacB = [nc.alloc_sbuf_tensor(f"evacB{i}", [128, 512], BF16)[:] for i in range(4)]
    st = make_state(em, nc, rows=un0[:][0:1, 0:2048])
    pb = st["pb"]

    tx = em.dsync(xb.rearrange("p (a x) -> p a x", a=NCH), fm_dram_ap(xl0_in, COLS),
                  sem="sqx")
    twc = em.dgps(wcd.rearrange("p (a x) -> p a x", a=NCH),
                  wcd_in.ap().rearrange("(a p) x -> p a x", p=128), sem="gqw")
    twq = em.dgps(wqu.rearrange("p (a x) -> p a x", a=NL),
                  wqu_in.ap().rearrange("(a p) x -> p a x", p=128), sem="gqw2")
    twk = em.dgps(wkv.rearrange("p (a x) -> p a x", a=NL),
                  wkv_in.ap().rearrange("(a p) x -> p a x", p=128), sem="gqw3")

    def x_ap(ch, c0, cw):
        return xb[:, ch * COLS + c0: ch * COLS + c0 + cw]

    def n_ap(ch, c0, cw):
        return xn1b[:, ch * COLS + c0: ch * COLS + c0 + cw]

    emit_ln(em, st, x_ap, n_ap, COLS, mean=True, src_dep=[tx])
    bar_ln = em.barrier()
    t_st1 = em.dsync(fm_dram_ap(xn1_o, COLS), xn1b.rearrange("p (a x) -> p a x", a=NCH),
                     deps=bar_ln, sem="sqst1")
    store_toks = [t_st1]
    nct = len(col_tiles(COLS))
    pfree = {}
    evac_free = [None] * 4
    lat_toks = []
    for lc in range(NL):
        for ci, (c0, cw) in enumerate(col_tiles(COLS)):
            bank = pb[f"m{ci % 4}"]
            tk = None
            for kc in range(NCH):
                tk = em.t(lambda e, lc=lc, kc=kc, c0=c0, cw=cw, bank=bank: e.matmul(
                    out=bank[:, 0:cw],
                    lhsT=wcd[:, kc * 512 + lc * 128:kc * 512 + (lc + 1) * 128],
                    rhs=xn1b[:, kc * COLS + c0: kc * COLS + c0 + cw],
                    start=(kc == 0), stop=(kc == NCH - 1)),
                    deps=([twc, pfree.get(ci % 4)] + bar_ln) if kc == 0 else [])
            tev = em.v(lambda e, lc=lc, c0=c0, cw=cw, bank=bank: e.tensor_copy(
                out=latb[:, lc * COLS + c0: lc * COLS + c0 + cw], in_=bank[:, 0:cw]),
                deps=[tk])
            pfree[ci % 4] = tev
            lat_toks.append(tev)
    bar_lat = em.barrier()
    for qc in range(NCH):
        for ci, (c0, cw) in enumerate(col_tiles(COLS)):
            bank = pb[f"m{ci % 4}"]
            tk = None
            for lc in range(NL):
                tk = em.t(lambda e, qc=qc, lc=lc, c0=c0, cw=cw, bank=bank: e.matmul(
                    out=bank[:, 0:cw],
                    lhsT=wqu[:, lc * C + qc * 128:lc * C + (qc + 1) * 128],
                    rhs=latb[:, lc * COLS + c0: lc * COLS + c0 + cw],
                    start=(lc == 0), stop=(lc == NL - 1)),
                    deps=([twq, pfree.get(ci % 4)] + bar_lat) if lc == 0 else [])
            ei = (qc * nct + ci) % 4
            tev = em.a(lambda e, cw=cw, bank=bank, ei=ei: e.activation(
                out=evac[ei][:, 0:cw].bitcast(BF16)[:, 0:cw], in_=bank[:, 0:cw],
                func=AF.Copy, scale=0.125), deps=[tk, evac_free[ei]])
            pfree[ci % 4] = tev
            tst = em.dsync(q_o.ap()[qc * 128:(qc + 1) * 128, c0:c0 + cw],
                           evac[ei][:, 0:cw].bitcast(BF16)[:, 0:cw],
                           deps=[tev], sem=f"sst{ei}")
            evac_free[ei] = tst
            store_toks.append(tst)
    for ci, (c0, cw) in enumerate(col_tiles(COLS)):
        bank = pb[f"m{ci % 4}"]
        tk = None
        for lc in range(NL):
            tk = em.t(lambda e, lc=lc, c0=c0, cw=cw, bank=bank: e.matmul(
                out=bank[:, 0:cw],
                lhsT=wkv[:, lc * 128:(lc + 1) * 128],
                rhs=latb[:, lc * COLS + c0: lc * COLS + c0 + cw],
                start=(lc == 0), stop=(lc == NL - 1)),
                deps=([twk, pfree.get(ci % 4)] + bar_lat) if lc == 0 else [])
        ei = ci % 4
        tev = em.v(lambda e, cw=cw, bank=bank, ei=ei: e.tensor_copy(
            out=evacB[ei][:, 0:cw], in_=bank[:, 0:cw]),
            deps=[tk, evac_free[ei]])
        pfree[ci % 4] = tev
        tst = em.dsync(k_o.ap()[0:64, c0:c0 + cw],
                       evacB[ei][0:64, 0:cw],
                       deps=[tev], sem=f"sst{ei}")
        evac_free[ei] = tst
        store_toks.append(tst)
    TW = min(128, COLS)
    ncc = COLS // TW
    for ti in range(ncc):
        bank = pb[f"m{ti % 4}"]
        tk = None
        for lc in range(NL):
            tk = em.t(lambda e, lc=lc, ti=ti, bank=bank: e.matmul(
                out=bank[0:TW, 0:64],
                lhsT=latb[:, lc * COLS + ti * TW:lc * COLS + (ti + 1) * TW],
                rhs=wkv[:, lc * 128 + 64:lc * 128 + 128],
                start=(lc == 0), stop=(lc == NL - 1)),
                deps=([twk, pfree.get(ti % 4)] + bar_lat) if lc == 0 else [])
        ei = ti % 4
        tev = em.v(lambda e, ti=ti, bank=bank, ei=ei: e.tensor_copy(
            out=evacB[ei][0:TW, 0:64], in_=bank[0:TW, 0:64]),
            deps=[tk, evac_free[ei]])
        pfree[ti % 4] = tev
        tst = em.dsync(vt_o.ap()[ti * TW:(ti + 1) * TW, :],
                       evacB[ei][0:TW, 0:64],
                       deps=[tev], sem=f"sst{ei}")
        evac_free[ei] = tst
        store_toks.append(tst)
    em.finalize(store_toks)
    return nc



def build_launch7(cfg):
    nc = bass.Bass("TRN2", target_bir_lowering=False, debug=False)
    em = Em(nc)
    T = cfg.T
    BT = B * T
    KW = min(512, T)
    QW2 = min(512, T)
    SQ = QW2 // 128
    NQ5 = T // QW2

    qA_in = nc.dram_tensor("qA_in", [64, BT], BF16, kind="ExternalInput")
    qB_in = nc.dram_tensor("qB_in", [64, BT], BF16, kind="ExternalInput")
    kf_in = nc.dram_tensor("kf_in", [64, BT], BF16, kind="ExternalInput")
    vt_in = nc.dram_tensor("vt_in", [BT, 64], BF16, kind="ExternalInput")
    m1_in = nc.dram_tensor("m1_in", [128, 2 * KW], F32, kind="ExternalInput")
    m2_in = nc.dram_tensor("m2_in", [128, 128 * (SQ - 1) + QW2], F32, kind="ExternalInput")
    y_o = nc.dram_tensor("y_o", [128, BT], BF16, kind="ExternalOutput")

    qs = [nc.alloc_sbuf_tensor("qA", [64, BT], BF16)[:],
          nc.alloc_sbuf_tensor("qB", [64, BT], BF16)[:]]
    kb = nc.alloc_sbuf_tensor("kb", [64, BT], BF16)[:]
    vt_sb = nc.alloc_sbuf_tensor("vt_sb", [128, (BT // 128) * 64], BF16)[:]
    ybuf = [nc.alloc_sbuf_tensor(f"ybuf{i}", [64, BT], BF16)[:] for i in range(2)]
    m1b = nc.alloc_sbuf_tensor("m1b", [128, 2 * KW], F32)[:]
    m2b = nc.alloc_sbuf_tensor("m2b", [128, 128 * (SQ - 1) + QW2], F32)[:]
    pexp = [nc.alloc_sbuf_tensor(f"pexp{i}", [128, QW2], BF16)[:] for i in range(2)]
    mcol = nc.alloc_sbuf_tensor("mcol", [128, 1], F32)[:]
    mtmp = nc.alloc_sbuf_tensor("mtmp", [128, 1], F32)[:]
    negm = nc.alloc_sbuf_tensor("negm", [1, QW2], BF16)[:]
    recr = nc.alloc_sbuf_tensor("recr", [1, QW2], F32)[:]
    ysc = nc.alloc_sbuf_tensor("ysc", [64, QW2], F32)[:]
    idf = nc.alloc_sbuf_tensor("idf", [128, 128], F32)[:]
    ones_row = nc.alloc_sbuf_tensor("ones_row", [1, 128], BF16)[:]
    ones_row32 = nc.alloc_sbuf_tensor("ones_row32", [1, 128], F32)[:]
    ones_col = nc.alloc_sbuf_tensor("ones_col", [128, 1], BF16)[:]
    pb = [nc.alloc_psum_tensor(f"pb{i}", [128, 512], F32)[:] for i in range(8)]

    tqa_ld = em.dsync(qs[0], qA_in.ap(), sem="sqqa")
    tqb_ld = em.dsync(qs[1], qB_in.ap(), sem="sqqb")
    tk = em.dsync(kb, kf_in.ap(), sem="sqk")
    tv = em.dsync(vt_sb.rearrange("p (n d) -> p n d", d=64),
                  vt_in.ap().rearrange("(n p) d -> p n d", p=128), sem="sqv")
    tm1 = em.dsync(m1b, m1_in.ap(), sem="sqm")
    tm2 = em.dsync(m2b, m2_in.ap(), sem="sqm2")
    ti0 = em.v(lambda e: e.memset(idf, 1.0))
    ti1 = em.p(lambda e: e.affine_select(out=idf, in_=idf, pattern=[[-1, 128]],
                                         compare_op=ALU.is_equal, fill=0.0,
                                         base=0, channel_multiplier=1), deps=[ti0])
    tor = em.v(lambda e: e.memset(ones_row, 1.0))
    tor32 = em.v(lambda e: e.memset(ones_row32, 1.0))
    toc = em.v(lambda e: e.memset(ones_col, 1.0))
    base = [tqa_ld, tqb_ld, tk, tv, tm1, tm2, ti1, tor, tor32, toc]

    y_toks = []
    p0_free = [None, None]
    p15_free = [None, None]
    p2_free = None
    p3_free = None
    p4_free = None
    p6_free = None
    mc_free = None
    mt_free = None
    neg_free = None
    pex_free = [None, None]
    rec_free = None
    ysc_free = None
    for hl in range(2):
        qh = qs[hl]
        for b in range(B):
            kb0 = b * T
            for q5 in range(NQ5):
                qgb = q5 * QW2
                for sq in range(SQ):
                    qb128 = qgb + sq * 128
                    nkt = qb128 // KW + 1
                    tred = None
                    for kt in range(nkt):
                        kp = kt % 2
                        bank0 = pb[0] if kp == 0 else pb[7]
                        tkm = em.t(lambda e, qh=qh, kt=kt, qb128=qb128, kb0=kb0,
                                   bank0=bank0: e.matmul(
                            out=bank0[:, 0:KW],
                            lhsT=qh[:, kb0 + qb128: kb0 + qb128 + 128],
                            rhs=kb[:, kb0 + kt * KW: kb0 + (kt + 1) * KW],
                            start=True, stop=True),
                            deps=(base + [p0_free[kp]]))
                        if kt == nkt - 1:
                            off = KW - (qb128 - kt * KW)
                            tkm = em.v(lambda e, off=off, bank0=bank0: e.tensor_add(
                                out=bank0[:, 0:KW], in0=bank0[:, 0:KW],
                                in1=m1b[:, off:off + KW]), deps=[tkm, tm1])
                        if kt == 0:
                            tred = em.v(lambda e, bank0=bank0: e.tensor_reduce(
                                out=mcol, in_=bank0[:, 0:KW], axis=AX.X, op=ALU.max),
                                deps=[tkm, mc_free])
                        else:
                            tr2 = em.v(lambda e, bank0=bank0: e.tensor_reduce(
                                out=mtmp, in_=bank0[:, 0:KW], axis=AX.X, op=ALU.max),
                                deps=[tkm, mt_free])
                            tred = em.v(lambda e: e.tensor_max(
                                out=mcol, in0=mcol, in1=mtmp), deps=[tred, tr2])
                            mt_free = tred
                        p0_free[kp] = tred
                    ttr = em.t(lambda e: e.transpose(
                        out=pb[6][0:1, 0:128], in_=mcol[:, 0:1], identity=idf),
                        deps=[tred, ti1, p6_free])
                    tng = em.a(lambda e, sq=sq: e.activation(
                        out=negm[0:1, sq * 128:(sq + 1) * 128], in_=pb[6][0:1, 0:128],
                        func=AF.Copy, scale=-1.0), deps=[ttr, neg_free])
                    p6_free = tng
                    mc_free = ttr
                nk2 = (qgb + QW2) // 128
                tsum = tpv = None
                tk2last = None
                for kt2 in range(nk2):
                    pp = kt2 % 2
                    bank = pb[1 + 4 * pp]
                    tk2 = em.t(lambda e, qh=qh, kt2=kt2, kb0=kb0, qgb=qgb, bank=bank:
                               e.matmul(
                        out=bank[:, 0:QW2],
                        lhsT=kb[:, kb0 + kt2 * 128: kb0 + (kt2 + 1) * 128],
                        rhs=qh[:, kb0 + qgb: kb0 + qgb + QW2],
                        start=True, stop=False), deps=base + [p15_free[pp]])
                    tk2 = em.t(lambda e, bank=bank: e.matmul(
                        out=bank[:, 0:QW2], lhsT=ones_row[0:1, 0:128],
                        rhs=negm[0:1, 0:QW2], start=False, stop=True),
                        deps=[tng, tor])
                    r = kt2 - (qgb // 128)
                    if r >= 0:
                        off = 128 * (SQ - 1) - 128 * r
                        tk2 = em.v(lambda e, off=off, bank=bank: e.tensor_add(
                            out=bank[:, 0:QW2], in0=bank[:, 0:QW2],
                            in1=m2b[:, off:off + QW2]), deps=[tk2, tm2])
                    tex = em.a(lambda e, pp=pp, bank=bank: e.activation(
                        out=pexp[pp][:, 0:QW2], in_=bank[:, 0:QW2], func=AF.Exp),
                        deps=[tk2, pex_free[pp]])
                    p15_free[pp] = tex
                    tsum = em.t(lambda e, pp=pp, first=(kt2 == 0), last=(kt2 == nk2 - 1):
                                e.matmul(
                        out=pb[2][0:1, 0:QW2], lhsT=ones_col[:, 0:1],
                        rhs=pexp[pp][:, 0:QW2], start=first, stop=last),
                        deps=([tex, toc, p2_free] if kt2 == 0 else [tex]))
                    vti = (kb0 + kt2 * 128) // 128
                    tpv = em.t(lambda e, pp=pp, vti=vti, first=(kt2 == 0),
                               last=(kt2 == nk2 - 1): e.matmul(
                        out=pb[3][0:64, 0:QW2],
                        lhsT=vt_sb[:, vti * 64:(vti + 1) * 64],
                        rhs=pexp[pp][:, 0:QW2], start=first, stop=last),
                        deps=([tv, p3_free] if kt2 == 0 else []))
                    pex_free[pp] = tpv
                    tk2last = tk2
                neg_free = tk2last
                trec = em.v(lambda e: e.reciprocal(out=recr[0:1, 0:QW2],
                                                   in_=pb[2][0:1, 0:QW2]),
                            deps=[tsum, rec_free])
                p2_free = trec
                tbc = em.t(lambda e: e.matmul(
                    out=pb[4][0:64, 0:QW2], lhsT=ones_row32[0:1, 0:64],
                    rhs=recr[0:1, 0:QW2],
                    start=True, stop=True), deps=[trec, tor32, p4_free])
                rec_free = tbc
                tyc = em.v(lambda e: e.tensor_copy(out=ysc[:, 0:QW2],
                                                   in_=pb[3][0:64, 0:QW2]),
                           deps=[tpv, ysc_free])
                p3_free = tyc
                tym = em.v(lambda e, hl=hl, kb0=kb0, qgb=qgb: e.tensor_mul(
                    out=ybuf[hl][:, kb0 + qgb: kb0 + qgb + QW2],
                    in0=ysc[:, 0:QW2], in1=pb[4][0:64, 0:QW2]),
                    deps=[tyc, tbc])
                p4_free = tym
                ysc_free = tym
                y_toks.append(tym)
    ts1 = em.dsync(y_o.ap()[0:64, :], ybuf[0], deps=y_toks, sem="sqy0")
    ts2 = em.dsync(y_o.ap()[64:128, :], ybuf[1], deps=y_toks, sem="sqy1")
    em.finalize([ts1, ts2])
    return nc


def launch7_inmaps(cfg, q_btC, k1, v1):
    T = cfg.T
    BT = B * T
    KW = min(512, T)
    QW2 = min(512, T)
    SQ = QW2 // 128
    q_cm = np.ascontiguousarray(np.asarray(q_btC, BFD).transpose(2, 0, 1).reshape(C, BT))
    kf = np.ascontiguousarray(np.asarray(k1, BFD).transpose(2, 0, 1).reshape(64, BT))
    vt = np.ascontiguousarray(np.asarray(v1, BFD).reshape(BT, 64))
    p = np.arange(128)[:, None]
    x1 = np.arange(2 * KW)[None, :]
    m1 = np.where(x1 <= p + KW, 0.0, -3e38).astype(np.float32)
    x2 = np.arange(128 * (SQ - 1) + QW2)[None, :]
    m2 = np.where(x2 >= p + 128 * (SQ - 1), 0.0, -3e38).astype(np.float32)
    ims = []
    for core in range(NCORES):
        ims.append({
            "qA_in": np.ascontiguousarray(q_cm[(2 * core) * 64:(2 * core + 1) * 64]),
            "qB_in": np.ascontiguousarray(q_cm[(2 * core + 1) * 64:(2 * core + 2) * 64]),
            "kf_in": kf, "vt_in": vt, "m1_in": m1, "m2_in": m2,
        })
    return ims


def launch7_unshard(cfg, youts):
    T = cfg.T
    y = np.empty((B, T, C), np.float32)
    for core in range(NCORES):
        arr = np.asarray(youts[core])
        if arr.dtype == np.uint16:
            arr = arr.view(BFD)
        arr = arr.astype(np.float32).reshape(128, B, T)
        y[:, :, (2 * core) * 64:(2 * core + 2) * 64] = arr.transpose(1, 2, 0)
    return y



def build_launch8(cfg):
    nc = bass.Bass("TRN2", target_bir_lowering=False, debug=False)
    em = Em(nc)
    COLS = cfg.COLS
    LB = COLS // B

    y_in = nc.dram_tensor("y_in", [C, COLS], BF16, kind="ExternalInput")
    xn1_in = nc.dram_tensor("xn1_in", [C, COLS], BF16, kind="ExternalInput")
    xl0_in = nc.dram_tensor("xl0_in", [C, COLS], F32, kind="ExternalInput")
    wcp_in = nc.dram_tensor("wcp_in", [C, C], F32, kind="ExternalInput")
    wts_in = {nm: nc.dram_tensor(nm, [C, C], F32, kind="ExternalInput")
              for nm in ("wfT", "ufT", "wiT", "uiT", "wcT", "ucT", "woT", "uoT")}
    bias_in = {nm: nc.dram_tensor(nm, [128, NCH], F32, kind="ExternalInput").ap()
               for nm in ("bf", "bi", "bc", "bo", "in_c")}
    out_o = nc.dram_tensor("out_o", [C, COLS], F32, kind="ExternalOutput")

    xn1b = nc.alloc_sbuf_tensor("xn1b", [128, NCH * COLS], BF16)[:]
    cb = nc.alloc_sbuf_tensor("cb", [128, NCH * LB], F32)[:]
    cbf = nc.alloc_sbuf_tensor("cbf", [128, NCH * LB], BF16)[:]
    wwb = nc.alloc_sbuf_tensor("wwb", [128, NCH * C], BF16)[:]
    uub = nc.alloc_sbuf_tensor("uub", [128, NCH * C], BF16)[:]
    wob = nc.alloc_sbuf_tensor("wob", [128, NCH * C], BF16)[:]
    yb = nc.alloc_sbuf_tensor("yb", [128, NCH * COLS], BF16)[:]
    ibuf = nc.alloc_sbuf_tensor("ibuf", [128, NCH * LB], F32)[:]
    gtile = [nc.alloc_sbuf_tensor(f"gtile{i}", [128, 512], F32)[:] for i in range(2)]
    xl0ct = nc.alloc_sbuf_tensor("xl0ct", [128, NCH * 512], F32)[:]
    biases = {nm: nc.alloc_sbuf_tensor("b_" + nm, [128, NCH], F32)[:]
              for nm in bias_in}
    ath = [nc.alloc_sbuf_tensor(f"ath{i}", [128, 512], F32)[:] for i in range(2)]
    tmp1, tmp2 = ath[0], ath[1]
    at2 = [nc.alloc_sbuf_tensor(f"at2{i}", [128, 512], F32)[:] for i in range(2)]
    evst = [nc.alloc_sbuf_tensor(f"evst{i}", [128, 512], F32)[:] for i in range(2)]
    pb = [nc.alloc_psum_tensor(f"pb{i}", [128, 512], F32)[:] for i in range(8)]

    tx1 = em.dsync(xn1b.rearrange("p (a x) -> p a x", a=NCH), fm_dram_ap(xn1_in, COLS),
                   sem="sqn")
    tbias = None
    for nm in bias_in:
        tbias = em.dsync(biases[nm], bias_in[nm], sem="sqb")
    cin_toks = []
    for ch in range(NCH):
        t1 = em.v(lambda e, ch=ch: e.memset(cb[:, ch * LB:(ch + 1) * LB], 1.0))
        t2 = em.v(lambda e, ch=ch: e.tensor_scalar_mul(
            out=cb[:, ch * LB:(ch + 1) * LB], in0=cb[:, ch * LB:(ch + 1) * LB],
            scalar1=biases["in_c"][:, ch:ch + 1]), deps=[t1, tbias])
        cin_toks.append(t2)
    gate_cfg = {"f": ("wfT", "ufT", "bf", AF.Sigmoid), "i": ("wiT", "uiT", "bi", AF.Sigmoid),
                "g": ("wcT", "ucT", "bc", AF.Tanh)}
    pfree = [None] * 8
    cup_toks = cin_toks
    cbf_toks = [None] * NCH
    gb_free = {g: [None] * NCH for g in "fig"}
    wld_free = None
    uld_free = None
    for step in range(B):
        new_cbf = []
        for ch in range(NCH):
            tcb = em.v(lambda e, ch=ch: e.tensor_copy(
                out=cbf[:, ch * LB:(ch + 1) * LB], in_=cb[:, ch * LB:(ch + 1) * LB]),
                deps=cup_toks + [cbf_toks[ch]])
            new_cbf.append(tcb)
        cbf_toks = new_cbf
        new_cup = [None] * NCH
        gt_free = [None] * 2
        for g in "fig":
            wnm, unm, bnm, func = gate_cfg[g]
            twl = em.dgps(wwb.rearrange("p (a x) -> p a x", a=NCH),
                          fm_dram_ap(wts_in[wnm], C), sem="gqw0", deps=[wld_free])
            tul = em.dgps(uub.rearrange("p (a x) -> p a x", a=NCH),
                          fm_dram_ap(wts_in[unm], C), sem="gqw1", deps=[uld_free])
            for oc in range(NCH):
                bank = pb[oc % 4]
                tk = None
                for kc in range(NCH):
                    tk = em.t(lambda e, oc=oc, kc=kc, step=step, bank=bank: e.matmul(
                        out=bank[:, 0:LB],
                        lhsT=wwb[:, kc * C + oc * 128:kc * C + (oc + 1) * 128],
                        rhs=xn1b[:, kc * COLS + step * LB: kc * COLS + (step + 1) * LB],
                        start=(kc == 0), stop=False),
                        deps=([twl, tx1, pfree[oc % 4]] if kc == 0 else []))
                for kc in range(NCH):
                    tk = em.t(lambda e, oc=oc, kc=kc, bank=bank: e.matmul(
                        out=bank[:, 0:LB],
                        lhsT=uub[:, kc * C + oc * 128:kc * C + (oc + 1) * 128],
                        rhs=cbf[:, kc * LB:(kc + 1) * LB],
                        start=False, stop=(kc == NCH - 1)),
                        deps=([tul] + cbf_toks) if kc == 0 else [])
                if g == "i":
                    tev = em.a(lambda e, oc=oc, bank=bank, bnm=bnm, func=func:
                               e.activation(
                        out=ibuf[:, oc * LB:(oc + 1) * LB], in_=bank[:, 0:LB],
                        func=func, bias=biases[bnm][:, oc:oc + 1]),
                        deps=[tk, tbias, gb_free["i"][oc]])
                    pfree[oc % 4] = tev
                    gb_free["i"][oc] = tev
                else:
                    pp = oc % 2
                    tev = em.a(lambda e, oc=oc, bank=bank, bnm=bnm, func=func, pp=pp:
                               e.activation(
                        out=gtile[pp][:, 0:LB], in_=bank[:, 0:LB],
                        func=func, bias=biases[bnm][:, oc:oc + 1]),
                        deps=[tk, tbias, gt_free[pp]])
                    pfree[oc % 4] = tev
                    if g == "f":
                        tcu = em.v(lambda e, oc=oc, pp=pp: e.tensor_mul(
                            out=cb[:, oc * LB:(oc + 1) * LB],
                            in0=cb[:, oc * LB:(oc + 1) * LB], in1=gtile[pp][:, 0:LB]),
                            deps=[tev] + cup_toks + ([new_cup[oc]] if new_cup[oc] else []))
                    else:
                        tmu = em.v(lambda e, oc=oc, pp=pp: e.tensor_mul(
                            out=gtile[pp][:, 0:LB], in0=gtile[pp][:, 0:LB],
                            in1=ibuf[:, oc * LB:(oc + 1) * LB]),
                            deps=[tev, gb_free["i"][oc]])
                        tcu = em.v(lambda e, oc=oc, pp=pp: e.tensor_add(
                            out=cb[:, oc * LB:(oc + 1) * LB],
                            in0=cb[:, oc * LB:(oc + 1) * LB], in1=gtile[pp][:, 0:LB]),
                            deps=[tmu, new_cup[oc]])
                        gb_free["i"][oc] = tmu
                    gt_free[pp] = tcu
                    new_cup[oc] = tcu
            wld_free = tk
            uld_free = tk
        cup_toks = new_cup
    new_cbf = []
    for ch in range(NCH):
        tcb = em.v(lambda e, ch=ch: e.tensor_copy(
            out=cbf[:, ch * LB:(ch + 1) * LB], in_=cb[:, ch * LB:(ch + 1) * LB]),
            deps=cup_toks + [cbf_toks[ch]])
        new_cbf.append(tcb)
    cbf_toks = new_cbf
    ty = em.dsync(yb.rearrange("p (a x) -> p a x", a=NCH), fm_dram_ap(y_in, COLS),
                  sem="sqy", deps=cup_toks + cbf_toks)
    twc = em.dgps(wwb.rearrange("p (a x) -> p a x", a=NCH),
                  wcp_in.ap().rearrange("(a p) x -> p a x", p=128), sem="gqw0",
                  deps=[wld_free])
    two = em.dgps(uub.rearrange("p (a x) -> p a x", a=NCH), fm_dram_ap(wts_in["woT"], C),
                  sem="gqw1", deps=[uld_free])
    tuo = em.dgps(wob.rearrange("p (a x) -> p a x", a=NCH), fm_dram_ap(wts_in["uoT"], C),
                  sem="gqw2")
    store_toks = []
    xl_free = None
    ath_free = [None] * 2
    at2_free = [None] * 2
    ev_free = [None] * 2
    for ci, (c0, cw) in enumerate(col_tiles(COLS, LB)):
        txl = em.dsync(xl0ct.rearrange("p (a x) -> p a x", a=NCH)[:, :, 0:cw],
                       fm_dram_ap(xl0_in, COLS)[:, :, c0:c0 + cw], sem="sqxl",
                       deps=[xl_free])
        for oc in range(NCH):
            pp = oc % 2
            bank_h = pb[4 + pp]
            bank_a = pb[6 + pp]
            tkh = None
            for kc in range(NCH):
                tkh = em.t(lambda e, oc=oc, kc=kc, c0=c0, cw=cw, bank_h=bank_h: e.matmul(
                    out=bank_h[:, 0:cw],
                    lhsT=uub[:, kc * C + oc * 128:kc * C + (oc + 1) * 128],
                    rhs=xn1b[:, kc * COLS + c0: kc * COLS + c0 + cw],
                    start=(kc == 0), stop=False),
                    deps=([two, tx1, pfree[4 + pp]] if kc == 0 else []))
            for kc in range(NCH):
                tkh = em.t(lambda e, oc=oc, kc=kc, cw=cw, bank_h=bank_h: e.matmul(
                    out=bank_h[:, 0:cw],
                    lhsT=wob[:, kc * C + oc * 128:kc * C + (oc + 1) * 128],
                    rhs=cbf[:, kc * LB: kc * LB + cw],
                    start=False, stop=(kc == NCH - 1)),
                    deps=([tuo] + cbf_toks) if kc == 0 else [])
            tevh = em.a(lambda e, oc=oc, cw=cw, bank_h=bank_h, pp=pp: e.activation(
                out=ath[pp][:, 0:cw], in_=bank_h[:, 0:cw], func=AF.Tanh,
                bias=biases["bo"][:, oc:oc + 1]), deps=[tkh, tbias, ath_free[pp]])
            pfree[4 + pp] = tevh
            tka = None
            for kc in range(NCH):
                tka = em.t(lambda e, oc=oc, kc=kc, c0=c0, cw=cw, bank_a=bank_a: e.matmul(
                    out=bank_a[:, 0:cw],
                    lhsT=wwb[:, kc * C + oc * 128:kc * C + (oc + 1) * 128],
                    rhs=yb[:, kc * COLS + c0: kc * COLS + c0 + cw],
                    start=(kc == 0), stop=(kc == NCH - 1)),
                    deps=([twc, ty, pfree[6 + pp]] if kc == 0 else []))
            ta1 = em.v(lambda e, cw=cw, bank_a=bank_a, pp=pp: e.tensor_add(
                out=at2[pp][:, 0:cw], in0=ath[pp][:, 0:cw], in1=bank_a[:, 0:cw]),
                deps=[tka, tevh, at2_free[pp]])
            pfree[6 + pp] = ta1
            ath_free[pp] = ta1
            ta2 = em.v(lambda e, oc=oc, cw=cw, pp=pp: e.tensor_add(
                out=evst[pp][:, 0:cw], in0=at2[pp][:, 0:cw],
                in1=xl0ct[:, oc * 512:oc * 512 + cw]), deps=[ta1, txl, ev_free[pp]])
            at2_free[pp] = ta2
            tst = em.dsync(out_o.ap()[oc * 128:(oc + 1) * 128, c0:c0 + cw],
                           evst[pp][:, 0:cw], deps=[ta2], sem=f"sst{pp}")
            ev_free[pp] = tst
            store_toks.append(tst)
        xl_free = store_toks[-1]
    em.finalize(store_toks)
    return nc



def _np(a):
    a = np.asarray(a)
    if a.dtype == np.uint16:
        a = a.view(BFD)
    return a


def _runner(nc, in_maps):
    res = run_bass_kernel_spmd(nc, in_maps, core_ids=list(range(NCORES)))
    return res.results


def unshard_vt(cfg, shards):
    out = np.empty((B, cfg.T, 64), dtype=shards[0].dtype)
    for core in range(NCORES):
        s = shards[core]
        for i, (b, t0) in enumerate(seg_starts_global(cfg, core)):
            out[b, t0:t0 + cfg.L] = s[i * cfg.L:(i + 1) * cfg.L]
    return out


def kernel(**inputs):
    d = inputs
    f32 = lambda k: np.asarray(d[k], np.float32)
    x = f32("L0_attn_x")
    cfg = Cfg(x.shape[1])

    nc = build_launch1(cfg)
    im = launch1_inmaps(cfg, x, d["L0_attn_ln1_w"], d["L0_attn_ln1_b"],
                        d["L0_attn_tmk"], d["L0_attn_tmv"], d["L0_attn_tmr"],
                        d["L0_attn_att_kw"], d["L0_attn_att_vw"], d["L0_attn_att_rw"])
    r = _runner(nc, im)
    xn = unshard_cols(cfg, [_np(c["xn_o"]) for c in r])
    kk = unshard_cols(cfg, [_np(c["k_o"]) for c in r])
    vv = unshard_cols(cfg, [_np(c["v_o"]) for c in r])
    sr = unshard_cols(cfg, [_np(c["sr_o"]) for c in r])

    nc = build_launch2(cfg, L2=512)
    im = launch2_inmaps(cfg, kk, vv, d["L0_attn_time_decay"], d["L0_attn_time_first"])
    r = _runner(nc, im)
    wkv = launch2_unshard(cfg, [_np(c["wkv_o"]) for c in r])

    nc = build_launch3(cfg)
    owT = np.ascontiguousarray(f32("L0_attn_att_ow").T)
    im = [{"wkv_in": shard_cols(cfg, c, wkv), "sr_in": shard_cols(cfg, c, sr),
           "xn_in": shard_cols(cfg, c, xn), "owT": owT} for c in range(NCORES)]
    r = _runner(nc, im)
    x1 = unshard_cols(cfg, [_np(c["x1_o"]) for c in r])

    nc = build_launch4a(cfg)
    im = []
    for c in range(NCORES):
        x1s = shard_cols(cfg, c, x1)
        hxh, hmask = halo_of(cfg, c, x1)
        im.append({"x1in": np.ascontiguousarray(np.concatenate([x1s, hxh], 1), np.float32),
                   "maskin": hmask,
                   "ln2w": chunked(d["L0_attn_ln2_w"]), "ln2b": chunked(d["L0_attn_ln2_b"]),
                   "ftmk": chunked(d["L0_attn_f_tmk"]), "ftmr": chunked(d["L0_attn_f_tmr"])})
    r = _runner(nc, im)
    xk2 = unshard_cols(cfg, [_np(c["xk2_o"]) for c in r])
    xr2 = unshard_cols(cfg, [_np(c["xr2_o"]) for c in r])

    nc = build_launch4b(cfg)
    fkwT = np.ascontiguousarray(f32("L0_attn_f_kw").T)
    fvwT = np.ascontiguousarray(f32("L0_attn_f_vw").T)
    frwT = np.ascontiguousarray(f32("L0_attn_f_rw").T)
    im = [{"xk2_in": shard_cols(cfg, c, xk2.astype(BFD)),
           "xr2_in": shard_cols(cfg, c, xr2.astype(BFD)),
           "x1_in": shard_cols(cfg, c, x1),
           "fkwT": fkwT, "fvwT": fvwT, "frwT": frwT} for c in range(NCORES)]
    r = _runner(nc, im)
    ao = unshard_cols(cfg, [_np(c["ao_o"]) for c in r])

    nc = build_launch5(cfg)
    im = [{"xn_in": shard_cols(cfg, c, xn), "x_in": shard_cols(cfg, c, x),
           "ao_in": shard_cols(cfg, c, ao),
           "wgT": np.ascontiguousarray(f32("L0_ffn_w_gate").T),
           "wuT": np.ascontiguousarray(f32("L0_ffn_w_up").T),
           "wdT": np.ascontiguousarray(f32("L0_ffn_w_down").T)} for c in range(NCORES)]
    r = _runner(nc, im)
    xl0 = unshard_cols(cfg, [_np(c["xl0_o"]) for c in r])

    nc = build_launch6(cfg)
    im = [{"xl0_in": shard_cols(cfg, c, xl0),
           "wcd_in": f32("L1_attn_Wc_down"),
           "wqu_in": f32("L1_attn_Wq_up"),
           "wkv_in": f32("L1_attn_Wkv_up")} for c in range(NCORES)]
    r = _runner(nc, im)
    xn1 = unshard_cols(cfg, [_np(c["xn1_o"]) for c in r])
    qq = unshard_cols(cfg, [_np(c["q_o"]) for c in r])
    k1 = unshard_cols(cfg, [_np(c["k_o"]) for c in r])
    v1 = unshard_vt(cfg, [_np(c["vt_o"]) for c in r])

    nc = build_launch7(cfg)
    im = launch7_inmaps(cfg, qq, k1, v1)
    r = _runner(nc, im)
    y = launch7_unshard(cfg, [_np(c["y_o"]) for c in r])

    nc = build_launch8(cfg)
    im = []
    for c in range(NCORES):
        m = {"y_in": shard_cols(cfg, c, y.astype(BFD)),
             "xn1_in": shard_cols(cfg, c, xn1),
             "xl0_in": shard_cols(cfg, c, xl0),
             "wcp_in": f32("L1_attn_Wc_proj")}
        for nm, key in (("wfT", "L1_ffn_wf"), ("ufT", "L1_ffn_uf"),
                        ("wiT", "L1_ffn_wi"), ("uiT", "L1_ffn_ui"),
                        ("wcT", "L1_ffn_wc"), ("ucT", "L1_ffn_uc"),
                        ("woT", "L1_ffn_wo"), ("uoT", "L1_ffn_uo")):
            m[nm] = np.ascontiguousarray(f32(key).T)
        for nm, key in (("bf", "L1_ffn_bf"), ("bi", "L1_ffn_bi"), ("bc", "L1_ffn_bc"),
                        ("bo", "L1_ffn_bo"), ("in_c", "L1_ffn_in_c")):
            m[nm] = chunked(d[key])
        im.append(m)
    r = _runner(nc, im)
    out = unshard_cols(cfg, [_np(c["out_o"]).astype(np.float32) for c in r])
    return np.ascontiguousarray(out)


# version 06 (final)
# speedup vs baseline: 1.0621x; 1.0117x over previous
import numpy as np
import ml_dtypes
import concourse.bass as bass
import concourse.mybir as mybir
from concourse.bass_utils import run_bass_kernel_spmd

F32 = mybir.dt.float32
BF16 = mybir.dt.bfloat16
FP16 = mybir.dt.float16
AF = mybir.ActivationFunctionType
ALU = mybir.AluOpType
AX = mybir.AxisListType

NCORES = 8
C = 1024
NCH = 8
B = 4
EPS = 1e-5
BFD = np.dtype(ml_dtypes.bfloat16)

ENGINES = ("sync", "gpsimd", "vector", "scalar", "tensor")


def _lst(x):
    if x is None:
        return []
    if isinstance(x, list):
        return [t for t in x if t is not None]
    return [x]


class Cfg:
    def __init__(self, T=4096):
        self.T = T
        self.L = T // 16
        self.SEGS = 2 * B
        self.COLS = self.SEGS * self.L
        self.CA = self.COLS + self.SEGS


class Em:
    def __init__(self, nc):
        self.nc = nc
        self.streams = {e: [] for e in ENGINES}
        self.sems = {}
        self.ctr = {}

    def _sem(self, name):
        if name not in self.sems:
            self.sems[name] = self.nc.alloc_semaphore("sem_" + name)
            self.ctr[name] = 0
        return self.sems[name]

    def _emit(self, eng, fn, deps, semname, n):
        self._sem(semname)
        self.ctr[semname] += n
        val = self.ctr[semname]
        self.streams[eng].append((fn, tuple(_lst(list(deps))), semname, n))
        return (semname, val)

    def v(self, fn, deps=()):
        return self._emit("vector", fn, deps, "v", 1)

    def a(self, fn, deps=()):
        return self._emit("scalar", fn, deps, "a", 1)

    def t(self, fn, deps=()):
        return self._emit("tensor", fn, deps, "t", 1)

    def p(self, fn, deps=()):
        return self._emit("gpsimd", fn, deps, "p", 1)

    def dsync(self, out, in_, deps=(), sem="sq"):
        return self._emit("sync", lambda e: e.dma_start(out=out, in_=in_), deps, sem, 16)

    def dscal(self, out, in_, deps=(), sem="adq"):
        return self._emit("scalar", lambda e: e.dma_start(out=out, in_=in_), deps, sem, 16)

    def dgps(self, out, in_, deps=(), sem="gq"):
        return self._emit("gpsimd", lambda e: e.dma_start(out=out, in_=in_), deps, sem, 16)

    def barrier(self):
        return [(s, self.ctr[s]) for s in self.ctr]

    def finalize(self, final_deps):
        nc = self.nc
        streams, sems = self.streams, self.sems

        def section(eng):
            def run(e):
                last = {}
                for fn, deps, semname, n in streams[eng]:
                    for (dsem, dval) in deps:
                        if last.get(dsem, 0) < dval:
                            e.wait_ge(sems[dsem], dval)
                            last[dsem] = dval
                    fn(e).then_inc(sems[semname], n)
                if eng == "sync":
                    for (dsem, dval) in final_deps:
                        if last.get(dsem, 0) < dval:
                            e.wait_ge(sems[dsem], dval)
                            last[dsem] = dval
            return run

        with nc.Block() as blk:
            blk.sync(section("sync"))
            blk.gpsimd(section("gpsimd"))
            blk.vector(section("vector"))
            blk.scalar(section("scalar"))
            blk.tensor(section("tensor"))



def seg_starts_global(cfg, core):
    out = []
    for b in range(B):
        for blkid in (core, 15 - core):
            out.append((b, blkid * cfg.L))
    return out


def shard_cols(cfg, core, arr_btC):
    cols = []
    for b in range(B):
        for blkid in (core, 15 - core):
            cols.append(arr_btC[b, blkid * cfg.L:(blkid + 1) * cfg.L, :])
    return np.ascontiguousarray(np.concatenate(cols, axis=0).T)


def unshard_cols(cfg, shards):
    T = cfg.T
    CC = shards[0].shape[0]
    out = np.empty((B, T, CC), dtype=shards[0].dtype)
    for core in range(NCORES):
        s = shards[core]
        i = 0
        for b in range(B):
            for blkid in (core, 15 - core):
                out[b, blkid * cfg.L:(blkid + 1) * cfg.L, :] = s[:, i * cfg.L:(i + 1) * cfg.L].T
                i += 1
    return out


def halo_of(cfg, core, arr_btC):
    CC = arr_btC.shape[2]
    h = np.zeros((CC, cfg.SEGS), dtype=np.float32)
    m = np.zeros((1, cfg.SEGS), dtype=np.float32)
    for i, (b, t0) in enumerate(seg_starts_global(cfg, core)):
        if t0 > 0:
            h[:, i] = arr_btC[b, t0 - 1, :]
            m[0, i] = 1.0
    return h, m


def chunked(vec):
    return np.ascontiguousarray(np.asarray(vec, np.float32).reshape(NCH, 128).T)


def fm_dram_ap(t, X):
    return t.ap().rearrange("(a p) x -> p a x", p=128)


def col_tiles(n, w=512):
    return [(i, min(w, n - i)) for i in range(0, n, w)]



def emit_ln(em, st, src_ap, dst_ap, ncols, mean=True, gamma=None, beta=None,
            src_dep=None, dst_war=None):
    pb, rows, sq, atile = st["pb"], st["rows"], st["sq"], st["atile"]
    ones_col, ones_row, dep_ones = st["ones_col"], st["ones_row"], st["dep_ones"]
    src_dep = _lst(src_dep)
    dst_war = _lst(dst_war)
    out_toks = []
    for (c0, cw) in col_tiles(ncols):
        sq_free = [st.get("sq0_free"), st.get("sq1_free")]
        tmm1 = tmm2 = None
        for ch in range(NCH):
            pp = ch % 2
            tsq = em.a(lambda e, ch=ch, c0=c0, cw=cw, pp=pp: e.activation(
                out=sq[pp][:, 0:cw], in_=src_ap(ch, c0, cw), func=AF.Square),
                deps=src_dep + [sq_free[pp]])
            if mean:
                tmm1 = em.t(lambda e, ch=ch, c0=c0, cw=cw: e.matmul(
                    out=pb["s1"][0:1, 0:cw], lhsT=ones_col[:, 0:1], rhs=src_ap(ch, c0, cw),
                    start=(ch == 0), stop=(ch == NCH - 1)),
                    deps=([dep_ones, st.get("pb_s1_free")] + src_dep) if ch == 0 else [])
            tmm2 = em.t(lambda e, cw=cw, pp=pp, ch=ch: e.matmul(
                out=pb["s2"][0:1, 0:cw], lhsT=ones_col[:, 0:1], rhs=sq[pp][:, 0:cw],
                start=(ch == 0), stop=(ch == NCH - 1)),
                deps=[dep_ones, tsq, st.get("pb_s2_free")] if ch == 0 else [tsq])
            sq_free[pp] = tmm2
        st["sq0_free"], st["sq1_free"] = sq_free
        r_row = rows[0:1, 0:cw]
        mr_row = rows[0:1, 512:512 + cw]
        tmp_row = rows[0:1, 1024:1024 + cw]
        t2_row = rows[0:1, 1536:1536 + cw]
        rows_free = st.get("rows_free")
        tk1 = em.v(lambda e, cw=cw, tmp_row=tmp_row: e.tensor_scalar_mul(
            out=tmp_row, in0=pb["s2"][0:1, 0:cw], scalar1=1.0 / C),
            deps=[tmm2, rows_free])
        tkm = None
        if mean:
            tkm = em.v(lambda e, cw=cw, mr_row=mr_row: e.tensor_scalar_mul(
                out=mr_row, in0=pb["s1"][0:1, 0:cw], scalar1=1.0 / C),
                deps=[tmm1, rows_free])
            tkn = em.v(lambda e, cw=cw, t2_row=t2_row, mr_row=mr_row: e.scalar_tensor_tensor(
                out=t2_row, in0=mr_row, scalar=-1.0, in1=mr_row,
                op0=ALU.mult, op1=ALU.mult), deps=[tkm, rows_free])
            tk1 = em.v(lambda e, cw=cw, tmp_row=tmp_row, t2_row=t2_row: e.tensor_add(
                out=tmp_row, in0=tmp_row, in1=t2_row), deps=[tk1, tkn])
        tk2 = em.a(lambda e, cw=cw, tmp_row=tmp_row: e.activation(
            out=tmp_row, in_=tmp_row, func=AF.Sqrt, bias=st["eps_col"][0:1, 0:1]),
            deps=[tk1, st["dep_eps"]])
        tk3 = em.v(lambda e, cw=cw, r_row=r_row, tmp_row=tmp_row: e.reciprocal(out=r_row, in_=tmp_row), deps=[tk2])
        st["pb_s2_free"] = tk1
        tk4 = None
        if mean:
            st["pb_s1_free"] = tkm
            tk4 = em.v(lambda e, cw=cw, mr_row=mr_row, r_row=r_row: e.tensor_mul(
                out=mr_row, in0=mr_row, in1=r_row), deps=[tk3])
        tb1 = em.t(lambda e, cw=cw, r_row=r_row: e.matmul(
            out=pb["b1"][:, 0:cw], lhsT=ones_row[0:1, :], rhs=r_row,
            start=True, stop=True), deps=[tk3, dep_ones, st.get("pb_b1_free")])
        tb2 = None
        if mean:
            tb2 = em.t(lambda e, cw=cw, mr_row=mr_row: e.matmul(
                out=pb["b2"][:, 0:cw], lhsT=ones_row[0:1, :], rhs=mr_row,
                start=True, stop=True), deps=[tk4, dep_ones, st.get("pb_b2_free")])
        st["rows_free"] = tb2 if mean else tb1
        last_ap = []
        for ch in range(NCH):
            pp = ch % 2
            at_free = st.get(f"atile{pp}_free")
            cur = em.v(lambda e, ch=ch, c0=c0, cw=cw, pp=pp: e.tensor_mul(
                out=atile[pp][:, 0:cw], in0=src_ap(ch, c0, cw), in1=pb["b1"][:, 0:cw]),
                deps=[tb1, at_free] + src_dep)
            if mean and gamma is not None:
                cur = em.v(lambda e, cw=cw, pp=pp: e.tensor_sub(
                    out=atile[pp][:, 0:cw], in0=atile[pp][:, 0:cw], in1=pb["b2"][:, 0:cw]),
                    deps=[tb2, cur])
                cur = em.v(lambda e, ch=ch, c0=c0, cw=cw, pp=pp: e.tensor_scalar(
                    out=dst_ap(ch, c0, cw), in0=atile[pp][:, 0:cw],
                    scalar1=gamma[0][:, ch:ch + 1], scalar2=beta[0][:, ch:ch + 1],
                    op0=ALU.mult, op1=ALU.add), deps=[cur, gamma[1]] + dst_war)
            elif mean:
                cur = em.v(lambda e, ch=ch, c0=c0, cw=cw, pp=pp: e.tensor_sub(
                    out=dst_ap(ch, c0, cw), in0=atile[pp][:, 0:cw], in1=pb["b2"][:, 0:cw]),
                    deps=[tb2, cur] + dst_war)
            elif gamma is not None:
                cur = em.v(lambda e, ch=ch, c0=c0, cw=cw, pp=pp: e.tensor_scalar(
                    out=dst_ap(ch, c0, cw), in0=atile[pp][:, 0:cw],
                    scalar1=gamma[0][:, ch:ch + 1], scalar2=beta[0][:, ch:ch + 1],
                    op0=ALU.mult, op1=ALU.add), deps=[cur, gamma[1]] + dst_war)
            else:
                cur = em.v(lambda e, ch=ch, c0=c0, cw=cw, pp=pp: e.tensor_copy(
                    out=dst_ap(ch, c0, cw), in_=atile[pp][:, 0:cw]),
                    deps=[cur] + dst_war)
            st[f"atile{pp}_free"] = cur
            last_ap.append(cur)
        st["pb_b1_free"] = last_ap[-1]
        if mean:
            st["pb_b2_free"] = last_ap[-1]
        out_toks.extend(last_ap)
    return out_toks


def make_state(em, nc, tag="", rows=None):
    pb = {k: nc.alloc_psum_tensor(f"pb_{k}{tag}", [128, 512], F32)[:]
          for k in ("s1", "s2", "b1", "b2", "m0", "m1", "m2", "m3")}
    if rows is None:
        rows = nc.alloc_sbuf_tensor(f"rows{tag}", [1, 2048], F32)[:]
    sq = [nc.alloc_sbuf_tensor(f"sq{i}{tag}", [128, 512], F32)[:] for i in range(2)]
    atile = [nc.alloc_sbuf_tensor(f"atile{i}{tag}", [128, 512], F32)[:] for i in range(2)]
    ones_col = nc.alloc_sbuf_tensor(f"ones_col{tag}", [128, 1], F32)[:]
    ones_row = nc.alloc_sbuf_tensor(f"ones_row{tag}", [1, 128], F32)[:]
    eps_col = nc.alloc_sbuf_tensor(f"eps_col{tag}", [128, 1], F32)[:]
    t1 = em.v(lambda e: e.memset(ones_col, 1.0))
    t2 = em.v(lambda e: e.memset(ones_row, 1.0))
    t3 = em.v(lambda e: e.memset(eps_col, float(EPS)))
    return dict(pb=pb, rows=rows, sq=sq, atile=atile, ones_col=ones_col,
                ones_row=ones_row, dep_ones=t2, eps_col=eps_col, dep_eps=t3)



def build_launch1(cfg, projs=(("kwT", "tmk"), ("vwT", "tmv"), ("rwT", "tmr"))):
    nc = bass.Bass("TRN2", target_bir_lowering=False, debug=False)
    em = Em(nc)
    CA, COLS, SEGS, L = cfg.CA, cfg.COLS, cfg.SEGS, cfg.L

    xin = nc.dram_tensor("xin", [C, CA], F32, kind="ExternalInput")
    maskin = nc.dram_tensor("maskin", [1, SEGS], F32, kind="ExternalInput").ap()
    prm_names = ("ln1w", "ln1b", "tmk", "tmv", "tmr")
    prm = {nm: nc.dram_tensor(nm, [128, NCH], F32, kind="ExternalInput").ap()
           for nm in prm_names}
    wts = {nm: nc.dram_tensor(nm, [C, C], F32, kind="ExternalInput")
           for nm in ("kwT", "vwT", "rwT")}
    xn_o = nc.dram_tensor("xn_o", [C, COLS], F32, kind="ExternalOutput")
    k_o = nc.dram_tensor("k_o", [C, COLS], F32, kind="ExternalOutput")
    v_o = nc.dram_tensor("v_o", [C, COLS], F32, kind="ExternalOutput")
    sr_o = nc.dram_tensor("sr_o", [C, COLS], F32, kind="ExternalOutput")
    proj_out = {"kwT": k_o, "vwT": v_o, "rwT": sr_o}

    bufA = nc.alloc_sbuf_tensor("bufA", [128, NCH * CA], F32)[:]
    bufB = nc.alloc_sbuf_tensor("bufB", [128, NCH * CA], F32)[:]
    mixb = nc.alloc_sbuf_tensor("mixb", [128, NCH * COLS], BF16)[:]
    un0 = nc.alloc_sbuf_tensor("un0", [128, max(COLS, 2048)], F32)
    t1b = un0[:]
    wbuf = nc.alloc_sbuf_tensor("wbuf", [128, NCH * C], BF16)[:]
    hm = nc.alloc_sbuf_tensor("hm", [128, NCH * SEGS], F32)[:]
    maskB = nc.alloc_sbuf_tensor("maskB", [128, SEGS], F32)[:]
    mrow = nc.alloc_sbuf_tensor("mrow", [1, SEGS], F32)[:]
    tm1 = {nm: nc.alloc_sbuf_tensor(f"tm1_{nm}", [128, NCH], F32)[:]
           for nm in ("tmk", "tmv", "tmr")}
    prms = {nm: nc.alloc_sbuf_tensor("s_" + nm, [128, NCH], F32)[:] for nm in prm_names}
    evac = [nc.alloc_sbuf_tensor(f"evac{i}", [128, 512], F32)[:] for i in range(4)]

    st = make_state(em, nc, rows=un0[:][0:1, 0:2048])
    pb = st["pb"]

    tx = em.dsync(bufA.rearrange("p (a x) -> p a x", a=NCH), fm_dram_ap(xin, CA), sem="sqx")
    for nm in prm_names:
        em.dsync(prms[nm], prm[nm], sem="sqp")
    tp_all = em.dsync(mrow, maskin, sem="sqp")
    tprm = {nm: tp_all for nm in prm_names}
    tmask = tp_all
    tm1_toks = {}
    for nm in ("tmk", "tmv", "tmr"):
        tm1_toks[nm] = em.v(lambda e, nm=nm: e.tensor_scalar(
            out=tm1[nm], in0=prms[nm], scalar1=-1.0, scalar2=1.0,
            op0=ALU.mult, op1=ALU.add), deps=[tprm[nm]])
    tmb0 = em.t(lambda e: e.matmul(out=pb["b2"][:, 0:SEGS], lhsT=st["ones_row"][0:1, :],
                                   rhs=mrow, start=True, stop=True),
                deps=[tmask, st["dep_ones"]])
    tmb = em.v(lambda e: e.tensor_copy(out=maskB, in_=pb["b2"][:, 0:SEGS]), deps=[tmb0])
    st["pb_b2_free"] = tmb

    def a_ap(ch, c0, cw):
        return bufA[:, ch * CA + c0: ch * CA + c0 + cw]

    def b_ap(ch, c0, cw):
        return bufB[:, ch * CA + c0: ch * CA + c0 + cw]

    emit_ln(em, st, a_ap, b_ap, CA, mean=True, src_dep=[tx])
    bar_xn = em.barrier()
    t_xn_store = em.dsync(fm_dram_ap(xn_o, COLS),
                          bufB.rearrange("p (a x) -> p a x", a=NCH)[:, :, 0:COLS],
                          deps=bar_xn, sem="sqxn")
    emit_ln(em, st, b_ap, a_ap, CA, mean=False,
            gamma=(prms["ln1w"], tprm["ln1w"]), beta=(prms["ln1b"], tprm["ln1b"]),
            src_dep=bar_xn, dst_war=bar_xn)
    bar_h = em.barrier()
    hm_toks = []
    for ch in range(NCH):
        hm_toks.append(em.v(lambda e, ch=ch: e.tensor_mul(
            out=hm[:, ch * SEGS:(ch + 1) * SEGS],
            in0=bufA[:, ch * CA + COLS: ch * CA + CA], in1=maskB),
            deps=bar_h + [tmb]))
    bar_hm = em.barrier()

    prev_proj_done = None
    prev_mm_done = None
    store_toks = []
    evac_free = [None] * 4
    for pi, (nm, tmnm) in enumerate(projs):
        twl = em.dgps(wbuf.rearrange("p (a x) -> p a x", a=NCH),
                      fm_dram_ap(wts[nm], C), deps=[prev_proj_done], sem=f"gqw{pi % 2}")
        for ch in range(NCH):
            war = [prev_proj_done, prev_mm_done]
            tt1 = em.v(lambda e, ch=ch, tmnm=tmnm: e.tensor_scalar_mul(
                out=t1b[:, 0:COLS], in0=bufA[:, ch * CA: ch * CA + COLS],
                scalar1=prms[tmnm][:, ch:ch + 1]),
                deps=bar_hm + [st.get("t1b_free"), tprm[tmnm]])
            tbig = em.v(lambda e, ch=ch, tmnm=tmnm: e.scalar_tensor_tensor(
                out=mixb[:, ch * COLS + 1: ch * COLS + COLS],
                in0=bufA[:, ch * CA: ch * CA + COLS - 1],
                scalar=tm1[tmnm][:, ch:ch + 1], in1=t1b[:, 1:COLS],
                op0=ALU.mult, op1=ALU.add), deps=[tt1, tm1_toks[tmnm]] + war)
            tfix = em.v(lambda e, ch=ch, tmnm=tmnm: e.scalar_tensor_tensor(
                out=mixb[:, ch * COLS: (ch + 1) * COLS].rearrange(
                    "p (s l) -> p s l", s=SEGS)[:, :, 0:1],
                in0=hm[:, ch * SEGS:(ch + 1) * SEGS].rearrange("p (s o) -> p s o", o=1),
                scalar=tm1[tmnm][:, ch:ch + 1],
                in1=t1b[:, 0:COLS].rearrange("p (s l) -> p s l", s=SEGS)[:, :, 0:1],
                op0=ALU.mult, op1=ALU.add), deps=[tt1, tbig, hm_toks[ch], tm1_toks[tmnm]] + war)
            st["t1b_free"] = tfix
        bar_mix = em.barrier()
        nct = len(col_tiles(COLS))
        for d in range(NCH):
            for ci, (c0, cw) in enumerate(col_tiles(COLS)):
                bank = pb[f"m{ci % 4}"]
                tk = None
                for kc in range(NCH):
                    tk = em.t(lambda e, d=d, kc=kc, c0=c0, cw=cw, bank=bank: e.matmul(
                        out=bank[:, 0:cw],
                        lhsT=wbuf[:, kc * C + d * 128:kc * C + (d + 1) * 128],
                        rhs=mixb[:, kc * COLS + c0: kc * COLS + c0 + cw],
                        start=(kc == 0), stop=(kc == NCH - 1)),
                        deps=([twl, st.get(f"pbm{ci % 4}_free")] + bar_mix) if kc == 0 else [])
                ei = (d * nct + ci) % 4
                if nm == "rwT":
                    tev = em.a(lambda e, cw=cw, bank=bank, ei=ei: e.activation(
                        out=evac[ei][:, 0:cw], in_=bank[:, 0:cw], func=AF.Sigmoid),
                        deps=[tk, evac_free[ei]])
                else:
                    tev = em.v(lambda e, cw=cw, bank=bank, ei=ei: e.tensor_copy(
                        out=evac[ei][:, 0:cw], in_=bank[:, 0:cw]),
                        deps=[tk, evac_free[ei]])
                st[f"pbm{ci % 4}_free"] = tev
                tst = em.dsync(proj_out[nm].ap()[d * 128:(d + 1) * 128, c0:c0 + cw],
                               evac[ei][:, 0:cw], deps=[tev], sem=f"sst{ei}")
                evac_free[ei] = tst
                store_toks.append(tst)
        prev_proj_done = store_toks[-1]
        prev_mm_done = tk

    em.finalize([t_xn_store] + store_toks)
    return nc


def launch1_inmaps(cfg, x_btC, ln1w, ln1b, tmk, tmv, tmr, att_kw, att_vw, att_rw):
    kwT = np.ascontiguousarray(np.asarray(att_kw, np.float32).T)
    vwT = np.ascontiguousarray(np.asarray(att_vw, np.float32).T)
    rwT = np.ascontiguousarray(np.asarray(att_rw, np.float32).T)
    in_maps = []
    for core in range(NCORES):
        xs = shard_cols(cfg, core, x_btC).astype(np.float32)
        hx, hmask = halo_of(cfg, core, x_btC)
        xin = np.ascontiguousarray(np.concatenate([xs, hx], axis=1), dtype=np.float32)
        in_maps.append({
            "xin": xin, "maskin": hmask,
            "ln1w": chunked(ln1w), "ln1b": chunked(ln1b),
            "tmk": chunked(tmk), "tmv": chunked(tmv), "tmr": chunked(tmr),
            "kwT": kwT, "vwT": vwT, "rwT": rwT,
        })
    return in_maps






def build_launch2(cfg, L2=512):
    nc = bass.Bass("TRN2", target_bir_lowering=False, debug=False)
    em = Em(nc)
    T = cfg.T
    L2 = min(L2, T)
    NRT = 4
    NCK = T // L2

    kin = nc.dram_tensor("kin", [512, T], F32, kind="ExternalInput")
    vin = nc.dram_tensor("vin", [512, T], F32, kind="ExternalInput")
    dec_in = nc.dram_tensor("dec_in", [128, NRT], F32, kind="ExternalInput").ap()
    tf_in = nc.dram_tensor("tf_in", [128, NRT], F32, kind="ExternalInput").ap()
    wkv_o = nc.dram_tensor("wkv_o", [512, T], F32, kind="ExternalOutput")

    kb = nc.alloc_sbuf_tensor("kb", [128, T], F32)[:]
    vb = nc.alloc_sbuf_tensor("vb", [128, T], F32)[:]
    ob = nc.alloc_sbuf_tensor("ob", [128, T], F32)[:]
    ekb = nc.alloc_sbuf_tensor("ekb", [128, L2], F32)[:]
    u1b = nc.alloc_sbuf_tensor("u1b", [128, L2], F32)[:]
    sab = nc.alloc_sbuf_tensor("sab", [128, L2], F32)[:]
    sbb = nc.alloc_sbuf_tensor("sbb", [128, L2], F32)[:]
    d1b = nc.alloc_sbuf_tensor("d1b", [128, L2 + 1], F32)[:]
    dnb = nc.alloc_sbuf_tensor("dnb", [128, L2], F32)[:]
    nnb = nc.alloc_sbuf_tensor("nnb", [128, L2], F32)[:]
    gt = nc.alloc_sbuf_tensor("gt", [128, L2], F32)[:]
    wramp = nc.alloc_sbuf_tensor("wramp", [128, L2 + 1], F32)[:]
    iot = nc.alloc_sbuf_tensor("iot", [128, L2 + 1], F32)[:]
    dec = nc.alloc_sbuf_tensor("dec", [128, NRT], F32)[:]
    tf = nc.alloc_sbuf_tensor("tf", [128, NRT], F32)[:]
    ew = nc.alloc_sbuf_tensor("ew", [128, 1], F32)[:]
    wcol = nc.alloc_sbuf_tensor("wcol", [128, 1], F32)[:]
    gcol = nc.alloc_sbuf_tensor("gcol", [128, 1], F32)[:]
    relu_u = nc.alloc_sbuf_tensor("relu_u", [128, 1], F32)[:]
    expu = nc.alloc_sbuf_tensor("expu", [128, 1], F32)[:]
    mx = nc.alloc_sbuf_tensor("mx", [128, 1], F32)[:]
    Mb = nc.alloc_sbuf_tensor("Mb", [128, 1], F32)[:]
    negM = nc.alloc_sbuf_tensor("negM", [128, 1], F32)[:]
    PmM = nc.alloc_sbuf_tensor("PmM", [128, 1], F32)[:]
    Pbuf = [nc.alloc_sbuf_tensor(f"P{i}", [128, 1], F32)[:] for i in range(2)]
    Abuf = [nc.alloc_sbuf_tensor(f"A{i}", [128, 1], F32)[:] for i in range(2)]
    Bbuf = [nc.alloc_sbuf_tensor(f"Bc{i}", [128, 1], F32)[:] for i in range(2)]

    tdec = em.dsync(dec, dec_in, sem="sqp")
    tdec = em.dsync(tf, tf_in, sem="sqp")
    tio = em.p(lambda e: e.iota(out=iot, pattern=[[1, L2 + 1]], base=0,
                                channel_multiplier=0,
                                allow_small_or_imprecise_dtypes=True))
    store_toks = []
    bar_prev = None
    for rt in range(NRT):
        ldeps = [] if bar_prev is None else bar_prev
        tkld = em.dsync(kb, kin.ap()[rt * 128:(rt + 1) * 128, :], sem="sqk", deps=ldeps)
        tvld = em.dsync(vb, vin.ap()[rt * 128:(rt + 1) * 128, :], sem="sqv", deps=ldeps)
        t1 = em.a(lambda e, rt=rt: e.activation(out=ew, in_=dec[:, rt:rt + 1],
                                                func=AF.Exp), deps=[tdec] + ldeps)
        t2 = em.v(lambda e: e.tensor_scalar_mul(out=wcol, in0=ew, scalar1=-1.0),
                  deps=[t1] + ldeps)
        t3 = em.a(lambda e: e.activation(out=gcol, in_=wcol, func=AF.Exp), deps=[t2])
        t4 = em.v(lambda e, rt=rt: e.tensor_scalar_max(out=relu_u, in0=tf[:, rt:rt + 1],
                                                       scalar1=0.0), deps=[tdec] + ldeps)
        t5 = em.a(lambda e, rt=rt: e.activation(out=expu, in_=tf[:, rt:rt + 1],
                                                func=AF.Exp), deps=[tdec] + ldeps)
        t6 = em.v(lambda e: e.memset(gt, 1.0), deps=ldeps)
        t7 = em.v(lambda e: e.tensor_scalar_mul(out=gt, in0=gt, scalar1=gcol[:, 0:1]),
                  deps=[t6, t3])
        t8 = em.v(lambda e: e.tensor_scalar_mul(out=wramp, in0=iot, scalar1=wcol[:, 0:1]),
                  deps=[tio, t2] + ldeps)
        c1 = em.v(lambda e: e.memset(Pbuf[0], -3e38), deps=ldeps)
        c2 = em.v(lambda e: e.memset(Abuf[0], 0.0), deps=ldeps)
        c3 = em.v(lambda e: e.memset(Bbuf[0], 0.0), deps=ldeps)
        bar = em.barrier()
        for ci in range(NCK):
            pc, qc = ci % 2, (ci + 1) % 2
            c0 = ci * L2
            P, A, Bc = Pbuf[pc], Abuf[pc], Bbuf[pc]
            P2, A2, B2 = Pbuf[qc], Abuf[qc], Bbuf[qc]
            k_ap = kb[:, c0:c0 + L2]
            v_ap = vb[:, c0:c0 + L2]
            tmx = em.v(lambda e, k_ap=k_ap: e.reduce_max(out=mx, in_=k_ap, axis=AX.X),
                       deps=bar)
            tM0 = em.v(lambda e: e.tensor_add(out=Mb, in0=mx, in1=relu_u), deps=[tmx] + bar)
            tM = em.v(lambda e, P=P: e.tensor_max(out=Mb, in0=Mb, in1=P), deps=[tM0] + bar)
            tnM = em.v(lambda e: e.tensor_scalar_mul(out=negM, in0=Mb, scalar1=-1.0),
                       deps=[tM] + bar)
            tek = em.a(lambda e, k_ap=k_ap: e.activation(out=ekb, in_=k_ap, func=AF.Exp,
                                                         bias=negM[:, 0:1]),
                       deps=[tnM] + bar)
            tu1 = em.v(lambda e, v_ap=v_ap: e.tensor_mul(out=u1b, in0=ekb, in1=v_ap),
                       deps=[tek] + bar)
            tsa = em.v(lambda e: e.tensor_tensor_scan(out=sab, data0=gt, data1=u1b,
                                                      initial=0.0, op0=ALU.mult,
                                                      op1=ALU.add), deps=[tu1] + bar)
            tsb = em.v(lambda e: e.tensor_tensor_scan(out=sbb, data0=gt, data1=ekb,
                                                      initial=0.0, op0=ALU.mult,
                                                      op1=ALU.add), deps=[tek] + bar)
            tpm = em.v(lambda e, P=P: e.tensor_sub(out=PmM, in0=P, in1=Mb),
                       deps=[tM] + bar)
            td1 = em.a(lambda e: e.activation(out=d1b, in_=wramp, func=AF.Exp,
                                              bias=PmM[:, 0:1]), deps=[tpm] + bar)
            tn1 = em.v(lambda e, A=A: e.tensor_scalar_mul(out=nnb, in0=d1b[:, 0:L2],
                                                          scalar1=A[:, 0:1]),
                       deps=[td1] + bar)
            tn2 = em.v(lambda e: e.tensor_add(out=nnb[:, 1:L2], in0=nnb[:, 1:L2],
                                              in1=sab[:, 0:L2 - 1]), deps=[tn1, tsa])
            tn3 = em.v(lambda e: e.scalar_tensor_tensor(out=nnb, in0=u1b,
                                                        scalar=expu[:, 0:1], in1=nnb,
                                                        op0=ALU.mult, op1=ALU.add),
                       deps=[tn2, tu1] + bar)
            td2 = em.v(lambda e, Bc=Bc: e.tensor_scalar_mul(out=dnb, in0=d1b[:, 0:L2],
                                                            scalar1=Bc[:, 0:1]),
                       deps=[td1, tn1] + bar)
            td3 = em.v(lambda e: e.tensor_add(out=dnb[:, 1:L2], in0=dnb[:, 1:L2],
                                              in1=sbb[:, 0:L2 - 1]), deps=[td2, tsb])
            td4 = em.v(lambda e: e.scalar_tensor_tensor(out=dnb, in0=ekb,
                                                        scalar=expu[:, 0:1], in1=dnb,
                                                        op0=ALU.mult, op1=ALU.add),
                       deps=[td3, tek] + bar)
            tl1 = em.a(lambda e: e.activation(out=dnb, in_=dnb, func=AF.Ln), deps=[td4])
            tl2 = em.a(lambda e: e.activation(out=dnb, in_=dnb, func=AF.Exp, scale=-1.0),
                       deps=[tl1])
            to = em.v(lambda e, c0=c0: e.tensor_mul(out=ob[:, c0:c0 + L2], in0=nnb,
                                                    in1=dnb), deps=[tl2, tn3] + bar)
            tca = em.v(lambda e, A=A, A2=A2: e.scalar_tensor_tensor(
                out=A2, in0=d1b[:, L2:L2 + 1], scalar=A[:, 0:1],
                in1=sab[:, L2 - 1:L2], op0=ALU.mult, op1=ALU.add),
                deps=[td1, tsa] + bar)
            tcb = em.v(lambda e, Bc=Bc, B2=B2: e.scalar_tensor_tensor(
                out=B2, in0=d1b[:, L2:L2 + 1], scalar=Bc[:, 0:1],
                in1=sbb[:, L2 - 1:L2], op0=ALU.mult, op1=ALU.add),
                deps=[td1, tsb] + bar)
            tcp = em.v(lambda e, P2=P2: e.tensor_copy(out=P2, in_=Mb), deps=[tM] + bar)
            bar = em.barrier()
        tst = em.dsync(wkv_o.ap()[rt * 128:(rt + 1) * 128, :], ob, sem="sqo", deps=bar)
        store_toks.append(tst)
        bar_prev = bar + [tst]
    em.finalize(store_toks)
    return nc


def launch2_inmaps(cfg, k_btC, v_btC, time_decay, time_first):
    T = cfg.T
    kr = np.ascontiguousarray(np.asarray(k_btC, np.float32).transpose(0, 2, 1).reshape(B * C, T))
    vr = np.ascontiguousarray(np.asarray(v_btC, np.float32).transpose(0, 2, 1).reshape(B * C, T))
    decf = np.tile(np.asarray(time_decay, np.float32), B)
    tff = np.tile(np.asarray(time_first, np.float32), B)
    in_maps = []
    for core in range(NCORES):
        rs = slice(core * 512, (core + 1) * 512)
        in_maps.append({
            "kin": kr[rs], "vin": vr[rs],
            "dec_in": np.ascontiguousarray(decf[rs].reshape(4, 128).T),
            "tf_in": np.ascontiguousarray(tff[rs].reshape(4, 128).T),
        })
    return in_maps


def launch2_unshard(cfg, outs):
    T = cfg.T
    full = np.concatenate(outs, axis=0)
    return full.reshape(B, C, T).transpose(0, 2, 1)



def emit_mix(em, st, srcbuf, src_stride, dstbuf, dst_stride, tm_sb, tm1_sb, hmbuf,
             ncols, SEGS, L, deps, t1b):
    toks = []
    for ch in range(NCH):
        tt1 = em.v(lambda e, ch=ch: e.tensor_scalar_mul(
            out=t1b[:, 0:ncols], in0=srcbuf[:, ch * src_stride: ch * src_stride + ncols],
            scalar1=tm_sb[:, ch:ch + 1]),
            deps=deps + [st.get("t1b_free")])
        tbig = em.v(lambda e, ch=ch: e.scalar_tensor_tensor(
            out=dstbuf[:, ch * dst_stride + 1: ch * dst_stride + ncols],
            in0=srcbuf[:, ch * src_stride: ch * src_stride + ncols - 1],
            scalar=tm1_sb[:, ch:ch + 1], in1=t1b[:, 1:ncols],
            op0=ALU.mult, op1=ALU.add), deps=[tt1] + deps)
        tfix = em.v(lambda e, ch=ch: e.scalar_tensor_tensor(
            out=dstbuf[:, ch * dst_stride: ch * dst_stride + ncols].rearrange(
                "p (s l) -> p s l", s=SEGS)[:, :, 0:1],
            in0=hmbuf[:, ch * SEGS:(ch + 1) * SEGS].rearrange("p (s o) -> p s o", o=1),
            scalar=tm1_sb[:, ch:ch + 1],
            in1=t1b[:, 0:ncols].rearrange("p (s l) -> p s l", s=SEGS)[:, :, 0:1],
            op0=ALU.mult, op1=ALU.add), deps=[tt1, tbig] + deps)
        st["t1b_free"] = tfix
        toks += [tbig, tfix]
    return toks



def build_launch3(cfg):
    nc = bass.Bass("TRN2", target_bir_lowering=False, debug=False)
    em = Em(nc)
    COLS = cfg.COLS

    wkv_in = nc.dram_tensor("wkv_in", [C, COLS], F32, kind="ExternalInput")
    sr_in = nc.dram_tensor("sr_in", [C, COLS], F32, kind="ExternalInput")
    xn_in = nc.dram_tensor("xn_in", [C, COLS], F32, kind="ExternalInput")
    owT = nc.dram_tensor("owT", [C, C], F32, kind="ExternalInput")
    x1_o = nc.dram_tensor("x1_o", [C, COLS], F32, kind="ExternalOutput")

    xnb = nc.alloc_sbuf_tensor("xnb", [128, NCH * COLS], F32)[:]
    rwkvb = nc.alloc_sbuf_tensor("rwkvb", [128, NCH * COLS], BF16)[:]
    wkvt = nc.alloc_sbuf_tensor("wkvt", [128, COLS], F32)[:]
    srt = nc.alloc_sbuf_tensor("srt", [128, COLS], F32)[:]
    wbuf = nc.alloc_sbuf_tensor("wbuf", [128, NCH * C], BF16)[:]
    evac = [nc.alloc_sbuf_tensor(f"evac{i}", [128, 512], F32)[:] for i in range(4)]
    pb = {f"m{i}": nc.alloc_psum_tensor(f"pbm{i}", [128, 512], F32)[:] for i in range(4)}

    txn = em.dsync(xnb.rearrange("p (a x) -> p a x", a=NCH), fm_dram_ap(xn_in, COLS),
                   sem="sqxn")
    twl = em.dgps(wbuf.rearrange("p (a x) -> p a x", a=NCH), fm_dram_ap(owT, C))
    tmul_prev = None
    mul_toks = []
    for ch in range(NCH):
        t1 = em.dsync(wkvt, wkv_in.ap()[ch * 128:(ch + 1) * 128, :], sem="sqw",
                      deps=[tmul_prev])
        t2 = em.dsync(srt, sr_in.ap()[ch * 128:(ch + 1) * 128, :], sem="sqs",
                      deps=[tmul_prev])
        tmul_prev = em.v(lambda e, ch=ch: e.tensor_mul(
            out=rwkvb[:, ch * COLS:(ch + 1) * COLS], in0=srt, in1=wkvt),
            deps=[t1, t2])
        mul_toks.append(tmul_prev)
    nct = len(col_tiles(COLS))
    store_toks = []
    evac_free = [None] * 4
    pbm_free = [None] * 4
    for cc in range(NCH):
        for ci, (c0, cw) in enumerate(col_tiles(COLS)):
            bank = pb[f"m{ci % 4}"]
            tk = None
            for dc in range(NCH):
                tk = em.t(lambda e, cc=cc, dc=dc, c0=c0, cw=cw, bank=bank: e.matmul(
                    out=bank[:, 0:cw],
                    lhsT=wbuf[:, dc * C + cc * 128:dc * C + (cc + 1) * 128],
                    rhs=rwkvb[:, dc * COLS + c0: dc * COLS + c0 + cw],
                    start=(dc == 0), stop=(dc == NCH - 1)),
                    deps=([twl, pbm_free[ci % 4]] + mul_toks) if dc == 0 else [])
            ei = (cc * nct + ci) % 4
            tev = em.v(lambda e, cc=cc, c0=c0, cw=cw, bank=bank, ei=ei: e.tensor_add(
                out=evac[ei][:, 0:cw], in0=bank[:, 0:cw],
                in1=xnb[:, cc * COLS + c0: cc * COLS + c0 + cw]),
                deps=[tk, txn, evac_free[ei]])
            pbm_free[ci % 4] = tev
            tst = em.dsync(x1_o.ap()[cc * 128:(cc + 1) * 128, c0:c0 + cw],
                           evac[ei][:, 0:cw], deps=[tev], sem=f"sst{ei}")
            evac_free[ei] = tst
            store_toks.append(tst)
    em.finalize(store_toks)
    return nc



def build_launch4a(cfg):
    nc = bass.Bass("TRN2", target_bir_lowering=False, debug=False)
    em = Em(nc)
    CA, COLS, SEGS, L = cfg.CA, cfg.COLS, cfg.SEGS, cfg.L

    x1in = nc.dram_tensor("x1in", [C, CA], F32, kind="ExternalInput")
    maskin = nc.dram_tensor("maskin", [1, SEGS], F32, kind="ExternalInput").ap()
    prm_names = ("ln2w", "ln2b", "ftmk", "ftmr")
    prm = {nm: nc.dram_tensor(nm, [128, NCH], F32, kind="ExternalInput").ap()
           for nm in prm_names}
    xk2_o = nc.dram_tensor("xk2_o", [C, COLS], BF16, kind="ExternalOutput")
    xr2_o = nc.dram_tensor("xr2_o", [C, COLS], BF16, kind="ExternalOutput")

    bufA = nc.alloc_sbuf_tensor("bufA", [128, NCH * CA], F32)[:]
    bufB = nc.alloc_sbuf_tensor("bufB", [128, NCH * CA], F32)[:]
    mixb = nc.alloc_sbuf_tensor("mixb", [128, NCH * COLS], BF16)[:]
    un0 = nc.alloc_sbuf_tensor("un0", [128, max(COLS, 2048)], F32)
    t1b = un0[:]
    hm = nc.alloc_sbuf_tensor("hm", [128, NCH * SEGS], F32)[:]
    maskB = nc.alloc_sbuf_tensor("maskB", [128, SEGS], F32)[:]
    mrow = nc.alloc_sbuf_tensor("mrow", [1, SEGS], F32)[:]
    tm1 = {nm: nc.alloc_sbuf_tensor(f"tm1_{nm}", [128, NCH], F32)[:]
           for nm in ("ftmk", "ftmr")}
    prms = {nm: nc.alloc_sbuf_tensor("s_" + nm, [128, NCH], F32)[:] for nm in prm_names}

    st = make_state(em, nc, rows=un0[:][0:1, 0:2048])
    pb = st["pb"]

    tx = em.dsync(bufA.rearrange("p (a x) -> p a x", a=NCH), fm_dram_ap(x1in, CA), sem="sqx")
    for nm in prm_names:
        em.dsync(prms[nm], prm[nm], sem="sqp")
    tp_all = em.dsync(mrow, maskin, sem="sqp")
    tm1_toks = {}
    for nm in ("ftmk", "ftmr"):
        tm1_toks[nm] = em.v(lambda e, nm=nm: e.tensor_scalar(
            out=tm1[nm], in0=prms[nm], scalar1=-1.0, scalar2=1.0,
            op0=ALU.mult, op1=ALU.add), deps=[tp_all])
    tmb0 = em.t(lambda e: e.matmul(out=pb["b2"][:, 0:SEGS], lhsT=st["ones_row"][0:1, :],
                                   rhs=mrow, start=True, stop=True),
                deps=[tp_all, st["dep_ones"]])
    tmb = em.v(lambda e: e.tensor_copy(out=maskB, in_=pb["b2"][:, 0:SEGS]), deps=[tmb0])
    st["pb_b2_free"] = tmb

    def a_ap(ch, c0, cw):
        return bufA[:, ch * CA + c0: ch * CA + c0 + cw]

    def b_ap(ch, c0, cw):
        return bufB[:, ch * CA + c0: ch * CA + c0 + cw]

    emit_ln(em, st, a_ap, b_ap, CA, mean=True,
            gamma=(prms["ln2w"], tp_all), beta=(prms["ln2b"], tp_all), src_dep=[tx])
    bar_h = em.barrier()
    hm_toks = []
    for ch in range(NCH):
        hm_toks.append(em.v(lambda e, ch=ch: e.tensor_mul(
            out=hm[:, ch * SEGS:(ch + 1) * SEGS],
            in0=bufB[:, ch * CA + COLS: ch * CA + CA], in1=maskB),
            deps=bar_h + [tmb]))
    bar_hm = em.barrier()
    store_toks = []
    for nm, out_t in (("ftmk", xk2_o), ("ftmr", xr2_o)):
        toks = emit_mix(em, st, bufB, CA, mixb, COLS, prms[nm], tm1[nm], hm,
                        COLS, SEGS, L, bar_hm + hm_toks + [tm1_toks[nm], tp_all]
                        + ([store_toks[-1]] if store_toks else []), t1b)
        tst = em.dsync(fm_dram_ap(out_t, COLS),
                       mixb.rearrange("p (a x) -> p a x", a=NCH),
                       deps=toks, sem="sqst" + nm)
        store_toks.append(tst)
    em.finalize(store_toks)
    return nc



def build_launch4b(cfg):
    nc = bass.Bass("TRN2", target_bir_lowering=False, debug=False)
    em = Em(nc)
    COLS = cfg.COLS
    FF = 4096
    NFC = FF // 128
    NFG = 4

    xk2_in = nc.dram_tensor("xk2_in", [C, COLS], BF16, kind="ExternalInput")
    xr2_in = nc.dram_tensor("xr2_in", [C, COLS], BF16, kind="ExternalInput")
    x1_in = nc.dram_tensor("x1_in", [C, COLS], F32, kind="ExternalInput")
    fkwT = nc.dram_tensor("fkwT", [C, FF], F32, kind="ExternalInput")
    fvwT = nc.dram_tensor("fvwT", [FF, C], F32, kind="ExternalInput")
    frwT = nc.dram_tensor("frwT", [C, C], F32, kind="ExternalInput")
    ao_o = nc.dram_tensor("ao_o", [C, COLS], F32, kind="ExternalOutput")

    xk2b = nc.alloc_sbuf_tensor("xk2b", [128, NCH * COLS], BF16)[:]
    xr2b = nc.alloc_sbuf_tensor("xr2b", [128, NCH * COLS], BF16)[:]
    kkb = nc.alloc_sbuf_tensor("kkb", [128, NFC * 512], BF16)[:]
    wst = nc.alloc_sbuf_tensor("wst", [128, 8192], BF16)[:]
    wst2 = nc.alloc_sbuf_tensor("wst2", [128, 8192], BF16)[:]
    x1ct = nc.alloc_sbuf_tensor("x1ct", [128, NCH * 512], F32)[:]
    sr2t = nc.alloc_sbuf_tensor("sr2t", [128, NCH * 512], F32)[:]
    at = [nc.alloc_sbuf_tensor(f"at{i}", [128, 512], F32)[:] for i in range(2)]
    evst = [nc.alloc_sbuf_tensor(f"evst{i}", [128, 512], F32)[:] for i in range(2)]
    pb = [nc.alloc_psum_tensor(f"pbk{i}", [128, 512], F32)[:] for i in range(8)]

    tk2 = em.dsync(xk2b.rearrange("p (a x) -> p a x", a=NCH), fm_dram_ap(xk2_in, COLS),
                   sem="sqk")
    tr2 = em.dsync(xr2b.rearrange("p (a x) -> p a x", a=NCH), fm_dram_ap(xr2_in, COLS),
                   sem="sqr")
    fkw3 = fkwT.ap().rearrange("(a p) f -> p a f", p=128)
    fvw3 = fvwT.ap().rearrange("(a p) c -> p a c", p=128)
    frw3 = frwT.ap().rearrange("(a p) c -> p a c", p=128)

    store_toks = []
    pfree = [None] * 8
    wst_free = None
    wst2_free = None
    x1_free = None
    sr2_free = [None] * NCH
    kk_free = [None] * NFC
    ev_free = [None] * 2
    at_free = [None] * 2
    for ci, (c0, cw) in enumerate(col_tiles(COLS)):
        tx1 = em.dsync(x1ct.rearrange("p (a x) -> p a x", a=NCH)[:, :, 0:cw],
                       fm_dram_ap(x1_in, COLS)[:, :, c0:c0 + cw], sem="sqx1",
                       deps=[x1_free])
        twr = em.dgps(wst.rearrange("p (a x) -> p a x", a=NCH), frw3, sem="gqw0",
                      deps=[wst_free])
        for cc in range(NCH):
            bi = cc % 4
            bank = pb[bi]
            tk = None
            for kc in range(NCH):
                tk = em.t(lambda e, cc=cc, kc=kc, c0=c0, cw=cw, bank=bank: e.matmul(
                    out=bank[:, 0:cw],
                    lhsT=wst[:, kc * 1024 + cc * 128:kc * 1024 + (cc + 1) * 128],
                    rhs=xr2b[:, kc * COLS + c0: kc * COLS + c0 + cw],
                    start=(kc == 0), stop=(kc == NCH - 1)),
                    deps=[twr, tr2, pfree[bi]] if kc == 0 else [])
            tev = em.a(lambda e, cc=cc, cw=cw, bank=bank: e.activation(
                out=sr2t[:, cc * 512:cc * 512 + cw], in_=bank[:, 0:cw],
                func=AF.Sigmoid), deps=[tk, sr2_free[cc]])
            pfree[bi] = tev
            sr2_free[cc] = tev
            wst_free = tk
        for fg in range(NFG):
            twk = em.dgps(wst.rearrange("p (a x) -> p a x", a=NCH),
                          fkw3[:, :, fg * 1024:(fg + 1) * 1024], sem="gqw1",
                          deps=[wst_free])
            for fi in range(NCH):
                fci = fg * 8 + fi
                bi = fi % 4
                bank = pb[bi]
                tk = None
                for kc in range(NCH):
                    tk = em.t(lambda e, fi=fi, kc=kc, c0=c0, cw=cw, bank=bank: e.matmul(
                        out=bank[:, 0:cw],
                        lhsT=wst[:, kc * 1024 + fi * 128:kc * 1024 + (fi + 1) * 128],
                        rhs=xk2b[:, kc * COLS + c0: kc * COLS + c0 + cw],
                        start=(kc == 0), stop=(kc == NCH - 1)),
                        deps=[twk, tk2, pfree[bi]] if kc == 0 else [])
                pp = fci % 2
                trl = em.v(lambda e, cw=cw, bank=bank, pp=pp: e.tensor_scalar_max(
                    out=at[pp][:, 0:cw], in0=bank[:, 0:cw], scalar1=0.0),
                    deps=[tk, at_free[pp]])
                tsqr = em.a(lambda e, fci=fci, cw=cw, pp=pp: e.activation(
                    out=kkb[:, fci * 512:fci * 512 + cw], in_=at[pp][:, 0:cw],
                    func=AF.Square), deps=[trl, kk_free[fci]])
                pfree[bi] = trl
                at_free[pp] = tsqr
                kk_free[fci] = tsqr
                wst_free = tk
        bar_kk = em.barrier()
        tk = None
        for fg in range(NFG):
            twv = em.dgps(wst2.rearrange("p (a x) -> p a x", a=NCH),
                          fvw3[:, fg * 8:(fg + 1) * 8, :], sem="gqw2",
                          deps=[wst2_free])
            for fi in range(NCH):
                fci = fg * 8 + fi
                for cc in range(NCH):
                    deps = []
                    if fci == 0:
                        deps = [twv, pfree[cc]] + bar_kk
                    elif fi == 0 and cc == 0:
                        deps = [twv]
                    tk = em.t(lambda e, fi=fi, cc=cc, fci=fci, c0=c0, cw=cw: e.matmul(
                        out=pb[cc][:, 0:cw],
                        lhsT=wst2[:, fi * 1024 + cc * 128:fi * 1024 + (cc + 1) * 128],
                        rhs=kkb[:, fci * 512:fci * 512 + cw],
                        start=(fci == 0), stop=(fci == NFC - 1)),
                        deps=deps)
            wst2_free = tk
        bar_kv = em.barrier()
        for cc in range(NCH):
            pp = cc % 2
            t1 = em.v(lambda e, cc=cc, cw=cw, pp=pp: e.tensor_mul(
                out=at[pp][:, 0:cw], in0=sr2t[:, cc * 512:cc * 512 + cw],
                in1=pb[cc][:, 0:cw]), deps=bar_kv + [at_free[pp]])
            t2 = em.v(lambda e, cc=cc, cw=cw, pp=pp: e.tensor_add(
                out=evst[pp][:, 0:cw], in0=at[pp][:, 0:cw],
                in1=x1ct[:, cc * 512:cc * 512 + cw]), deps=[t1, tx1, ev_free[pp]])
            at_free[pp] = t2
            pfree[cc] = t1
            sr2_free[cc] = t1
            tst = em.dsync(ao_o.ap()[cc * 128:(cc + 1) * 128, c0:c0 + cw],
                           evst[pp][:, 0:cw], deps=[t2], sem=f"sstao{pp}")
            ev_free[pp] = tst
            store_toks.append(tst)
        x1_free = store_toks[-1]
    em.finalize(store_toks)
    return nc



def emit_round_exact(em, tile, w, y_scr, b_scr, deps):
    t0 = em.v(lambda e: e.tensor_scalar_add(out=y_scr[:, 0:w], in0=tile[:, 0:w],
                                            scalar1=256.5), deps=deps)
    tr = em.v(lambda e: e.tensor_copy(out=tile[:, 0:w], in_=y_scr[:, 0:w]), deps=[t0])
    for k in (256.0, 128.0, 64.0, 32.0, 16.0, 8.0, 4.0, 2.0, 1.0):
        tb = em.v(lambda e, k=k: e.tensor_single_scalar(
            out=b_scr[:, 0:w], in_=tile[:, 0:w], scalar=k, op=ALU.is_ge), deps=[tr])
        tr = em.v(lambda e, k=k: e.scalar_tensor_tensor(
            out=tile[:, 0:w], in0=b_scr[:, 0:w], scalar=-k, in1=tile[:, 0:w],
            op0=ALU.mult, op1=ALU.add), deps=[tb])
    t1 = em.v(lambda e: e.scalar_tensor_tensor(
        out=tile[:, 0:w], in0=tile[:, 0:w], scalar=-1.0, in1=y_scr[:, 0:w],
        op0=ALU.mult, op1=ALU.add), deps=[tr])
    t2 = em.v(lambda e: e.tensor_scalar_add(out=tile[:, 0:w], in0=tile[:, 0:w],
                                            scalar1=-256.0), deps=[t1])
    return t2



def build_launch5(cfg, exact_round=True, debug=False, cut=None, skip_pre=False):
    nc = bass.Bass("TRN2", target_bir_lowering=False, debug=False)
    em = Em(nc)
    COLS = cfg.COLS
    FF = 4096
    NFC = FF // 128
    NFG = 4

    xn_in = nc.dram_tensor("xn_in", [C, COLS], F32, kind="ExternalInput")
    x_in = nc.dram_tensor("x_in", [C, COLS], F32, kind="ExternalInput")
    ao_in = nc.dram_tensor("ao_in", [C, COLS], F32, kind="ExternalInput")
    wgT = nc.dram_tensor("wgT", [C, FF], F32, kind="ExternalInput")
    wuT = nc.dram_tensor("wuT", [C, FF], F32, kind="ExternalInput")
    wdT = nc.dram_tensor("wdT", [FF, C], F32, kind="ExternalInput")
    xl0_o = nc.dram_tensor("xl0_o", [C, COLS], F32, kind="ExternalOutput")
    if debug:
        aq1_d = nc.dram_tensor("aq1_d", [C, 512], BF16, kind="ExternalOutput")
        gu_d = nc.dram_tensor("gu_d", [4096, 512], BF16, kind="ExternalOutput")
        aq2_d = nc.dram_tensor("aq2_d", [4096, 512], BF16, kind="ExternalOutput")
        rows_d = nc.dram_tensor("rows_d", [1, 2048], F32, kind="ExternalOutput")
        tern_d = nc.dram_tensor("tern_d", [128, 8192], BF16, kind="ExternalOutput")

    xnct = nc.alloc_sbuf_tensor("xnct", [128, NCH * 512], F32)[:]
    xct = nc.alloc_sbuf_tensor("xct", [128, NCH * 512], F32)[:]
    aoct = nc.alloc_sbuf_tensor("aoct", [128, NCH * 512], F32)[:]
    aq1b = nc.alloc_sbuf_tensor("aq1b", [128, NCH * 512], BF16)[:]
    gub = nc.alloc_sbuf_tensor("gub", [128, NFC * 512], BF16)[:]
    aq2b = nc.alloc_sbuf_tensor("aq2b", [128, NFC * 512], BF16)[:]
    tgb = nc.alloc_sbuf_tensor("tgb", [128, 8192], BF16)[:]
    tub = nc.alloc_sbuf_tensor("tub", [128, 8192], BF16)[:]
    tern_scr = {"g": nc.dram_tensor("tern_g", [C, 4096], BF16, kind="Internal"),
                "u": nc.dram_tensor("tern_u", [C, 4096], BF16, kind="Internal"),
                "d": nc.dram_tensor("tern_d_scr", [4096, C], BF16, kind="Internal")}
    t3scr = {k: tern_scr[k].ap().rearrange("(a p) f -> p a f", p=128) for k in tern_scr}
    gsb = nc.alloc_sbuf_tensor("gsb", [128, 512], F32)[:]
    usb = nc.alloc_sbuf_tensor("usb", [128, 512], F32)[:]
    mxb = nc.alloc_sbuf_tensor("mxb", [128, 512], F32)[:]
    mcols = nc.alloc_sbuf_tensor("mcols", [128, 4], F32)[:]
    at = [nc.alloc_sbuf_tensor(f"at{i}", [128, 512], F32)[:] for i in range(2)]
    ysc = nc.alloc_sbuf_tensor("ysc", [128, 512], F32)[:]
    bsc = nc.alloc_sbuf_tensor("bsc", [128, 512], F32)[:]
    evst = [nc.alloc_sbuf_tensor(f"evst{i}", [128, 512], F32)[:] for i in range(2)]
    rows = nc.alloc_sbuf_tensor("rows", [1, 2048], F32)[:]
    acc = nc.alloc_sbuf_tensor("acc", [128, 48], F32)[:]
    ones_col = nc.alloc_sbuf_tensor("ones_col", [128, 1], F32)[:]
    ones_row = nc.alloc_sbuf_tensor("ones_row", [1, 128], F32)[:]
    sc1 = nc.alloc_sbuf_tensor("sc1", [1, 8], F32)[:]
    thcol = {w: nc.alloc_sbuf_tensor(f"thcol_{w}", [128, 1], F32)[:] for w in "gud"}
    nthcol = {w: nc.alloc_sbuf_tensor(f"nthcol_{w}", [128, 1], F32)[:] for w in "gud"}
    ninv = {w: nc.alloc_sbuf_tensor(f"ninv_{w}", [128, 1], F32)[:] for w in "gud"}
    pb = [nc.alloc_psum_tensor(f"pbk{i}", [128, 512], F32)[:] for i in range(8)]
    stb = [nc.alloc_sbuf_tensor(f"stb{i}", [128, 2048], F32)[:] for i in range(2)]

    t_on1 = em.v(lambda e: e.memset(ones_col, 1.0))
    t_on2 = em.v(lambda e: e.memset(ones_row, 1.0))
    wmap = {"g": wgT, "u": wuT, "d": wdT}
    w3 = {"g": wgT.ap().rearrange("(a p) f -> p a f", p=128),
          "u": wuT.ap().rearrange("(a p) f -> p a f", p=128),
          "d": wdT.ap().rearrange("(a p) c -> p a c", p=128)}
    NPC = (NCH * 4096) // 2048

    def wpiece(view, wn, pc):
        if wn != "d":
            a = pc // 2
            fr = (pc % 2) * 2048
            return view[:, a:a + 1, fr:fr + 2048]
        return view[:, pc * 2:pc * 2 + 2, :]

    def spiece(flat_ap, wn):
        a = 1 if wn != "d" else 2
        return flat_ap.rearrange("p (a f) -> p a f", a=a)

    if skip_pre is True:
        tz = em.v(lambda e: e.memset(thcol["g"], 0.01))
        for wn in "ud":
            tz = em.v(lambda e, wn=wn: e.memset(thcol[wn], 0.01), deps=[tz])
        for wn in "gud":
            tz = em.v(lambda e, wn=wn: e.memset(nthcol[wn], -0.01), deps=[tz])
            tz = em.v(lambda e, wn=wn: e.memset(ninv[wn], -1.0), deps=[tz])
        bc_toks = {wn: [tz] for wn in "gud"}
        bar_tern = em.barrier()
    wst_free = [None, None]
    acc_toks = {}
    for wi, wn in enumerate("gud") if skip_pre is not True else []:
        piece_toks = []
        accp = [None] * 4
        for pc in range(NPC):
            pp = pc % 2
            tl = em.dsync(spiece(stb[pp], wn), wpiece(w3[wn], wn, pc),
                          sem=f"sqs{pp}", deps=[wst_free[pp]])
            ta = em.a(lambda e, wi=wi, pc=pc, pp=pp: e.activation(
                out=stb[pp], in_=stb[pp], func=AF.Abs,
                accum_out=acc[:, wi * 16 + pc:wi * 16 + pc + 1]),
                deps=[tl])
            wst_free[pp] = ta
            piece_toks.append(ta)
        acc_toks[wn] = piece_toks
    rows_toks = {}
    pbs_free = None
    for wi, wn in enumerate("gud") if skip_pre is not True else []:
        tm = em.t(lambda e, wi=wi: e.matmul(
            out=pb[6][0:1, 0:16], lhsT=ones_col[:, 0:1], rhs=acc[:, wi * 16:wi * 16 + 16],
            start=True, stop=True), deps=[t_on1, pbs_free] + acc_toks[wn])
        ts1 = em.v(lambda e, wi=wi: e.reduce_sum(
            out=rows[0:1, wi:wi + 1], in_=pb[6][0:1, 0:16], axis=AX.X), deps=[tm])
        pbs_free = ts1
        ts2 = em.v(lambda e, wi=wi: e.tensor_scalar(
            out=rows[0:1, 8 + wi:9 + wi], in0=rows[0:1, wi:wi + 1],
            scalar1=1.0 / (C * FF), scalar2=1e-5, op0=ALU.mult, op1=ALU.max),
            deps=[ts1])
        rows_toks[wn] = ts2
    bc_toks = {} if not skip_pre else bc_toks
    for wi, wn in enumerate("gud") if skip_pre is not True else []:
        t1 = em.v(lambda e, wi=wi: e.tensor_scalar_mul(
            out=rows[0:1, 16 + wi:17 + wi], in0=rows[0:1, 8 + wi:9 + wi], scalar1=0.5),
            deps=[rows_toks[wn]])
        t2 = em.t(lambda e, wi=wi: e.matmul(
            out=pb[6][:, 0:1], lhsT=ones_row[0:1, :], rhs=rows[0:1, 16 + wi:17 + wi],
            start=True, stop=True), deps=[t1, t_on2, pbs_free])
        t3 = em.v(lambda e, wn=wn: e.tensor_copy(out=thcol[wn], in_=pb[6][:, 0:1]),
                  deps=[t2])
        t4 = em.v(lambda e, wn=wn: e.tensor_scalar_mul(out=nthcol[wn], in0=thcol[wn],
                                                       scalar1=-1.0), deps=[t3])
        t5 = em.t(lambda e, wi=wi: e.matmul(
            out=pb[6][:, 2:3], lhsT=ones_row[0:1, :], rhs=rows[0:1, 8 + wi:9 + wi],
            start=True, stop=True), deps=[rows_toks[wn], t_on2, t3])
        t6 = em.v(lambda e, wn=wn: e.tensor_scalar_mul(out=ninv[wn], in0=pb[6][:, 2:3],
                                                       scalar1=-1.0), deps=[t5])
        pbs_free = t6
        bc_toks[wn] = [t3, t4, t6]
    bar_w = em.barrier()
    tern_store_toks = []
    tsc_free = [None, None]
    for wn in ("gud" if not skip_pre else ""):
        for pc in range(NPC):
            pp = pc % 2
            tl = em.dsync(stb[pp].rearrange("p (a f) -> p a f", a=1), wpiece(w3[wn], wn, pc),
                          sem=f"sqs{pp}", deps=[wst_free[pp]] + (bar_w if pc < 2 else []))
            tslice = tgb[:, pp * 2048:(pp + 1) * 2048]
            tt1 = em.v(lambda e, wn=wn, pp=pp, tslice=tslice: e.tensor_single_scalar(
                out=tslice, in_=stb[pp], scalar=thcol[wn][:, 0:1],
                op=ALU.is_ge), deps=[tl, tsc_free[pp]] + bc_toks[wn])
            tt2 = em.v(lambda e, wn=wn, pp=pp, tslice=tslice: e.scalar_tensor_tensor(
                out=tslice, in0=stb[pp], scalar=nthcol[wn][:, 0:1],
                in1=tslice, op0=ALU.is_le, op1=ALU.subtract), deps=[tt1])
            wst_free[pp] = tt2
            tsc = em.dsync(wpiece(t3scr[wn], wn, pc), spiece(tslice, wn),
                           sem=f"sqscw{pp}", deps=[tt2])
            tsc_free[pp] = tsc
            tern_store_toks.append(tsc)
    bar_tern = em.barrier() if skip_pre is not True else bar_tern
    if cut == "tern":
        em.finalize(bar_tern)
        return nc

    idf = nc.alloc_sbuf_tensor("idf", [128, 128], F32)[:]
    ti0 = em.v(lambda e: e.memset(idf, 1.0))
    ti1 = em.p(lambda e: e.affine_select(out=idf, in_=idf, pattern=[[-1, 128]],
                                         compare_op=ALU.is_equal, fill=0.0,
                                         base=0, channel_multiplier=1), deps=[ti0])
    if cut == "idf":
        em.finalize(em.barrier())
        return nc

    store_toks = []
    pfree = [None] * 8
    x_free = ao_free = xn_free = None
    aq1_free = None
    gu_free = [None] * NFC
    aq2_free = [None] * NFC
    at_free = [None, ti1]
    ev_free = [None] * 2
    tern_g_free = None
    tern_u_free = None
    mx_free = None
    mc_free = None
    ysc_free = bsc_free = None
    gs_free = us_free = None
    rows_free2 = None
    for ci, (c0, cw) in enumerate(col_tiles(COLS)):
        nct128 = (cw + 127) // 128
        txn = em.dsync(xnct.rearrange("p (a x) -> p a x", a=NCH)[:, :, 0:cw],
                       fm_dram_ap(xn_in, COLS)[:, :, c0:c0 + cw], sem="sqxn",
                       deps=[xn_free])
        tx = em.dsync(xct.rearrange("p (a x) -> p a x", a=NCH)[:, :, 0:cw],
                      fm_dram_ap(x_in, COLS)[:, :, c0:c0 + cw], sem="sqx",
                      deps=[x_free])
        tao = em.dsync(aoct.rearrange("p (a x) -> p a x", a=NCH)[:, :, 0:cw],
                       fm_dram_ap(ao_in, COLS)[:, :, c0:c0 + cw], sem="sqao",
                       deps=[ao_free])
        if cut == "loads":
            em.finalize(em.barrier())
            return nc
        tmx = em.a(lambda e, cw=cw: e.activation(
            out=mxb[:, 0:cw], in_=xnct[:, 0:cw], func=AF.Abs), deps=[txn, mx_free])
        for ch in range(1, NCH):
            tab = em.a(lambda e, ch=ch, cw=cw: e.activation(
                out=at[1][:, 0:cw], in_=xnct[:, ch * 512:ch * 512 + cw], func=AF.Abs),
                deps=[txn, at_free[1]])
            tmx = em.v(lambda e, cw=cw: e.tensor_max(
                out=mxb[:, 0:cw], in0=mxb[:, 0:cw], in1=at[1][:, 0:cw]),
                deps=[tmx, tab])
            at_free[1] = tmx
        if cut == "rowmax":
            em.finalize(em.barrier())
            return nc
        ttr = None
        for tt_i in range(nct128):
            tw = min(128, cw - tt_i * 128)
            t1 = em.t(lambda e, tt_i=tt_i, tw=tw: e.transpose(
                out=pb[7][0:tw, 0:128], in_=mxb[:, tt_i * 128:tt_i * 128 + tw],
                identity=idf), deps=[tmx, ti1, mc_free, ttr])
            ttr = em.v(lambda e, tt_i=tt_i, tw=tw: e.reduce_max(
                out=mcols[0:tw, tt_i:tt_i + 1], in_=pb[7][0:tw, 0:128], axis=AX.X),
                deps=[t1, ttr])
        if cut == "transp":
            em.finalize(em.barrier())
            return nc
        if nct128 == 1:
            trow = em.dsync(rows[0:1, 1024:1024 + cw], mcols[0:cw, 0:1],
                            sem="sqrow", deps=[ttr, rows_free2])
        else:
            for tt_i in range(nct128):
                trow = em.dsync(rows[0:1, 1024 + tt_i * 128:1024 + (tt_i + 1) * 128],
                                mcols[:, tt_i:tt_i + 1], sem="sqrow",
                                deps=[ttr, rows_free2] if tt_i == 0 else [])
        tr1 = em.v(lambda e, cw=cw: e.tensor_scalar_max(
            out=rows[0:1, 1024:1024 + cw], in0=rows[0:1, 1024:1024 + cw], scalar1=1e-5),
            deps=[trow])
        tr2 = em.v(lambda e, cw=cw: e.reciprocal(out=rows[0:1, 0:cw],
                                                 in_=rows[0:1, 1024:1024 + cw]),
                   deps=[tr1])
        tr3 = em.v(lambda e, cw=cw: e.tensor_scalar_mul(
            out=rows[0:1, 0:cw], in0=rows[0:1, 0:cw], scalar1=127.0), deps=[tr2])
        tr4 = em.v(lambda e, cw=cw: e.tensor_scalar_mul(
            out=rows[0:1, 512:512 + cw], in0=rows[0:1, 1024:1024 + cw],
            scalar1=1.0 / 127.0), deps=[tr1, tr2])
        if cut == "rows":
            em.finalize(em.barrier())
            return nc
        tb1 = em.t(lambda e, cw=cw: e.matmul(out=pb[6][:, 0:cw], lhsT=ones_row[0:1, :],
                                             rhs=rows[0:1, 0:cw], start=True, stop=True),
                   deps=[tr3, t_on2, pbs_free, pfree[6]])
        tb2 = em.t(lambda e, cw=cw: e.matmul(out=pb[7][:, 0:cw], lhsT=ones_row[0:1, :],
                                             rhs=rows[0:1, 512:512 + cw],
                                             start=True, stop=True),
                   deps=[tr4, t_on2, ttr, pfree[7]])
        rows_free2 = tb2
        pbs_free = tb2
        if cut == "bcast":
            em.finalize(em.barrier())
            return nc
        taq_last = None
        for ch in range(NCH):
            tq1 = em.v(lambda e, ch=ch, cw=cw: e.tensor_mul(
                out=at[0][:, 0:cw], in0=xnct[:, ch * 512:ch * 512 + cw],
                in1=pb[6][:, 0:cw]), deps=[tb1, txn, at_free[0]])
            trd = emit_round_exact(em, at[0], cw, ysc, bsc, [tq1, ysc_free, bsc_free])
            taq = em.v(lambda e, ch=ch, cw=cw: e.tensor_mul(
                out=aq1b[:, ch * 512:ch * 512 + cw], in0=at[0][:, 0:cw],
                in1=pb[7][:, 0:cw]), deps=[trd, tb2, aq1_free])
            at_free[0] = taq
            ysc_free = bsc_free = trd
            taq_last = taq
        aq1_free = taq_last
        pfree[6] = taq_last
        pfree[7] = taq_last
        bar_aq1 = em.barrier()
        if cut == "aq1":
            em.finalize(bar_aq1)
            return nc
        for fg in range(NFG):
            ttg2 = em.dgps(tgb.rearrange("p (a x) -> p a x", a=NCH),
                           t3scr["g"][:, :, fg * 1024:(fg + 1) * 1024], sem="gqg",
                           deps=[tern_g_free] + (bar_aq1 + bar_tern if fg == 0 else []))
            ttu2 = em.dgps(tub.rearrange("p (a x) -> p a x", a=NCH),
                           t3scr["u"][:, :, fg * 1024:(fg + 1) * 1024], sem="gqu",
                           deps=[tern_u_free] + (bar_aq1 + bar_tern if fg == 0 else []))
            for fi in range(NCH):
                fci = fg * 8 + fi
                bg, bu = pb[fi % 3], pb[3 + fi % 3]
                tkg = tku = None
                for kc in range(NCH):
                    tkg = em.t(lambda e, fi=fi, kc=kc, cw=cw, bg=bg: e.matmul(
                        out=bg[:, 0:cw],
                        lhsT=tgb[:, kc * 1024 + fi * 128:kc * 1024 + (fi + 1) * 128],
                        rhs=aq1b[:, kc * 512:kc * 512 + cw],
                        start=(kc == 0), stop=(kc == NCH - 1)),
                        deps=[ttg2, pfree[fi % 3]] + bar_aq1 if kc == 0 else [])
                for kc in range(NCH):
                    tku = em.t(lambda e, fi=fi, kc=kc, cw=cw, bu=bu: e.matmul(
                        out=bu[:, 0:cw],
                        lhsT=tub[:, kc * 1024 + fi * 128:kc * 1024 + (fi + 1) * 128],
                        rhs=aq1b[:, kc * 512:kc * 512 + cw],
                        start=(kc == 0), stop=(kc == NCH - 1)),
                        deps=[ttu2, pfree[3 + fi % 3]] + bar_aq1 if kc == 0 else [])
                tg = em.a(lambda e, cw=cw, bg=bg: e.activation(
                    out=gsb[:, 0:cw], in_=bg[:, 0:cw], func=AF.Silu,
                    scale=ninv["g"][:, 0:1]), deps=[tkg, gs_free] + bc_toks["g"])
                tu = em.a(lambda e, cw=cw, bu=bu: e.activation(
                    out=usb[:, 0:cw], in_=bu[:, 0:cw], func=AF.Copy,
                    scale=ninv["u"][:, 0:1]), deps=[tku, us_free] + bc_toks["u"])
                tgu = em.v(lambda e, fci=fci, cw=cw: e.tensor_mul(
                    out=gub[:, fci * 512:fci * 512 + cw], in0=gsb[:, 0:cw],
                    in1=usb[:, 0:cw]), deps=[tg, tu, gu_free[fci]])
                pfree[fi % 3] = tg
                pfree[3 + fi % 3] = tu
                gs_free = tgu
                us_free = tgu
                gu_free[fci] = tgu
            tern_g_free = tkg
            tern_u_free = tku
        bar_gu = em.barrier()
        if cut == "gu":
            em.finalize(bar_gu)
            return nc
        tmx2 = em.a(lambda e, cw=cw: e.activation(
            out=mxb[:, 0:cw], in_=gub[:, 0:cw], func=AF.Abs), deps=bar_gu + [tmx])
        for fci in range(1, NFC):
            tab = em.a(lambda e, fci=fci, cw=cw: e.activation(
                out=at[1][:, 0:cw], in_=gub[:, fci * 512:fci * 512 + cw], func=AF.Abs),
                deps=bar_gu + [at_free[1]])
            tmx2 = em.v(lambda e, cw=cw: e.tensor_max(
                out=mxb[:, 0:cw], in0=mxb[:, 0:cw], in1=at[1][:, 0:cw]),
                deps=[tmx2, tab])
            at_free[1] = tmx2
        ttr2 = None
        for tt_i in range(nct128):
            tw = min(128, cw - tt_i * 128)
            t1 = em.t(lambda e, tt_i=tt_i, tw=tw: e.transpose(
                out=pb[7][0:tw, 0:128], in_=mxb[:, tt_i * 128:tt_i * 128 + tw],
                identity=idf), deps=[tmx2, ti1, pfree[7], ttr2])
            ttr2 = em.v(lambda e, tt_i=tt_i, tw=tw: e.reduce_max(
                out=mcols[0:tw, tt_i:tt_i + 1], in_=pb[7][0:tw, 0:128], axis=AX.X),
                deps=[t1, ttr2 or trow])
            pfree[7] = ttr2
        if nct128 == 1:
            trow2 = em.dsync(rows[0:1, 1024:1024 + cw], mcols[0:cw, 0:1],
                             sem="sqrow", deps=[ttr2, tb2])
        else:
            for tt_i in range(nct128):
                trow2 = em.dsync(rows[0:1, 1024 + tt_i * 128:1024 + (tt_i + 1) * 128],
                                 mcols[:, tt_i:tt_i + 1], sem="sqrow",
                                 deps=[ttr2, tb2] if tt_i == 0 else [])
        tr1 = em.v(lambda e, cw=cw: e.tensor_scalar_max(
            out=rows[0:1, 1024:1024 + cw], in0=rows[0:1, 1024:1024 + cw], scalar1=1e-5),
            deps=[trow2])
        tr2b = em.v(lambda e, cw=cw: e.reciprocal(out=rows[0:1, 0:cw],
                                                  in_=rows[0:1, 1024:1024 + cw]),
                    deps=[tr1])
        tr3b = em.v(lambda e, cw=cw: e.tensor_scalar_mul(
            out=rows[0:1, 0:cw], in0=rows[0:1, 0:cw], scalar1=127.0), deps=[tr2b])
        tr4b = em.v(lambda e, cw=cw: e.tensor_scalar_mul(
            out=rows[0:1, 512:512 + cw], in0=rows[0:1, 1024:1024 + cw],
            scalar1=1.0 / 127.0), deps=[tr1, tr2b])
        tb1b = em.t(lambda e, cw=cw: e.matmul(out=pb[6][:, 0:cw], lhsT=ones_row[0:1, :],
                                              rhs=rows[0:1, 0:cw], start=True, stop=True),
                    deps=[tr3b, t_on2, pfree[6]])
        tb2b = em.t(lambda e, cw=cw: e.matmul(out=pb[7][:, 0:cw], lhsT=ones_row[0:1, :],
                                              rhs=rows[0:1, 512:512 + cw],
                                              start=True, stop=True),
                    deps=[tr4b, t_on2, pfree[7]])
        taq2_last = None
        for fci in range(NFC):
            tq1 = em.v(lambda e, fci=fci, cw=cw: e.tensor_mul(
                out=at[0][:, 0:cw], in0=gub[:, fci * 512:fci * 512 + cw],
                in1=pb[6][:, 0:cw]), deps=[tb1b, at_free[0]] + bar_gu)
            trd = emit_round_exact(em, at[0], cw, ysc, bsc, [tq1, ysc_free, bsc_free])
            taq = em.v(lambda e, fci=fci, cw=cw: e.tensor_mul(
                out=aq2b[:, fci * 512:fci * 512 + cw], in0=at[0][:, 0:cw],
                in1=pb[7][:, 0:cw]), deps=[trd, tb2b, aq2_free[fci]])
            at_free[0] = taq
            ysc_free = bsc_free = trd
            aq2_free[fci] = taq
            taq2_last = taq
        pfree[6] = taq2_last
        pfree[7] = taq2_last
        bar_aq2 = em.barrier()
        if cut == "aq2":
            em.finalize(bar_aq2)
            return nc
        if debug and ci == 0:
            store_toks.append(em.dsync(fm_dram_ap(aq1_d, 512)[:, :, 0:cw],
                                       aq1b.rearrange("p (a x) -> p a x", a=NCH)[:, :, 0:cw],
                                       deps=bar_aq2, sem="sqdbg"))
            store_toks.append(em.dsync(gu_d.ap().rearrange("(a p) x -> p a x", p=128)[:, :, 0:cw],
                                       gub.rearrange("p (a x) -> p a x", a=NFC)[:, :, 0:cw],
                                       deps=bar_aq2, sem="sqdbg2"))
            store_toks.append(em.dsync(aq2_d.ap().rearrange("(a p) x -> p a x", p=128)[:, :, 0:cw],
                                       aq2b.rearrange("p (a x) -> p a x", a=NFC)[:, :, 0:cw],
                                       deps=bar_aq2, sem="sqdbg3"))
            store_toks.append(em.dsync(tern_d.ap(), tub, deps=bar_aq2, sem="sqdbg5"))
        tdl = None
        for fg in range(NFG):
            ttd2 = em.dgps(tgb.rearrange("p (a x) -> p a x", a=NCH),
                           t3scr["d"][:, fg * 8:(fg + 1) * 8, :], sem="gqd",
                           deps=[tern_g_free] + (bar_aq2 if fg == 0 else [])) 
            for fi in range(NCH):
                fci = fg * 8 + fi
                for cc in range(NCH):
                    deps = []
                    if fci == 0:
                        deps = [ttd2, pfree[cc]] + bar_aq2
                    elif fi == 0 and cc == 0:
                        deps = [ttd2]
                    tdl = em.t(lambda e, fi=fi, cc=cc, fci=fci, cw=cw: e.matmul(
                        out=pb[cc][:, 0:cw],
                        lhsT=tgb[:, fi * 1024 + cc * 128:fi * 1024 + (cc + 1) * 128],
                        rhs=aq2b[:, fci * 512:fci * 512 + cw],
                        start=(fci == 0), stop=(fci == NFC - 1)),
                        deps=deps)
            tern_g_free = tdl
        bar_dn = em.barrier()
        for cc in range(NCH):
            pp = cc % 2
            t1 = em.a(lambda e, cc=cc, cw=cw, pp=pp: e.activation(
                out=at[pp][:, 0:cw], in_=pb[cc][:, 0:cw], func=AF.Copy,
                scale=ninv["d"][:, 0:1]), deps=bar_dn + [at_free[pp]] + bc_toks["d"])
            t2 = em.v(lambda e, cc=cc, cw=cw, pp=pp: e.tensor_add(
                out=at[pp][:, 0:cw], in0=at[pp][:, 0:cw],
                in1=xct[:, cc * 512:cc * 512 + cw]), deps=[t1, tx])
            t3 = em.v(lambda e, cc=cc, cw=cw, pp=pp: e.tensor_add(
                out=evst[pp][:, 0:cw], in0=at[pp][:, 0:cw],
                in1=aoct[:, cc * 512:cc * 512 + cw]), deps=[t2, tao, ev_free[pp]])
            at_free[pp] = t3
            pfree[cc] = t3
            tst = em.dsync(xl0_o.ap()[cc * 128:(cc + 1) * 128, c0:c0 + cw],
                           evst[pp][:, 0:cw], deps=[t3], sem=f"sstx{pp}")
            ev_free[pp] = tst
            store_toks.append(tst)
        x_free = store_toks[-1]
        ao_free = store_toks[-1]
        xn_free = taq_last
        mx_free = tmx2
        mc_free = ttr2
    em.finalize(store_toks)
    return nc



def build_launch6(cfg):
    nc = bass.Bass("TRN2", target_bir_lowering=False, debug=False)
    em = Em(nc)
    COLS = cfg.COLS
    NL = 4

    xl0_in = nc.dram_tensor("xl0_in", [C, COLS], F32, kind="ExternalInput")
    wcd_in = nc.dram_tensor("wcd_in", [C, 512], F32, kind="ExternalInput")
    wqu_in = nc.dram_tensor("wqu_in", [512, C], F32, kind="ExternalInput")
    wkv_in = nc.dram_tensor("wkv_in", [512, 128], F32, kind="ExternalInput")
    xn1_o = nc.dram_tensor("xn1_o", [C, COLS], BF16, kind="ExternalOutput")
    q_o = nc.dram_tensor("q_o", [C, COLS], BF16, kind="ExternalOutput")
    k_o = nc.dram_tensor("k_o", [64, COLS], BF16, kind="ExternalOutput")
    vt_o = nc.dram_tensor("vt_o", [COLS, 64], BF16, kind="ExternalOutput")

    xb = nc.alloc_sbuf_tensor("xb", [128, NCH * COLS], F32)[:]
    xn1b = nc.alloc_sbuf_tensor("xn1b", [128, NCH * COLS], BF16)[:]
    latb = nc.alloc_sbuf_tensor("latb", [128, NL * COLS], BF16)[:]
    wcd = nc.alloc_sbuf_tensor("wcd", [128, NCH * 512], BF16)[:]
    wqu = nc.alloc_sbuf_tensor("wqu", [128, NL * C], BF16)[:]
    wkv = nc.alloc_sbuf_tensor("wkv", [128, NL * 128], BF16)[:]
    un0 = nc.alloc_sbuf_tensor("un0", [128, 2048], F32)
    evac = [nc.alloc_sbuf_tensor(f"evac{i}", [128, 512], F32)[:] for i in range(4)]
    evacB = [nc.alloc_sbuf_tensor(f"evacB{i}", [128, 512], BF16)[:] for i in range(4)]
    st = make_state(em, nc, rows=un0[:][0:1, 0:2048])
    pb = st["pb"]

    tx = em.dsync(xb.rearrange("p (a x) -> p a x", a=NCH), fm_dram_ap(xl0_in, COLS),
                  sem="sqx")
    twc = em.dgps(wcd.rearrange("p (a x) -> p a x", a=NCH),
                  wcd_in.ap().rearrange("(a p) x -> p a x", p=128), sem="gqw")
    twq = em.dgps(wqu.rearrange("p (a x) -> p a x", a=NL),
                  wqu_in.ap().rearrange("(a p) x -> p a x", p=128), sem="gqw2")
    twk = em.dgps(wkv.rearrange("p (a x) -> p a x", a=NL),
                  wkv_in.ap().rearrange("(a p) x -> p a x", p=128), sem="gqw3")

    def x_ap(ch, c0, cw):
        return xb[:, ch * COLS + c0: ch * COLS + c0 + cw]

    def n_ap(ch, c0, cw):
        return xn1b[:, ch * COLS + c0: ch * COLS + c0 + cw]

    emit_ln(em, st, x_ap, n_ap, COLS, mean=True, src_dep=[tx])
    bar_ln = em.barrier()
    t_st1 = em.dsync(fm_dram_ap(xn1_o, COLS), xn1b.rearrange("p (a x) -> p a x", a=NCH),
                     deps=bar_ln, sem="sqst1")
    store_toks = [t_st1]
    nct = len(col_tiles(COLS))
    pfree = {}
    evac_free = [None] * 4
    lat_toks = []
    for lc in range(NL):
        for ci, (c0, cw) in enumerate(col_tiles(COLS)):
            bank = pb[f"m{ci % 4}"]
            tk = None
            for kc in range(NCH):
                tk = em.t(lambda e, lc=lc, kc=kc, c0=c0, cw=cw, bank=bank: e.matmul(
                    out=bank[:, 0:cw],
                    lhsT=wcd[:, kc * 512 + lc * 128:kc * 512 + (lc + 1) * 128],
                    rhs=xn1b[:, kc * COLS + c0: kc * COLS + c0 + cw],
                    start=(kc == 0), stop=(kc == NCH - 1)),
                    deps=([twc, pfree.get(ci % 4)] + bar_ln) if kc == 0 else [])
            tev = em.v(lambda e, lc=lc, c0=c0, cw=cw, bank=bank: e.tensor_copy(
                out=latb[:, lc * COLS + c0: lc * COLS + c0 + cw], in_=bank[:, 0:cw]),
                deps=[tk])
            pfree[ci % 4] = tev
            lat_toks.append(tev)
    bar_lat = em.barrier()
    for qc in range(NCH):
        for ci, (c0, cw) in enumerate(col_tiles(COLS)):
            bank = pb[f"m{ci % 4}"]
            tk = None
            for lc in range(NL):
                tk = em.t(lambda e, qc=qc, lc=lc, c0=c0, cw=cw, bank=bank: e.matmul(
                    out=bank[:, 0:cw],
                    lhsT=wqu[:, lc * C + qc * 128:lc * C + (qc + 1) * 128],
                    rhs=latb[:, lc * COLS + c0: lc * COLS + c0 + cw],
                    start=(lc == 0), stop=(lc == NL - 1)),
                    deps=([twq, pfree.get(ci % 4)] + bar_lat) if lc == 0 else [])
            ei = (qc * nct + ci) % 4
            tev = em.a(lambda e, cw=cw, bank=bank, ei=ei: e.activation(
                out=evac[ei][:, 0:cw].bitcast(BF16)[:, 0:cw], in_=bank[:, 0:cw],
                func=AF.Copy, scale=0.125), deps=[tk, evac_free[ei]])
            pfree[ci % 4] = tev
            tst = em.dsync(q_o.ap()[qc * 128:(qc + 1) * 128, c0:c0 + cw],
                           evac[ei][:, 0:cw].bitcast(BF16)[:, 0:cw],
                           deps=[tev], sem=f"sst{ei}")
            evac_free[ei] = tst
            store_toks.append(tst)
    for ci, (c0, cw) in enumerate(col_tiles(COLS)):
        bank = pb[f"m{ci % 4}"]
        tk = None
        for lc in range(NL):
            tk = em.t(lambda e, lc=lc, c0=c0, cw=cw, bank=bank: e.matmul(
                out=bank[:, 0:cw],
                lhsT=wkv[:, lc * 128:(lc + 1) * 128],
                rhs=latb[:, lc * COLS + c0: lc * COLS + c0 + cw],
                start=(lc == 0), stop=(lc == NL - 1)),
                deps=([twk, pfree.get(ci % 4)] + bar_lat) if lc == 0 else [])
        ei = ci % 4
        tev = em.v(lambda e, cw=cw, bank=bank, ei=ei: e.tensor_copy(
            out=evacB[ei][:, 0:cw], in_=bank[:, 0:cw]),
            deps=[tk, evac_free[ei]])
        pfree[ci % 4] = tev
        tst = em.dsync(k_o.ap()[0:64, c0:c0 + cw],
                       evacB[ei][0:64, 0:cw],
                       deps=[tev], sem=f"sst{ei}")
        evac_free[ei] = tst
        store_toks.append(tst)
    TW = min(128, COLS)
    ncc = COLS // TW
    for ti in range(ncc):
        bank = pb[f"m{ti % 4}"]
        tk = None
        for lc in range(NL):
            tk = em.t(lambda e, lc=lc, ti=ti, bank=bank: e.matmul(
                out=bank[0:TW, 0:64],
                lhsT=latb[:, lc * COLS + ti * TW:lc * COLS + (ti + 1) * TW],
                rhs=wkv[:, lc * 128 + 64:lc * 128 + 128],
                start=(lc == 0), stop=(lc == NL - 1)),
                deps=([twk, pfree.get(ti % 4)] + bar_lat) if lc == 0 else [])
        ei = ti % 4
        tev = em.v(lambda e, ti=ti, bank=bank, ei=ei: e.tensor_copy(
            out=evacB[ei][0:TW, 0:64], in_=bank[0:TW, 0:64]),
            deps=[tk, evac_free[ei]])
        pfree[ti % 4] = tev
        tst = em.dsync(vt_o.ap()[ti * TW:(ti + 1) * TW, :],
                       evacB[ei][0:TW, 0:64],
                       deps=[tev], sem=f"sst{ei}")
        evac_free[ei] = tst
        store_toks.append(tst)
    em.finalize(store_toks)
    return nc



def build_launch7(cfg):
    nc = bass.Bass("TRN2", target_bir_lowering=False, debug=False)
    em = Em(nc)
    T = cfg.T
    BT = B * T
    KW = min(512, T)
    QW2 = min(512, T)
    SQ = QW2 // 128
    NQ5 = T // QW2

    qA_in = nc.dram_tensor("qA_in", [64, BT], BF16, kind="ExternalInput")
    qB_in = nc.dram_tensor("qB_in", [64, BT], BF16, kind="ExternalInput")
    kf_in = nc.dram_tensor("kf_in", [64, BT], BF16, kind="ExternalInput")
    vt_in = nc.dram_tensor("vt_in", [BT, 64], BF16, kind="ExternalInput")
    m1_in = nc.dram_tensor("m1_in", [128, 2 * KW], F32, kind="ExternalInput")
    m2_in = nc.dram_tensor("m2_in", [128, 128 * (SQ - 1) + QW2], F32, kind="ExternalInput")
    y_o = nc.dram_tensor("y_o", [128, BT], BF16, kind="ExternalOutput")

    qs = [nc.alloc_sbuf_tensor("qA", [64, BT], BF16)[:],
          nc.alloc_sbuf_tensor("qB", [64, BT], BF16)[:]]
    kb = nc.alloc_sbuf_tensor("kb", [64, BT], BF16)[:]
    vt_sb = nc.alloc_sbuf_tensor("vt_sb", [128, (BT // 128) * 64], BF16)[:]
    ybuf = [nc.alloc_sbuf_tensor(f"ybuf{i}", [64, BT], BF16)[:] for i in range(2)]
    m1b = nc.alloc_sbuf_tensor("m1b", [128, 2 * KW], F32)[:]
    m2b = nc.alloc_sbuf_tensor("m2b", [128, 128 * (SQ - 1) + QW2], F32)[:]
    pexp = [nc.alloc_sbuf_tensor(f"pexp{i}", [128, QW2], BF16)[:] for i in range(2)]
    mcol = nc.alloc_sbuf_tensor("mcol", [128, 1], F32)[:]
    mtmp = nc.alloc_sbuf_tensor("mtmp", [128, 1], F32)[:]
    negm = nc.alloc_sbuf_tensor("negm", [1, QW2], BF16)[:]
    recr = nc.alloc_sbuf_tensor("recr", [1, QW2], F32)[:]
    ysc = nc.alloc_sbuf_tensor("ysc", [64, QW2], F32)[:]
    idf = nc.alloc_sbuf_tensor("idf", [128, 128], F32)[:]
    ones_row = nc.alloc_sbuf_tensor("ones_row", [1, 128], BF16)[:]
    ones_row32 = nc.alloc_sbuf_tensor("ones_row32", [1, 128], F32)[:]
    ones_col = nc.alloc_sbuf_tensor("ones_col", [128, 1], BF16)[:]
    pb = [nc.alloc_psum_tensor(f"pb{i}", [128, 512], F32)[:] for i in range(8)]

    tqa_ld = em.dsync(qs[0], qA_in.ap(), sem="sqqa")
    tqb_ld = em.dsync(qs[1], qB_in.ap(), sem="sqqb")
    tk = em.dsync(kb, kf_in.ap(), sem="sqk")
    tv = em.dsync(vt_sb.rearrange("p (n d) -> p n d", d=64),
                  vt_in.ap().rearrange("(n p) d -> p n d", p=128), sem="sqv")
    tm1 = em.dsync(m1b, m1_in.ap(), sem="sqm")
    tm2 = em.dsync(m2b, m2_in.ap(), sem="sqm2")
    ti0 = em.v(lambda e: e.memset(idf, 1.0))
    ti1 = em.p(lambda e: e.affine_select(out=idf, in_=idf, pattern=[[-1, 128]],
                                         compare_op=ALU.is_equal, fill=0.0,
                                         base=0, channel_multiplier=1), deps=[ti0])
    tor = em.v(lambda e: e.memset(ones_row, 1.0))
    tor32 = em.v(lambda e: e.memset(ones_row32, 1.0))
    toc = em.v(lambda e: e.memset(ones_col, 1.0))
    base = [tqa_ld, tqb_ld, tk, tv, tm1, tm2, ti1, tor, tor32, toc]

    y_toks = []
    p0_free = [None, None]
    p15_free = [None, None]
    p2_free = None
    p3_free = None
    p4_free = None
    p6_free = None
    mc_free = None
    mt_free = None
    neg_free = None
    pex_free = [None, None]
    rec_free = None
    ysc_free = None
    for hl in range(2):
        qh = qs[hl]
        for b in range(B):
            kb0 = b * T
            for q5 in range(NQ5):
                qgb = q5 * QW2
                for sq in range(SQ):
                    qb128 = qgb + sq * 128
                    nkt = qb128 // KW + 1
                    tred = None
                    for kt in range(nkt):
                        kp = kt % 2
                        bank0 = pb[0] if kp == 0 else pb[7]
                        tkm = em.t(lambda e, qh=qh, kt=kt, qb128=qb128, kb0=kb0,
                                   bank0=bank0: e.matmul(
                            out=bank0[:, 0:KW],
                            lhsT=qh[:, kb0 + qb128: kb0 + qb128 + 128],
                            rhs=kb[:, kb0 + kt * KW: kb0 + (kt + 1) * KW],
                            start=True, stop=True),
                            deps=(base + [p0_free[kp]]))
                        if kt == nkt - 1:
                            off = KW - (qb128 - kt * KW)
                            tkm = em.v(lambda e, off=off, bank0=bank0: e.tensor_add(
                                out=bank0[:, 0:KW], in0=bank0[:, 0:KW],
                                in1=m1b[:, off:off + KW]), deps=[tkm, tm1])
                        if kt == 0:
                            tred = em.v(lambda e, bank0=bank0: e.tensor_reduce(
                                out=mcol, in_=bank0[:, 0:KW], axis=AX.X, op=ALU.max),
                                deps=[tkm, mc_free])
                        else:
                            tr2 = em.v(lambda e, bank0=bank0: e.tensor_reduce(
                                out=mtmp, in_=bank0[:, 0:KW], axis=AX.X, op=ALU.max),
                                deps=[tkm, mt_free])
                            tred = em.v(lambda e: e.tensor_max(
                                out=mcol, in0=mcol, in1=mtmp), deps=[tred, tr2])
                            mt_free = tred
                        p0_free[kp] = tred
                    ttr = em.t(lambda e: e.transpose(
                        out=pb[6][0:1, 0:128], in_=mcol[:, 0:1], identity=idf),
                        deps=[tred, ti1, p6_free])
                    tng = em.a(lambda e, sq=sq: e.activation(
                        out=negm[0:1, sq * 128:(sq + 1) * 128], in_=pb[6][0:1, 0:128],
                        func=AF.Copy, scale=-1.0), deps=[ttr, neg_free])
                    p6_free = tng
                    mc_free = ttr
                nk2 = (qgb + QW2) // 128
                tsum = tpv = None
                tk2last = None
                for kt2 in range(nk2):
                    pp = kt2 % 2
                    bank = pb[1 + 4 * pp]
                    tk2 = em.t(lambda e, qh=qh, kt2=kt2, kb0=kb0, qgb=qgb, bank=bank:
                               e.matmul(
                        out=bank[:, 0:QW2],
                        lhsT=kb[:, kb0 + kt2 * 128: kb0 + (kt2 + 1) * 128],
                        rhs=qh[:, kb0 + qgb: kb0 + qgb + QW2],
                        start=True, stop=False), deps=base + [p15_free[pp]])
                    tk2 = em.t(lambda e, bank=bank: e.matmul(
                        out=bank[:, 0:QW2], lhsT=ones_row[0:1, 0:128],
                        rhs=negm[0:1, 0:QW2], start=False, stop=True),
                        deps=[tng, tor])
                    r = kt2 - (qgb // 128)
                    if r >= 0:
                        off = 128 * (SQ - 1) - 128 * r
                        tk2 = em.v(lambda e, off=off, bank=bank: e.tensor_add(
                            out=bank[:, 0:QW2], in0=bank[:, 0:QW2],
                            in1=m2b[:, off:off + QW2]), deps=[tk2, tm2])
                    tex = em.a(lambda e, pp=pp, bank=bank: e.activation(
                        out=pexp[pp][:, 0:QW2], in_=bank[:, 0:QW2], func=AF.Exp),
                        deps=[tk2, pex_free[pp]])
                    p15_free[pp] = tex
                    tsum = em.t(lambda e, pp=pp, first=(kt2 == 0), last=(kt2 == nk2 - 1):
                                e.matmul(
                        out=pb[2][0:1, 0:QW2], lhsT=ones_col[:, 0:1],
                        rhs=pexp[pp][:, 0:QW2], start=first, stop=last),
                        deps=([tex, toc, p2_free] if kt2 == 0 else [tex]))
                    vti = (kb0 + kt2 * 128) // 128
                    tpv = em.t(lambda e, pp=pp, vti=vti, first=(kt2 == 0),
                               last=(kt2 == nk2 - 1): e.matmul(
                        out=pb[3][0:64, 0:QW2],
                        lhsT=vt_sb[:, vti * 64:(vti + 1) * 64],
                        rhs=pexp[pp][:, 0:QW2], start=first, stop=last),
                        deps=([tv, p3_free] if kt2 == 0 else []))
                    pex_free[pp] = tpv
                    tk2last = tk2
                neg_free = tk2last
                trec = em.v(lambda e: e.reciprocal(out=recr[0:1, 0:QW2],
                                                   in_=pb[2][0:1, 0:QW2]),
                            deps=[tsum, rec_free])
                p2_free = trec
                tbc = em.t(lambda e: e.matmul(
                    out=pb[4][0:64, 0:QW2], lhsT=ones_row32[0:1, 0:64],
                    rhs=recr[0:1, 0:QW2],
                    start=True, stop=True), deps=[trec, tor32, p4_free])
                rec_free = tbc
                tyc = em.v(lambda e: e.tensor_copy(out=ysc[:, 0:QW2],
                                                   in_=pb[3][0:64, 0:QW2]),
                           deps=[tpv, ysc_free])
                p3_free = tyc
                tym = em.v(lambda e, hl=hl, kb0=kb0, qgb=qgb: e.tensor_mul(
                    out=ybuf[hl][:, kb0 + qgb: kb0 + qgb + QW2],
                    in0=ysc[:, 0:QW2], in1=pb[4][0:64, 0:QW2]),
                    deps=[tyc, tbc])
                p4_free = tym
                ysc_free = tym
                y_toks.append(tym)
    ts1 = em.dsync(y_o.ap()[0:64, :], ybuf[0], deps=y_toks, sem="sqy0")
    ts2 = em.dsync(y_o.ap()[64:128, :], ybuf[1], deps=y_toks, sem="sqy1")
    em.finalize([ts1, ts2])
    return nc


def launch7_inmaps(cfg, q_btC, k1, v1):
    T = cfg.T
    BT = B * T
    KW = min(512, T)
    QW2 = min(512, T)
    SQ = QW2 // 128
    q_cm = np.ascontiguousarray(np.asarray(q_btC, BFD).transpose(2, 0, 1).reshape(C, BT))
    kf = np.ascontiguousarray(np.asarray(k1, BFD).transpose(2, 0, 1).reshape(64, BT))
    vt = np.ascontiguousarray(np.asarray(v1, BFD).reshape(BT, 64))
    p = np.arange(128)[:, None]
    x1 = np.arange(2 * KW)[None, :]
    m1 = np.where(x1 <= p + KW, 0.0, -3e38).astype(np.float32)
    x2 = np.arange(128 * (SQ - 1) + QW2)[None, :]
    m2 = np.where(x2 >= p + 128 * (SQ - 1), 0.0, -3e38).astype(np.float32)
    ims = []
    for core in range(NCORES):
        ims.append({
            "qA_in": np.ascontiguousarray(q_cm[(2 * core) * 64:(2 * core + 1) * 64]),
            "qB_in": np.ascontiguousarray(q_cm[(2 * core + 1) * 64:(2 * core + 2) * 64]),
            "kf_in": kf, "vt_in": vt, "m1_in": m1, "m2_in": m2,
        })
    return ims


def launch7_unshard(cfg, youts):
    T = cfg.T
    y = np.empty((B, T, C), np.float32)
    for core in range(NCORES):
        arr = np.asarray(youts[core])
        if arr.dtype == np.uint16:
            arr = arr.view(BFD)
        arr = arr.astype(np.float32).reshape(128, B, T)
        y[:, :, (2 * core) * 64:(2 * core + 2) * 64] = arr.transpose(1, 2, 0)
    return y



def build_launch8(cfg):
    nc = bass.Bass("TRN2", target_bir_lowering=False, debug=False)
    em = Em(nc)
    COLS = cfg.COLS
    LB = COLS // B

    y_in = nc.dram_tensor("y_in", [C, COLS], BF16, kind="ExternalInput")
    xn1_in = nc.dram_tensor("xn1_in", [C, COLS], BF16, kind="ExternalInput")
    xl0_in = nc.dram_tensor("xl0_in", [C, COLS], F32, kind="ExternalInput")
    wcp_in = nc.dram_tensor("wcp_in", [C, C], F32, kind="ExternalInput")
    wts_in = {nm: nc.dram_tensor(nm, [C, C], F32, kind="ExternalInput")
              for nm in ("wfT", "ufT", "wiT", "uiT", "wcT", "ucT", "woT", "uoT")}
    bias_in = {nm: nc.dram_tensor(nm, [128, NCH], F32, kind="ExternalInput").ap()
               for nm in ("bf", "bi", "bc", "bo", "in_c")}
    out_o = nc.dram_tensor("out_o", [C, COLS], F32, kind="ExternalOutput")

    xn1b = nc.alloc_sbuf_tensor("xn1b", [128, NCH * COLS], BF16)[:]
    cb = nc.alloc_sbuf_tensor("cb", [128, NCH * LB], F32)[:]
    cbf = nc.alloc_sbuf_tensor("cbf", [128, NCH * LB], BF16)[:]
    wwb = nc.alloc_sbuf_tensor("wwb", [128, NCH * C], BF16)[:]
    uub = nc.alloc_sbuf_tensor("uub", [128, NCH * C], BF16)[:]
    wob = nc.alloc_sbuf_tensor("wob", [128, NCH * C], BF16)[:]
    yb = nc.alloc_sbuf_tensor("yb", [128, NCH * COLS], BF16)[:]
    ibuf = nc.alloc_sbuf_tensor("ibuf", [128, NCH * LB], F32)[:]
    gtile = [nc.alloc_sbuf_tensor(f"gtile{i}", [128, 512], F32)[:] for i in range(2)]
    xl0ct = nc.alloc_sbuf_tensor("xl0ct", [128, NCH * 512], F32)[:]
    biases = {nm: nc.alloc_sbuf_tensor("b_" + nm, [128, NCH], F32)[:]
              for nm in bias_in}
    ath = [nc.alloc_sbuf_tensor(f"ath{i}", [128, 512], F32)[:] for i in range(2)]
    tmp1, tmp2 = ath[0], ath[1]
    at2 = [nc.alloc_sbuf_tensor(f"at2{i}", [128, 512], F32)[:] for i in range(2)]
    evst = [nc.alloc_sbuf_tensor(f"evst{i}", [128, 512], F32)[:] for i in range(2)]
    pb = [nc.alloc_psum_tensor(f"pb{i}", [128, 512], F32)[:] for i in range(8)]

    tx1 = em.dsync(xn1b.rearrange("p (a x) -> p a x", a=NCH), fm_dram_ap(xn1_in, COLS),
                   sem="sqn")
    tbias = None
    for nm in bias_in:
        tbias = em.dsync(biases[nm], bias_in[nm], sem="sqb")
    cin_toks = []
    for ch in range(NCH):
        t1 = em.v(lambda e, ch=ch: e.memset(cb[:, ch * LB:(ch + 1) * LB], 1.0))
        t2 = em.v(lambda e, ch=ch: e.tensor_scalar_mul(
            out=cb[:, ch * LB:(ch + 1) * LB], in0=cb[:, ch * LB:(ch + 1) * LB],
            scalar1=biases["in_c"][:, ch:ch + 1]), deps=[t1, tbias])
        cin_toks.append(t2)
    gate_cfg = {"f": ("wfT", "ufT", "bf", AF.Sigmoid), "i": ("wiT", "uiT", "bi", AF.Sigmoid),
                "g": ("wcT", "ucT", "bc", AF.Tanh)}
    pfree = [None] * 8
    cup_toks = cin_toks
    cbf_toks = [None] * NCH
    gb_free = {g: [None] * NCH for g in "fig"}
    wld_free = None
    uld_free = None
    for step in range(B):
        new_cbf = []
        for ch in range(NCH):
            tcb = em.v(lambda e, ch=ch: e.tensor_copy(
                out=cbf[:, ch * LB:(ch + 1) * LB], in_=cb[:, ch * LB:(ch + 1) * LB]),
                deps=cup_toks + [cbf_toks[ch]])
            new_cbf.append(tcb)
        cbf_toks = new_cbf
        new_cup = [None] * NCH
        gt_free = [None] * 2
        for g in "fig":
            wnm, unm, bnm, func = gate_cfg[g]
            twl = em.dgps(wwb.rearrange("p (a x) -> p a x", a=NCH),
                          fm_dram_ap(wts_in[wnm], C), sem="gqw0", deps=[wld_free])
            tul = em.dgps(uub.rearrange("p (a x) -> p a x", a=NCH),
                          fm_dram_ap(wts_in[unm], C), sem="gqw1", deps=[uld_free])
            for oc in range(NCH):
                bank = pb[oc % 4]
                tk = None
                for kc in range(NCH):
                    tk = em.t(lambda e, oc=oc, kc=kc, step=step, bank=bank: e.matmul(
                        out=bank[:, 0:LB],
                        lhsT=wwb[:, kc * C + oc * 128:kc * C + (oc + 1) * 128],
                        rhs=xn1b[:, kc * COLS + step * LB: kc * COLS + (step + 1) * LB],
                        start=(kc == 0), stop=False),
                        deps=([twl, tx1, pfree[oc % 4]] if kc == 0 else []))
                for kc in range(NCH):
                    tk = em.t(lambda e, oc=oc, kc=kc, bank=bank: e.matmul(
                        out=bank[:, 0:LB],
                        lhsT=uub[:, kc * C + oc * 128:kc * C + (oc + 1) * 128],
                        rhs=cbf[:, kc * LB:(kc + 1) * LB],
                        start=False, stop=(kc == NCH - 1)),
                        deps=([tul] + cbf_toks) if kc == 0 else [])
                if g == "i":
                    tev = em.a(lambda e, oc=oc, bank=bank, bnm=bnm, func=func:
                               e.activation(
                        out=ibuf[:, oc * LB:(oc + 1) * LB], in_=bank[:, 0:LB],
                        func=func, bias=biases[bnm][:, oc:oc + 1]),
                        deps=[tk, tbias, gb_free["i"][oc]])
                    pfree[oc % 4] = tev
                    gb_free["i"][oc] = tev
                else:
                    pp = oc % 2
                    tev = em.a(lambda e, oc=oc, bank=bank, bnm=bnm, func=func, pp=pp:
                               e.activation(
                        out=gtile[pp][:, 0:LB], in_=bank[:, 0:LB],
                        func=func, bias=biases[bnm][:, oc:oc + 1]),
                        deps=[tk, tbias, gt_free[pp]])
                    pfree[oc % 4] = tev
                    if g == "f":
                        tcu = em.v(lambda e, oc=oc, pp=pp: e.tensor_mul(
                            out=cb[:, oc * LB:(oc + 1) * LB],
                            in0=cb[:, oc * LB:(oc + 1) * LB], in1=gtile[pp][:, 0:LB]),
                            deps=[tev] + cup_toks + ([new_cup[oc]] if new_cup[oc] else []))
                    else:
                        tmu = em.v(lambda e, oc=oc, pp=pp: e.tensor_mul(
                            out=gtile[pp][:, 0:LB], in0=gtile[pp][:, 0:LB],
                            in1=ibuf[:, oc * LB:(oc + 1) * LB]),
                            deps=[tev, gb_free["i"][oc]])
                        tcu = em.v(lambda e, oc=oc, pp=pp: e.tensor_add(
                            out=cb[:, oc * LB:(oc + 1) * LB],
                            in0=cb[:, oc * LB:(oc + 1) * LB], in1=gtile[pp][:, 0:LB]),
                            deps=[tmu, new_cup[oc]])
                        gb_free["i"][oc] = tmu
                    gt_free[pp] = tcu
                    new_cup[oc] = tcu
            wld_free = tk
            uld_free = tk
        cup_toks = new_cup
    new_cbf = []
    for ch in range(NCH):
        tcb = em.v(lambda e, ch=ch: e.tensor_copy(
            out=cbf[:, ch * LB:(ch + 1) * LB], in_=cb[:, ch * LB:(ch + 1) * LB]),
            deps=cup_toks + [cbf_toks[ch]])
        new_cbf.append(tcb)
    cbf_toks = new_cbf
    ty = em.dsync(yb.rearrange("p (a x) -> p a x", a=NCH), fm_dram_ap(y_in, COLS),
                  sem="sqy", deps=cup_toks + cbf_toks)
    twc = em.dgps(wwb.rearrange("p (a x) -> p a x", a=NCH),
                  wcp_in.ap().rearrange("(a p) x -> p a x", p=128), sem="gqw0",
                  deps=[wld_free])
    two = em.dgps(uub.rearrange("p (a x) -> p a x", a=NCH), fm_dram_ap(wts_in["woT"], C),
                  sem="gqw1", deps=[uld_free])
    tuo = em.dgps(wob.rearrange("p (a x) -> p a x", a=NCH), fm_dram_ap(wts_in["uoT"], C),
                  sem="gqw2")
    store_toks = []
    xl_free = None
    ath_free = [None] * 2
    at2_free = [None] * 2
    ev_free = [None] * 2
    for ci, (c0, cw) in enumerate(col_tiles(COLS, LB)):
        txl = em.dsync(xl0ct.rearrange("p (a x) -> p a x", a=NCH)[:, :, 0:cw],
                       fm_dram_ap(xl0_in, COLS)[:, :, c0:c0 + cw], sem="sqxl",
                       deps=[xl_free])
        for oc in range(NCH):
            pp = oc % 2
            bank_h = pb[4 + pp]
            bank_a = pb[6 + pp]
            tkh = None
            for kc in range(NCH):
                tkh = em.t(lambda e, oc=oc, kc=kc, c0=c0, cw=cw, bank_h=bank_h: e.matmul(
                    out=bank_h[:, 0:cw],
                    lhsT=uub[:, kc * C + oc * 128:kc * C + (oc + 1) * 128],
                    rhs=xn1b[:, kc * COLS + c0: kc * COLS + c0 + cw],
                    start=(kc == 0), stop=False),
                    deps=([two, tx1, pfree[4 + pp]] if kc == 0 else []))
            for kc in range(NCH):
                tkh = em.t(lambda e, oc=oc, kc=kc, cw=cw, bank_h=bank_h: e.matmul(
                    out=bank_h[:, 0:cw],
                    lhsT=wob[:, kc * C + oc * 128:kc * C + (oc + 1) * 128],
                    rhs=cbf[:, kc * LB: kc * LB + cw],
                    start=False, stop=(kc == NCH - 1)),
                    deps=([tuo] + cbf_toks) if kc == 0 else [])
            tevh = em.a(lambda e, oc=oc, cw=cw, bank_h=bank_h, pp=pp: e.activation(
                out=ath[pp][:, 0:cw], in_=bank_h[:, 0:cw], func=AF.Tanh,
                bias=biases["bo"][:, oc:oc + 1]), deps=[tkh, tbias, ath_free[pp]])
            pfree[4 + pp] = tevh
            tka = None
            for kc in range(NCH):
                tka = em.t(lambda e, oc=oc, kc=kc, c0=c0, cw=cw, bank_a=bank_a: e.matmul(
                    out=bank_a[:, 0:cw],
                    lhsT=wwb[:, kc * C + oc * 128:kc * C + (oc + 1) * 128],
                    rhs=yb[:, kc * COLS + c0: kc * COLS + c0 + cw],
                    start=(kc == 0), stop=(kc == NCH - 1)),
                    deps=([twc, ty, pfree[6 + pp]] if kc == 0 else []))
            ta1 = em.v(lambda e, cw=cw, bank_a=bank_a, pp=pp: e.tensor_add(
                out=at2[pp][:, 0:cw], in0=ath[pp][:, 0:cw], in1=bank_a[:, 0:cw]),
                deps=[tka, tevh, at2_free[pp]])
            pfree[6 + pp] = ta1
            ath_free[pp] = ta1
            ta2 = em.v(lambda e, oc=oc, cw=cw, pp=pp: e.tensor_add(
                out=evst[pp][:, 0:cw], in0=at2[pp][:, 0:cw],
                in1=xl0ct[:, oc * 512:oc * 512 + cw]), deps=[ta1, txl, ev_free[pp]])
            at2_free[pp] = ta2
            tst = em.dsync(out_o.ap()[oc * 128:(oc + 1) * 128, c0:c0 + cw],
                           evst[pp][:, 0:cw], deps=[ta2], sem=f"sst{pp}")
            ev_free[pp] = tst
            store_toks.append(tst)
        xl_free = store_toks[-1]
    em.finalize(store_toks)
    return nc



def _np(a):
    a = np.asarray(a)
    if a.dtype == np.uint16:
        a = a.view(BFD)
    return a


def _runner(nc, in_maps):
    res = run_bass_kernel_spmd(nc, in_maps, core_ids=list(range(NCORES)))
    return res.results


def unshard_vt(cfg, shards):
    out = np.empty((B, cfg.T, 64), dtype=shards[0].dtype)
    for core in range(NCORES):
        s = shards[core]
        for i, (b, t0) in enumerate(seg_starts_global(cfg, core)):
            out[b, t0:t0 + cfg.L] = s[i * cfg.L:(i + 1) * cfg.L]
    return out


def kernel(**inputs):
    d = inputs
    f32 = lambda k: np.asarray(d[k], np.float32)
    x = f32("L0_attn_x")
    cfg = Cfg(x.shape[1])

    nc = build_launch1(cfg)
    im = launch1_inmaps(cfg, x, d["L0_attn_ln1_w"], d["L0_attn_ln1_b"],
                        d["L0_attn_tmk"], d["L0_attn_tmv"], d["L0_attn_tmr"],
                        d["L0_attn_att_kw"], d["L0_attn_att_vw"], d["L0_attn_att_rw"])
    r = _runner(nc, im)
    xn = unshard_cols(cfg, [_np(c["xn_o"]) for c in r])
    kk = unshard_cols(cfg, [_np(c["k_o"]) for c in r])
    vv = unshard_cols(cfg, [_np(c["v_o"]) for c in r])
    sr = unshard_cols(cfg, [_np(c["sr_o"]) for c in r])

    nc = build_launch2(cfg, L2=1024)
    im = launch2_inmaps(cfg, kk, vv, d["L0_attn_time_decay"], d["L0_attn_time_first"])
    r = _runner(nc, im)
    wkv = launch2_unshard(cfg, [_np(c["wkv_o"]) for c in r])

    nc = build_launch3(cfg)
    owT = np.ascontiguousarray(f32("L0_attn_att_ow").T)
    im = [{"wkv_in": shard_cols(cfg, c, wkv), "sr_in": shard_cols(cfg, c, sr),
           "xn_in": shard_cols(cfg, c, xn), "owT": owT} for c in range(NCORES)]
    r = _runner(nc, im)
    x1 = unshard_cols(cfg, [_np(c["x1_o"]) for c in r])

    nc = build_launch4a(cfg)
    im = []
    for c in range(NCORES):
        x1s = shard_cols(cfg, c, x1)
        hxh, hmask = halo_of(cfg, c, x1)
        im.append({"x1in": np.ascontiguousarray(np.concatenate([x1s, hxh], 1), np.float32),
                   "maskin": hmask,
                   "ln2w": chunked(d["L0_attn_ln2_w"]), "ln2b": chunked(d["L0_attn_ln2_b"]),
                   "ftmk": chunked(d["L0_attn_f_tmk"]), "ftmr": chunked(d["L0_attn_f_tmr"])})
    r = _runner(nc, im)
    xk2 = unshard_cols(cfg, [_np(c["xk2_o"]) for c in r])
    xr2 = unshard_cols(cfg, [_np(c["xr2_o"]) for c in r])

    nc = build_launch4b(cfg)
    fkwT = np.ascontiguousarray(f32("L0_attn_f_kw").T)
    fvwT = np.ascontiguousarray(f32("L0_attn_f_vw").T)
    frwT = np.ascontiguousarray(f32("L0_attn_f_rw").T)
    im = [{"xk2_in": shard_cols(cfg, c, xk2.astype(BFD)),
           "xr2_in": shard_cols(cfg, c, xr2.astype(BFD)),
           "x1_in": shard_cols(cfg, c, x1),
           "fkwT": fkwT, "fvwT": fvwT, "frwT": frwT} for c in range(NCORES)]
    r = _runner(nc, im)
    ao = unshard_cols(cfg, [_np(c["ao_o"]) for c in r])

    nc = build_launch5(cfg)
    im = [{"xn_in": shard_cols(cfg, c, xn), "x_in": shard_cols(cfg, c, x),
           "ao_in": shard_cols(cfg, c, ao),
           "wgT": np.ascontiguousarray(f32("L0_ffn_w_gate").T),
           "wuT": np.ascontiguousarray(f32("L0_ffn_w_up").T),
           "wdT": np.ascontiguousarray(f32("L0_ffn_w_down").T)} for c in range(NCORES)]
    r = _runner(nc, im)
    xl0 = unshard_cols(cfg, [_np(c["xl0_o"]) for c in r])

    nc = build_launch6(cfg)
    im = [{"xl0_in": shard_cols(cfg, c, xl0),
           "wcd_in": f32("L1_attn_Wc_down"),
           "wqu_in": f32("L1_attn_Wq_up"),
           "wkv_in": f32("L1_attn_Wkv_up")} for c in range(NCORES)]
    r = _runner(nc, im)
    xn1 = unshard_cols(cfg, [_np(c["xn1_o"]) for c in r])
    qq = unshard_cols(cfg, [_np(c["q_o"]) for c in r])
    k1 = unshard_cols(cfg, [_np(c["k_o"]) for c in r])
    v1 = unshard_vt(cfg, [_np(c["vt_o"]) for c in r])

    nc = build_launch7(cfg)
    im = launch7_inmaps(cfg, qq, k1, v1)
    r = _runner(nc, im)
    y = launch7_unshard(cfg, [_np(c["y_o"]) for c in r])

    nc = build_launch8(cfg)
    im = []
    for c in range(NCORES):
        m = {"y_in": shard_cols(cfg, c, y.astype(BFD)),
             "xn1_in": shard_cols(cfg, c, xn1),
             "xl0_in": shard_cols(cfg, c, xl0),
             "wcp_in": f32("L1_attn_Wc_proj")}
        for nm, key in (("wfT", "L1_ffn_wf"), ("ufT", "L1_ffn_uf"),
                        ("wiT", "L1_ffn_wi"), ("uiT", "L1_ffn_ui"),
                        ("wcT", "L1_ffn_wc"), ("ucT", "L1_ffn_uc"),
                        ("woT", "L1_ffn_wo"), ("uoT", "L1_ffn_uo")):
            m[nm] = np.ascontiguousarray(f32(key).T)
        for nm, key in (("bf", "L1_ffn_bf"), ("bi", "L1_ffn_bi"), ("bc", "L1_ffn_bc"),
                        ("bo", "L1_ffn_bo"), ("in_c", "L1_ffn_in_c")):
            m[nm] = chunked(d[key])
        im.append(m)
    r = _runner(nc, im)
    out = unshard_cols(cfg, [_np(c["out_o"]).astype(np.float32) for c in r])
    return np.ascontiguousarray(out)
